# version 11
# baseline (speedup 1.0000x reference)
import numpy as np
import ml_dtypes
import concourse.bass as bass
import concourse.mybir as mybir
from concourse.bass_utils import run_bass_kernel_spmd
from contextlib import ExitStack

F32 = mybir.dt.float32
BF16 = mybir.dt.bfloat16
I32 = mybir.dt.int32
AF = mybir.ActivationFunctionType
ALU = mybir.AluOpType
AX = mybir.AxisListType

B = 8
S = 4096
D = 1024
NH = 16
DH = 64
DC = D // 128
TC = 512
NCH = S // TC
NKT = S // 128
HID = 3072
HCN = HID // 128
EPS = 1e-5
QSCALE = 0.125
QDEQ = 1.0 / 127.0
DIFF_SCALE = 1.0 / (8 * 16 * 4096 * 64)
THRESH = 0.01
FACTOR = 0.1

BF16NP = ml_dtypes.bfloat16


def set_shape(s=4096, tc=512):
    global S, TC, NCH, NKT, DIFF_SCALE
    S, TC = s, tc
    NCH = S // TC
    NKT = S // 128
    DIFF_SCALE = 1.0 / (8 * 16 * S * 64)


class Sem:
    def __init__(self, nc, name):
        self.h = nc.alloc_semaphore(name)
        self.n = 0

    def inc(self, inst, k=1):
        inst.then_inc(self.h, k)
        self.n += k
        return self.n

    def wait(self, eng, val=None):
        eng.wait_ge(self.h, self.n if val is None else val)


class P:

    def __init__(self, nc):
        self.nc = nc
        self.pe = Sem(nc, "s_pe")
        self.act = Sem(nc, "s_act")
        self.dve = Sem(nc, "s_dve")
        self.pool = Sem(nc, "s_pool")
        self.dsy = Sem(nc, "d_sync")
        self.dsc = Sem(nc, "d_scal")
        self.dgp = Sem(nc, "d_gp")
        self.dou = Sem(nc, "d_out")

    def bar(self):
        self.nc.all_engine_barrier()



def ln_rows(p, stats_ps, n, ncols, inv_n, rows):
    nc = p.nc
    rstd = rows[:, 0, :]
    brow = rows[:, 1, :]
    mean = rows[:, 2, :]
    m2 = rows[:, 3, :]
    lnt = rows[:, 4, :]
    p.dve.inc(nc.vector.tensor_scalar_mul(mean, stats_ps[0:n, :], inv_n))
    p.dve.inc(nc.vector.tensor_scalar_mul(rstd, stats_ps[32:32 + n, :], inv_n))
    psum_free = p.dve.n
    p.dve.wait(nc.vector)
    p.dve.inc(nc.vector.tensor_mul(m2, mean, mean))
    p.dve.wait(nc.vector)
    p.dve.inc(nc.vector.scalar_tensor_tensor(rstd, rstd, EPS, m2, ALU.add, ALU.subtract))
    p.dve.wait(nc.scalar)
    p.act.inc(nc.scalar.activation(lnt, rstd, AF.Ln, bias=0.0, scale=1.0))
    p.act.wait(nc.scalar)
    p.act.inc(nc.scalar.activation(rstd, lnt, AF.Exp, bias=0.0, scale=-0.5))
    p.act.wait(nc.vector)
    p.dve.inc(nc.vector.scalar_tensor_tensor(brow, mean, -1.0, rstd, ALU.mult, ALU.mult))
    return psum_free


def stat_sums(p, stats_ps, xb, xsqb, chunks, ncols, ones_col):
    nc = p.nc
    for c in range(chunks):
        nc.tensor.matmul(stats_ps[0:1, :], ones_col, xb[:, c, :],
                         start=(c == 0), stop=(c == chunks - 1),
                         tile_position=(0, 0))
    for c in range(chunks):
        inst = nc.tensor.matmul(stats_ps[32:33, :], ones_col, xsqb[:, c, :],
                                start=(c == 0), stop=(c == chunks - 1),
                                tile_position=(0, 32))
    p.pe.inc(inst)



def build_program(phases=("pre", "t16", "a", "gf", "b", "c", "attn", "ffn"),
                  debug=False):
    nc = bass.Bass("TRN2", target_bir_lowering=False)
    p = P(nc)
    dt = {}

    def din(name, shape, dtyp=F32):
        dt[name] = nc.dram_tensor(name, list(shape), dtyp, kind="ExternalInput").ap()
        return dt[name]

    def dint(name, shape, dtyp=F32):
        dt[name] = nc.dram_tensor(name, list(shape), dtyp, kind="Internal").ap()
        return dt[name]

    def dout(name, shape, dtyp=F32):
        dt[name] = nc.dram_tensor(name, list(shape), dtyp, kind="ExternalOutput").ap()
        return dt[name]

    din("x_fm", (D, S))
    din("x16_fm", (D, 128))
    din("qwT", (D, D)); din("kwT", (D, D)); din("vwT", (D, D)); din("owT", (D, D))
    din("lqwT", (DH, DH)); din("lkwT", (DH, DH)); din("lvwT", (DH, DH))
    din("qb_c", (128, DC)); din("vb_c", (128, DC)); din("ob_c", (128, DC))
    din("lna_c", (128, DC)); din("lnb_c", (128, DC))
    din("lnc2", (128, 1)); din("lnd2", (128, 1))
    din("lqb2", (128, 1)); din("lkb2", (128, 1)); din("lvb2", (128, 1))
    din("cnw1", (D, D)); din("cnw2", (D, 512)); din("cnw3", (512, 256)); din("cnw4", (256, 1))
    din("gcw1", (D, 512)); din("gcw2", (512, 256)); din("gcw3", (256, 1))
    din("cnb1_c", (128, 8)); din("cnb2_c", (128, 4)); din("cnb3_c", (128, 2)); din("cnb4", (1, 1))
    din("gcb1_c", (128, 4)); din("gcb2_c", (128, 2)); din("gcb3", (1, 1))
    din("ar", (1, 1))
    din("wqkv", (D, 3 * D)); din("bq_c", (128, DC)); din("bk_c", (128, DC)); din("bv_row", (1, D))
    din("wproj", (D, D)); din("bp_c", (128, DC))
    din("wgT", (D, HID), I32); din("wuT", (D, HID), I32); din("wdT", (HID, D), I32)
    din("ident", (128, 128), BF16)
    din("mask16", (128, 128), BF16)
    din("lneg", (128, 128), BF16)
    din("rbig", (128, 896), BF16)
    din("sel2r", (2, 128))
    din("sel33", (33, 128))
    din("sel2c", (128, 2), BF16)

    dsc_kind = dout if debug else dint
    dsc_kind("xn_sp", (D, S))
    dsc_kind("x1_sp", (D, S))
    dsc_kind("xattn_sp", (D, S), BF16)
    dsc_kind("xn1_sp", (D, S), BF16)
    dout("y", (D, S))
    if debug:
        for nm, shp, dty in (("dbg_q16", (128, DC, 128), F32), ("dbg_k16", (128, DC, 128), BF16),
                             ("dbg_v16", (128, DC, 128), BF16), ("dbg_kn16", (128, DC, 128), BF16),
                             ("dbg_vi", (128, NH * DH), BF16), ("dbg_attn", (128, DC, 128), F32),
                             ("dbg_y16", (128, DC, 16), F32), ("dbg_xn16", (128, DC, 128), F32),
                             ("dbg_x1pre", (128, DC, 16), F32)):
            dout(nm, shp, dty)

    es_glob = ExitStack()

    sb = {}

    def asb(name, shape, dtyp):
        h = es_glob.enter_context(nc.sbuf_tensor(name, list(shape), dtyp))
        sb[name] = h
        return h

    ones_col_bf = asb("ones_col_bf", (128, 1), BF16)
    ones_row_f = asb("ones_row_f", (1, 128), F32)
    ones_row_bf = asb("ones_row_bf", (1, 128), BF16)
    ident = asb("ident_sb", (128, 128), BF16)
    mask16 = asb("mask16_sb", (128, 128), BF16)
    lneg = asb("lneg_sb", (128, 128), BF16)
    rbig = asb("rbig_sb", (128, 896), BF16)
    sel2r = asb("sel2r_sb", (2, 128), F32)
    sel33 = asb("sel33_sb", (33, 128), F32)
    sel2c = asb("sel2c_sb", (128, 2), BF16)
    cols = {}
    for nm, w in (("qb_c", DC), ("vb_c", DC), ("ob_c", DC), ("lna_c", DC),
                  ("lnb_c", DC), ("bq_c", DC), ("bk_c", DC), ("bp_c", DC),
                  ("cnb1_c", 8), ("cnb2_c", 4), ("cnb3_c", 2),
                  ("gcb1_c", 4), ("gcb2_c", 2)):
        cols[nm] = asb(nm + "_sb", (128, w), F32)
    for nm in ("lnc2", "lnd2", "lqb2", "lkb2", "lvb2"):
        cols[nm] = asb(nm + "_sb", (128, 1), F32)
    for nm in ("cnb4", "gcb3", "ar"):
        cols[nm] = asb(nm + "_sb", (1, 1), F32)
    bq8 = asb("bq8", (128, DC), F32)
    gacc = asb("gacc", (128, DC, NCH), F32)
    gctx_bf = asb("gctx_bf", (128, DC), BF16)
    c1p = asb("c1p", (1, 1), F32)
    c2p = asb("c2p", (1, 1), F32)
    y16 = asb("y16", (128, DC, 16), F32)

    nc.vector.memset(ones_col_bf[:], 1.0)
    nc.vector.memset(ones_row_f[:], 1.0)
    p.dve.inc(nc.vector.memset(ones_row_bf[:], 1.0))
    for nm in ("ident", "mask16", "lneg", "rbig"):
        p.dsy.inc(nc.sync.dma_start(sb[nm + "_sb"][:], dt[nm][:]), 16)
    p.dsy.inc(nc.sync.dma_start(sel2r[:], dt["sel2r"][:]), 16)
    p.dsy.inc(nc.sync.dma_start(sel33[:], dt["sel33"][:]), 16)
    p.dsy.inc(nc.sync.dma_start(sel2c[:], dt["sel2c"][:]), 16)
    for nm in cols:
        p.dsy.inc(nc.sync.dma_start(cols[nm][:], dt[nm][:]), 16)
    p.dsy.wait(nc.vector)
    p.dve.inc(nc.vector.tensor_scalar_mul(bq8[:], cols["bq_c"][:], QSCALE))
    p.bar()

    if "t16" in phases:
        with ExitStack() as es:
            t = lambda n, s, d=F32: es.enter_context(nc.sbuf_tensor(n, list(s), d))
            x16 = t("x16", (128, DC, 128))
            x16b = t("x16b", (128, DC, 128), BF16)
            x16sq = t("x16sq", (128, DC, 128), BF16)
            xn16 = t("xn16", (128, DC, 128))
            a16 = t("a16", (128, DC, 128), BF16)
            b16 = t("b16", (128, DC, 128), BF16)
            wq = t("wq16", (128, DC, D), BF16)
            wk = t("wk16", (128, DC, D), BF16)
            wv = t("wv16", (128, DC, D), BF16)
            wo = t("wo16", (128, DC, D), BF16)
            lbd = {}
            for nm in ("lqwT", "lkwT", "lvwT"):
                lbd[nm] = t(nm + "_bd", (128, 128), BF16)
            rows2 = t("rows2", (1, 5, 128))
            rows4 = t("rows4", (2, 5, 128))
            q16 = t("q16", (128, DC, 128))
            k16 = t("k16", (128, DC, 128), BF16)
            v16 = t("v16", (128, DC, 128), BF16)
            kn16 = t("kn16", (128, DC, 128), BF16)
            vi_tok = t("vi_tok", (128, NH * DH), BF16)
            kk32 = t("kk32", (128, 128))
            kkb = t("kkb", (128, 128), BF16)
            kksq = t("kksq", (128, 128), BF16)
            qcur = [t("qcurA", (128, DC, 128)), t("qcurB", (128, DC, 128))]
            qcb = t("qcb", (128, DC, 128), BF16)
            attn_o = [t("attnA", (128, DC, 128)), t("attnB", (128, DC, 128))]
            prev = [t("prevA", (128, DC, 128)), t("prevB", (128, DC, 128))]
            iout = t("iout", (128, DC, 128))
            pn = t("pn", (128, 128), BF16)
            pt = t("pt", (128, 128), BF16)
            rowred = t("rowred", (128, 1))
            rrecip = t("rrecip", (128, 1))
            dtmp = t("dtmp", (128, 128))
            dcol = t("dcol", (128, DC))
            dcol1 = t("dcol1", (128, 1))
            dcolb = t("dcolb", (128, 1), BF16)
            dscal = t("dscal", (1, 1))
            flt = t("flt", (1, 6))
            donec = t("donec", (128, 1))
            omc = t("omc", (128, 1))
            omnc = t("omnc", (128, 1))

            ps_a = es.enter_context(nc.psum_tensor("ps16a", [128, 128], F32))
            ps_b = es.enter_context(nc.psum_tensor("ps16b", [128, 128], F32))
            ps_st = es.enter_context(nc.psum_tensor("ps16st", [34, 128], F32))
            ps_bc = es.enter_context(nc.psum_tensor("ps16bc", [128, 128], F32))
            ps_bc2 = es.enter_context(nc.psum_tensor("ps16bc2", [128, 128], F32))
            ps_t = es.enter_context(nc.psum_tensor("ps16t", [128, 128], BF16))
            ps_s = es.enter_context(nc.psum_tensor("ps16s", [1, 1], F32))
            ps_c = es.enter_context(nc.psum_tensor("ps16c", [128, 1], F32))

            p.dsy.inc(nc.sync.dma_start(
                x16[:], dt["x16_fm"].rearrange("(c p) t -> p c t", p=128)), 16)
            for wsb, wnm in ((wq, "qwT"), (wk, "kwT"), (wv, "vwT"), (wo, "owT")):
                p.dgp.inc(nc.gpsimd.dma_start(
                    wsb[:], dt[wnm].rearrange("(c p) m -> p c m", p=128)), 16)
            for nm in ("lqwT", "lkwT", "lvwT"):
                p.dve.inc(nc.vector.memset(lbd[nm][0:64, 64:128], 0.0))
                p.dve.inc(nc.vector.memset(lbd[nm][64:128, 0:64], 0.0))
                p.dgp.inc(nc.gpsimd.dma_start(lbd[nm][0:64, 0:64], dt[nm][:]), 16)
                p.dgp.inc(nc.gpsimd.dma_start(lbd[nm][64:128, 64:128], dt[nm][:]), 16)

            p.dsy.wait(nc.vector)
            p.dsy.wait(nc.scalar)
            p.dve.inc(nc.vector.tensor_copy(x16b[:], x16[:]))
            p.act.inc(nc.scalar.activation(x16sq[:], x16[:], AF.Square))
            p.dve.wait(nc.tensor)
            p.act.wait(nc.tensor)
            stat_sums(p, ps_st, x16b, x16sq, DC, 128, ones_col_bf[:])
            p.pe.wait(nc.vector)
            p.pe.wait(nc.scalar)
            ln_rows(p, ps_st, 1, 128, 1.0 / D, rows2)
            p.dve.wait(nc.tensor)
            p.act.wait(nc.tensor)
            nc.tensor.matmul(ps_bc[:], ones_row_f[:], rows2[:, 0, :], start=True, stop=True)
            p.pe.inc(nc.tensor.matmul(ps_bc2[:], ones_row_f[:], rows2[:, 1, :], start=True, stop=True))
            p.pe.wait(nc.vector)
            for c in range(DC):
                p.dve.inc(nc.vector.tensor_mul(iout[:, c, :], x16[:, c, :], ps_bc[:]))
            p.dve.wait(nc.vector)
            for c in range(DC):
                p.dve.inc(nc.vector.tensor_add(xn16[:, c, :], iout[:, c, :], ps_bc2[:]))
            p.dve.wait(nc.scalar)
            for c in range(DC):
                p.act.inc(nc.scalar.activation(a16[:, c, :], xn16[:, c, :], AF.Identity,
                                               bias=0.0, scale=cols["lna_c"][:, c:c + 1]))
                p.act.inc(nc.scalar.activation(b16[:, c, :], xn16[:, c, :], AF.Identity,
                                               bias=0.0, scale=cols["lnb_c"][:, c:c + 1]))

            p.act.wait(nc.tensor)
            p.dgp.wait(nc.tensor)
            def proj16(wsb, src, dst, bias_cols, dst_bf=None):
                for mc in range(DC):
                    ps = (ps_a, ps_b)[mc % 2]
                    if dve_free[mc % 2]:
                        p.dve.wait(nc.tensor, dve_free[mc % 2])
                    for kc in range(DC):
                        inst = nc.tensor.matmul(ps[:], wsb[:, kc, mc * 128:(mc + 1) * 128],
                                                src[:, kc, :], start=(kc == 0), stop=(kc == DC - 1))
                    p.pe.inc(inst)
                    p.pe.wait(nc.vector)
                    if bias_cols is None:
                        i2 = nc.vector.tensor_copy(dst[:, mc, :], ps[:])
                    else:
                        i2 = nc.vector.tensor_scalar_add(dst[:, mc, :], ps[:], bias_cols[:, mc:mc + 1])
                    m = p.dve.inc(i2)
                    if dst_bf is not None:
                        p.dve.wait(nc.vector)
                        m = p.dve.inc(nc.vector.tensor_copy(dst_bf[:, mc, :], dst[:, mc, :]))
                    dve_free[mc % 2] = m
            dve_free = {0: 0, 1: 0}
            proj16(wq, a16, q16, cols["qb_c"])
            proj16(wk, b16, k16, None)
            proj16(wv, b16, v16, cols["vb_c"])

            p.dve.wait(nc.tensor)
            for hp in range(DC):
                p.pe.inc(nc.tensor.matmul(ps_a[:], lbd["lkwT"][:], k16[:, hp, :],
                                          start=True, stop=True))
                p.pe.wait(nc.vector)
                p.pe.wait(nc.scalar)
                p.dve.inc(nc.vector.tensor_scalar_add(kk32[:], ps_a[:], cols["lkb2"][:]))
                p.dve.wait(nc.vector)
                p.dve.wait(nc.scalar)
                p.dve.inc(nc.vector.tensor_copy(kkb[:], kk32[:]))
                p.act.inc(nc.scalar.activation(kksq[:], kk32[:], AF.Square))
                p.dve.wait(nc.tensor)
                p.act.wait(nc.tensor)
                nc.tensor.matmul(ps_st[0:2, :], sel2c[:, 0:2], kkb[:],
                                 start=True, stop=True, tile_position=(0, 0))
                inst = nc.tensor.matmul(ps_st[32:34, :], sel2c[:, 0:2], kksq[:],
                                        start=True, stop=True, tile_position=(0, 32))
                p.pe.inc(inst)
                p.pe.wait(nc.vector)
                p.pe.wait(nc.scalar)
                ln_rows(p, ps_st, 2, 128, 1.0 / DH, rows4)
                p.dve.wait(nc.tensor)
                p.act.wait(nc.tensor)
                nc.tensor.matmul(ps_bc[:], sel2r[:], rows4[:, 0, :], start=True, stop=True)
                p.pe.inc(nc.tensor.matmul(ps_bc2[:], sel2r[:], rows4[:, 1, :], start=True, stop=True))
                p.pe.wait(nc.vector)
                p.dve.inc(nc.vector.tensor_mul(dtmp[:], kk32[:], ps_bc[:]))
                p.dve.wait(nc.vector)
                p.dve.inc(nc.vector.tensor_add(dtmp[:], dtmp[:], ps_bc2[:]))
                p.dve.wait(nc.scalar)
                p.act.inc(nc.scalar.activation(kn16[:, hp, :], dtmp[:], AF.Identity,
                                               bias=0.0, scale=cols["lnd2"][:]))
                p.act.wait(nc.tensor)
                p.pe.inc(nc.tensor.matmul(ps_b[:], lbd["lvwT"][:], v16[:, hp, :],
                                          start=True, stop=True))
                p.pe.wait(nc.vector)
                p.dve.inc(nc.vector.tensor_scalar_add(kkb[:], ps_b[:], cols["lvb2"][:]))
                p.dve.wait(nc.tensor)
                p.pe.inc(nc.tensor.transpose(ps_t[:], kkb[:], ident[:]))
                p.pe.wait(nc.vector)
                p.dve.inc(nc.vector.tensor_copy(
                    vi_tok[:, hp * 128:(hp + 1) * 128], ps_t[:]))
                p.dve.wait(nc.tensor)

            for c in range(DC):
                p.dve.inc(nc.vector.tensor_copy(qcur[0][:, c, :], q16[:, c, :]))
            p.dve.wait(nc.vector)

            cur, nxt = 0, 1
            for it in range(3):
                p.dve.wait(nc.vector)
                for c in range(DC):
                    p.dve.inc(nc.vector.tensor_copy(qcb[:, c, :], qcur[cur][:, c, :]))
                p.dve.wait(nc.tensor)
                for hp in range(DC):
                    p.pe.inc(nc.tensor.matmul(ps_a[:], lbd["lqwT"][:], qcb[:, hp, :],
                                              start=True, stop=True))
                    p.pe.wait(nc.vector)
                    p.pe.wait(nc.scalar)
                    p.dve.inc(nc.vector.tensor_scalar_add(kk32[:], ps_a[:], cols["lqb2"][:]))
                    p.dve.wait(nc.vector)
                    p.dve.wait(nc.scalar)
                    p.dve.inc(nc.vector.tensor_copy(kkb[:], kk32[:]))
                    p.act.inc(nc.scalar.activation(kksq[:], kk32[:], AF.Square))
                    p.dve.wait(nc.tensor)
                    p.act.wait(nc.tensor)
                    nc.tensor.matmul(ps_st[0:2, :], sel2c[:, 0:2], kkb[:],
                                     start=True, stop=True, tile_position=(0, 0))
                    inst = nc.tensor.matmul(ps_st[32:34, :], sel2c[:, 0:2], kksq[:],
                                            start=True, stop=True, tile_position=(0, 32))
                    p.pe.inc(inst)
                    p.pe.wait(nc.vector)
                    p.pe.wait(nc.scalar)
                    ln_rows(p, ps_st, 2, 128, 1.0 / DH, rows4)
                    p.dve.wait(nc.tensor)
                    p.act.wait(nc.tensor)
                    nc.tensor.matmul(ps_bc[:], sel2r[:], rows4[:, 0, :], start=True, stop=True)
                    p.pe.inc(nc.tensor.matmul(ps_bc2[:], sel2r[:], rows4[:, 1, :],
                                              start=True, stop=True))
                    p.pe.wait(nc.vector)
                    p.dve.inc(nc.vector.tensor_mul(dtmp[:], kk32[:], ps_bc[:]))
                    p.dve.wait(nc.vector)
                    p.dve.inc(nc.vector.tensor_add(dtmp[:], dtmp[:], ps_bc2[:]))
                    p.dve.wait(nc.scalar)
                    p.pe.wait(nc.scalar)
                    p.act.inc(nc.scalar.activation(qcb[:, hp, :], dtmp[:], AF.Identity,
                                                   bias=0.0, scale=cols["lnc2"][:]))
                p.act.wait(nc.tensor)
                for h in range(NH):
                    hp, base = h // 2, (h % 2) * 64
                    qn_h = qcb[base:base + 64, hp, :]
                    kn_h = kn16[base:base + 64, hp, :]
                    p.pe.inc(nc.tensor.matmul(ps_a[:], qn_h, kn_h, start=True, stop=True))
                    p.pe.wait(nc.scalar)
                    p.act.inc(nc.scalar.activation(pn[:], ps_a[:], AF.Exp, bias=0.0,
                                                   scale=QSCALE))
                    p.act.wait(nc.vector)
                    p.dve.inc(nc.vector.tensor_mul(pn[:], pn[:], mask16[:]))
                    p.dve.wait(nc.vector)
                    p.dve.inc(nc.vector.reduce_sum(rowred[:], pn[:], AX.X))
                    p.dve.wait(nc.vector)
                    p.dve.inc(nc.vector.reciprocal(rrecip[:], rowred[:]))
                    p.dve.wait(nc.vector)
                    p.dve.inc(nc.vector.tensor_scalar_mul(pn[:], pn[:], rrecip[:]))
                    p.dve.wait(nc.tensor)
                    p.pe.inc(nc.tensor.transpose(ps_t[:], pn[:], ident[:]))
                    p.pe.wait(nc.vector)
                    p.dve.inc(nc.vector.tensor_copy(pt[:], ps_t[:]))
                    p.dve.wait(nc.tensor)
                    p.pe.inc(nc.tensor.matmul(ps_b[0:64, :], vi_tok[:, h * 64:(h + 1) * 64],
                                              pt[:], start=True, stop=True))
                    p.pe.wait(nc.vector)
                    p.dve.inc(nc.vector.tensor_copy(iout[base:base + 64, hp, :], ps_b[0:64, :]))
                p.dve.wait(nc.vector)
                if it == 0:
                    for c in range(DC):
                        p.dve.inc(nc.vector.tensor_copy(attn_o[nxt][:, c, :], iout[:, c, :]))
                        p.dve.inc(nc.vector.tensor_copy(prev[nxt][:, c, :], iout[:, c, :]))
                        p.dve.inc(nc.vector.tensor_add(qcur[nxt][:, c, :], qcur[cur][:, c, :],
                                                       iout[:, c, :]))
                else:
                    for c in range(DC):
                        p.dve.inc(nc.vector.tensor_sub(dtmp[:], iout[:, c, :], prev[cur][:, c, :]))
                        p.dve.wait(nc.vector)
                        p.dve.inc(nc.vector.tensor_reduce(dcol[:, c:c + 1], dtmp[:], AX.X,
                                                          ALU.add, apply_absolute_value=True))
                        p.dve.wait(nc.vector)
                    p.dve.inc(nc.vector.reduce_sum(dcol1[:], dcol[:], AX.X))
                    p.dve.wait(nc.vector)
                    p.dve.inc(nc.vector.tensor_copy(dcolb[:], dcol1[:]))
                    p.dve.wait(nc.tensor)
                    p.pe.inc(nc.tensor.matmul(ps_s[:], dcolb[:], ones_col_bf[:],
                                              start=True, stop=True))
                    p.pe.wait(nc.vector)
                    fl_diff, fl_t1, fl_lt, fl_brk, fl_done, fl_om = [flt[:, i:i + 1] for i in range(6)]
                    p.dve.inc(nc.vector.tensor_scalar_mul(fl_diff, ps_s[:], DIFF_SCALE))
                    p.dve.wait(nc.vector)
                    p.dve.inc(nc.vector.tensor_scalar(fl_t1, fl_diff, FACTOR, THRESH,
                                                      ALU.mult, ALU.add))
                    p.dve.wait(nc.vector)
                    p.dve.inc(nc.vector.tensor_tensor(fl_lt, fl_diff, fl_t1, ALU.is_lt))
                    p.dve.wait(nc.vector)
                    if it == 1:
                        p.dve.inc(nc.vector.tensor_copy(fl_brk, fl_lt))
                        p.dve.wait(nc.vector)
                        p.dve.inc(nc.vector.tensor_copy(fl_done, flt[:, 2:3]))
                        p.dve.wait(nc.vector)
                        for c in range(DC):
                            p.dve.inc(nc.vector.tensor_copy(attn_o[nxt][:, c, :], iout[:, c, :]))
                            p.dve.inc(nc.vector.tensor_copy(prev[nxt][:, c, :], iout[:, c, :]))
                        p.dve.wait(nc.vector)
                        p.dve.inc(nc.vector.tensor_scalar(fl_om, fl_brk, -1.0, 1.0,
                                                          ALU.mult, ALU.add))
                        p.dve.wait(nc.tensor)
                        p.pe.inc(nc.tensor.matmul(ps_c[:], ones_row_f[:], fl_om,
                                                  start=True, stop=True))
                        p.pe.wait(nc.vector)
                        p.dve.inc(nc.vector.tensor_copy(omnc[:], ps_c[:]))
                        p.dve.wait(nc.vector)
                        for c in range(DC):
                            p.dve.wait(nc.vector)
                            p.dve.inc(nc.vector.tensor_scalar_mul(dtmp[:], iout[:, c, :], omnc[:]))
                            p.dve.wait(nc.vector)
                            p.dve.inc(nc.vector.tensor_add(qcur[nxt][:, c, :],
                                                           qcur[cur][:, c, :], dtmp[:]))
                        p.dve.inc(nc.vector.tensor_copy(fl_done, fl_brk))
                        p.dve.wait(nc.vector)
                    else:
                        p.dve.inc(nc.vector.tensor_scalar(fl_om, fl_done, -1.0, 1.0,
                                                          ALU.mult, ALU.add))
                        p.dve.wait(nc.tensor)
                        nc.tensor.matmul(ps_c[:], ones_row_f[:], fl_done, start=True, stop=True)
                        p.pe.inc(nc.tensor.matmul(ps_bc[0:128, 0:1], ones_row_f[:], fl_om,
                                                  start=True, stop=True))
                        p.pe.wait(nc.vector)
                        p.dve.inc(nc.vector.tensor_copy(donec[:], ps_c[:]))
                        p.dve.inc(nc.vector.tensor_copy(omc[:], ps_bc[0:128, 0:1]))
                        p.dve.wait(nc.vector)
                        for c in range(DC):
                            p.dve.wait(nc.vector)
                            p.dve.inc(nc.vector.tensor_scalar_mul(dtmp[:], iout[:, c, :], omc[:]))
                            p.dve.wait(nc.vector)
                            p.dve.inc(nc.vector.tensor_scalar_mul(iout[:, c, :],
                                                                  attn_o[cur][:, c, :], donec[:]))
                            p.dve.wait(nc.vector)
                            p.dve.inc(nc.vector.tensor_add(attn_o[nxt][:, c, :], dtmp[:],
                                                           iout[:, c, :]))
                            p.dve.wait(nc.vector)
                cur, nxt = nxt, cur
                p.dve.wait(nc.tensor)
                p.dve.wait(nc.scalar)

            if "dbg_q16" in dt:
                nc.sync.wait_ge(p.dve.h, p.dve.n)
                nc.sync.wait_ge(p.act.h, p.act.n)
                for nm, sbh in (("dbg_q16", q16), ("dbg_k16", k16), ("dbg_v16", v16),
                                ("dbg_kn16", kn16), ("dbg_vi", vi_tok),
                                ("dbg_attn", attn_o[cur]), ("dbg_xn16", xn16)):
                    p.dou.inc(nc.sync.dma_start(dt[nm][:], sbh[:]), 16)
                nc.sync.wait_ge(p.dou.h, p.dou.n)
            for c in range(DC):
                p.dve.inc(nc.vector.tensor_copy(qcb[:, c, :], attn_o[cur][:, c, :]))
            p.dve.wait(nc.tensor)
            for mc in range(DC):
                for kc in range(DC):
                    inst = nc.tensor.matmul(ps_a[:], wo[:, kc, mc * 128:(mc + 1) * 128],
                                            qcb[:, kc, :], start=(kc == 0), stop=(kc == DC - 1))
                p.pe.inc(inst)
                p.pe.wait(nc.vector)
                p.dve.inc(nc.vector.tensor_copy(y16[:, mc, :], ps_a[0:128, 0:16]))
                p.dve.wait(nc.tensor)
            if "dbg_y16" in dt:
                nc.sync.wait_ge(p.dve.h, p.dve.n)
                p.dou.inc(nc.sync.dma_start(dt["dbg_y16"][:], y16[:]), 16)
                nc.sync.wait_ge(p.dou.h, p.dou.n)
        p.bar()

    if "a" in phases:
        with ExitStack() as es:
            t = lambda n, s, d=F32: es.enter_context(nc.sbuf_tensor(n, list(s), d))
            xch = [t("xchA", (128, DC, TC)), t("xchB", (128, DC, TC))]
            dsx = [Sem(nc, "dsxA0"), Sem(nc, "dsxA1")]
            xb = t("xb", (128, DC, TC), BF16)
            xsq = t("xsq", (128, DC, TC), BF16)
            xn = t("xnA", (128, DC, TC))
            tmul = t("tmulA", (128, DC, TC))
            rows2 = t("rows2a", (1, 5, TC))
            ps_st = es.enter_context(nc.psum_tensor("psAst", [34, TC], F32))
            ps_r = es.enter_context(nc.psum_tensor("psAr", [128, TC], F32))
            ps_m = es.enter_context(nc.psum_tensor("psAm", [128, TC], F32))
            buf_free = [0, 0]
            buf_freea = [0, 0]
            st_free = 0
            sp_done = 0
            for ch in range(NCH + 1):
                if ch < NCH:
                    bi = ch % 2
                    nc.sync.wait_ge(p.dve.h, buf_free[bi])
                    nc.sync.wait_ge(p.act.h, buf_freea[bi])
                    dsx[bi].inc(nc.sync.dma_start(
                        xch[bi][:], dt["x_fm"][:, ch * TC:(ch + 1) * TC]
                        .rearrange("(c p) t -> p c t", p=128)), 16)
                if ch == 0:
                    continue
                i = ch - 1
                bi = i % 2
                dsx[bi].wait(nc.vector)
                dsx[bi].wait(nc.scalar)
                nc.vector.wait_ge(p.dou.h, sp_done)
                nc.vector.wait_ge(p.pe.h, p.pe.n)
                nc.scalar.wait_ge(p.pe.h, p.pe.n)
                p.dve.inc(nc.vector.tensor_copy(xb[:], xch[bi][:]))
                ba = p.act.inc(nc.scalar.activation(xsq[:], xch[bi][:], AF.Square))
                buf_freea[bi] = ba
                p.dve.wait(nc.tensor)
                p.act.wait(nc.tensor)
                nc.tensor.wait_ge(p.dve.h, st_free)
                stat_sums(p, ps_st, xb, xsq, DC, TC, ones_col_bf[:])
                p.pe.wait(nc.vector)
                p.pe.wait(nc.scalar)
                st_free = ln_rows(p, ps_st, 1, TC, 1.0 / D, rows2)
                st_free = p.dve.n
                p.dve.wait(nc.tensor)
                p.act.wait(nc.tensor)
                nc.tensor.matmul(ps_r[:], ones_row_f[:], rows2[:, 0, :], start=True, stop=True)
                p.pe.inc(nc.tensor.matmul(ps_m[:], ones_row_f[:], rows2[:, 1, :],
                                          start=True, stop=True))
                p.pe.wait(nc.vector)
                for c in range(DC):
                    p.dve.inc(nc.vector.tensor_mul(tmul[:, c, :], xch[bi][:, c, :], ps_r[:]))
                buf_free[bi] = p.dve.n
                p.dve.wait(nc.vector)
                for c in range(DC):
                    p.dve.inc(nc.vector.tensor_add(xn[:, c, :], tmul[:, c, :], ps_m[:]))
                p.dve.wait(nc.vector)
                for c in range(DC):
                    p.dve.inc(nc.vector.reduce_sum(gacc[:, c, i:i + 1], xn[:, c, :], AX.X))
                nc.sync.wait_ge(p.dve.h, p.dve.n)
                sp_done = p.dou.inc(nc.sync.dma_start(
                    dt["xn_sp"][:, i * TC:(i + 1) * TC].rearrange("(c p) t -> p c t", p=128),
                    xn[:]), 16)
            nc.sync.wait_ge(p.dou.h, p.dou.n)
        p.bar()

    if "gf" in phases:
        with ExitStack() as es:
            t = lambda n, s, d=F32: es.enter_context(nc.sbuf_tensor(n, list(s), d))
            gsum = t("gsum", (128, DC))
            h1g = t("h1g", (128, 4), BF16)
            h2g = t("h2g", (128, 2), BF16)
            w1 = t("gw1", (128, DC, 512), BF16)
            w2 = t("gw2", (128, 4, 256), BF16)
            w3 = t("gw3", (128, 2, 1), BF16)
            fl = t("gfl", (1, 4))
            ps_g = es.enter_context(nc.psum_tensor("psG", [128, 512], F32))
            ps_g2 = es.enter_context(nc.psum_tensor("psG2", [1, 1], F32))
            p.dgp.inc(nc.gpsimd.dma_start(w1[:], dt["gcw1"].rearrange("(c p) m -> p c m", p=128)), 16)
            p.dgp.inc(nc.gpsimd.dma_start(w2[:], dt["gcw2"].rearrange("(c p) m -> p c m", p=128)), 16)
            with nc.allow_non_contiguous_dma(reason="tiny [256,1] weight"):
                p.dgp.inc(nc.gpsimd.dma_start(w3[:], dt["gcw3"].rearrange("(c p) m -> p c m", p=128)), 16)
            p.dve.inc(nc.vector.tensor_reduce(gsum[:], gacc[:], AX.X, ALU.add))
            p.dve.wait(nc.vector)
            p.dve.inc(nc.vector.tensor_scalar_mul(gctx_bf[:], gsum[:], 1.0 / S))
            p.dve.wait(nc.tensor)
            p.dgp.wait(nc.tensor)
            for mc in range(4):
                for kc in range(DC):
                    inst = nc.tensor.matmul(ps_g[:, mc:mc + 1],
                                            w1[:, kc, mc * 128:(mc + 1) * 128],
                                            gctx_bf[:, kc:kc + 1],
                                            start=(mc == 0 and kc == 0),
                                            stop=(mc == 3 and kc == DC - 1))
                p.pe.inc(inst)
            p.pe.wait(nc.scalar)
            for mc in range(4):
                p.act.inc(nc.scalar.activation(h1g[:, mc:mc + 1], ps_g[:, mc:mc + 1], AF.Gelu,
                                               bias=cols["gcb1_c"][:, mc:mc + 1], scale=1.0))
            p.act.wait(nc.tensor)
            for mc in range(2):
                for kc in range(4):
                    inst = nc.tensor.matmul(ps_g[:, 8 + mc:9 + mc],
                                            w2[:, kc, mc * 128:(mc + 1) * 128],
                                            h1g[:, kc:kc + 1],
                                            start=(mc == 0 and kc == 0),
                                            stop=(mc == 1 and kc == 3))
                p.pe.inc(inst)
            p.pe.wait(nc.scalar)
            for mc in range(2):
                p.act.inc(nc.scalar.activation(h2g[:, mc:mc + 1], ps_g[:, 8 + mc:9 + mc], AF.Gelu,
                                               bias=cols["gcb2_c"][:, mc:mc + 1], scale=1.0))
            p.act.wait(nc.tensor)
            for kc in range(2):
                inst = nc.tensor.matmul(ps_g2[:], w3[:, kc, :], h2g[:, kc:kc + 1],
                                        start=(kc == 0), stop=(kc == 1))
            p.pe.inc(inst)
            p.pe.wait(nc.vector)
            fgf, ft, fc1, fc2 = [fl[:, i:i + 1] for i in range(4)]
            p.dve.inc(nc.vector.tensor_tensor(ft, ps_g2[:], cols["gcb3"][:], ALU.add))
            p.dve.wait(nc.scalar)
            p.act.inc(nc.scalar.activation(ft, ft, AF.Tanh, bias=0.0, scale=0.5))
            p.act.wait(nc.vector)
            p.dve.inc(nc.vector.tensor_scalar(fgf, ft, 0.5, 0.5, ALU.mult, ALU.add))
            p.dve.wait(nc.vector)
            p.dve.inc(nc.vector.tensor_scalar(fc1, fgf, 1.4, -1.0, ALU.mult, ALU.add))
            p.dve.wait(nc.vector)
            p.dve.inc(nc.vector.tensor_tensor(fc1, fc1, cols["ar"][:], ALU.mult))
            p.dve.inc(nc.vector.tensor_scalar_mul(fc2, cols["ar"][:], 0.3))
            p.dve.wait(nc.vector)
            p.dve.inc(nc.vector.tensor_tensor(c2p[:], fc2, fc2, ALU.bypass))
            p.dve.inc(nc.vector.tensor_scalar(ft, fc2, 1.0, 0.1, ALU.mult, ALU.add))
            p.dve.wait(nc.vector)
            p.dve.inc(nc.vector.tensor_tensor(c1p[:], fc1, ft, ALU.add))
        p.bar()

    build_rest(p, dt, sb, cols, phases)

    return nc, dt


def build_rest(p, dt, sb, cols, phases):
    nc = p.nc
    ones_col_bf = sb["ones_col_bf"]
    ones_row_f = sb["ones_row_f"]
    ones_row_bf = sb["ones_row_bf"]
    sel2r = sb["sel2r_sb"]
    sel33 = sb["sel33_sb"]
    gacc = sb["gacc"]
    y16 = sb["y16"]
    c1p, c2p = sb["c1p"], sb["c2p"]
    bq8 = sb["bq8"]

    if "b" in phases:
        with ExitStack() as es:
            t = lambda n, s, d=F32: es.enter_context(nc.sbuf_tensor(n, list(s), d))
            xch = [t("xchB0", (128, DC, TC)), t("xchB1", (128, DC, TC))]
            xnch = [t("xnchB0", (128, DC, TC)), t("xnchB1", (128, DC, TC))]
            outp = [t("outB0", (128, DC, TC)), t("outB1", (128, DC, TC))]
            tw = [t("twB0", (128, DC, TC)), t("twB1", (128, DC, TC))]
            trow = [t("trowB0", (1, TC)), t("trowB1", (1, TC))]
            cplrow = [t("cplB0", (1, TC)), t("cplB1", (1, TC))]
            ldb = Sem(nc, "ldB")
            outb = [t("outbB0", (128, DC, TC), BF16), t("outbB1", (128, DC, TC), BF16)]
            h1 = t("h1B", (128, DC, TC), BF16)
            h2 = t("h2B", (128, 4, TC), BF16)
            h3 = t("h3B", (128, 2, TC), BF16)
            w1 = t("cw1", (128, DC, D), BF16)
            w2 = t("cw2", (128, DC, 512), BF16)
            w3 = t("cw3", (128, 4, 256), BF16)
            w4 = t("cw4", (128, 2, 1), BF16)
            ps_h = [es.enter_context(nc.psum_tensor("psB0", [128, TC], F32)),
                    es.enter_context(nc.psum_tensor("psB1", [128, TC], F32))]
            ps_t4 = [es.enter_context(nc.psum_tensor("psBt4a", [1, TC], F32)),
                     es.enter_context(nc.psum_tensor("psBt4b", [1, TC], F32))]
            ps_cb = [es.enter_context(nc.psum_tensor("psBcba", [128, TC], F32)),
                     es.enter_context(nc.psum_tensor("psBcbb", [128, TC], F32))]
            p.dgp.inc(nc.gpsimd.dma_start(w1[:], dt["cnw1"].rearrange("(c p) m -> p c m", p=128)), 16)
            p.dgp.inc(nc.gpsimd.dma_start(w2[:], dt["cnw2"].rearrange("(c p) m -> p c m", p=128)), 16)
            p.dgp.inc(nc.gpsimd.dma_start(w3[:], dt["cnw3"].rearrange("(c p) m -> p c m", p=128)), 16)
            with nc.allow_non_contiguous_dma(reason="tiny [256,1] weight"):
                p.dgp.inc(nc.gpsimd.dma_start(w4[:], dt["cnw4"].rearrange("(c p) m -> p c m", p=128)), 16)
            p.dgp.wait(nc.tensor)
            ld_mark = [0, 0]
            copy_mark = [0, 0]
            x1_mark = [0, 0]
            st_mark = [0, 0]
            cbfree = [0, 0]
            t4free = [0, 0]
            hfree = [p.act.n, p.act.n]
            w1_done = [0, 0]
            t4mm = [0, 0]
            cpl_mark = [0, 0]
            cb_mm = [0, 0]

            def emit_loads(i, s):
                nc.sync.wait_ge(p.dve.h, x1_mark[s])
                ldb.inc(nc.sync.dma_start(
                    xch[s][:], dt["x_fm"][:, i * TC:(i + 1) * TC]
                    .rearrange("(c p) t -> p c t", p=128)), 16)
                nc.sync.wait_ge(p.dve.h, copy_mark[s])
                ldb.inc(nc.sync.dma_start(
                    xnch[s][:], dt["xn_sp"][:, i * TC:(i + 1) * TC]
                    .rearrange("(c p) t -> p c t", p=128)), 16)
                ld_mark[s] = ldb.n

            def emit_init(s):
                nc.vector.wait_ge(ldb.h, ld_mark[s])
                for c in range(DC):
                    p.dve.inc(nc.vector.tensor_copy(outp[s][:, c, :], xnch[s][:, c, :]))
                copy_mark[s] = p.dve.n

            ob_mark = [0, 0]

            def emit_copy(s):
                nc.vector.wait_ge(p.pe.h, w1_done[s])
                p.dve.wait(nc.vector)
                p.dve.inc(nc.vector.tensor_copy(outb[s][:], outp[s][:]))
                ob_mark[s] = p.dve.n

            def emit_mms(s):
                for mc in range(DC):
                    bk = mc % 2
                    nc.tensor.wait_ge(p.act.h, hfree[bk])
                    if mc == 0:
                        nc.tensor.wait_ge(p.dve.h, ob_mark[s])
                    for kc in range(DC):
                        inst = nc.tensor.matmul(ps_h[bk][:], w1[:, kc, mc * 128:(mc + 1) * 128],
                                                outb[s][:, kc, :], start=(kc == 0), stop=(kc == DC - 1))
                    p.pe.inc(inst)
                    p.pe.wait(nc.scalar)
                    hfree[bk] = p.act.inc(nc.scalar.activation(h1[:, mc, :], ps_h[bk][:], AF.Gelu,
                                                               bias=cols["cnb1_c"][:, mc:mc + 1], scale=1.0))
                w1_done[s] = p.pe.n
                p.act.wait(nc.tensor)
                for mc in range(4):
                    bk = mc % 2
                    nc.tensor.wait_ge(p.act.h, hfree[bk])
                    for kc in range(DC):
                        inst = nc.tensor.matmul(ps_h[bk][:], w2[:, kc, mc * 128:(mc + 1) * 128],
                                                h1[:, kc, :], start=(kc == 0), stop=(kc == DC - 1))
                    p.pe.inc(inst)
                    p.pe.wait(nc.scalar)
                    hfree[bk] = p.act.inc(nc.scalar.activation(h2[:, mc, :], ps_h[bk][:], AF.Gelu,
                                                               bias=cols["cnb2_c"][:, mc:mc + 1], scale=1.0))
                p.act.wait(nc.tensor)
                for mc in range(2):
                    bk = mc % 2
                    nc.tensor.wait_ge(p.act.h, hfree[bk])
                    for kc in range(4):
                        inst = nc.tensor.matmul(ps_h[bk][:], w3[:, kc, mc * 128:(mc + 1) * 128],
                                                h2[:, kc, :], start=(kc == 0), stop=(kc == 3))
                    p.pe.inc(inst)
                    p.pe.wait(nc.scalar)
                    hfree[bk] = p.act.inc(nc.scalar.activation(h3[:, mc, :], ps_h[bk][:], AF.Gelu,
                                                               bias=cols["cnb3_c"][:, mc:mc + 1], scale=1.0))
                p.act.wait(nc.tensor)
                nc.tensor.wait_ge(p.dve.h, t4free[s])
                for kc in range(2):
                    inst = nc.tensor.matmul(ps_t4[s][:], w4[:, kc, :], h3[:, kc, :],
                                            start=(kc == 0), stop=(kc == 1))
                t4mm[s] = p.pe.inc(inst)

            def emit_tail1(s):
                nc.vector.wait_ge(p.pe.h, t4mm[s])
                p.dve.inc(nc.vector.tensor_scalar_add(trow[s][:], ps_t4[s][:], cols["cnb4"][:]))
                t4free[s] = p.dve.n
                nc.scalar.wait_ge(p.dve.h, t4free[s])
                tanh_m = p.act.inc(nc.scalar.activation(trow[s][:], trow[s][:], AF.Tanh,
                                                        bias=0.0, scale=0.5))
                nc.vector.wait_ge(p.act.h, tanh_m)
                p.dve.inc(nc.vector.tensor_scalar(cplrow[s][:], trow[s][:], c2p[:], None, ALU.mult))
                p.dve.wait(nc.vector)
                cpl_mark[s] = p.dve.inc(nc.vector.tensor_scalar(cplrow[s][:], cplrow[s][:], c1p[:],
                                                                None, ALU.add))
                nc.tensor.wait_ge(p.dve.h, max(cpl_mark[s], cbfree[s]))
                cb_mm[s] = p.pe.inc(nc.tensor.matmul(ps_cb[s][:], ones_row_f[:], cplrow[s][:],
                                                     start=True, stop=True))

            def emit_tail2(s, cyc):
                nc.vector.wait_ge(p.pe.h, cb_mm[s])
                if cyc == 0:
                    nc.vector.wait_ge(p.dou.h, st_mark[s])
                for j in range(DC):
                    src = (j + 4) % DC
                    p.dve.inc(nc.vector.tensor_mul(tw[s][:, j, :], outp[s][:, src, :], ps_cb[s][:]))
                cbfree[s] = p.dve.n
                p.dve.wait(nc.vector)
                for j in range(DC):
                    op = ALU.add if j in (0, 1, 6, 7) else ALU.subtract
                    p.dve.inc(nc.vector.tensor_tensor(outp[s][:, j, :], outp[s][:, j, :],
                                                      tw[s][:, j, :], op))

            def emit_x1(i, s):
                nc.vector.wait_ge(ldb.h, ld_mark[s])
                p.dve.wait(nc.vector)
                for c in range(DC):
                    p.dve.inc(nc.vector.tensor_add(tw[s][:, c, :], xch[s][:, c, :],
                                                   outp[s][:, c, :]))
                x1_mark[s] = p.dve.n
                p.dve.wait(nc.vector)
                for c in range(DC):
                    p.dve.inc(nc.vector.tensor_scalar_add(tw[s][:, c, :], tw[s][:, c, :],
                                                          cols["ob_c"][:, c:c + 1]))
                if i == 0 and "t16" in phases:
                    p.dve.wait(nc.vector)
                    for c in range(DC):
                        p.dve.inc(nc.vector.tensor_add(tw[s][:, c, 0:16],
                                                       tw[s][:, c, 0:16], y16[:, c, :]))
                nc.sync.wait_ge(p.dve.h, p.dve.n)
                st_mark[s] = p.dou.inc(nc.sync.dma_start(
                    dt["x1_sp"][:, i * TC:(i + 1) * TC].rearrange("(c p) t -> p c t", p=128),
                    tw[s][:]), 16)

            emit_loads(0, 0)
            emit_loads(1, 1)
            for pr in range(NCH // 2):
                a, b = 2 * pr, 2 * pr + 1
                emit_init(0)
                emit_init(1)
                emit_copy(0)
                for cyc in range(3):
                    emit_mms(0)
                    emit_tail1(0)
                    emit_copy(1)
                    emit_tail2(0, cyc)
                    emit_mms(1)
                    emit_tail1(1)
                    if cyc < 2:
                        emit_copy(0)
                    emit_tail2(1, cyc)
                emit_x1(a, 0)
                if a + 2 < NCH:
                    emit_loads(a + 2, 0)
                emit_x1(b, 1)
                if b + 2 < NCH:
                    emit_loads(b + 2, 1)
            nc.sync.wait_ge(p.dou.h, p.dou.n)
        p.bar()

    k_res = v_res = None
    wq_res = wpj_res = None
    wq_mark = 0
    if "c" in phases or "attn" in phases:
        es_kv = ExitStack()
        k_res = es_kv.enter_context(nc.sbuf_tensor("k_res", [128, DC, S], BF16))
        v_res = es_kv.enter_context(nc.sbuf_tensor("v_res", [128, NKT, NH, DH + 1], BF16))

    def load_wq_wpj():
        p.dgp.inc(nc.gpsimd.dma_start(
            wq_res[:], dt["wqkv"][:, 0:D].rearrange("(c p) m -> p c m", p=128)), 16)
        p.dgp.inc(nc.gpsimd.dma_start(
            wpj_res[:], dt["wproj"].rearrange("(c p) m -> p c m", p=128)), 16)
        return p.dgp.n
    if "c" in phases:
        with ExitStack() as es:
            t = lambda n, s, d=F32: es.enter_context(nc.sbuf_tensor(n, list(s), d))
            xch = t("xchC", (128, DC, TC))
            ldc = Sem(nc, "ldC")
            xb = t("xbC", (128, DC, TC), BF16)
            xsq = t("xsqC", (128, DC, TC), BF16)
            xn1 = t("xn1C", (128, DC, TC), BF16)
            rows2 = t("rows2c", (1, 5, TC))
            wk = t("wkC", (128, DC, D), BF16)
            ps_st = es.enter_context(nc.psum_tensor("psCst", [34, TC], F32))
            ps_r = es.enter_context(nc.psum_tensor("psCr", [128, TC], F32))
            ps_m = es.enter_context(nc.psum_tensor("psCm", [128, TC], F32))
            ps_k = [es.enter_context(nc.psum_tensor("psCk0", [128, TC], F32)),
                    es.enter_context(nc.psum_tensor("psCk1", [128, TC], F32))]
            p.dgp.inc(nc.gpsimd.dma_start(
                wk[:], dt["wqkv"][:, D:2 * D].rearrange("(c p) m -> p c m", p=128)), 16)
            p.dgp.wait(nc.tensor)
            stat_mark = 0
            bc_mark = 0
            kmm_mark = 0
            sp_done = 0
            st_free = 0
            addm = 0
            ldc.inc(nc.sync.dma_start(
                xch[:], dt["x1_sp"][:, 0:TC].rearrange("(c p) t -> p c t", p=128)), 16)
            for i in range(NCH):
                ldc.wait(nc.vector)
                ldc.wait(nc.scalar)
                nc.vector.wait_ge(p.pe.h, stat_mark)
                nc.scalar.wait_ge(p.pe.h, stat_mark)
                p.dve.inc(nc.vector.tensor_copy(xb[:], xch[:]))
                p.act.inc(nc.scalar.activation(xsq[:], xch[:], AF.Square))
                p.dve.wait(nc.tensor)
                p.act.wait(nc.tensor)
                nc.tensor.wait_ge(p.dve.h, st_free)
                stat_sums(p, ps_st, xb, xsq, DC, TC, ones_col_bf[:])
                stat_mark = p.pe.n
                nc.vector.wait_ge(p.pe.h, bc_mark)
                nc.scalar.wait_ge(p.pe.h, bc_mark)
                p.pe.wait(nc.vector)
                p.pe.wait(nc.scalar)
                st_free = ln_rows(p, ps_st, 1, TC, 1.0 / D, rows2)
                st_free = p.dve.n
                p.dve.wait(nc.tensor)
                p.act.wait(nc.tensor)
                nc.tensor.wait_ge(p.dve.h, addm)
                nc.tensor.matmul(ps_r[:], ones_row_f[:], rows2[:, 0, :], start=True, stop=True)
                p.pe.inc(nc.tensor.matmul(ps_m[:], ones_row_f[:], rows2[:, 1, :],
                                          start=True, stop=True))
                bc_mark = p.pe.n
                p.pe.wait(nc.vector)
                for c in range(DC):
                    p.dve.inc(nc.vector.tensor_mul(xch[:, c, :], xch[:, c, :], ps_r[:]))
                p.dve.wait(nc.vector)
                nc.vector.wait_ge(p.pe.h, kmm_mark)
                nc.vector.wait_ge(p.dou.h, sp_done)
                for c in range(DC):
                    p.dve.inc(nc.vector.tensor_add(xn1[:, c, :], xch[:, c, :], ps_m[:]))
                addm = p.dve.n
                if i + 1 < NCH:
                    nc.sync.wait_ge(p.dve.h, p.dve.n)
                    ldc.inc(nc.sync.dma_start(
                        xch[:], dt["x1_sp"][:, (i + 1) * TC:(i + 2) * TC]
                        .rearrange("(c p) t -> p c t", p=128)), 16)
                nc.sync.wait_ge(p.dve.h, p.dve.n)
                sp_done = p.dou.inc(nc.sync.dma_start(
                    dt["xn1_sp"][:, i * TC:(i + 1) * TC].rearrange("(c p) t -> p c t", p=128),
                    xn1[:]), 16)
                p.dve.wait(nc.tensor)
                kfree = [p.dve.n, p.dve.n]
                for mc in range(DC):
                    bk = mc % 2
                    nc.tensor.wait_ge(p.dve.h, kfree[bk])
                    for kc in range(DC):
                        inst = nc.tensor.matmul(ps_k[bk][:], wk[:, kc, mc * 128:(mc + 1) * 128],
                                                xn1[:, kc, :], start=(kc == 0), stop=(kc == DC - 1))
                    p.pe.inc(inst)
                    p.pe.wait(nc.vector)
                    kfree[bk] = p.dve.inc(nc.vector.tensor_scalar_add(
                        k_res[:, mc, i * TC:(i + 1) * TC], ps_k[bk][:], cols["bk_c"][:, mc:mc + 1]))
                kmm_mark = p.pe.n
            nc.sync.wait_ge(p.dou.h, p.dou.n)
        p.bar()

        if "attn" in phases:
            wq_res = es_kv.enter_context(nc.sbuf_tensor("wq_res", [128, DC, D], BF16))
            wpj_res = es_kv.enter_context(nc.sbuf_tensor("wpj_res", [128, DC, D], BF16))
            wq_mark = load_wq_wpj()

        with ExitStack() as es:
            t = lambda n, s, d=F32: es.enter_context(nc.sbuf_tensor(n, list(s), d))
            xn1b = [t("xn1C2a", (128, DC, TC), BF16), t("xn1C2b", (128, DC, TC), BF16)]
            ldc2 = Sem(nc, "ldC2")
            ldbv = Sem(nc, "ldBV")
            wv = t("wvC2", (128, DC, D), BF16)
            bvrow_f = t("bvrowC2", (1, D), F32)
            bv_bf = t("bvbfC2", (1, D), BF16)
            ps_v = [es.enter_context(nc.psum_tensor("psCv0", [128, 512], F32)),
                    es.enter_context(nc.psum_tensor("psCv1", [128, 512], F32))]
            p.dgp.inc(nc.gpsimd.dma_start(
                wv[:], dt["wqkv"][:, 2 * D:3 * D].rearrange("(c p) m -> p c m", p=128)), 16)
            ldbv.inc(nc.sync.dma_start(bvrow_f[:], dt["bv_row"][:]), 16)
            ldbv.wait(nc.vector)
            p.dve.inc(nc.vector.tensor_copy(bv_bf[:], bvrow_f[:]))
            p.dve.inc(nc.vector.memset(v_res[:, :, :, DH:DH + 1], 1.0))
            p.dgp.wait(nc.tensor)
            p.dve.wait(nc.tensor)
            ld2_mark = [0, 0]
            vmm_mark = [0, 0]
            for s in range(2):
                ldc2.inc(nc.sync.dma_start(
                    xn1b[s][:], dt["xn1_sp"][:, s * TC:(s + 1) * TC]
                    .rearrange("(c p) t -> p c t", p=128)), 16)
                ld2_mark[s] = ldc2.n
            for i in range(NCH):
                s = i % 2
                xn1 = xn1b[s]
                nc.tensor.wait_ge(ldc2.h, ld2_mark[s])
                kfree = [p.dve.n, p.dve.n]
                for ts4 in range(TC // 128):
                    kt = i * (TC // 128) + ts4
                    for half in range(2):
                        bk = half
                        nc.tensor.wait_ge(p.dve.h, kfree[bk])
                        for kc in range(DC):
                            nc.tensor.matmul(ps_v[bk][:], xn1[:, kc, ts4 * 128:(ts4 + 1) * 128],
                                             wv[:, kc, half * 512:(half + 1) * 512],
                                             start=(kc == 0), stop=False)
                        inst = nc.tensor.matmul(ps_v[bk][:], ones_row_bf[:],
                                                bv_bf[:, half * 512:(half + 1) * 512],
                                                start=False, stop=True)
                        p.pe.inc(inst)
                        p.pe.wait(nc.vector)
                        kfree[bk] = p.dve.inc(nc.vector.tensor_copy(
                            v_res[:, kt, half * 8:(half + 1) * 8, 0:DH],
                            ps_v[bk][:].rearrange("p (h d) -> p h d", d=DH)))
                vmm_mark[s] = p.pe.n
                if i + 2 < NCH:
                    nc.sync.wait_ge(p.pe.h, vmm_mark[s])
                    ldc2.inc(nc.sync.dma_start(
                        xn1b[s][:], dt["xn1_sp"][:, (i + 2) * TC:(i + 3) * TC]
                        .rearrange("(c p) t -> p c t", p=128)), 16)
                    ld2_mark[s] = ldc2.n
        p.bar()

    if "attn" in phases:
        with ExitStack() as es:
            t = lambda n, s, d=F32: es.enter_context(nc.sbuf_tensor(n, list(s), d))
            xn1c = t("xn1At", (128, DC, TC), BF16)
            lda = Sem(nc, "ldAt")
            qfm = t("qAt", (128, DC, TC), BF16)
            pbuf = [t("pA0", (128, TC), BF16), t("pA1", (128, TC), BF16),
                    t("pA2", (128, TC), BF16), t("pA3", (128, TC), BF16)]
            o2s = t("o2sAt", (128, TC))
            r2ab = t("r2abAt", (33, TC))
            nb_sb = t("nbsbAt", (128, TC))
            aout = t("aoutAt", (128, DC, TC), BF16)
            lneg = sb["lneg_sb"]
            rbig = sb["rbig_sb"]
            ps_se = es.enter_context(nc.psum_tensor("psSe", [128, TC], F32))
            ps_so = es.enter_context(nc.psum_tensor("psSo", [128, TC], F32))
            ps_se2 = es.enter_context(nc.psum_tensor("psSe2", [128, TC], F32))
            ps_so2 = es.enter_context(nc.psum_tensor("psSo2", [128, TC], F32))
            ps_o2e = es.enter_context(nc.psum_tensor("psO2e", [128, TC], F32))
            ps_o2o = es.enter_context(nc.psum_tensor("psO2o", [128, TC], F32))
            ps_o2e2 = es.enter_context(nc.psum_tensor("psO2e2", [128, TC], F32))
            ps_o2o2 = es.enter_context(nc.psum_tensor("psO2o2", [128, TC], F32))
            SE = [ps_se, ps_se2]
            SO = [ps_so, ps_so2]
            O2E = [ps_o2e, ps_o2e2]
            O2O = [ps_o2o, ps_o2o2]
            if wq_res is None:
                wq_res = es_kv.enter_context(nc.sbuf_tensor("wq_res", [128, DC, D], BF16))
                wpj_res = es_kv.enter_context(nc.sbuf_tensor("wpj_res", [128, DC, D], BF16))
            if wq_mark == 0:
                wq_mark = load_wq_wpj()
            p.dve.inc(nc.vector.memset(r2ab[:], 0.0))
            qp_mark = 0
            ev_mark = 0
            stq = [0] * DC
            for j in range(NCH):
                q0 = j * TC
                nkt = (j + 1) * (TC // 128)
                if j == 0:
                    lda.inc(nc.sync.dma_start(
                        xn1c[:], dt["xn1_sp"][:, q0:q0 + TC]
                        .rearrange("(c p) t -> p c t", p=128)), 16)
                    nc.tensor.wait_ge(p.dgp.h, wq_mark)
                lda.wait(nc.tensor)
                qfree = [p.act.n, p.act.n]
                for mc in range(DC):
                    bk = mc % 2
                    nc.tensor.wait_ge(p.act.h, qfree[bk])
                    if mc < 2:
                        nc.tensor.wait_ge(p.dve.h, ev_mark)
                    for kc in range(DC):
                        inst = nc.tensor.matmul(SE[bk][:], wq_res[:, kc, mc * 128:(mc + 1) * 128],
                                                xn1c[:, kc, :], start=(kc == 0), stop=(kc == DC - 1))
                    p.pe.inc(inst)
                    p.pe.wait(nc.scalar)
                    nc.scalar.wait_ge(p.dou.h, stq[mc])
                    qfree[bk] = p.act.inc(nc.scalar.activation(
                        qfm[:, mc, :], SE[bk][:], AF.Identity,
                        bias=bq8[:, mc:mc + 1], scale=QSCALE))
                qp_mark = p.pe.n
                if j + 1 < NCH:
                    nc.sync.wait_ge(p.pe.h, qp_mark)
                    lda.inc(nc.sync.dma_start(
                        xn1c[:], dt["xn1_sp"][:, q0 + TC:q0 + 2 * TC]
                        .rearrange("(c p) t -> p c t", p=128)), 16)
                p.act.wait(nc.tensor)
                sfree = [p.act.n, p.act.n]
                o2free = [p.dve.n, p.dve.n]
                state = {"nbdone": p.dve.n, "nbm": p.pe.n}
                prev_norm = None

                def emit_norm(pn):
                    phd, pob, r2m = pn
                    nc.tensor.wait_ge(p.dve.h, r2m)
                    NB = O2E[pob]
                    state["nbm"] = p.pe.inc(nc.tensor.matmul(
                        NB[:], sel33[:], r2ab[:], start=True, stop=True))
                    nc.vector.wait_ge(p.pe.h, state["nbm"])
                    nc.vector.reciprocal(nb_sb[:], NB[:])
                    state["nbdone"] = p.dve.inc(
                        nc.vector.tensor_mul(aout[:, phd, :], o2s[:], nb_sb[:]))
                    o2free[pob] = state["nbdone"]

                def emit_scores(hp, kt, sbk):
                    diag = kt * 128 + 128 > q0
                    base = q0 - kt * 128
                    nc.tensor.wait_ge(p.act.h, sfree[sbk])
                    nc.tensor.matmul(SE[sbk][:], k_res[0:64, hp, kt * 128:(kt + 1) * 128],
                                     qfm[0:64, hp, :], start=True, stop=not diag,
                                     tile_position=(0, 0))
                    inst = nc.tensor.matmul(SO[sbk][:], k_res[64:128, hp, kt * 128:(kt + 1) * 128],
                                            qfm[64:128, hp, :], start=True, stop=not diag,
                                            tile_position=(64, 0))
                    if diag:
                        w = min(TC, 128 - base)
                        rb = rbig[:, 384 + base:384 + base + w]
                        nc.tensor.matmul(SE[sbk][:, 0:w], lneg[:], rb, start=False, stop=True)
                        inst = nc.tensor.matmul(SO[sbk][:, 0:w], lneg[:], rb, start=False, stop=True)
                    return p.pe.inc(inst)

                gl = [(hp, kt) for hp in range(DC) for kt in range(nkt)]
                mm_mark = {}
                pv_mark = {}
                mm_mark[0] = emit_scores(0, 0, 0)
                for g, (hp, kt) in enumerate(gl):
                    he, ho = 2 * hp, 2 * hp + 1
                    ob = hp % 2
                    sbk = g % 2
                    if kt == 1 and prev_norm is not None:
                        emit_norm(prev_norm)
                        prev_norm = None
                    nc.scalar.wait_ge(p.pe.h, mm_mark[g])
                    if g >= 2:
                        nc.scalar.wait_ge(p.pe.h, pv_mark[g - 2])
                    nc.scalar.activation(pbuf[2 * sbk][:], SE[sbk][:], AF.Exp,
                                         bias=0.0, scale=1.0)
                    m = p.act.inc(nc.scalar.activation(pbuf[2 * sbk + 1][:], SO[sbk][:],
                                                       AF.Exp, bias=0.0, scale=1.0))
                    sfree[sbk] = m
                    if g + 1 < len(gl):
                        nhp, nkt_ = gl[g + 1]
                        mm_mark[g + 1] = emit_scores(nhp, nkt_, (g + 1) % 2)
                    if kt == 0:
                        nc.tensor.wait_ge(p.dve.h, o2free[ob])
                    nc.tensor.wait_ge(p.act.h, m)
                    nc.tensor.matmul(O2E[ob][0:65, :], v_res[:, kt, he, :], pbuf[2 * sbk][:],
                                     start=(kt == 0), stop=(kt == nkt - 1),
                                     tile_position=(0, 0))
                    pv_mark[g] = p.pe.inc(nc.tensor.matmul(
                        O2O[ob][0:65, :], v_res[:, kt, ho, :], pbuf[2 * sbk + 1][:],
                        start=(kt == 0), stop=(kt == nkt - 1),
                        tile_position=(0, 0)))
                    if kt == nkt - 1:
                        nc.vector.wait_ge(p.pe.h, max(pv_mark[g], state["nbm"]))
                        nc.vector.tensor_copy(r2ab[0:1, :], O2E[ob][64:65, :])
                        nc.vector.tensor_copy(r2ab[32:33, :], O2O[ob][64:65, :])
                        nc.vector.tensor_copy(o2s[0:64, :], O2E[ob][0:64, :])
                        r2m = p.dve.inc(nc.vector.tensor_copy(o2s[64:128, :], O2O[ob][0:64, :]))
                        prev_norm = (hp, ob, r2m)
                emit_norm(prev_norm)
                prev_norm = None
                nc.tensor.wait_ge(p.dve.h, state["nbdone"])
                nc.tensor.wait_ge(p.act.h, max(sfree))
                ofree = [p.dve.n, p.dve.n]
                for mc in range(DC):
                    bk = mc % 2
                    nc.tensor.wait_ge(p.dve.h, ofree[bk])
                    for kc in range(DC):
                        inst = nc.tensor.matmul(SE[bk][:], wpj_res[:, kc, mc * 128:(mc + 1) * 128],
                                                aout[:, kc, :], start=(kc == 0), stop=(kc == DC - 1))
                    p.pe.inc(inst)
                    p.pe.wait(nc.vector)
                    ofree[bk] = p.dve.inc(nc.vector.tensor_scalar_add(
                        qfm[:, mc, :], SE[bk][:], cols["bp_c"][:, mc:mc + 1]))
                    nc.sync.wait_ge(p.dve.h, p.dve.n)
                    stq[mc] = p.dou.inc(nc.sync.dma_start(
                        dt["xattn_sp"][mc * 128:(mc + 1) * 128, q0:q0 + TC],
                        qfm[:, mc, :]), 16)
                ev_mark = p.dve.n
            nc.sync.wait_ge(p.dou.h, p.dou.n)
        p.bar()

    if k_res is not None:
        es_kv.close()

    if "ffn" in phases:
        with ExitStack() as es:
            t = lambda n, s, d=F32: es.enter_context(nc.sbuf_tensor(n, list(s), d))
            wg = t("wgF", (128, DC, HID), BF16)
            wu = t("wuF", (128, DC, HID), BF16)
            wd = t("wdF", (128, HCN, D), BF16)
            xn1c = t("xn1F", (128, DC, TC), BF16)
            ldf = Sem(nc, "ldF")
            hbuf = t("hF", (128, HCN, TC), BF16)
            xat = t("xatF", (128, DC, TC), BF16)
            xac = t("xaF", (128, DC, TC))
            ps_g = [es.enter_context(nc.psum_tensor("psF0", [128, TC], F32)),
                    es.enter_context(nc.psum_tensor("psF1", [128, TC], F32))]
            ps_u = [es.enter_context(nc.psum_tensor("psFu0", [128, TC], F32)),
                    es.enter_context(nc.psum_tensor("psFu1", [128, TC], F32))]
            ps_d = [es.enter_context(nc.psum_tensor("psFd0", [128, TC], F32)),
                    es.enter_context(nc.psum_tensor("psFd1", [128, TC], F32))]
            p.dgp.inc(nc.gpsimd.dma_start(
                wg[:], dt["wgT"].rearrange("(c p) m -> p c m", p=128)), 16)
            p.dgp.inc(nc.gpsimd.dma_start(
                wu[:], dt["wuT"].rearrange("(c p) m -> p c m", p=128)), 16)
            p.dgp.inc(nc.gpsimd.dma_start(
                wd[:], dt["wdT"].rearrange("(c p) m -> p c m", p=128)), 16)
            p.dgp.wait(nc.tensor)
            ldx = Sem(nc, "ldxF")
            yst_mark = 0
            for i in range(NCH):
                if i == 0:
                    ldf.inc(nc.sync.dma_start(
                        xn1c[:], dt["xn1_sp"][:, 0:TC]
                        .rearrange("(c p) t -> p c t", p=128)), 16)
                    ldx.inc(nc.sync.dma_start(
                        xat[:], dt["xattn_sp"][:, 0:TC]
                        .rearrange("(c p) t -> p c t", p=128)), 16)
                    ldx.inc(nc.sync.dma_start(
                        xac[:], dt["x1_sp"][:, 0:TC]
                        .rearrange("(c p) t -> p c t", p=128)), 16)
                ldf.wait(nc.tensor)
                gfree = [p.act.n] * 2
                ufree = [p.dve.n] * 2
                for hc in range(HCN):
                    bk = hc % 2
                    nc.tensor.wait_ge(p.act.h, gfree[bk])
                    for kc in range(DC):
                        inst = nc.tensor.matmul(ps_g[bk][:], wg[:, kc, hc * 128:(hc + 1) * 128],
                                                xn1c[:, kc, :], start=(kc == 0), stop=(kc == DC - 1))
                    p.pe.inc(inst)
                    nc.tensor.wait_ge(p.dve.h, ufree[bk])
                    for kc in range(DC):
                        inst = nc.tensor.matmul(ps_u[bk][:], wu[:, kc, hc * 128:(hc + 1) * 128],
                                                xn1c[:, kc, :], start=(kc == 0), stop=(kc == DC - 1))
                    p.pe.inc(inst)
                    p.pe.wait(nc.scalar, p.pe.n - 1)
                    gfree[bk] = p.act.inc(nc.scalar.activation(hbuf[:, hc, :], ps_g[bk][:],
                                                               AF.Silu, bias=0.0, scale=QDEQ))
                    p.pe.wait(nc.vector)
                    p.act.wait(nc.vector)
                    ufree[bk] = p.dve.inc(nc.vector.tensor_mul(hbuf[:, hc, :], hbuf[:, hc, :],
                                                               ps_u[bk][:]))
                if i + 1 < NCH:
                    nc.sync.wait_ge(p.pe.h, p.pe.n)
                    ldf.inc(nc.sync.dma_start(
                        xn1c[:], dt["xn1_sp"][:, (i + 1) * TC:(i + 2) * TC]
                        .rearrange("(c p) t -> p c t", p=128)), 16)
                ldx.wait(nc.vector)
                for c in range(DC):
                    p.dve.inc(nc.vector.tensor_tensor(xac[:, c, :], xac[:, c, :],
                                                      xat[:, c, :], ALU.add))
                p.dve.wait(nc.tensor)
                dfree = [p.dve.n] * 2
                for mc in range(DC):
                    bk = mc % 2
                    nc.tensor.wait_ge(p.dve.h, dfree[bk])
                    for hc in range(HCN):
                        inst = nc.tensor.matmul(ps_d[bk][:], wd[:, hc, mc * 128:(mc + 1) * 128],
                                                hbuf[:, hc, :], start=(hc == 0), stop=(hc == HCN - 1))
                    p.pe.inc(inst)
                    p.pe.wait(nc.vector)
                    dfree[bk] = p.dve.inc(nc.vector.scalar_tensor_tensor(
                        xac[:, mc, :], ps_d[bk][:], QDEQ * QDEQ, xac[:, mc, :],
                        ALU.mult, ALU.add))
                nc.sync.wait_ge(p.dve.h, p.dve.n)
                yst_mark = p.dou.inc(nc.sync.dma_start(
                    dt["y"][:, i * TC:(i + 1) * TC].rearrange("(c p) t -> p c t", p=128),
                    xac[:]), 16)
                if i + 1 < NCH:
                    nc.sync.wait_ge(p.dou.h, yst_mark)
                    ldx.inc(nc.sync.dma_start(
                        xat[:], dt["xattn_sp"][:, (i + 1) * TC:(i + 2) * TC]
                        .rearrange("(c p) t -> p c t", p=128)), 16)
                    ldx.inc(nc.sync.dma_start(
                        xac[:], dt["x1_sp"][:, (i + 1) * TC:(i + 2) * TC]
                        .rearrange("(c p) t -> p c t", p=128)), 16)
            nc.sync.wait_ge(p.dou.h, p.dou.n)
        p.bar()
    nc.sync.wait_ge(p.dou.h, p.dou.n)



def _cols(v, nch):
    v = np.ascontiguousarray(np.asarray(v, dtype=np.float32))
    return np.ascontiguousarray(v.reshape(nch, 128).T)


def _t(a):
    return np.ascontiguousarray(np.asarray(a, dtype=np.float32).T)


def _sel33():
    s = np.zeros((33, 128), np.float32)
    s[0, 0:64] = 1.0
    s[32, 64:128] = 1.0
    return s


def prepare_inputs(inputs):
    f = lambda k: np.asarray(inputs[k])
    shared = {
        "qwT": _t(f("L0_attn_qw")), "kwT": _t(f("L0_attn_kw")),
        "vwT": _t(f("L0_attn_vw")), "owT": _t(f("L0_attn_ow")),
        "lqwT": _t(f("L0_attn_lqw")), "lkwT": _t(f("L0_attn_lkw")),
        "lvwT": _t(f("L0_attn_lvw")),
        "qb_c": _cols(f("L0_attn_qb"), 8), "vb_c": _cols(f("L0_attn_vb"), 8),
        "ob_c": _cols(f("L0_attn_ob"), 8),
        "lna_c": _cols(f("L0_attn_lna_w"), 8), "lnb_c": _cols(f("L0_attn_lnb_w"), 8),
        "lnc2": np.ascontiguousarray(np.tile(np.asarray(f("L0_attn_lnc_w"), np.float32), 2)[:, None]),
        "lnd2": np.ascontiguousarray(np.tile(np.asarray(f("L0_attn_lnd_w"), np.float32), 2)[:, None]),
        "lqb2": np.ascontiguousarray(np.tile(np.asarray(f("L0_attn_lqb"), np.float32), 2)[:, None]),
        "lkb2": np.ascontiguousarray(np.tile(np.asarray(f("L0_attn_lkb"), np.float32), 2)[:, None]),
        "lvb2": np.ascontiguousarray(np.tile(np.asarray(f("L0_attn_lvb"), np.float32), 2)[:, None]),
        "cnw1": np.ascontiguousarray(np.asarray(f("L0_ffn_cn_w1"), np.float32)),
        "cnw2": np.ascontiguousarray(np.asarray(f("L0_ffn_cn_w2"), np.float32)),
        "cnw3": np.ascontiguousarray(np.asarray(f("L0_ffn_cn_w3"), np.float32)),
        "cnw4": np.ascontiguousarray(np.asarray(f("L0_ffn_cn_w4"), np.float32)),
        "gcw1": np.ascontiguousarray(np.asarray(f("L0_ffn_gc_w1"), np.float32)),
        "gcw2": np.ascontiguousarray(np.asarray(f("L0_ffn_gc_w2"), np.float32)),
        "gcw3": np.ascontiguousarray(np.asarray(f("L0_ffn_gc_w3"), np.float32)),
        "cnb1_c": _cols(f("L0_ffn_cn_b1"), 8), "cnb2_c": _cols(f("L0_ffn_cn_b2"), 4),
        "cnb3_c": _cols(f("L0_ffn_cn_b3"), 2),
        "cnb4": np.asarray(f("L0_ffn_cn_b4"), np.float32).reshape(1, 1),
        "gcb1_c": _cols(f("L0_ffn_gc_b1"), 4), "gcb2_c": _cols(f("L0_ffn_gc_b2"), 2),
        "gcb3": np.asarray(f("L0_ffn_gc_b3"), np.float32).reshape(1, 1),
        "ar": np.asarray(f("L0_ffn_adaptive_range"), np.float32).reshape(1, 1),
        "wqkv": np.ascontiguousarray(np.asarray(f("L1_attn_W_qkv"), np.float32)),
        "bq_c": _cols(np.asarray(f("L1_attn_b_qkv"), np.float32)[0:D], 8),
        "bk_c": _cols(np.asarray(f("L1_attn_b_qkv"), np.float32)[D:2 * D], 8),
        "bv_row": np.ascontiguousarray(np.asarray(f("L1_attn_b_qkv"), np.float32)[2 * D:3 * D].reshape(1, D)),
        "wproj": np.ascontiguousarray(np.asarray(f("L1_attn_W_proj"), np.float32)),
        "bp_c": _cols(f("L1_attn_b_proj"), 8),
        "wgT": np.ascontiguousarray(np.asarray(f("L1_ffn_w_gate"), np.int32).T),
        "wuT": np.ascontiguousarray(np.asarray(f("L1_ffn_w_up"), np.int32).T),
        "wdT": np.ascontiguousarray(np.asarray(f("L1_ffn_w_down"), np.int32).T),
        "ident": np.eye(128, dtype=BF16NP),
        "lneg": (-60.0 * np.triu(np.ones((128, 128), np.float32))).astype(BF16NP),
        "rbig": (np.arange(128)[:, None] >= np.arange(896)[None, :] - 383)
                .astype(np.float32).astype(BF16NP),
        "mask16": np.kron(np.eye(8, dtype=np.float32),
                          np.ones((16, 16), np.float32)).astype(BF16NP),
        "sel2r": np.ascontiguousarray(np.concatenate(
            [np.concatenate([np.ones(64), np.zeros(64)]).reshape(1, 128),
             np.concatenate([np.zeros(64), np.ones(64)]).reshape(1, 128)]).astype(np.float32)),
        "sel33": _sel33(),
        "sel2c": np.ascontiguousarray(np.concatenate(
            [np.concatenate([np.ones(64), np.zeros(64)]).reshape(1, 128),
             np.concatenate([np.zeros(64), np.ones(64)]).reshape(1, 128)]).astype(np.float32).T
        ).astype(BF16NP),
    }
    x = np.asarray(inputs["L0_attn_x"], dtype=np.float32)
    x16 = x[:, 0:16, :]
    in_maps = []
    for c in range(B):
        m = dict(shared)
        m["x_fm"] = np.ascontiguousarray(x[c].T)
        x16r = np.roll(x16, -c, axis=0).reshape(128, D)
        m["x16_fm"] = np.ascontiguousarray(x16r.T)
        in_maps.append(m)
    return in_maps


def kernel(**inputs):
    in_maps = prepare_inputs(inputs)
    nc, _ = build_program()
    res = run_bass_kernel_spmd(nc, in_maps, core_ids=list(range(B)))
    out = np.stack([np.asarray(r["y"], dtype=np.float32).T for r in res.results])
    return out


# version 12
# speedup vs baseline: 1.1964x; 1.1964x over previous
import numpy as np
import ml_dtypes
import concourse.bass as bass
import concourse.mybir as mybir
from concourse.bass_utils import run_bass_kernel_spmd
from contextlib import ExitStack

F32 = mybir.dt.float32
BF16 = mybir.dt.bfloat16
I32 = mybir.dt.int32
AF = mybir.ActivationFunctionType
ALU = mybir.AluOpType
AX = mybir.AxisListType

B = 8
S = 4096
D = 1024
NH = 16
DH = 64
DC = D // 128
TC = 512
NCH = S // TC
NKT = S // 128
HID = 3072
HCN = HID // 128
EPS = 1e-5
QSCALE = 0.125
QDEQ = 1.0 / 127.0
DIFF_SCALE = 1.0 / (8 * 16 * 4096 * 64)
THRESH = 0.01
FACTOR = 0.1

BF16NP = ml_dtypes.bfloat16


def set_shape(s=4096, tc=512):
    global S, TC, NCH, NKT, DIFF_SCALE
    S, TC = s, tc
    NCH = S // TC
    NKT = S // 128
    DIFF_SCALE = 1.0 / (8 * 16 * S * 64)


class Sem:
    def __init__(self, nc, name):
        self.h = nc.alloc_semaphore(name)
        self.n = 0

    def inc(self, inst, k=1):
        inst.then_inc(self.h, k)
        self.n += k
        return self.n

    def wait(self, eng, val=None):
        eng.wait_ge(self.h, self.n if val is None else val)


class P:

    def __init__(self, nc):
        self.nc = nc
        self.pe = Sem(nc, "s_pe")
        self.act = Sem(nc, "s_act")
        self.dve = Sem(nc, "s_dve")
        self.pool = Sem(nc, "s_pool")
        self.dsy = Sem(nc, "d_sync")
        self.dsc = Sem(nc, "d_scal")
        self.dgp = Sem(nc, "d_gp")
        self.dou = Sem(nc, "d_out")

    def bar(self):
        self.nc.all_engine_barrier()



def ln_rows(p, stats_ps, n, ncols, inv_n, rows):
    nc = p.nc
    rstd = rows[:, 0, :]
    brow = rows[:, 1, :]
    mean = rows[:, 2, :]
    m2 = rows[:, 3, :]
    lnt = rows[:, 4, :]
    p.dve.inc(nc.vector.tensor_scalar_mul(mean, stats_ps[0:n, :], inv_n))
    p.dve.inc(nc.vector.tensor_scalar_mul(rstd, stats_ps[32:32 + n, :], inv_n))
    psum_free = p.dve.n
    p.dve.wait(nc.vector)
    p.dve.inc(nc.vector.tensor_mul(m2, mean, mean))
    p.dve.wait(nc.vector)
    p.dve.inc(nc.vector.scalar_tensor_tensor(rstd, rstd, EPS, m2, ALU.add, ALU.subtract))
    p.dve.wait(nc.scalar)
    p.act.inc(nc.scalar.activation(lnt, rstd, AF.Ln, bias=0.0, scale=1.0))
    p.act.wait(nc.scalar)
    p.act.inc(nc.scalar.activation(rstd, lnt, AF.Exp, bias=0.0, scale=-0.5))
    p.act.wait(nc.vector)
    p.dve.inc(nc.vector.scalar_tensor_tensor(brow, mean, -1.0, rstd, ALU.mult, ALU.mult))
    return psum_free


def stat_sums(p, stats_ps, xb, xsqb, chunks, ncols, ones_col):
    nc = p.nc
    for c in range(chunks):
        nc.tensor.matmul(stats_ps[0:1, :], ones_col, xb[:, c, :],
                         start=(c == 0), stop=(c == chunks - 1),
                         tile_position=(0, 0))
    for c in range(chunks):
        inst = nc.tensor.matmul(stats_ps[32:33, :], ones_col, xsqb[:, c, :],
                                start=(c == 0), stop=(c == chunks - 1),
                                tile_position=(0, 32))
    p.pe.inc(inst)



def build_program(phases=("pre", "t16", "a", "gf", "b", "c", "attn", "ffn"),
                  debug=False):
    nc = bass.Bass("TRN2", target_bir_lowering=False)
    p = P(nc)
    dt = {}

    def din(name, shape, dtyp=F32):
        dt[name] = nc.dram_tensor(name, list(shape), dtyp, kind="ExternalInput").ap()
        return dt[name]

    def dint(name, shape, dtyp=F32):
        dt[name] = nc.dram_tensor(name, list(shape), dtyp, kind="Internal").ap()
        return dt[name]

    def dout(name, shape, dtyp=F32):
        dt[name] = nc.dram_tensor(name, list(shape), dtyp, kind="ExternalOutput").ap()
        return dt[name]

    din("x_fm", (D, S))
    din("x16_fm", (D, 128))
    din("qwT", (D, D)); din("kwT", (D, D)); din("vwT", (D, D)); din("owT", (D, D))
    din("lqwT", (DH, DH)); din("lkwT", (DH, DH)); din("lvwT", (DH, DH))
    din("qb_c", (128, DC)); din("vb_c", (128, DC)); din("ob_c", (128, DC))
    din("lna_c", (128, DC)); din("lnb_c", (128, DC))
    din("lnc2", (128, 1)); din("lnd2", (128, 1))
    din("lqb2", (128, 1)); din("lkb2", (128, 1)); din("lvb2", (128, 1))
    din("cnw1", (D, D)); din("cnw2", (D, 512)); din("cnw3", (512, 256)); din("cnw4", (256, 1))
    din("gcw1", (D, 512)); din("gcw2", (512, 256)); din("gcw3", (256, 1))
    din("cnb1_c", (128, 8)); din("cnb2_c", (128, 4)); din("cnb3_c", (128, 2)); din("cnb4", (1, 1))
    din("gcb1_c", (128, 4)); din("gcb2_c", (128, 2)); din("gcb3", (1, 1))
    din("ar", (1, 1))
    din("wqkv", (D, 3 * D)); din("bq_c", (128, DC)); din("bk_c", (128, DC)); din("bv_row", (1, D))
    din("wproj", (D, D)); din("bp_c", (128, DC))
    din("wgT", (D, HID), I32); din("wuT", (D, HID), I32); din("wdT", (HID, D), I32)
    din("ident", (128, 128), BF16)
    din("mask16", (128, 128), BF16)
    din("lneg", (128, 128), BF16)
    din("rbig", (128, 896), BF16)
    din("sel2r", (2, 128))
    din("sel33", (33, 128))
    din("sel2c", (128, 2), BF16)

    dsc_kind = dout if debug else dint
    dsc_kind("xn_sp", (D, S))
    dsc_kind("x1_sp", (D, S))
    dsc_kind("xattn_sp", (D, S), BF16)
    dsc_kind("xn1_sp", (D, S), BF16)
    dout("y", (D, S))
    if debug:
        for nm, shp, dty in (("dbg_q16", (128, DC, 128), F32), ("dbg_k16", (128, DC, 128), BF16),
                             ("dbg_v16", (128, DC, 128), BF16), ("dbg_kn16", (128, DC, 128), BF16),
                             ("dbg_vi", (128, NH * DH), BF16), ("dbg_attn", (128, DC, 128), F32),
                             ("dbg_y16", (128, DC, 16), F32), ("dbg_xn16", (128, DC, 128), F32),
                             ("dbg_x1pre", (128, DC, 16), F32)):
            dout(nm, shp, dty)

    es_glob = ExitStack()

    sb = {}

    def asb(name, shape, dtyp):
        h = es_glob.enter_context(nc.sbuf_tensor(name, list(shape), dtyp))
        sb[name] = h
        return h

    ones_col_bf = asb("ones_col_bf", (128, 1), BF16)
    ones_row_f = asb("ones_row_f", (1, 128), F32)
    ones_row_bf = asb("ones_row_bf", (1, 128), BF16)
    ident = asb("ident_sb", (128, 128), BF16)
    mask16 = asb("mask16_sb", (128, 128), BF16)
    lneg = asb("lneg_sb", (128, 128), BF16)
    rbig = asb("rbig_sb", (128, 896), BF16)
    sel2r = asb("sel2r_sb", (2, 128), F32)
    sel33 = asb("sel33_sb", (33, 128), F32)
    sel2c = asb("sel2c_sb", (128, 2), BF16)
    cols = {}
    for nm, w in (("qb_c", DC), ("vb_c", DC), ("ob_c", DC), ("lna_c", DC),
                  ("lnb_c", DC), ("bq_c", DC), ("bk_c", DC), ("bp_c", DC),
                  ("cnb1_c", 8), ("cnb2_c", 4), ("cnb3_c", 2),
                  ("gcb1_c", 4), ("gcb2_c", 2)):
        cols[nm] = asb(nm + "_sb", (128, w), F32)
    for nm in ("lnc2", "lnd2", "lqb2", "lkb2", "lvb2"):
        cols[nm] = asb(nm + "_sb", (128, 1), F32)
    for nm in ("cnb4", "gcb3", "ar"):
        cols[nm] = asb(nm + "_sb", (1, 1), F32)
    bq8 = asb("bq8", (128, DC), F32)
    gacc = asb("gacc", (128, DC, NCH), F32)
    gctx_bf = asb("gctx_bf", (128, DC), BF16)
    c1p = asb("c1p", (1, 1), F32)
    c2p = asb("c2p", (1, 1), F32)
    y16 = asb("y16", (128, DC, 16), F32)

    nc.vector.memset(ones_col_bf[:], 1.0)
    nc.vector.memset(ones_row_f[:], 1.0)
    p.dve.inc(nc.vector.memset(ones_row_bf[:], 1.0))
    for nm in ("ident", "mask16", "lneg", "rbig"):
        p.dsy.inc(nc.sync.dma_start(sb[nm + "_sb"][:], dt[nm][:]), 16)
    p.dsy.inc(nc.sync.dma_start(sel2r[:], dt["sel2r"][:]), 16)
    p.dsy.inc(nc.sync.dma_start(sel33[:], dt["sel33"][:]), 16)
    p.dsy.inc(nc.sync.dma_start(sel2c[:], dt["sel2c"][:]), 16)
    for nm in cols:
        p.dsy.inc(nc.sync.dma_start(cols[nm][:], dt[nm][:]), 16)
    p.dsy.wait(nc.vector)
    p.dve.inc(nc.vector.tensor_scalar_mul(bq8[:], cols["bq_c"][:], QSCALE))
    p.bar()

    if "t16" in phases:
        with ExitStack() as es:
            t = lambda n, s, d=F32: es.enter_context(nc.sbuf_tensor(n, list(s), d))
            x16 = t("x16", (128, DC, 128))
            x16b = t("x16b", (128, DC, 128), BF16)
            x16sq = t("x16sq", (128, DC, 128), BF16)
            xn16 = t("xn16", (128, DC, 128))
            a16 = t("a16", (128, DC, 128), BF16)
            b16 = t("b16", (128, DC, 128), BF16)
            wq = t("wq16", (128, DC, D), BF16)
            wk = t("wk16", (128, DC, D), BF16)
            wv = t("wv16", (128, DC, D), BF16)
            wo = t("wo16", (128, DC, D), BF16)
            lbd = {}
            for nm in ("lqwT", "lkwT", "lvwT"):
                lbd[nm] = t(nm + "_bd", (128, 128), BF16)
            rows2 = t("rows2", (1, 5, 128))
            rows4 = t("rows4", (2, 5, 128))
            q16 = t("q16", (128, DC, 128))
            k16 = t("k16", (128, DC, 128), BF16)
            v16 = t("v16", (128, DC, 128), BF16)
            kn16 = t("kn16", (128, DC, 128), BF16)
            vi_tok = t("vi_tok", (128, NH * DH), BF16)
            kk32 = t("kk32", (128, 128))
            kkb = t("kkb", (128, 128), BF16)
            kksq = t("kksq", (128, 128), BF16)
            qcur = [t("qcurA", (128, DC, 128)), t("qcurB", (128, DC, 128))]
            qcb = t("qcb", (128, DC, 128), BF16)
            attn_o = [t("attnA", (128, DC, 128)), t("attnB", (128, DC, 128))]
            prev = [t("prevA", (128, DC, 128)), t("prevB", (128, DC, 128))]
            iout = t("iout", (128, DC, 128))
            pn = t("pn", (128, 128), BF16)
            pt = t("pt", (128, 128), BF16)
            rowred = t("rowred", (128, 1))
            rrecip = t("rrecip", (128, 1))
            dtmp = t("dtmp", (128, 128))
            dcol = t("dcol", (128, DC))
            dcol1 = t("dcol1", (128, 1))
            dcolb = t("dcolb", (128, 1), BF16)
            dscal = t("dscal", (1, 1))
            flt = t("flt", (1, 6))
            donec = t("donec", (128, 1))
            omc = t("omc", (128, 1))
            omnc = t("omnc", (128, 1))

            ps_a = es.enter_context(nc.psum_tensor("ps16a", [128, 128], F32))
            ps_b = es.enter_context(nc.psum_tensor("ps16b", [128, 128], F32))
            ps_st = es.enter_context(nc.psum_tensor("ps16st", [34, 128], F32))
            ps_bc = es.enter_context(nc.psum_tensor("ps16bc", [128, 128], F32))
            ps_bc2 = es.enter_context(nc.psum_tensor("ps16bc2", [128, 128], F32))
            ps_t = es.enter_context(nc.psum_tensor("ps16t", [128, 128], BF16))
            ps_s = es.enter_context(nc.psum_tensor("ps16s", [1, 1], F32))
            ps_c = es.enter_context(nc.psum_tensor("ps16c", [128, 1], F32))

            p.dsy.inc(nc.sync.dma_start(
                x16[:], dt["x16_fm"].rearrange("(c p) t -> p c t", p=128)), 16)
            for wsb, wnm in ((wq, "qwT"), (wk, "kwT"), (wv, "vwT"), (wo, "owT")):
                p.dgp.inc(nc.gpsimd.dma_start(
                    wsb[:], dt[wnm].rearrange("(c p) m -> p c m", p=128)), 16)
            for nm in ("lqwT", "lkwT", "lvwT"):
                p.dve.inc(nc.vector.memset(lbd[nm][0:64, 64:128], 0.0))
                p.dve.inc(nc.vector.memset(lbd[nm][64:128, 0:64], 0.0))
                p.dgp.inc(nc.gpsimd.dma_start(lbd[nm][0:64, 0:64], dt[nm][:]), 16)
                p.dgp.inc(nc.gpsimd.dma_start(lbd[nm][64:128, 64:128], dt[nm][:]), 16)

            p.dsy.wait(nc.vector)
            p.dsy.wait(nc.scalar)
            p.dve.inc(nc.vector.tensor_copy(x16b[:], x16[:]))
            p.act.inc(nc.scalar.activation(x16sq[:], x16[:], AF.Square))
            p.dve.wait(nc.tensor)
            p.act.wait(nc.tensor)
            stat_sums(p, ps_st, x16b, x16sq, DC, 128, ones_col_bf[:])
            p.pe.wait(nc.vector)
            p.pe.wait(nc.scalar)
            ln_rows(p, ps_st, 1, 128, 1.0 / D, rows2)
            p.dve.wait(nc.tensor)
            p.act.wait(nc.tensor)
            nc.tensor.matmul(ps_bc[:], ones_row_f[:], rows2[:, 0, :], start=True, stop=True)
            p.pe.inc(nc.tensor.matmul(ps_bc2[:], ones_row_f[:], rows2[:, 1, :], start=True, stop=True))
            p.pe.wait(nc.vector)
            for c in range(DC):
                p.dve.inc(nc.vector.tensor_mul(iout[:, c, :], x16[:, c, :], ps_bc[:]))
            p.dve.wait(nc.vector)
            for c in range(DC):
                p.dve.inc(nc.vector.tensor_add(xn16[:, c, :], iout[:, c, :], ps_bc2[:]))
            p.dve.wait(nc.scalar)
            for c in range(DC):
                p.act.inc(nc.scalar.activation(a16[:, c, :], xn16[:, c, :], AF.Identity,
                                               bias=0.0, scale=cols["lna_c"][:, c:c + 1]))
                p.act.inc(nc.scalar.activation(b16[:, c, :], xn16[:, c, :], AF.Identity,
                                               bias=0.0, scale=cols["lnb_c"][:, c:c + 1]))

            p.act.wait(nc.tensor)
            p.dgp.wait(nc.tensor)
            def proj16(wsb, src, dst, bias_cols, dst_bf=None):
                for mc in range(DC):
                    ps = (ps_a, ps_b)[mc % 2]
                    if dve_free[mc % 2]:
                        p.dve.wait(nc.tensor, dve_free[mc % 2])
                    for kc in range(DC):
                        inst = nc.tensor.matmul(ps[:], wsb[:, kc, mc * 128:(mc + 1) * 128],
                                                src[:, kc, :], start=(kc == 0), stop=(kc == DC - 1))
                    p.pe.inc(inst)
                    p.pe.wait(nc.vector)
                    if bias_cols is None:
                        i2 = nc.vector.tensor_copy(dst[:, mc, :], ps[:])
                    else:
                        i2 = nc.vector.tensor_scalar_add(dst[:, mc, :], ps[:], bias_cols[:, mc:mc + 1])
                    m = p.dve.inc(i2)
                    if dst_bf is not None:
                        p.dve.wait(nc.vector)
                        m = p.dve.inc(nc.vector.tensor_copy(dst_bf[:, mc, :], dst[:, mc, :]))
                    dve_free[mc % 2] = m
            dve_free = {0: 0, 1: 0}
            proj16(wq, a16, q16, cols["qb_c"])
            proj16(wk, b16, k16, None)
            proj16(wv, b16, v16, cols["vb_c"])

            p.dve.wait(nc.tensor)
            for hp in range(DC):
                p.pe.inc(nc.tensor.matmul(ps_a[:], lbd["lkwT"][:], k16[:, hp, :],
                                          start=True, stop=True))
                p.pe.wait(nc.vector)
                p.pe.wait(nc.scalar)
                p.dve.inc(nc.vector.tensor_scalar_add(kk32[:], ps_a[:], cols["lkb2"][:]))
                p.dve.wait(nc.vector)
                p.dve.wait(nc.scalar)
                p.dve.inc(nc.vector.tensor_copy(kkb[:], kk32[:]))
                p.act.inc(nc.scalar.activation(kksq[:], kk32[:], AF.Square))
                p.dve.wait(nc.tensor)
                p.act.wait(nc.tensor)
                nc.tensor.matmul(ps_st[0:2, :], sel2c[:, 0:2], kkb[:],
                                 start=True, stop=True, tile_position=(0, 0))
                inst = nc.tensor.matmul(ps_st[32:34, :], sel2c[:, 0:2], kksq[:],
                                        start=True, stop=True, tile_position=(0, 32))
                p.pe.inc(inst)
                p.pe.wait(nc.vector)
                p.pe.wait(nc.scalar)
                ln_rows(p, ps_st, 2, 128, 1.0 / DH, rows4)
                p.dve.wait(nc.tensor)
                p.act.wait(nc.tensor)
                nc.tensor.matmul(ps_bc[:], sel2r[:], rows4[:, 0, :], start=True, stop=True)
                p.pe.inc(nc.tensor.matmul(ps_bc2[:], sel2r[:], rows4[:, 1, :], start=True, stop=True))
                p.pe.wait(nc.vector)
                p.dve.inc(nc.vector.tensor_mul(dtmp[:], kk32[:], ps_bc[:]))
                p.dve.wait(nc.vector)
                p.dve.inc(nc.vector.tensor_add(dtmp[:], dtmp[:], ps_bc2[:]))
                p.dve.wait(nc.scalar)
                p.act.inc(nc.scalar.activation(kn16[:, hp, :], dtmp[:], AF.Identity,
                                               bias=0.0, scale=cols["lnd2"][:]))
                p.act.wait(nc.tensor)
                p.pe.inc(nc.tensor.matmul(ps_b[:], lbd["lvwT"][:], v16[:, hp, :],
                                          start=True, stop=True))
                p.pe.wait(nc.vector)
                p.dve.inc(nc.vector.tensor_scalar_add(kkb[:], ps_b[:], cols["lvb2"][:]))
                p.dve.wait(nc.tensor)
                p.pe.inc(nc.tensor.transpose(ps_t[:], kkb[:], ident[:]))
                p.pe.wait(nc.vector)
                p.dve.inc(nc.vector.tensor_copy(
                    vi_tok[:, hp * 128:(hp + 1) * 128], ps_t[:]))
                p.dve.wait(nc.tensor)

            for c in range(DC):
                p.dve.inc(nc.vector.tensor_copy(qcur[0][:, c, :], q16[:, c, :]))
            p.dve.wait(nc.vector)

            cur, nxt = 0, 1
            for it in range(3):
                p.dve.wait(nc.vector)
                for c in range(DC):
                    p.dve.inc(nc.vector.tensor_copy(qcb[:, c, :], qcur[cur][:, c, :]))
                p.dve.wait(nc.tensor)
                for hp in range(DC):
                    p.pe.inc(nc.tensor.matmul(ps_a[:], lbd["lqwT"][:], qcb[:, hp, :],
                                              start=True, stop=True))
                    p.pe.wait(nc.vector)
                    p.pe.wait(nc.scalar)
                    p.dve.inc(nc.vector.tensor_scalar_add(kk32[:], ps_a[:], cols["lqb2"][:]))
                    p.dve.wait(nc.vector)
                    p.dve.wait(nc.scalar)
                    p.dve.inc(nc.vector.tensor_copy(kkb[:], kk32[:]))
                    p.act.inc(nc.scalar.activation(kksq[:], kk32[:], AF.Square))
                    p.dve.wait(nc.tensor)
                    p.act.wait(nc.tensor)
                    nc.tensor.matmul(ps_st[0:2, :], sel2c[:, 0:2], kkb[:],
                                     start=True, stop=True, tile_position=(0, 0))
                    inst = nc.tensor.matmul(ps_st[32:34, :], sel2c[:, 0:2], kksq[:],
                                            start=True, stop=True, tile_position=(0, 32))
                    p.pe.inc(inst)
                    p.pe.wait(nc.vector)
                    p.pe.wait(nc.scalar)
                    ln_rows(p, ps_st, 2, 128, 1.0 / DH, rows4)
                    p.dve.wait(nc.tensor)
                    p.act.wait(nc.tensor)
                    nc.tensor.matmul(ps_bc[:], sel2r[:], rows4[:, 0, :], start=True, stop=True)
                    p.pe.inc(nc.tensor.matmul(ps_bc2[:], sel2r[:], rows4[:, 1, :],
                                              start=True, stop=True))
                    p.pe.wait(nc.vector)
                    p.dve.inc(nc.vector.tensor_mul(dtmp[:], kk32[:], ps_bc[:]))
                    p.dve.wait(nc.vector)
                    p.dve.inc(nc.vector.tensor_add(dtmp[:], dtmp[:], ps_bc2[:]))
                    p.dve.wait(nc.scalar)
                    p.pe.wait(nc.scalar)
                    p.act.inc(nc.scalar.activation(qcb[:, hp, :], dtmp[:], AF.Identity,
                                                   bias=0.0, scale=cols["lnc2"][:]))
                p.act.wait(nc.tensor)
                for h in range(NH):
                    hp, base = h // 2, (h % 2) * 64
                    qn_h = qcb[base:base + 64, hp, :]
                    kn_h = kn16[base:base + 64, hp, :]
                    p.pe.inc(nc.tensor.matmul(ps_a[:], qn_h, kn_h, start=True, stop=True))
                    p.pe.wait(nc.scalar)
                    p.act.inc(nc.scalar.activation(pn[:], ps_a[:], AF.Exp, bias=0.0,
                                                   scale=QSCALE))
                    p.act.wait(nc.vector)
                    p.dve.inc(nc.vector.tensor_mul(pn[:], pn[:], mask16[:]))
                    p.dve.wait(nc.vector)
                    p.dve.inc(nc.vector.reduce_sum(rowred[:], pn[:], AX.X))
                    p.dve.wait(nc.vector)
                    p.dve.inc(nc.vector.reciprocal(rrecip[:], rowred[:]))
                    p.dve.wait(nc.vector)
                    p.dve.inc(nc.vector.tensor_scalar_mul(pn[:], pn[:], rrecip[:]))
                    p.dve.wait(nc.tensor)
                    p.pe.inc(nc.tensor.transpose(ps_t[:], pn[:], ident[:]))
                    p.pe.wait(nc.vector)
                    p.dve.inc(nc.vector.tensor_copy(pt[:], ps_t[:]))
                    p.dve.wait(nc.tensor)
                    p.pe.inc(nc.tensor.matmul(ps_b[0:64, :], vi_tok[:, h * 64:(h + 1) * 64],
                                              pt[:], start=True, stop=True))
                    p.pe.wait(nc.vector)
                    p.dve.inc(nc.vector.tensor_copy(iout[base:base + 64, hp, :], ps_b[0:64, :]))
                p.dve.wait(nc.vector)
                if it == 0:
                    for c in range(DC):
                        p.dve.inc(nc.vector.tensor_copy(attn_o[nxt][:, c, :], iout[:, c, :]))
                        p.dve.inc(nc.vector.tensor_copy(prev[nxt][:, c, :], iout[:, c, :]))
                        p.dve.inc(nc.vector.tensor_add(qcur[nxt][:, c, :], qcur[cur][:, c, :],
                                                       iout[:, c, :]))
                else:
                    for c in range(DC):
                        p.dve.inc(nc.vector.tensor_sub(dtmp[:], iout[:, c, :], prev[cur][:, c, :]))
                        p.dve.wait(nc.vector)
                        p.dve.inc(nc.vector.tensor_reduce(dcol[:, c:c + 1], dtmp[:], AX.X,
                                                          ALU.add, apply_absolute_value=True))
                        p.dve.wait(nc.vector)
                    p.dve.inc(nc.vector.reduce_sum(dcol1[:], dcol[:], AX.X))
                    p.dve.wait(nc.vector)
                    p.dve.inc(nc.vector.tensor_copy(dcolb[:], dcol1[:]))
                    p.dve.wait(nc.tensor)
                    p.pe.inc(nc.tensor.matmul(ps_s[:], dcolb[:], ones_col_bf[:],
                                              start=True, stop=True))
                    p.pe.wait(nc.vector)
                    fl_diff, fl_t1, fl_lt, fl_brk, fl_done, fl_om = [flt[:, i:i + 1] for i in range(6)]
                    p.dve.inc(nc.vector.tensor_scalar_mul(fl_diff, ps_s[:], DIFF_SCALE))
                    p.dve.wait(nc.vector)
                    p.dve.inc(nc.vector.tensor_scalar(fl_t1, fl_diff, FACTOR, THRESH,
                                                      ALU.mult, ALU.add))
                    p.dve.wait(nc.vector)
                    p.dve.inc(nc.vector.tensor_tensor(fl_lt, fl_diff, fl_t1, ALU.is_lt))
                    p.dve.wait(nc.vector)
                    if it == 1:
                        p.dve.inc(nc.vector.tensor_copy(fl_brk, fl_lt))
                        p.dve.wait(nc.vector)
                        p.dve.inc(nc.vector.tensor_copy(fl_done, flt[:, 2:3]))
                        p.dve.wait(nc.vector)
                        for c in range(DC):
                            p.dve.inc(nc.vector.tensor_copy(attn_o[nxt][:, c, :], iout[:, c, :]))
                            p.dve.inc(nc.vector.tensor_copy(prev[nxt][:, c, :], iout[:, c, :]))
                        p.dve.wait(nc.vector)
                        p.dve.inc(nc.vector.tensor_scalar(fl_om, fl_brk, -1.0, 1.0,
                                                          ALU.mult, ALU.add))
                        p.dve.wait(nc.tensor)
                        p.pe.inc(nc.tensor.matmul(ps_c[:], ones_row_f[:], fl_om,
                                                  start=True, stop=True))
                        p.pe.wait(nc.vector)
                        p.dve.inc(nc.vector.tensor_copy(omnc[:], ps_c[:]))
                        p.dve.wait(nc.vector)
                        for c in range(DC):
                            p.dve.wait(nc.vector)
                            p.dve.inc(nc.vector.tensor_scalar_mul(dtmp[:], iout[:, c, :], omnc[:]))
                            p.dve.wait(nc.vector)
                            p.dve.inc(nc.vector.tensor_add(qcur[nxt][:, c, :],
                                                           qcur[cur][:, c, :], dtmp[:]))
                        p.dve.inc(nc.vector.tensor_copy(fl_done, fl_brk))
                        p.dve.wait(nc.vector)
                    else:
                        p.dve.inc(nc.vector.tensor_scalar(fl_om, fl_done, -1.0, 1.0,
                                                          ALU.mult, ALU.add))
                        p.dve.wait(nc.tensor)
                        nc.tensor.matmul(ps_c[:], ones_row_f[:], fl_done, start=True, stop=True)
                        p.pe.inc(nc.tensor.matmul(ps_bc[0:128, 0:1], ones_row_f[:], fl_om,
                                                  start=True, stop=True))
                        p.pe.wait(nc.vector)
                        p.dve.inc(nc.vector.tensor_copy(donec[:], ps_c[:]))
                        p.dve.inc(nc.vector.tensor_copy(omc[:], ps_bc[0:128, 0:1]))
                        p.dve.wait(nc.vector)
                        for c in range(DC):
                            p.dve.wait(nc.vector)
                            p.dve.inc(nc.vector.tensor_scalar_mul(dtmp[:], iout[:, c, :], omc[:]))
                            p.dve.wait(nc.vector)
                            p.dve.inc(nc.vector.tensor_scalar_mul(iout[:, c, :],
                                                                  attn_o[cur][:, c, :], donec[:]))
                            p.dve.wait(nc.vector)
                            p.dve.inc(nc.vector.tensor_add(attn_o[nxt][:, c, :], dtmp[:],
                                                           iout[:, c, :]))
                            p.dve.wait(nc.vector)
                cur, nxt = nxt, cur
                p.dve.wait(nc.tensor)
                p.dve.wait(nc.scalar)

            if "dbg_q16" in dt:
                nc.sync.wait_ge(p.dve.h, p.dve.n)
                nc.sync.wait_ge(p.act.h, p.act.n)
                for nm, sbh in (("dbg_q16", q16), ("dbg_k16", k16), ("dbg_v16", v16),
                                ("dbg_kn16", kn16), ("dbg_vi", vi_tok),
                                ("dbg_attn", attn_o[cur]), ("dbg_xn16", xn16)):
                    p.dou.inc(nc.sync.dma_start(dt[nm][:], sbh[:]), 16)
                nc.sync.wait_ge(p.dou.h, p.dou.n)
            for c in range(DC):
                p.dve.inc(nc.vector.tensor_copy(qcb[:, c, :], attn_o[cur][:, c, :]))
            p.dve.wait(nc.tensor)
            for mc in range(DC):
                for kc in range(DC):
                    inst = nc.tensor.matmul(ps_a[:], wo[:, kc, mc * 128:(mc + 1) * 128],
                                            qcb[:, kc, :], start=(kc == 0), stop=(kc == DC - 1))
                p.pe.inc(inst)
                p.pe.wait(nc.vector)
                p.dve.inc(nc.vector.tensor_copy(y16[:, mc, :], ps_a[0:128, 0:16]))
                p.dve.wait(nc.tensor)
            if "dbg_y16" in dt:
                nc.sync.wait_ge(p.dve.h, p.dve.n)
                p.dou.inc(nc.sync.dma_start(dt["dbg_y16"][:], y16[:]), 16)
                nc.sync.wait_ge(p.dou.h, p.dou.n)
        p.bar()

    if "a" in phases:
        with ExitStack() as es:
            t = lambda n, s, d=F32: es.enter_context(nc.sbuf_tensor(n, list(s), d))
            xch = [t("xchA", (128, DC, TC)), t("xchB", (128, DC, TC))]
            dsx = [Sem(nc, "dsxA0"), Sem(nc, "dsxA1")]
            xb = t("xb", (128, DC, TC), BF16)
            xsq = t("xsq", (128, DC, TC), BF16)
            xn = [t("xnA0", (128, DC, TC)), t("xnA1", (128, DC, TC))]
            tmul = t("tmulA", (128, DC, TC))
            rows2 = t("rows2a", (1, 5, TC))
            ps_st = es.enter_context(nc.psum_tensor("psAst", [34, TC], F32))
            ps_r = es.enter_context(nc.psum_tensor("psAr", [128, TC], F32))
            ps_m = es.enter_context(nc.psum_tensor("psAm", [128, TC], F32))
            buf_free = [0, 0]
            buf_freea = [0, 0]
            st_free = 0
            stat_mark = 0
            bc_mark = 0
            addm = 0
            sp_done = [0, 0]
            for ch in range(NCH + 1):
                if ch < NCH:
                    bi = ch % 2
                    nc.sync.wait_ge(p.dve.h, buf_free[bi])
                    nc.sync.wait_ge(p.act.h, buf_freea[bi])
                    dsx[bi].inc(nc.sync.dma_start(
                        xch[bi][:], dt["x_fm"][:, ch * TC:(ch + 1) * TC]
                        .rearrange("(c p) t -> p c t", p=128)), 16)
                if ch == 0:
                    continue
                i = ch - 1
                bi = i % 2
                si = i % 2
                dsx[bi].wait(nc.vector)
                dsx[bi].wait(nc.scalar)
                nc.vector.wait_ge(p.pe.h, stat_mark)
                nc.scalar.wait_ge(p.pe.h, stat_mark)
                p.dve.inc(nc.vector.tensor_copy(xb[:], xch[bi][:]))
                ba = p.act.inc(nc.scalar.activation(xsq[:], xch[bi][:], AF.Square))
                buf_freea[bi] = ba
                p.dve.wait(nc.tensor)
                p.act.wait(nc.tensor)
                nc.tensor.wait_ge(p.dve.h, st_free)
                stat_sums(p, ps_st, xb, xsq, DC, TC, ones_col_bf[:])
                stat_mark = p.pe.n
                nc.vector.wait_ge(p.pe.h, bc_mark)
                nc.scalar.wait_ge(p.pe.h, bc_mark)
                p.pe.wait(nc.vector)
                p.pe.wait(nc.scalar)
                st_free = ln_rows(p, ps_st, 1, TC, 1.0 / D, rows2)
                st_free = p.dve.n
                p.dve.wait(nc.tensor)
                p.act.wait(nc.tensor)
                nc.tensor.wait_ge(p.dve.h, addm)
                nc.tensor.matmul(ps_r[:], ones_row_f[:], rows2[:, 0, :], start=True, stop=True)
                p.pe.inc(nc.tensor.matmul(ps_m[:], ones_row_f[:], rows2[:, 1, :],
                                          start=True, stop=True))
                bc_mark = p.pe.n
                p.pe.wait(nc.vector)
                for c in range(DC):
                    p.dve.inc(nc.vector.tensor_mul(tmul[:, c, :], xch[bi][:, c, :], ps_r[:]))
                buf_free[bi] = p.dve.n
                p.dve.wait(nc.vector)
                nc.vector.wait_ge(p.dou.h, sp_done[si])
                for c in range(DC):
                    p.dve.inc(nc.vector.tensor_add(xn[si][:, c, :], tmul[:, c, :], ps_m[:]))
                addm = p.dve.n
                p.dve.wait(nc.vector)
                for c in range(DC):
                    p.dve.inc(nc.vector.reduce_sum(gacc[:, c, i:i + 1], xn[si][:, c, :], AX.X))
                nc.sync.wait_ge(p.dve.h, p.dve.n)
                sp_done[si] = p.dou.inc(nc.sync.dma_start(
                    dt["xn_sp"][:, i * TC:(i + 1) * TC].rearrange("(c p) t -> p c t", p=128),
                    xn[si][:]), 16)
            nc.sync.wait_ge(p.dou.h, p.dou.n)
        p.bar()

    if "gf" in phases:
        with ExitStack() as es:
            t = lambda n, s, d=F32: es.enter_context(nc.sbuf_tensor(n, list(s), d))
            gsum = t("gsum", (128, DC))
            h1g = t("h1g", (128, 4), BF16)
            h2g = t("h2g", (128, 2), BF16)
            w1 = t("gw1", (128, DC, 512), BF16)
            w2 = t("gw2", (128, 4, 256), BF16)
            w3 = t("gw3", (128, 2, 1), BF16)
            fl = t("gfl", (1, 4))
            ps_g = es.enter_context(nc.psum_tensor("psG", [128, 512], F32))
            ps_g2 = es.enter_context(nc.psum_tensor("psG2", [1, 1], F32))
            p.dgp.inc(nc.gpsimd.dma_start(w1[:], dt["gcw1"].rearrange("(c p) m -> p c m", p=128)), 16)
            p.dgp.inc(nc.gpsimd.dma_start(w2[:], dt["gcw2"].rearrange("(c p) m -> p c m", p=128)), 16)
            with nc.allow_non_contiguous_dma(reason="tiny [256,1] weight"):
                p.dgp.inc(nc.gpsimd.dma_start(w3[:], dt["gcw3"].rearrange("(c p) m -> p c m", p=128)), 16)
            p.dve.inc(nc.vector.tensor_reduce(gsum[:], gacc[:], AX.X, ALU.add))
            p.dve.wait(nc.vector)
            p.dve.inc(nc.vector.tensor_scalar_mul(gctx_bf[:], gsum[:], 1.0 / S))
            p.dve.wait(nc.tensor)
            p.dgp.wait(nc.tensor)
            for mc in range(4):
                for kc in range(DC):
                    inst = nc.tensor.matmul(ps_g[:, mc:mc + 1],
                                            w1[:, kc, mc * 128:(mc + 1) * 128],
                                            gctx_bf[:, kc:kc + 1],
                                            start=(mc == 0 and kc == 0),
                                            stop=(mc == 3 and kc == DC - 1))
                p.pe.inc(inst)
            p.pe.wait(nc.scalar)
            for mc in range(4):
                p.act.inc(nc.scalar.activation(h1g[:, mc:mc + 1], ps_g[:, mc:mc + 1], AF.Gelu,
                                               bias=cols["gcb1_c"][:, mc:mc + 1], scale=1.0))
            p.act.wait(nc.tensor)
            for mc in range(2):
                for kc in range(4):
                    inst = nc.tensor.matmul(ps_g[:, 8 + mc:9 + mc],
                                            w2[:, kc, mc * 128:(mc + 1) * 128],
                                            h1g[:, kc:kc + 1],
                                            start=(mc == 0 and kc == 0),
                                            stop=(mc == 1 and kc == 3))
                p.pe.inc(inst)
            p.pe.wait(nc.scalar)
            for mc in range(2):
                p.act.inc(nc.scalar.activation(h2g[:, mc:mc + 1], ps_g[:, 8 + mc:9 + mc], AF.Gelu,
                                               bias=cols["gcb2_c"][:, mc:mc + 1], scale=1.0))
            p.act.wait(nc.tensor)
            for kc in range(2):
                inst = nc.tensor.matmul(ps_g2[:], w3[:, kc, :], h2g[:, kc:kc + 1],
                                        start=(kc == 0), stop=(kc == 1))
            p.pe.inc(inst)
            p.pe.wait(nc.vector)
            fgf, ft, fc1, fc2 = [fl[:, i:i + 1] for i in range(4)]
            p.dve.inc(nc.vector.tensor_tensor(ft, ps_g2[:], cols["gcb3"][:], ALU.add))
            p.dve.wait(nc.scalar)
            p.act.inc(nc.scalar.activation(ft, ft, AF.Tanh, bias=0.0, scale=0.5))
            p.act.wait(nc.vector)
            p.dve.inc(nc.vector.tensor_scalar(fgf, ft, 0.5, 0.5, ALU.mult, ALU.add))
            p.dve.wait(nc.vector)
            p.dve.inc(nc.vector.tensor_scalar(fc1, fgf, 1.4, -1.0, ALU.mult, ALU.add))
            p.dve.wait(nc.vector)
            p.dve.inc(nc.vector.tensor_tensor(fc1, fc1, cols["ar"][:], ALU.mult))
            p.dve.inc(nc.vector.tensor_scalar_mul(fc2, cols["ar"][:], 0.3))
            p.dve.wait(nc.vector)
            p.dve.inc(nc.vector.tensor_tensor(c2p[:], fc2, fc2, ALU.bypass))
            p.dve.inc(nc.vector.tensor_scalar(ft, fc2, 1.0, 0.1, ALU.mult, ALU.add))
            p.dve.wait(nc.vector)
            p.dve.inc(nc.vector.tensor_tensor(c1p[:], fc1, ft, ALU.add))
        p.bar()

    build_rest(p, dt, sb, cols, phases)

    return nc, dt


def build_rest(p, dt, sb, cols, phases):
    nc = p.nc
    ones_col_bf = sb["ones_col_bf"]
    ones_row_f = sb["ones_row_f"]
    ones_row_bf = sb["ones_row_bf"]
    sel2r = sb["sel2r_sb"]
    sel33 = sb["sel33_sb"]
    gacc = sb["gacc"]
    y16 = sb["y16"]
    c1p, c2p = sb["c1p"], sb["c2p"]
    bq8 = sb["bq8"]

    if "b" in phases:
        with ExitStack() as es:
            t = lambda n, s, d=F32: es.enter_context(nc.sbuf_tensor(n, list(s), d))
            xch = [t("xchB0", (128, DC, TC)), t("xchB1", (128, DC, TC))]
            xnch = [t("xnchB0", (128, DC, TC)), t("xnchB1", (128, DC, TC))]
            outp = [t("outB0", (128, DC, TC)), t("outB1", (128, DC, TC))]
            tw = [t("twB0", (128, DC, TC)), t("twB1", (128, DC, TC))]
            trow = [t("trowB0", (1, TC)), t("trowB1", (1, TC))]
            cplrow = [t("cplB0", (1, TC)), t("cplB1", (1, TC))]
            ldb = Sem(nc, "ldB")
            outb = [t("outbB0", (128, DC, TC), BF16), t("outbB1", (128, DC, TC), BF16)]
            h1 = t("h1B", (128, DC, TC), BF16)
            h2 = t("h2B", (128, 4, TC), BF16)
            h3 = t("h3B", (128, 2, TC), BF16)
            w1 = t("cw1", (128, DC, D), BF16)
            w2 = t("cw2", (128, DC, 512), BF16)
            w3 = t("cw3", (128, 4, 256), BF16)
            w4 = t("cw4", (128, 2, 1), BF16)
            ps_h = [es.enter_context(nc.psum_tensor("psB0", [128, TC], F32)),
                    es.enter_context(nc.psum_tensor("psB1", [128, TC], F32))]
            ps_t4 = [es.enter_context(nc.psum_tensor("psBt4a", [1, TC], F32)),
                     es.enter_context(nc.psum_tensor("psBt4b", [1, TC], F32))]
            ps_cb = [es.enter_context(nc.psum_tensor("psBcba", [128, TC], F32)),
                     es.enter_context(nc.psum_tensor("psBcbb", [128, TC], F32))]
            p.dgp.inc(nc.gpsimd.dma_start(w1[:], dt["cnw1"].rearrange("(c p) m -> p c m", p=128)), 16)
            p.dgp.inc(nc.gpsimd.dma_start(w2[:], dt["cnw2"].rearrange("(c p) m -> p c m", p=128)), 16)
            p.dgp.inc(nc.gpsimd.dma_start(w3[:], dt["cnw3"].rearrange("(c p) m -> p c m", p=128)), 16)
            with nc.allow_non_contiguous_dma(reason="tiny [256,1] weight"):
                p.dgp.inc(nc.gpsimd.dma_start(w4[:], dt["cnw4"].rearrange("(c p) m -> p c m", p=128)), 16)
            p.dgp.wait(nc.tensor)
            ld_mark = [0, 0]
            copy_mark = [0, 0]
            x1_mark = [0, 0]
            st_mark = [0, 0]
            cbfree = [0, 0]
            t4free = [0, 0]
            hfree = [p.act.n, p.act.n]
            w1_done = [0, 0]
            t4mm = [0, 0]
            cpl_mark = [0, 0]
            cb_mm = [0, 0]

            def emit_loads(i, s):
                nc.sync.wait_ge(p.dve.h, x1_mark[s])
                ldb.inc(nc.sync.dma_start(
                    xch[s][:], dt["x_fm"][:, i * TC:(i + 1) * TC]
                    .rearrange("(c p) t -> p c t", p=128)), 16)
                nc.sync.wait_ge(p.dve.h, copy_mark[s])
                ldb.inc(nc.sync.dma_start(
                    xnch[s][:], dt["xn_sp"][:, i * TC:(i + 1) * TC]
                    .rearrange("(c p) t -> p c t", p=128)), 16)
                ld_mark[s] = ldb.n

            def emit_init(s):
                nc.vector.wait_ge(ldb.h, ld_mark[s])
                for c in range(DC):
                    p.dve.inc(nc.vector.tensor_copy(outp[s][:, c, :], xnch[s][:, c, :]))
                copy_mark[s] = p.dve.n

            ob_mark = [0, 0]

            def emit_copy(s):
                nc.vector.wait_ge(p.pe.h, w1_done[s])
                p.dve.wait(nc.vector)
                p.dve.inc(nc.vector.tensor_copy(outb[s][:], outp[s][:]))
                ob_mark[s] = p.dve.n

            def emit_mms(s):
                for mc in range(DC):
                    bk = mc % 2
                    nc.tensor.wait_ge(p.act.h, hfree[bk])
                    if mc == 0:
                        nc.tensor.wait_ge(p.dve.h, ob_mark[s])
                    for kc in range(DC):
                        inst = nc.tensor.matmul(ps_h[bk][:], w1[:, kc, mc * 128:(mc + 1) * 128],
                                                outb[s][:, kc, :], start=(kc == 0), stop=(kc == DC - 1))
                    p.pe.inc(inst)
                    p.pe.wait(nc.scalar)
                    hfree[bk] = p.act.inc(nc.scalar.activation(h1[:, mc, :], ps_h[bk][:], AF.Gelu,
                                                               bias=cols["cnb1_c"][:, mc:mc + 1], scale=1.0))
                w1_done[s] = p.pe.n
                p.act.wait(nc.tensor)
                for mc in range(4):
                    bk = mc % 2
                    nc.tensor.wait_ge(p.act.h, hfree[bk])
                    for kc in range(DC):
                        inst = nc.tensor.matmul(ps_h[bk][:], w2[:, kc, mc * 128:(mc + 1) * 128],
                                                h1[:, kc, :], start=(kc == 0), stop=(kc == DC - 1))
                    p.pe.inc(inst)
                    p.pe.wait(nc.scalar)
                    hfree[bk] = p.act.inc(nc.scalar.activation(h2[:, mc, :], ps_h[bk][:], AF.Gelu,
                                                               bias=cols["cnb2_c"][:, mc:mc + 1], scale=1.0))
                p.act.wait(nc.tensor)
                for mc in range(2):
                    bk = mc % 2
                    nc.tensor.wait_ge(p.act.h, hfree[bk])
                    for kc in range(4):
                        inst = nc.tensor.matmul(ps_h[bk][:], w3[:, kc, mc * 128:(mc + 1) * 128],
                                                h2[:, kc, :], start=(kc == 0), stop=(kc == 3))
                    p.pe.inc(inst)
                    p.pe.wait(nc.scalar)
                    hfree[bk] = p.act.inc(nc.scalar.activation(h3[:, mc, :], ps_h[bk][:], AF.Gelu,
                                                               bias=cols["cnb3_c"][:, mc:mc + 1], scale=1.0))
                p.act.wait(nc.tensor)
                nc.tensor.wait_ge(p.dve.h, t4free[s])
                for kc in range(2):
                    inst = nc.tensor.matmul(ps_t4[s][:], w4[:, kc, :], h3[:, kc, :],
                                            start=(kc == 0), stop=(kc == 1))
                t4mm[s] = p.pe.inc(inst)

            def emit_tail1(s):
                nc.vector.wait_ge(p.pe.h, t4mm[s])
                p.dve.inc(nc.vector.tensor_scalar_add(trow[s][:], ps_t4[s][:], cols["cnb4"][:]))
                t4free[s] = p.dve.n
                nc.scalar.wait_ge(p.dve.h, t4free[s])
                tanh_m = p.act.inc(nc.scalar.activation(trow[s][:], trow[s][:], AF.Tanh,
                                                        bias=0.0, scale=0.5))
                nc.vector.wait_ge(p.act.h, tanh_m)
                p.dve.inc(nc.vector.tensor_scalar(cplrow[s][:], trow[s][:], c2p[:], None, ALU.mult))
                p.dve.wait(nc.vector)
                cpl_mark[s] = p.dve.inc(nc.vector.tensor_scalar(cplrow[s][:], cplrow[s][:], c1p[:],
                                                                None, ALU.add))
                nc.tensor.wait_ge(p.dve.h, max(cpl_mark[s], cbfree[s]))
                cb_mm[s] = p.pe.inc(nc.tensor.matmul(ps_cb[s][:], ones_row_f[:], cplrow[s][:],
                                                     start=True, stop=True))

            def emit_tail2(s, cyc):
                nc.vector.wait_ge(p.pe.h, cb_mm[s])
                if cyc == 0:
                    nc.vector.wait_ge(p.dou.h, st_mark[s])
                for j in range(DC):
                    src = (j + 4) % DC
                    p.dve.inc(nc.vector.tensor_mul(tw[s][:, j, :], outp[s][:, src, :], ps_cb[s][:]))
                cbfree[s] = p.dve.n
                p.dve.wait(nc.vector)
                for j in range(DC):
                    op = ALU.add if j in (0, 1, 6, 7) else ALU.subtract
                    p.dve.inc(nc.vector.tensor_tensor(outp[s][:, j, :], outp[s][:, j, :],
                                                      tw[s][:, j, :], op))

            def emit_x1(i, s):
                nc.vector.wait_ge(ldb.h, ld_mark[s])
                p.dve.wait(nc.vector)
                for c in range(DC):
                    p.dve.inc(nc.vector.tensor_add(tw[s][:, c, :], xch[s][:, c, :],
                                                   outp[s][:, c, :]))
                x1_mark[s] = p.dve.n
                p.dve.wait(nc.vector)
                for c in range(DC):
                    p.dve.inc(nc.vector.tensor_scalar_add(tw[s][:, c, :], tw[s][:, c, :],
                                                          cols["ob_c"][:, c:c + 1]))
                if i == 0 and "t16" in phases:
                    p.dve.wait(nc.vector)
                    for c in range(DC):
                        p.dve.inc(nc.vector.tensor_add(tw[s][:, c, 0:16],
                                                       tw[s][:, c, 0:16], y16[:, c, :]))
                nc.sync.wait_ge(p.dve.h, p.dve.n)
                st_mark[s] = p.dou.inc(nc.sync.dma_start(
                    dt["x1_sp"][:, i * TC:(i + 1) * TC].rearrange("(c p) t -> p c t", p=128),
                    tw[s][:]), 16)

            emit_loads(0, 0)
            emit_loads(1, 1)
            for pr in range(NCH // 2):
                a, b = 2 * pr, 2 * pr + 1
                emit_init(0)
                emit_init(1)
                emit_copy(0)
                for cyc in range(3):
                    emit_mms(0)
                    emit_tail1(0)
                    emit_copy(1)
                    emit_tail2(0, cyc)
                    emit_mms(1)
                    emit_tail1(1)
                    if cyc < 2:
                        emit_copy(0)
                    emit_tail2(1, cyc)
                emit_x1(a, 0)
                if a + 2 < NCH:
                    emit_loads(a + 2, 0)
                emit_x1(b, 1)
                if b + 2 < NCH:
                    emit_loads(b + 2, 1)
            nc.sync.wait_ge(p.dou.h, p.dou.n)
        p.bar()

    k_res = v_res = None
    wq_res = wpj_res = None
    wq_mark = 0
    if "c" in phases or "attn" in phases:
        es_kv = ExitStack()
        k_res = es_kv.enter_context(nc.sbuf_tensor("k_res", [128, DC, S], BF16))
        v_res = es_kv.enter_context(nc.sbuf_tensor("v_res", [128, NKT, NH, DH + 1], BF16))

    def load_wq_wpj():
        p.dgp.inc(nc.gpsimd.dma_start(
            wq_res[:], dt["wqkv"][:, 0:D].rearrange("(c p) m -> p c m", p=128)), 16)
        p.dgp.inc(nc.gpsimd.dma_start(
            wpj_res[:], dt["wproj"].rearrange("(c p) m -> p c m", p=128)), 16)
        return p.dgp.n
    if "c" in phases:
        with ExitStack() as es:
            t = lambda n, s, d=F32: es.enter_context(nc.sbuf_tensor(n, list(s), d))
            xch = t("xchC", (128, DC, TC))
            ldc = Sem(nc, "ldC")
            xb = t("xbC", (128, DC, TC), BF16)
            xsq = t("xsqC", (128, DC, TC), BF16)
            xn1 = t("xn1C", (128, DC, TC), BF16)
            rows2 = t("rows2c", (1, 5, TC))
            wk = t("wkC", (128, DC, D), BF16)
            ps_st = es.enter_context(nc.psum_tensor("psCst", [34, TC], F32))
            ps_r = es.enter_context(nc.psum_tensor("psCr", [128, TC], F32))
            ps_m = es.enter_context(nc.psum_tensor("psCm", [128, TC], F32))
            ps_k = [es.enter_context(nc.psum_tensor("psCk0", [128, TC], F32)),
                    es.enter_context(nc.psum_tensor("psCk1", [128, TC], F32))]
            p.dgp.inc(nc.gpsimd.dma_start(
                wk[:], dt["wqkv"][:, D:2 * D].rearrange("(c p) m -> p c m", p=128)), 16)
            p.dgp.wait(nc.tensor)
            stat_mark = 0
            bc_mark = 0
            kmm_mark = 0
            sp_done = 0
            st_free = 0
            addm = 0
            ldc.inc(nc.sync.dma_start(
                xch[:], dt["x1_sp"][:, 0:TC].rearrange("(c p) t -> p c t", p=128)), 16)
            for i in range(NCH):
                ldc.wait(nc.vector)
                ldc.wait(nc.scalar)
                nc.vector.wait_ge(p.pe.h, stat_mark)
                nc.scalar.wait_ge(p.pe.h, stat_mark)
                p.dve.inc(nc.vector.tensor_copy(xb[:], xch[:]))
                p.act.inc(nc.scalar.activation(xsq[:], xch[:], AF.Square))
                p.dve.wait(nc.tensor)
                p.act.wait(nc.tensor)
                nc.tensor.wait_ge(p.dve.h, st_free)
                stat_sums(p, ps_st, xb, xsq, DC, TC, ones_col_bf[:])
                stat_mark = p.pe.n
                nc.vector.wait_ge(p.pe.h, bc_mark)
                nc.scalar.wait_ge(p.pe.h, bc_mark)
                p.pe.wait(nc.vector)
                p.pe.wait(nc.scalar)
                st_free = ln_rows(p, ps_st, 1, TC, 1.0 / D, rows2)
                st_free = p.dve.n
                p.dve.wait(nc.tensor)
                p.act.wait(nc.tensor)
                nc.tensor.wait_ge(p.dve.h, addm)
                nc.tensor.matmul(ps_r[:], ones_row_f[:], rows2[:, 0, :], start=True, stop=True)
                p.pe.inc(nc.tensor.matmul(ps_m[:], ones_row_f[:], rows2[:, 1, :],
                                          start=True, stop=True))
                bc_mark = p.pe.n
                p.pe.wait(nc.vector)
                for c in range(DC):
                    p.dve.inc(nc.vector.tensor_mul(xch[:, c, :], xch[:, c, :], ps_r[:]))
                p.dve.wait(nc.vector)
                nc.vector.wait_ge(p.pe.h, kmm_mark)
                nc.vector.wait_ge(p.dou.h, sp_done)
                for c in range(DC):
                    p.dve.inc(nc.vector.tensor_add(xn1[:, c, :], xch[:, c, :], ps_m[:]))
                addm = p.dve.n
                if i + 1 < NCH:
                    nc.sync.wait_ge(p.dve.h, p.dve.n)
                    ldc.inc(nc.sync.dma_start(
                        xch[:], dt["x1_sp"][:, (i + 1) * TC:(i + 2) * TC]
                        .rearrange("(c p) t -> p c t", p=128)), 16)
                nc.sync.wait_ge(p.dve.h, p.dve.n)
                sp_done = p.dou.inc(nc.sync.dma_start(
                    dt["xn1_sp"][:, i * TC:(i + 1) * TC].rearrange("(c p) t -> p c t", p=128),
                    xn1[:]), 16)
                p.dve.wait(nc.tensor)
                kfree = [p.dve.n, p.dve.n]
                for mc in range(DC):
                    bk = mc % 2
                    nc.tensor.wait_ge(p.dve.h, kfree[bk])
                    for kc in range(DC):
                        inst = nc.tensor.matmul(ps_k[bk][:], wk[:, kc, mc * 128:(mc + 1) * 128],
                                                xn1[:, kc, :], start=(kc == 0), stop=(kc == DC - 1))
                    p.pe.inc(inst)
                    p.pe.wait(nc.vector)
                    kfree[bk] = p.dve.inc(nc.vector.tensor_scalar_add(
                        k_res[:, mc, i * TC:(i + 1) * TC], ps_k[bk][:], cols["bk_c"][:, mc:mc + 1]))
                kmm_mark = p.pe.n
            nc.sync.wait_ge(p.dou.h, p.dou.n)
        p.bar()

        if "attn" in phases:
            wq_res = es_kv.enter_context(nc.sbuf_tensor("wq_res", [128, DC, D], BF16))
            wpj_res = es_kv.enter_context(nc.sbuf_tensor("wpj_res", [128, DC, D], BF16))
            wq_mark = load_wq_wpj()

        with ExitStack() as es:
            t = lambda n, s, d=F32: es.enter_context(nc.sbuf_tensor(n, list(s), d))
            xn1b = [t("xn1C2a", (128, DC, TC), BF16), t("xn1C2b", (128, DC, TC), BF16)]
            ldc2 = Sem(nc, "ldC2")
            ldbv = Sem(nc, "ldBV")
            wv = t("wvC2", (128, DC, D), BF16)
            bvrow_f = t("bvrowC2", (1, D), F32)
            bv_bf = t("bvbfC2", (1, D), BF16)
            ps_v = [es.enter_context(nc.psum_tensor("psCv0", [128, 512], F32)),
                    es.enter_context(nc.psum_tensor("psCv1", [128, 512], F32))]
            p.dgp.inc(nc.gpsimd.dma_start(
                wv[:], dt["wqkv"][:, 2 * D:3 * D].rearrange("(c p) m -> p c m", p=128)), 16)
            ldbv.inc(nc.sync.dma_start(bvrow_f[:], dt["bv_row"][:]), 16)
            ldbv.wait(nc.vector)
            p.dve.inc(nc.vector.tensor_copy(bv_bf[:], bvrow_f[:]))
            p.dve.inc(nc.vector.memset(v_res[:, :, :, DH:DH + 1], 1.0))
            p.dgp.wait(nc.tensor)
            p.dve.wait(nc.tensor)
            ld2_mark = [0, 0]
            vmm_mark = [0, 0]
            for s in range(2):
                ldc2.inc(nc.sync.dma_start(
                    xn1b[s][:], dt["xn1_sp"][:, s * TC:(s + 1) * TC]
                    .rearrange("(c p) t -> p c t", p=128)), 16)
                ld2_mark[s] = ldc2.n
            for i in range(NCH):
                s = i % 2
                xn1 = xn1b[s]
                nc.tensor.wait_ge(ldc2.h, ld2_mark[s])
                kfree = [p.dve.n, p.dve.n]
                for ts4 in range(TC // 128):
                    kt = i * (TC // 128) + ts4
                    for half in range(2):
                        bk = half
                        nc.tensor.wait_ge(p.dve.h, kfree[bk])
                        for kc in range(DC):
                            nc.tensor.matmul(ps_v[bk][:], xn1[:, kc, ts4 * 128:(ts4 + 1) * 128],
                                             wv[:, kc, half * 512:(half + 1) * 512],
                                             start=(kc == 0), stop=False)
                        inst = nc.tensor.matmul(ps_v[bk][:], ones_row_bf[:],
                                                bv_bf[:, half * 512:(half + 1) * 512],
                                                start=False, stop=True)
                        p.pe.inc(inst)
                        p.pe.wait(nc.vector)
                        kfree[bk] = p.dve.inc(nc.vector.tensor_copy(
                            v_res[:, kt, half * 8:(half + 1) * 8, 0:DH],
                            ps_v[bk][:].rearrange("p (h d) -> p h d", d=DH)))
                vmm_mark[s] = p.pe.n
                if i + 2 < NCH:
                    nc.sync.wait_ge(p.pe.h, vmm_mark[s])
                    ldc2.inc(nc.sync.dma_start(
                        xn1b[s][:], dt["xn1_sp"][:, (i + 2) * TC:(i + 3) * TC]
                        .rearrange("(c p) t -> p c t", p=128)), 16)
                    ld2_mark[s] = ldc2.n
        p.bar()

    if "attn" in phases:
        with ExitStack() as es:
            t = lambda n, s, d=F32: es.enter_context(nc.sbuf_tensor(n, list(s), d))
            xn1c = t("xn1At", (128, DC, TC), BF16)
            lda = Sem(nc, "ldAt")
            qfm = t("qAt", (128, DC, TC), BF16)
            pbuf = [t("pA0", (128, TC), BF16), t("pA1", (128, TC), BF16),
                    t("pA2", (128, TC), BF16), t("pA3", (128, TC), BF16)]
            o2s = t("o2sAt", (128, TC))
            r2ab = t("r2abAt", (33, TC))
            nb_sb = t("nbsbAt", (128, TC))
            aout = t("aoutAt", (128, DC, TC), BF16)
            lneg = sb["lneg_sb"]
            rbig = sb["rbig_sb"]
            ps_se = es.enter_context(nc.psum_tensor("psSe", [128, TC], F32))
            ps_so = es.enter_context(nc.psum_tensor("psSo", [128, TC], F32))
            ps_se2 = es.enter_context(nc.psum_tensor("psSe2", [128, TC], F32))
            ps_so2 = es.enter_context(nc.psum_tensor("psSo2", [128, TC], F32))
            ps_o2e = es.enter_context(nc.psum_tensor("psO2e", [128, TC], F32))
            ps_o2o = es.enter_context(nc.psum_tensor("psO2o", [128, TC], F32))
            ps_o2e2 = es.enter_context(nc.psum_tensor("psO2e2", [128, TC], F32))
            ps_o2o2 = es.enter_context(nc.psum_tensor("psO2o2", [128, TC], F32))
            SE = [ps_se, ps_se2]
            SO = [ps_so, ps_so2]
            O2E = [ps_o2e, ps_o2e2]
            O2O = [ps_o2o, ps_o2o2]
            if wq_res is None:
                wq_res = es_kv.enter_context(nc.sbuf_tensor("wq_res", [128, DC, D], BF16))
                wpj_res = es_kv.enter_context(nc.sbuf_tensor("wpj_res", [128, DC, D], BF16))
            if wq_mark == 0:
                wq_mark = load_wq_wpj()
            p.dve.inc(nc.vector.memset(r2ab[:], 0.0))
            qp_mark = 0
            ev_mark = 0
            stq = [0] * DC
            for j in range(NCH):
                q0 = j * TC
                nkt = (j + 1) * (TC // 128)
                if j == 0:
                    lda.inc(nc.sync.dma_start(
                        xn1c[:], dt["xn1_sp"][:, q0:q0 + TC]
                        .rearrange("(c p) t -> p c t", p=128)), 16)
                    nc.tensor.wait_ge(p.dgp.h, wq_mark)
                lda.wait(nc.tensor)
                qfree = [p.act.n, p.act.n]
                for mc in range(DC):
                    bk = mc % 2
                    nc.tensor.wait_ge(p.act.h, qfree[bk])
                    if mc < 2:
                        nc.tensor.wait_ge(p.dve.h, ev_mark)
                    for kc in range(DC):
                        inst = nc.tensor.matmul(SE[bk][:], wq_res[:, kc, mc * 128:(mc + 1) * 128],
                                                xn1c[:, kc, :], start=(kc == 0), stop=(kc == DC - 1))
                    p.pe.inc(inst)
                    p.pe.wait(nc.scalar)
                    nc.scalar.wait_ge(p.dou.h, stq[mc])
                    qfree[bk] = p.act.inc(nc.scalar.activation(
                        qfm[:, mc, :], SE[bk][:], AF.Identity,
                        bias=bq8[:, mc:mc + 1], scale=QSCALE))
                qp_mark = p.pe.n
                if j + 1 < NCH:
                    nc.sync.wait_ge(p.pe.h, qp_mark)
                    lda.inc(nc.sync.dma_start(
                        xn1c[:], dt["xn1_sp"][:, q0 + TC:q0 + 2 * TC]
                        .rearrange("(c p) t -> p c t", p=128)), 16)
                p.act.wait(nc.tensor)
                sfree = [p.act.n, p.act.n]
                o2free = [p.dve.n, p.dve.n]
                state = {"nbdone": p.dve.n, "nbm": p.pe.n}
                prev_norm = None

                def emit_norm(pn):
                    phd, pob, r2m = pn
                    nc.tensor.wait_ge(p.dve.h, r2m)
                    NB = O2E[pob]
                    state["nbm"] = p.pe.inc(nc.tensor.matmul(
                        NB[:], sel33[:], r2ab[:], start=True, stop=True))
                    nc.vector.wait_ge(p.pe.h, state["nbm"])
                    nc.vector.reciprocal(nb_sb[:], NB[:])
                    state["nbdone"] = p.dve.inc(
                        nc.vector.tensor_mul(aout[:, phd, :], o2s[:], nb_sb[:]))
                    o2free[pob] = state["nbdone"]

                def emit_scores(hp, kt, sbk):
                    diag = kt * 128 + 128 > q0
                    base = q0 - kt * 128
                    nc.tensor.wait_ge(p.act.h, sfree[sbk])
                    nc.tensor.matmul(SE[sbk][:], k_res[0:64, hp, kt * 128:(kt + 1) * 128],
                                     qfm[0:64, hp, :], start=True, stop=not diag,
                                     tile_position=(0, 0))
                    inst = nc.tensor.matmul(SO[sbk][:], k_res[64:128, hp, kt * 128:(kt + 1) * 128],
                                            qfm[64:128, hp, :], start=True, stop=not diag,
                                            tile_position=(64, 0))
                    if diag:
                        w = min(TC, 128 - base)
                        rb = rbig[:, 384 + base:384 + base + w]
                        nc.tensor.matmul(SE[sbk][:, 0:w], lneg[:], rb, start=False, stop=True)
                        inst = nc.tensor.matmul(SO[sbk][:, 0:w], lneg[:], rb, start=False, stop=True)
                    return p.pe.inc(inst)

                gl = [(hp, kt) for hp in range(DC) for kt in range(nkt)]
                mm_mark = {}
                pv_mark = {}
                mm_mark[0] = emit_scores(0, 0, 0)
                for g, (hp, kt) in enumerate(gl):
                    he, ho = 2 * hp, 2 * hp + 1
                    ob = hp % 2
                    sbk = g % 2
                    if kt == 1 and prev_norm is not None:
                        emit_norm(prev_norm)
                        prev_norm = None
                    nc.scalar.wait_ge(p.pe.h, mm_mark[g])
                    if g >= 2:
                        nc.scalar.wait_ge(p.pe.h, pv_mark[g - 2])
                    nc.scalar.activation(pbuf[2 * sbk][:], SE[sbk][:], AF.Exp,
                                         bias=0.0, scale=1.0)
                    m = p.act.inc(nc.scalar.activation(pbuf[2 * sbk + 1][:], SO[sbk][:],
                                                       AF.Exp, bias=0.0, scale=1.0))
                    sfree[sbk] = m
                    if g + 1 < len(gl):
                        nhp, nkt_ = gl[g + 1]
                        mm_mark[g + 1] = emit_scores(nhp, nkt_, (g + 1) % 2)
                    if kt == 0:
                        nc.tensor.wait_ge(p.dve.h, o2free[ob])
                    nc.tensor.wait_ge(p.act.h, m)
                    nc.tensor.matmul(O2E[ob][0:65, :], v_res[:, kt, he, :], pbuf[2 * sbk][:],
                                     start=(kt == 0), stop=(kt == nkt - 1),
                                     tile_position=(0, 0))
                    pv_mark[g] = p.pe.inc(nc.tensor.matmul(
                        O2O[ob][0:65, :], v_res[:, kt, ho, :], pbuf[2 * sbk + 1][:],
                        start=(kt == 0), stop=(kt == nkt - 1),
                        tile_position=(0, 0)))
                    if kt == nkt - 1:
                        nc.vector.wait_ge(p.pe.h, max(pv_mark[g], state["nbm"]))
                        nc.vector.tensor_copy(r2ab[0:1, :], O2E[ob][64:65, :])
                        nc.vector.tensor_copy(r2ab[32:33, :], O2O[ob][64:65, :])
                        nc.vector.tensor_copy(o2s[0:64, :], O2E[ob][0:64, :])
                        r2m = p.dve.inc(nc.vector.tensor_copy(o2s[64:128, :], O2O[ob][0:64, :]))
                        prev_norm = (hp, ob, r2m)
                emit_norm(prev_norm)
                prev_norm = None
                nc.tensor.wait_ge(p.dve.h, state["nbdone"])
                nc.tensor.wait_ge(p.act.h, max(sfree))
                ofree = [p.dve.n, p.dve.n]
                for mc in range(DC):
                    bk = mc % 2
                    nc.tensor.wait_ge(p.dve.h, ofree[bk])
                    for kc in range(DC):
                        inst = nc.tensor.matmul(SE[bk][:], wpj_res[:, kc, mc * 128:(mc + 1) * 128],
                                                aout[:, kc, :], start=(kc == 0), stop=(kc == DC - 1))
                    p.pe.inc(inst)
                    p.pe.wait(nc.vector)
                    ofree[bk] = p.dve.inc(nc.vector.tensor_scalar_add(
                        qfm[:, mc, :], SE[bk][:], cols["bp_c"][:, mc:mc + 1]))
                    nc.sync.wait_ge(p.dve.h, p.dve.n)
                    stq[mc] = p.dou.inc(nc.sync.dma_start(
                        dt["xattn_sp"][mc * 128:(mc + 1) * 128, q0:q0 + TC],
                        qfm[:, mc, :]), 16)
                ev_mark = p.dve.n
            nc.sync.wait_ge(p.dou.h, p.dou.n)
        p.bar()

    if k_res is not None:
        es_kv.close()

    if "ffn" in phases:
        with ExitStack() as es:
            t = lambda n, s, d=F32: es.enter_context(nc.sbuf_tensor(n, list(s), d))
            wg = t("wgF", (128, DC, HID), BF16)
            wu = t("wuF", (128, DC, HID), BF16)
            wd = t("wdF", (128, HCN, D), BF16)
            xn1c = t("xn1F", (128, DC, TC), BF16)
            ldf = Sem(nc, "ldF")
            hbuf = t("hF", (128, HCN, TC), BF16)
            xat = t("xatF", (128, DC, TC), BF16)
            xac = t("xaF", (128, DC, TC))
            ps_g = [es.enter_context(nc.psum_tensor("psF0", [128, TC], F32)),
                    es.enter_context(nc.psum_tensor("psF1", [128, TC], F32))]
            ps_u = [es.enter_context(nc.psum_tensor("psFu0", [128, TC], F32)),
                    es.enter_context(nc.psum_tensor("psFu1", [128, TC], F32))]
            ps_d = [es.enter_context(nc.psum_tensor("psFd0", [128, TC], F32)),
                    es.enter_context(nc.psum_tensor("psFd1", [128, TC], F32))]
            p.dgp.inc(nc.gpsimd.dma_start(
                wg[:], dt["wgT"].rearrange("(c p) m -> p c m", p=128)), 16)
            p.dgp.inc(nc.gpsimd.dma_start(
                wu[:], dt["wuT"].rearrange("(c p) m -> p c m", p=128)), 16)
            p.dgp.inc(nc.gpsimd.dma_start(
                wd[:], dt["wdT"].rearrange("(c p) m -> p c m", p=128)), 16)
            p.dgp.wait(nc.tensor)
            ldx = Sem(nc, "ldxF")
            yst_mark = 0
            for i in range(NCH):
                if i == 0:
                    ldf.inc(nc.sync.dma_start(
                        xn1c[:], dt["xn1_sp"][:, 0:TC]
                        .rearrange("(c p) t -> p c t", p=128)), 16)
                    ldx.inc(nc.sync.dma_start(
                        xat[:], dt["xattn_sp"][:, 0:TC]
                        .rearrange("(c p) t -> p c t", p=128)), 16)
                    ldx.inc(nc.sync.dma_start(
                        xac[:], dt["x1_sp"][:, 0:TC]
                        .rearrange("(c p) t -> p c t", p=128)), 16)
                ldf.wait(nc.tensor)
                gfree = [p.act.n] * 2
                ufree = [p.dve.n] * 2
                for hc in range(HCN):
                    bk = hc % 2
                    nc.tensor.wait_ge(p.act.h, gfree[bk])
                    for kc in range(DC):
                        inst = nc.tensor.matmul(ps_g[bk][:], wg[:, kc, hc * 128:(hc + 1) * 128],
                                                xn1c[:, kc, :], start=(kc == 0), stop=(kc == DC - 1))
                    p.pe.inc(inst)
                    nc.tensor.wait_ge(p.dve.h, ufree[bk])
                    for kc in range(DC):
                        inst = nc.tensor.matmul(ps_u[bk][:], wu[:, kc, hc * 128:(hc + 1) * 128],
                                                xn1c[:, kc, :], start=(kc == 0), stop=(kc == DC - 1))
                    p.pe.inc(inst)
                    p.pe.wait(nc.scalar, p.pe.n - 1)
                    gfree[bk] = p.act.inc(nc.scalar.activation(hbuf[:, hc, :], ps_g[bk][:],
                                                               AF.Silu, bias=0.0, scale=QDEQ))
                    p.pe.wait(nc.vector)
                    p.act.wait(nc.vector)
                    ufree[bk] = p.dve.inc(nc.vector.tensor_mul(hbuf[:, hc, :], hbuf[:, hc, :],
                                                               ps_u[bk][:]))
                if i + 1 < NCH:
                    nc.sync.wait_ge(p.pe.h, p.pe.n)
                    ldf.inc(nc.sync.dma_start(
                        xn1c[:], dt["xn1_sp"][:, (i + 1) * TC:(i + 2) * TC]
                        .rearrange("(c p) t -> p c t", p=128)), 16)
                ldx.wait(nc.vector)
                for c in range(DC):
                    p.dve.inc(nc.vector.tensor_tensor(xac[:, c, :], xac[:, c, :],
                                                      xat[:, c, :], ALU.add))
                p.dve.wait(nc.tensor)
                dfree = [p.dve.n] * 2
                for mc in range(DC):
                    bk = mc % 2
                    nc.tensor.wait_ge(p.dve.h, dfree[bk])
                    for hc in range(HCN):
                        inst = nc.tensor.matmul(ps_d[bk][:], wd[:, hc, mc * 128:(mc + 1) * 128],
                                                hbuf[:, hc, :], start=(hc == 0), stop=(hc == HCN - 1))
                    p.pe.inc(inst)
                    p.pe.wait(nc.vector)
                    dfree[bk] = p.dve.inc(nc.vector.scalar_tensor_tensor(
                        xac[:, mc, :], ps_d[bk][:], QDEQ * QDEQ, xac[:, mc, :],
                        ALU.mult, ALU.add))
                nc.sync.wait_ge(p.dve.h, p.dve.n)
                yst_mark = p.dou.inc(nc.sync.dma_start(
                    dt["y"][:, i * TC:(i + 1) * TC].rearrange("(c p) t -> p c t", p=128),
                    xac[:]), 16)
                if i + 1 < NCH:
                    nc.sync.wait_ge(p.dou.h, yst_mark)
                    ldx.inc(nc.sync.dma_start(
                        xat[:], dt["xattn_sp"][:, (i + 1) * TC:(i + 2) * TC]
                        .rearrange("(c p) t -> p c t", p=128)), 16)
                    ldx.inc(nc.sync.dma_start(
                        xac[:], dt["x1_sp"][:, (i + 1) * TC:(i + 2) * TC]
                        .rearrange("(c p) t -> p c t", p=128)), 16)
            nc.sync.wait_ge(p.dou.h, p.dou.n)
        p.bar()
    nc.sync.wait_ge(p.dou.h, p.dou.n)



def _cols(v, nch):
    v = np.ascontiguousarray(np.asarray(v, dtype=np.float32))
    return np.ascontiguousarray(v.reshape(nch, 128).T)


def _t(a):
    return np.ascontiguousarray(np.asarray(a, dtype=np.float32).T)


def _sel33():
    s = np.zeros((33, 128), np.float32)
    s[0, 0:64] = 1.0
    s[32, 64:128] = 1.0
    return s


def prepare_inputs(inputs):
    f = lambda k: np.asarray(inputs[k])
    shared = {
        "qwT": _t(f("L0_attn_qw")), "kwT": _t(f("L0_attn_kw")),
        "vwT": _t(f("L0_attn_vw")), "owT": _t(f("L0_attn_ow")),
        "lqwT": _t(f("L0_attn_lqw")), "lkwT": _t(f("L0_attn_lkw")),
        "lvwT": _t(f("L0_attn_lvw")),
        "qb_c": _cols(f("L0_attn_qb"), 8), "vb_c": _cols(f("L0_attn_vb"), 8),
        "ob_c": _cols(f("L0_attn_ob"), 8),
        "lna_c": _cols(f("L0_attn_lna_w"), 8), "lnb_c": _cols(f("L0_attn_lnb_w"), 8),
        "lnc2": np.ascontiguousarray(np.tile(np.asarray(f("L0_attn_lnc_w"), np.float32), 2)[:, None]),
        "lnd2": np.ascontiguousarray(np.tile(np.asarray(f("L0_attn_lnd_w"), np.float32), 2)[:, None]),
        "lqb2": np.ascontiguousarray(np.tile(np.asarray(f("L0_attn_lqb"), np.float32), 2)[:, None]),
        "lkb2": np.ascontiguousarray(np.tile(np.asarray(f("L0_attn_lkb"), np.float32), 2)[:, None]),
        "lvb2": np.ascontiguousarray(np.tile(np.asarray(f("L0_attn_lvb"), np.float32), 2)[:, None]),
        "cnw1": np.ascontiguousarray(np.asarray(f("L0_ffn_cn_w1"), np.float32)),
        "cnw2": np.ascontiguousarray(np.asarray(f("L0_ffn_cn_w2"), np.float32)),
        "cnw3": np.ascontiguousarray(np.asarray(f("L0_ffn_cn_w3"), np.float32)),
        "cnw4": np.ascontiguousarray(np.asarray(f("L0_ffn_cn_w4"), np.float32)),
        "gcw1": np.ascontiguousarray(np.asarray(f("L0_ffn_gc_w1"), np.float32)),
        "gcw2": np.ascontiguousarray(np.asarray(f("L0_ffn_gc_w2"), np.float32)),
        "gcw3": np.ascontiguousarray(np.asarray(f("L0_ffn_gc_w3"), np.float32)),
        "cnb1_c": _cols(f("L0_ffn_cn_b1"), 8), "cnb2_c": _cols(f("L0_ffn_cn_b2"), 4),
        "cnb3_c": _cols(f("L0_ffn_cn_b3"), 2),
        "cnb4": np.asarray(f("L0_ffn_cn_b4"), np.float32).reshape(1, 1),
        "gcb1_c": _cols(f("L0_ffn_gc_b1"), 4), "gcb2_c": _cols(f("L0_ffn_gc_b2"), 2),
        "gcb3": np.asarray(f("L0_ffn_gc_b3"), np.float32).reshape(1, 1),
        "ar": np.asarray(f("L0_ffn_adaptive_range"), np.float32).reshape(1, 1),
        "wqkv": np.ascontiguousarray(np.asarray(f("L1_attn_W_qkv"), np.float32)),
        "bq_c": _cols(np.asarray(f("L1_attn_b_qkv"), np.float32)[0:D], 8),
        "bk_c": _cols(np.asarray(f("L1_attn_b_qkv"), np.float32)[D:2 * D], 8),
        "bv_row": np.ascontiguousarray(np.asarray(f("L1_attn_b_qkv"), np.float32)[2 * D:3 * D].reshape(1, D)),
        "wproj": np.ascontiguousarray(np.asarray(f("L1_attn_W_proj"), np.float32)),
        "bp_c": _cols(f("L1_attn_b_proj"), 8),
        "wgT": np.ascontiguousarray(np.asarray(f("L1_ffn_w_gate"), np.int32).T),
        "wuT": np.ascontiguousarray(np.asarray(f("L1_ffn_w_up"), np.int32).T),
        "wdT": np.ascontiguousarray(np.asarray(f("L1_ffn_w_down"), np.int32).T),
        "ident": np.eye(128, dtype=BF16NP),
        "lneg": (-60.0 * np.triu(np.ones((128, 128), np.float32))).astype(BF16NP),
        "rbig": (np.arange(128)[:, None] >= np.arange(896)[None, :] - 383)
                .astype(np.float32).astype(BF16NP),
        "mask16": np.kron(np.eye(8, dtype=np.float32),
                          np.ones((16, 16), np.float32)).astype(BF16NP),
        "sel2r": np.ascontiguousarray(np.concatenate(
            [np.concatenate([np.ones(64), np.zeros(64)]).reshape(1, 128),
             np.concatenate([np.zeros(64), np.ones(64)]).reshape(1, 128)]).astype(np.float32)),
        "sel33": _sel33(),
        "sel2c": np.ascontiguousarray(np.concatenate(
            [np.concatenate([np.ones(64), np.zeros(64)]).reshape(1, 128),
             np.concatenate([np.zeros(64), np.ones(64)]).reshape(1, 128)]).astype(np.float32).T
        ).astype(BF16NP),
    }
    x = np.asarray(inputs["L0_attn_x"], dtype=np.float32)
    x16 = x[:, 0:16, :]
    in_maps = []
    for c in range(B):
        m = dict(shared)
        m["x_fm"] = np.ascontiguousarray(x[c].T)
        x16r = np.roll(x16, -c, axis=0).reshape(128, D)
        m["x16_fm"] = np.ascontiguousarray(x16r.T)
        in_maps.append(m)
    return in_maps


def kernel(**inputs):
    in_maps = prepare_inputs(inputs)
    nc, _ = build_program()
    res = run_bass_kernel_spmd(nc, in_maps, core_ids=list(range(B)))
    out = np.stack([np.asarray(r["y"], dtype=np.float32).T for r in res.results])
    return out


# version 14
# speedup vs baseline: 1.2103x; 1.0116x over previous
import numpy as np
import ml_dtypes
import concourse.bass as bass
import concourse.mybir as mybir
from concourse.bass_utils import run_bass_kernel_spmd
from contextlib import ExitStack

F32 = mybir.dt.float32
BF16 = mybir.dt.bfloat16
I32 = mybir.dt.int32
AF = mybir.ActivationFunctionType
ALU = mybir.AluOpType
AX = mybir.AxisListType

B = 8
S = 4096
D = 1024
NH = 16
DH = 64
DC = D // 128
TC = 512
NCH = S // TC
NKT = S // 128
HID = 3072
HCN = HID // 128
EPS = 1e-5
QSCALE = 0.125
QDEQ = 1.0 / 127.0
DIFF_SCALE = 1.0 / (8 * 16 * 4096 * 64)
THRESH = 0.01
FACTOR = 0.1

BF16NP = ml_dtypes.bfloat16


def set_shape(s=4096, tc=512):
    global S, TC, NCH, NKT, DIFF_SCALE
    S, TC = s, tc
    NCH = S // TC
    NKT = S // 128
    DIFF_SCALE = 1.0 / (8 * 16 * S * 64)


class Sem:
    def __init__(self, nc, name):
        self.h = nc.alloc_semaphore(name)
        self.n = 0

    def inc(self, inst, k=1):
        inst.then_inc(self.h, k)
        self.n += k
        return self.n

    def wait(self, eng, val=None):
        eng.wait_ge(self.h, self.n if val is None else val)


class P:

    def __init__(self, nc):
        self.nc = nc
        self.pe = Sem(nc, "s_pe")
        self.act = Sem(nc, "s_act")
        self.dve = Sem(nc, "s_dve")
        self.pool = Sem(nc, "s_pool")
        self.dsy = Sem(nc, "d_sync")
        self.dsc = Sem(nc, "d_scal")
        self.dgp = Sem(nc, "d_gp")
        self.dou = Sem(nc, "d_out")

    def bar(self):
        self.nc.all_engine_barrier()



def ln_rows(p, stats_ps, n, ncols, inv_n, rows):
    nc = p.nc
    rstd = rows[:, 0, :]
    brow = rows[:, 1, :]
    mean = rows[:, 2, :]
    m2 = rows[:, 3, :]
    lnt = rows[:, 4, :]
    p.dve.inc(nc.vector.tensor_scalar_mul(mean, stats_ps[0:n, :], inv_n))
    p.dve.inc(nc.vector.tensor_scalar_mul(rstd, stats_ps[32:32 + n, :], inv_n))
    psum_free = p.dve.n
    p.dve.wait(nc.vector)
    p.dve.inc(nc.vector.tensor_mul(m2, mean, mean))
    p.dve.wait(nc.vector)
    p.dve.inc(nc.vector.scalar_tensor_tensor(rstd, rstd, EPS, m2, ALU.add, ALU.subtract))
    p.dve.wait(nc.scalar)
    p.act.inc(nc.scalar.activation(lnt, rstd, AF.Ln, bias=0.0, scale=1.0))
    p.act.wait(nc.scalar)
    p.act.inc(nc.scalar.activation(rstd, lnt, AF.Exp, bias=0.0, scale=-0.5))
    p.act.wait(nc.vector)
    p.dve.inc(nc.vector.scalar_tensor_tensor(brow, mean, -1.0, rstd, ALU.mult, ALU.mult))
    return psum_free


def stat_sums(p, stats_ps, xb, xsqb, chunks, ncols, ones_col):
    nc = p.nc
    for c in range(chunks):
        nc.tensor.matmul(stats_ps[0:1, :], ones_col, xb[:, c, :],
                         start=(c == 0), stop=(c == chunks - 1),
                         tile_position=(0, 0))
    for c in range(chunks):
        inst = nc.tensor.matmul(stats_ps[32:33, :], ones_col, xsqb[:, c, :],
                                start=(c == 0), stop=(c == chunks - 1),
                                tile_position=(0, 32))
    p.pe.inc(inst)



def build_program(phases=("pre", "t16", "a", "gf", "b", "c", "attn", "ffn"),
                  debug=False):
    nc = bass.Bass("TRN2", target_bir_lowering=False)
    p = P(nc)
    dt = {}

    def din(name, shape, dtyp=F32):
        dt[name] = nc.dram_tensor(name, list(shape), dtyp, kind="ExternalInput").ap()
        return dt[name]

    def dint(name, shape, dtyp=F32):
        dt[name] = nc.dram_tensor(name, list(shape), dtyp, kind="Internal").ap()
        return dt[name]

    def dout(name, shape, dtyp=F32):
        dt[name] = nc.dram_tensor(name, list(shape), dtyp, kind="ExternalOutput").ap()
        return dt[name]

    din("x_fm", (D, S))
    din("x16_fm", (D, 128))
    din("qwT", (D, D)); din("kwT", (D, D)); din("vwT", (D, D)); din("owT", (D, D))
    din("lqwT", (DH, DH)); din("lkwT", (DH, DH)); din("lvwT", (DH, DH))
    din("qb_c", (128, DC)); din("vb_c", (128, DC)); din("ob_c", (128, DC))
    din("lna_c", (128, DC)); din("lnb_c", (128, DC))
    din("lnc2", (128, 1)); din("lnd2", (128, 1))
    din("lqb2", (128, 1)); din("lkb2", (128, 1)); din("lvb2", (128, 1))
    din("cnw1", (D, D)); din("cnw2", (D, 512)); din("cnw3", (512, 256)); din("cnw4", (256, 1))
    din("gcw1", (D, 512)); din("gcw2", (512, 256)); din("gcw3", (256, 1))
    din("cnb1_c", (128, 8)); din("cnb2_c", (128, 4)); din("cnb3_c", (128, 2)); din("cnb4", (1, 1))
    din("gcb1_c", (128, 4)); din("gcb2_c", (128, 2)); din("gcb3", (1, 1))
    din("ar", (1, 1))
    din("wqkv", (D, 3 * D)); din("bq_c", (128, DC)); din("bk_c", (128, DC)); din("bv_row", (1, D))
    din("wproj", (D, D)); din("bp_c", (128, DC))
    din("wgT", (D, HID), I32); din("wuT", (D, HID), I32); din("wdT", (HID, D), I32)
    din("ident", (128, 128), BF16)
    din("mask16", (128, 128), BF16)
    din("lneg", (128, 128), BF16)
    din("rbig", (128, 896), BF16)
    din("sel2r", (2, 128))
    din("sel33", (33, 128))
    din("sel2c", (128, 2), BF16)

    dsc_kind = dout if debug else dint
    dsc_kind("xn_sp", (D, S))
    dsc_kind("x1_sp", (D, S))
    dsc_kind("xattn_sp", (D, S), BF16)
    dsc_kind("xn1_sp", (D, S), BF16)
    dout("y", (D, S))
    if debug:
        for nm, shp, dty in (("dbg_q16", (128, DC, 128), F32), ("dbg_k16", (128, DC, 128), BF16),
                             ("dbg_v16", (128, DC, 128), BF16), ("dbg_kn16", (128, DC, 128), BF16),
                             ("dbg_vi", (128, NH * DH), BF16), ("dbg_attn", (128, DC, 128), F32),
                             ("dbg_y16", (128, DC, 16), F32), ("dbg_xn16", (128, DC, 128), F32),
                             ("dbg_x1pre", (128, DC, 16), F32)):
            dout(nm, shp, dty)

    es_glob = ExitStack()

    sb = {}

    def asb(name, shape, dtyp):
        h = es_glob.enter_context(nc.sbuf_tensor(name, list(shape), dtyp))
        sb[name] = h
        return h

    ones_col_bf = asb("ones_col_bf", (128, 1), BF16)
    ones_row_f = asb("ones_row_f", (1, 128), F32)
    ones_row_bf = asb("ones_row_bf", (1, 128), BF16)
    ident = asb("ident_sb", (128, 128), BF16)
    mask16 = asb("mask16_sb", (128, 128), BF16)
    lneg = asb("lneg_sb", (128, 128), BF16)
    rbig = asb("rbig_sb", (128, 896), BF16)
    sel2r = asb("sel2r_sb", (2, 128), F32)
    sel33 = asb("sel33_sb", (33, 128), F32)
    sel2c = asb("sel2c_sb", (128, 2), BF16)
    cols = {}
    for nm, w in (("qb_c", DC), ("vb_c", DC), ("ob_c", DC), ("lna_c", DC),
                  ("lnb_c", DC), ("bq_c", DC), ("bk_c", DC), ("bp_c", DC),
                  ("cnb1_c", 8), ("cnb2_c", 4), ("cnb3_c", 2),
                  ("gcb1_c", 4), ("gcb2_c", 2)):
        cols[nm] = asb(nm + "_sb", (128, w), F32)
    for nm in ("lnc2", "lnd2", "lqb2", "lkb2", "lvb2"):
        cols[nm] = asb(nm + "_sb", (128, 1), F32)
    for nm in ("cnb4", "gcb3", "ar"):
        cols[nm] = asb(nm + "_sb", (1, 1), F32)
    bq8 = asb("bq8", (128, DC), F32)
    gacc = asb("gacc", (128, DC, NCH), F32)
    gctx_bf = asb("gctx_bf", (128, DC), BF16)
    c1p = asb("c1p", (1, 1), F32)
    c2p = asb("c2p", (1, 1), F32)
    y16 = asb("y16", (128, DC, 16), F32)

    nc.vector.memset(ones_col_bf[:], 1.0)
    nc.vector.memset(ones_row_f[:], 1.0)
    p.dve.inc(nc.vector.memset(ones_row_bf[:], 1.0))
    for nm in ("ident", "mask16", "lneg", "rbig"):
        p.dsy.inc(nc.sync.dma_start(sb[nm + "_sb"][:], dt[nm][:]), 16)
    p.dsy.inc(nc.sync.dma_start(sel2r[:], dt["sel2r"][:]), 16)
    p.dsy.inc(nc.sync.dma_start(sel33[:], dt["sel33"][:]), 16)
    p.dsy.inc(nc.sync.dma_start(sel2c[:], dt["sel2c"][:]), 16)
    for nm in cols:
        p.dsy.inc(nc.sync.dma_start(cols[nm][:], dt[nm][:]), 16)
    p.dsy.wait(nc.vector)
    p.dve.inc(nc.vector.tensor_scalar_mul(bq8[:], cols["bq_c"][:], QSCALE))
    p.bar()

    if "t16" in phases:
        with ExitStack() as es:
            t = lambda n, s, d=F32: es.enter_context(nc.sbuf_tensor(n, list(s), d))
            x16 = t("x16", (128, DC, 128))
            x16b = t("x16b", (128, DC, 128), BF16)
            x16sq = t("x16sq", (128, DC, 128), BF16)
            xn16 = t("xn16", (128, DC, 128))
            a16 = t("a16", (128, DC, 128), BF16)
            b16 = t("b16", (128, DC, 128), BF16)
            wq = t("wq16", (128, DC, D), BF16)
            wk = t("wk16", (128, DC, D), BF16)
            wv = t("wv16", (128, DC, D), BF16)
            wo = t("wo16", (128, DC, D), BF16)
            lbd = {}
            for nm in ("lqwT", "lkwT", "lvwT"):
                lbd[nm] = t(nm + "_bd", (128, 128), BF16)
            rows2 = t("rows2", (1, 5, 128))
            rows4 = t("rows4", (2, 5, 512))
            q16 = t("q16", (128, DC, 128))
            k16 = t("k16", (128, DC, 128), BF16)
            v16 = t("v16", (128, DC, 128), BF16)
            kn16 = t("kn16", (128, DC, 128), BF16)
            vi_tok = t("vi_tok", (128, NH * DH), BF16)
            kk32 = t("kk32", (128, 512))
            kkb = t("kkb", (128, 512), BF16)
            kksq = t("kksq", (128, 512), BF16)
            qcur = [t("qcurA", (128, DC, 128)), t("qcurB", (128, DC, 128))]
            qcb = t("qcb", (128, DC, 128), BF16)
            attn_o = [t("attnA", (128, DC, 128)), t("attnB", (128, DC, 128))]
            prev = [t("prevA", (128, DC, 128)), t("prevB", (128, DC, 128))]
            iout = t("iout", (128, DC, 128))
            pn = t("pn", (128, 128), BF16)
            pt = t("pt", (128, 128), BF16)
            rowred = t("rowred", (128, 1))
            rrecip = t("rrecip", (128, 1))
            dtmp = t("dtmp", (128, 512))
            dcol = t("dcol", (128, DC))
            dcol1 = t("dcol1", (128, 1))
            dcolb = t("dcolb", (128, 1), BF16)
            dscal = t("dscal", (1, 1))
            flt = t("flt", (1, 6))
            donec = t("donec", (128, 1))
            omc = t("omc", (128, 1))
            omnc = t("omnc", (128, 1))

            ps_a = es.enter_context(nc.psum_tensor("ps16a", [128, 512], F32))
            ps_b = es.enter_context(nc.psum_tensor("ps16b", [128, 512], F32))
            ps_st = es.enter_context(nc.psum_tensor("ps16st", [34, 512], F32))
            ps_bc = es.enter_context(nc.psum_tensor("ps16bc", [128, 512], F32))
            ps_bc2 = es.enter_context(nc.psum_tensor("ps16bc2", [128, 512], F32))
            ps_t = es.enter_context(nc.psum_tensor("ps16t", [128, 128], BF16))
            ps_s = es.enter_context(nc.psum_tensor("ps16s", [1, 1], F32))
            ps_c = es.enter_context(nc.psum_tensor("ps16c", [128, 1], F32))

            p.dsy.inc(nc.sync.dma_start(
                x16[:], dt["x16_fm"].rearrange("(c p) t -> p c t", p=128)), 16)
            for wsb, wnm in ((wq, "qwT"), (wk, "kwT"), (wv, "vwT"), (wo, "owT")):
                p.dgp.inc(nc.gpsimd.dma_start(
                    wsb[:], dt[wnm].rearrange("(c p) m -> p c m", p=128)), 16)
            for nm in ("lqwT", "lkwT", "lvwT"):
                p.dve.inc(nc.vector.memset(lbd[nm][0:64, 64:128], 0.0))
                p.dve.inc(nc.vector.memset(lbd[nm][64:128, 0:64], 0.0))
                p.dgp.inc(nc.gpsimd.dma_start(lbd[nm][0:64, 0:64], dt[nm][:]), 16)
                p.dgp.inc(nc.gpsimd.dma_start(lbd[nm][64:128, 64:128], dt[nm][:]), 16)

            p.dsy.wait(nc.vector)
            p.dsy.wait(nc.scalar)
            p.dve.inc(nc.vector.tensor_copy(x16b[:], x16[:]))
            p.act.inc(nc.scalar.activation(x16sq[:], x16[:], AF.Square))
            p.dve.wait(nc.tensor)
            p.act.wait(nc.tensor)
            stat_sums(p, ps_st[:, 0:128], x16b, x16sq, DC, 128, ones_col_bf[:])
            p.pe.wait(nc.vector)
            p.pe.wait(nc.scalar)
            ln_rows(p, ps_st[:, 0:128], 1, 128, 1.0 / D, rows2)
            p.dve.wait(nc.tensor)
            p.act.wait(nc.tensor)
            nc.tensor.matmul(ps_bc[:, 0:128], ones_row_f[:], rows2[:, 0, :], start=True, stop=True)
            p.pe.inc(nc.tensor.matmul(ps_bc2[:, 0:128], ones_row_f[:], rows2[:, 1, :],
                                      start=True, stop=True))
            p.pe.wait(nc.vector)
            for c in range(DC):
                p.dve.inc(nc.vector.tensor_mul(iout[:, c, :], x16[:, c, :], ps_bc[:, 0:128]))
            p.dve.wait(nc.vector)
            for c in range(DC):
                p.dve.inc(nc.vector.tensor_add(xn16[:, c, :], iout[:, c, :], ps_bc2[:, 0:128]))
            p.dve.wait(nc.scalar)
            for c in range(DC):
                p.act.inc(nc.scalar.activation(a16[:, c, :], xn16[:, c, :], AF.Identity,
                                               bias=0.0, scale=cols["lna_c"][:, c:c + 1]))
                p.act.inc(nc.scalar.activation(b16[:, c, :], xn16[:, c, :], AF.Identity,
                                               bias=0.0, scale=cols["lnb_c"][:, c:c + 1]))

            p.act.wait(nc.tensor)
            p.dgp.wait(nc.tensor)
            def proj16(wsb, src, dst, bias_cols, dst_bf=None):
                for mc in range(DC):
                    ps = (ps_a, ps_b)[mc % 2][:, 0:128]
                    if dve_free[mc % 2]:
                        p.dve.wait(nc.tensor, dve_free[mc % 2])
                    for kc in range(DC):
                        inst = nc.tensor.matmul(ps, wsb[:, kc, mc * 128:(mc + 1) * 128],
                                                src[:, kc, :], start=(kc == 0), stop=(kc == DC - 1))
                    p.pe.inc(inst)
                    p.pe.wait(nc.vector)
                    if bias_cols is None:
                        i2 = nc.vector.tensor_copy(dst[:, mc, :], ps)
                    else:
                        i2 = nc.vector.tensor_scalar_add(dst[:, mc, :], ps, bias_cols[:, mc:mc + 1])
                    m = p.dve.inc(i2)
                    if dst_bf is not None:
                        p.dve.wait(nc.vector)
                        m = p.dve.inc(nc.vector.tensor_copy(dst_bf[:, mc, :], dst[:, mc, :]))
                    dve_free[mc % 2] = m
            dve_free = {0: 0, 1: 0}
            proj16(wq, a16, q16, cols["qb_c"])
            proj16(wk, b16, k16, None)
            proj16(wv, b16, v16, cols["vb_c"])

            p.dve.wait(nc.tensor)
            for g in range(DC // 4):
                h0 = 4 * g
                k4 = k16[:, h0:h0 + 4, :].rearrange("p a b -> p (a b)")
                v4 = v16[:, h0:h0 + 4, :].rearrange("p a b -> p (a b)")
                kn4 = kn16[:, h0:h0 + 4, :].rearrange("p a b -> p (a b)")
                p.pe.inc(nc.tensor.matmul(ps_a[:], lbd["lkwT"][:], k4,
                                          start=True, stop=True))
                p.pe.wait(nc.vector)
                p.pe.wait(nc.scalar)
                p.dve.inc(nc.vector.tensor_scalar_add(kk32[:], ps_a[:], cols["lkb2"][:]))
                p.dve.wait(nc.vector)
                p.dve.wait(nc.scalar)
                p.dve.inc(nc.vector.tensor_copy(kkb[:], kk32[:]))
                p.act.inc(nc.scalar.activation(kksq[:], kk32[:], AF.Square))
                p.dve.wait(nc.tensor)
                p.act.wait(nc.tensor)
                nc.tensor.matmul(ps_st[0:2, :], sel2c[:, 0:2], kkb[:],
                                 start=True, stop=True, tile_position=(0, 0))
                inst = nc.tensor.matmul(ps_st[32:34, :], sel2c[:, 0:2], kksq[:],
                                        start=True, stop=True, tile_position=(0, 32))
                p.pe.inc(inst)
                p.pe.wait(nc.vector)
                p.pe.wait(nc.scalar)
                ln_rows(p, ps_st, 2, 512, 1.0 / DH, rows4)
                p.dve.wait(nc.tensor)
                p.act.wait(nc.tensor)
                nc.tensor.matmul(ps_bc[:], sel2r[:], rows4[:, 0, :], start=True, stop=True)
                p.pe.inc(nc.tensor.matmul(ps_bc2[:], sel2r[:], rows4[:, 1, :], start=True, stop=True))
                p.pe.wait(nc.vector)
                p.dve.inc(nc.vector.tensor_mul(dtmp[:], kk32[:], ps_bc[:]))
                p.dve.wait(nc.vector)
                p.dve.inc(nc.vector.tensor_add(dtmp[:], dtmp[:], ps_bc2[:]))
                p.dve.wait(nc.scalar)
                p.act.inc(nc.scalar.activation(kn4, dtmp[:], AF.Identity,
                                               bias=0.0, scale=cols["lnd2"][:]))
                p.act.wait(nc.tensor)
                p.pe.inc(nc.tensor.matmul(ps_b[:], lbd["lvwT"][:], v4,
                                          start=True, stop=True))
                p.pe.wait(nc.vector)
                p.dve.inc(nc.vector.tensor_scalar_add(kkb[:], ps_b[:], cols["lvb2"][:]))
                p.dve.wait(nc.tensor)
                for q4 in range(4):
                    hp = h0 + q4
                    p.pe.inc(nc.tensor.transpose(ps_t[:], kkb[:, q4 * 128:(q4 + 1) * 128],
                                                 ident[:]))
                    p.pe.wait(nc.vector)
                    p.dve.inc(nc.vector.tensor_copy(
                        vi_tok[:, hp * 128:(hp + 1) * 128], ps_t[:]))
                    p.dve.wait(nc.tensor)

            for c in range(DC):
                p.dve.inc(nc.vector.tensor_copy(qcur[0][:, c, :], q16[:, c, :]))
            p.dve.wait(nc.vector)

            cur, nxt = 0, 1
            for it in range(3):
                p.dve.wait(nc.vector)
                for c in range(DC):
                    p.dve.inc(nc.vector.tensor_copy(qcb[:, c, :], qcur[cur][:, c, :]))
                p.dve.wait(nc.tensor)
                for g in range(DC // 4):
                    h0 = 4 * g
                    q4 = qcb[:, h0:h0 + 4, :].rearrange("p a b -> p (a b)")
                    p.pe.inc(nc.tensor.matmul(ps_a[:], lbd["lqwT"][:], q4,
                                              start=True, stop=True))
                    p.pe.wait(nc.vector)
                    p.pe.wait(nc.scalar)
                    p.dve.inc(nc.vector.tensor_scalar_add(kk32[:], ps_a[:], cols["lqb2"][:]))
                    p.dve.wait(nc.vector)
                    p.dve.wait(nc.scalar)
                    p.dve.inc(nc.vector.tensor_copy(kkb[:], kk32[:]))
                    p.act.inc(nc.scalar.activation(kksq[:], kk32[:], AF.Square))
                    p.dve.wait(nc.tensor)
                    p.act.wait(nc.tensor)
                    nc.tensor.matmul(ps_st[0:2, :], sel2c[:, 0:2], kkb[:],
                                     start=True, stop=True, tile_position=(0, 0))
                    inst = nc.tensor.matmul(ps_st[32:34, :], sel2c[:, 0:2], kksq[:],
                                            start=True, stop=True, tile_position=(0, 32))
                    p.pe.inc(inst)
                    p.pe.wait(nc.vector)
                    p.pe.wait(nc.scalar)
                    ln_rows(p, ps_st, 2, 512, 1.0 / DH, rows4)
                    p.dve.wait(nc.tensor)
                    p.act.wait(nc.tensor)
                    nc.tensor.matmul(ps_bc[:], sel2r[:], rows4[:, 0, :], start=True, stop=True)
                    p.pe.inc(nc.tensor.matmul(ps_bc2[:], sel2r[:], rows4[:, 1, :],
                                              start=True, stop=True))
                    p.pe.wait(nc.vector)
                    p.dve.inc(nc.vector.tensor_mul(dtmp[:], kk32[:], ps_bc[:]))
                    p.dve.wait(nc.vector)
                    p.dve.inc(nc.vector.tensor_add(dtmp[:], dtmp[:], ps_bc2[:]))
                    p.dve.wait(nc.scalar)
                    p.pe.wait(nc.scalar)
                    p.act.inc(nc.scalar.activation(q4, dtmp[:], AF.Identity,
                                                   bias=0.0, scale=cols["lnc2"][:]))
                p.act.wait(nc.tensor)
                for h in range(NH):
                    hp, base = h // 2, (h % 2) * 64
                    qn_h = qcb[base:base + 64, hp, :]
                    kn_h = kn16[base:base + 64, hp, :]
                    p.pe.inc(nc.tensor.matmul(ps_a[:, 0:128], qn_h, kn_h, start=True, stop=True))
                    p.pe.wait(nc.scalar)
                    p.act.inc(nc.scalar.activation(pn[:], ps_a[:, 0:128], AF.Exp, bias=0.0,
                                                   scale=QSCALE))
                    p.act.wait(nc.vector)
                    p.dve.inc(nc.vector.tensor_mul(pn[:], pn[:], mask16[:]))
                    p.dve.wait(nc.vector)
                    p.dve.inc(nc.vector.reduce_sum(rowred[:], pn[:], AX.X))
                    p.dve.wait(nc.vector)
                    p.dve.inc(nc.vector.reciprocal(rrecip[:], rowred[:]))
                    p.dve.wait(nc.vector)
                    p.dve.inc(nc.vector.tensor_scalar_mul(pn[:], pn[:], rrecip[:]))
                    p.dve.wait(nc.tensor)
                    p.pe.inc(nc.tensor.transpose(ps_t[:], pn[:], ident[:]))
                    p.pe.wait(nc.vector)
                    p.dve.inc(nc.vector.tensor_copy(pt[:], ps_t[:]))
                    p.dve.wait(nc.tensor)
                    p.pe.inc(nc.tensor.matmul(ps_b[0:64, 0:128], vi_tok[:, h * 64:(h + 1) * 64],
                                              pt[:], start=True, stop=True))
                    p.pe.wait(nc.vector)
                    p.dve.inc(nc.vector.tensor_copy(iout[base:base + 64, hp, :],
                                                    ps_b[0:64, 0:128]))
                p.dve.wait(nc.vector)
                if it == 0:
                    for c in range(DC):
                        p.dve.inc(nc.vector.tensor_copy(attn_o[nxt][:, c, :], iout[:, c, :]))
                        p.dve.inc(nc.vector.tensor_copy(prev[nxt][:, c, :], iout[:, c, :]))
                        p.dve.inc(nc.vector.tensor_add(qcur[nxt][:, c, :], qcur[cur][:, c, :],
                                                       iout[:, c, :]))
                else:
                    for c in range(DC):
                        p.dve.inc(nc.vector.tensor_sub(dtmp[:, 0:128], iout[:, c, :], prev[cur][:, c, :]))
                        p.dve.wait(nc.vector)
                        p.dve.inc(nc.vector.tensor_reduce(dcol[:, c:c + 1], dtmp[:, 0:128], AX.X,
                                                          ALU.add, apply_absolute_value=True))
                        p.dve.wait(nc.vector)
                    p.dve.inc(nc.vector.reduce_sum(dcol1[:], dcol[:], AX.X))
                    p.dve.wait(nc.vector)
                    p.dve.inc(nc.vector.tensor_copy(dcolb[:], dcol1[:]))
                    p.dve.wait(nc.tensor)
                    p.pe.inc(nc.tensor.matmul(ps_s[:], dcolb[:], ones_col_bf[:],
                                              start=True, stop=True))
                    p.pe.wait(nc.vector)
                    fl_diff, fl_t1, fl_lt, fl_brk, fl_done, fl_om = [flt[:, i:i + 1] for i in range(6)]
                    p.dve.inc(nc.vector.tensor_scalar_mul(fl_diff, ps_s[:], DIFF_SCALE))
                    p.dve.wait(nc.vector)
                    p.dve.inc(nc.vector.tensor_scalar(fl_t1, fl_diff, FACTOR, THRESH,
                                                      ALU.mult, ALU.add))
                    p.dve.wait(nc.vector)
                    p.dve.inc(nc.vector.tensor_tensor(fl_lt, fl_diff, fl_t1, ALU.is_lt))
                    p.dve.wait(nc.vector)
                    if it == 1:
                        p.dve.inc(nc.vector.tensor_copy(fl_brk, fl_lt))
                        p.dve.wait(nc.vector)
                        p.dve.inc(nc.vector.tensor_copy(fl_done, flt[:, 2:3]))
                        p.dve.wait(nc.vector)
                        for c in range(DC):
                            p.dve.inc(nc.vector.tensor_copy(attn_o[nxt][:, c, :], iout[:, c, :]))
                            p.dve.inc(nc.vector.tensor_copy(prev[nxt][:, c, :], iout[:, c, :]))
                        p.dve.wait(nc.vector)
                        p.dve.inc(nc.vector.tensor_scalar(fl_om, fl_brk, -1.0, 1.0,
                                                          ALU.mult, ALU.add))
                        p.dve.wait(nc.tensor)
                        p.pe.inc(nc.tensor.matmul(ps_c[:], ones_row_f[:], fl_om,
                                                  start=True, stop=True))
                        p.pe.wait(nc.vector)
                        p.dve.inc(nc.vector.tensor_copy(omnc[:], ps_c[:]))
                        p.dve.wait(nc.vector)
                        for c in range(DC):
                            p.dve.wait(nc.vector)
                            p.dve.inc(nc.vector.tensor_scalar_mul(dtmp[:, 0:128], iout[:, c, :], omnc[:]))
                            p.dve.wait(nc.vector)
                            p.dve.inc(nc.vector.tensor_add(qcur[nxt][:, c, :],
                                                           qcur[cur][:, c, :], dtmp[:, 0:128]))
                        p.dve.inc(nc.vector.tensor_copy(fl_done, fl_brk))
                        p.dve.wait(nc.vector)
                    else:
                        p.dve.inc(nc.vector.tensor_scalar(fl_om, fl_done, -1.0, 1.0,
                                                          ALU.mult, ALU.add))
                        p.dve.wait(nc.tensor)
                        nc.tensor.matmul(ps_c[:], ones_row_f[:], fl_done, start=True, stop=True)
                        p.pe.inc(nc.tensor.matmul(ps_bc[0:128, 0:1], ones_row_f[:], fl_om,
                                                  start=True, stop=True))
                        p.pe.wait(nc.vector)
                        p.dve.inc(nc.vector.tensor_copy(donec[:], ps_c[:]))
                        p.dve.inc(nc.vector.tensor_copy(omc[:], ps_bc[0:128, 0:1]))
                        p.dve.wait(nc.vector)
                        for c in range(DC):
                            p.dve.wait(nc.vector)
                            p.dve.inc(nc.vector.tensor_scalar_mul(dtmp[:, 0:128], iout[:, c, :], omc[:]))
                            p.dve.wait(nc.vector)
                            p.dve.inc(nc.vector.tensor_scalar_mul(iout[:, c, :],
                                                                  attn_o[cur][:, c, :], donec[:]))
                            p.dve.wait(nc.vector)
                            p.dve.inc(nc.vector.tensor_add(attn_o[nxt][:, c, :], dtmp[:, 0:128],
                                                           iout[:, c, :]))
                            p.dve.wait(nc.vector)
                cur, nxt = nxt, cur
                p.dve.wait(nc.tensor)
                p.dve.wait(nc.scalar)

            if "dbg_q16" in dt:
                nc.sync.wait_ge(p.dve.h, p.dve.n)
                nc.sync.wait_ge(p.act.h, p.act.n)
                for nm, sbh in (("dbg_q16", q16), ("dbg_k16", k16), ("dbg_v16", v16),
                                ("dbg_kn16", kn16), ("dbg_vi", vi_tok),
                                ("dbg_attn", attn_o[cur]), ("dbg_xn16", xn16)):
                    p.dou.inc(nc.sync.dma_start(dt[nm][:], sbh[:]), 16)
                nc.sync.wait_ge(p.dou.h, p.dou.n)
            for c in range(DC):
                p.dve.inc(nc.vector.tensor_copy(qcb[:, c, :], attn_o[cur][:, c, :]))
            p.dve.wait(nc.tensor)
            for mc in range(DC):
                for kc in range(DC):
                    inst = nc.tensor.matmul(ps_a[:, 0:128], wo[:, kc, mc * 128:(mc + 1) * 128],
                                            qcb[:, kc, :], start=(kc == 0), stop=(kc == DC - 1))
                p.pe.inc(inst)
                p.pe.wait(nc.vector)
                p.dve.inc(nc.vector.tensor_copy(y16[:, mc, :], ps_a[0:128, 0:16]))
                p.dve.wait(nc.tensor)
            if "dbg_y16" in dt:
                nc.sync.wait_ge(p.dve.h, p.dve.n)
                p.dou.inc(nc.sync.dma_start(dt["dbg_y16"][:], y16[:]), 16)
                nc.sync.wait_ge(p.dou.h, p.dou.n)
        p.bar()

    if "a" in phases:
        with ExitStack() as es:
            t = lambda n, s, d=F32: es.enter_context(nc.sbuf_tensor(n, list(s), d))
            xch = [t("xchA", (128, DC, TC)), t("xchB", (128, DC, TC))]
            dsx = [Sem(nc, "dsxA0"), Sem(nc, "dsxA1")]
            xb = t("xb", (128, DC, TC), BF16)
            xsq = t("xsq", (128, DC, TC), BF16)
            xn = [t("xnA0", (128, DC, TC)), t("xnA1", (128, DC, TC))]
            tmul = t("tmulA", (128, DC, TC))
            rows2 = t("rows2a", (1, 5, TC))
            ps_st = es.enter_context(nc.psum_tensor("psAst", [34, TC], F32))
            ps_r = es.enter_context(nc.psum_tensor("psAr", [128, TC], F32))
            ps_m = es.enter_context(nc.psum_tensor("psAm", [128, TC], F32))
            buf_free = [0, 0]
            buf_freea = [0, 0]
            st_free = 0
            stat_mark = 0
            bc_mark = 0
            addm = 0
            sp_done = [0, 0]
            for ch in range(NCH + 1):
                if ch < NCH:
                    bi = ch % 2
                    nc.sync.wait_ge(p.dve.h, buf_free[bi])
                    nc.sync.wait_ge(p.act.h, buf_freea[bi])
                    dsx[bi].inc(nc.sync.dma_start(
                        xch[bi][:], dt["x_fm"][:, ch * TC:(ch + 1) * TC]
                        .rearrange("(c p) t -> p c t", p=128)), 16)
                if ch == 0:
                    continue
                i = ch - 1
                bi = i % 2
                si = i % 2
                dsx[bi].wait(nc.vector)
                dsx[bi].wait(nc.scalar)
                nc.vector.wait_ge(p.pe.h, stat_mark)
                nc.scalar.wait_ge(p.pe.h, stat_mark)
                p.dve.inc(nc.vector.tensor_copy(xb[:], xch[bi][:]))
                ba = p.act.inc(nc.scalar.activation(xsq[:], xch[bi][:], AF.Square))
                buf_freea[bi] = ba
                p.dve.wait(nc.tensor)
                p.act.wait(nc.tensor)
                nc.tensor.wait_ge(p.dve.h, st_free)
                stat_sums(p, ps_st, xb, xsq, DC, TC, ones_col_bf[:])
                stat_mark = p.pe.n
                nc.vector.wait_ge(p.pe.h, bc_mark)
                nc.scalar.wait_ge(p.pe.h, bc_mark)
                p.pe.wait(nc.vector)
                p.pe.wait(nc.scalar)
                st_free = ln_rows(p, ps_st, 1, TC, 1.0 / D, rows2)
                st_free = p.dve.n
                p.dve.wait(nc.tensor)
                p.act.wait(nc.tensor)
                nc.tensor.wait_ge(p.dve.h, addm)
                nc.tensor.matmul(ps_r[:], ones_row_f[:], rows2[:, 0, :], start=True, stop=True)
                p.pe.inc(nc.tensor.matmul(ps_m[:], ones_row_f[:], rows2[:, 1, :],
                                          start=True, stop=True))
                bc_mark = p.pe.n
                p.pe.wait(nc.vector)
                for c in range(DC):
                    p.dve.inc(nc.vector.tensor_mul(tmul[:, c, :], xch[bi][:, c, :], ps_r[:]))
                buf_free[bi] = p.dve.n
                p.dve.wait(nc.vector)
                nc.vector.wait_ge(p.dou.h, sp_done[si])
                for c in range(DC):
                    p.dve.inc(nc.vector.tensor_add(xn[si][:, c, :], tmul[:, c, :], ps_m[:]))
                addm = p.dve.n
                p.dve.wait(nc.vector)
                for c in range(DC):
                    p.dve.inc(nc.vector.reduce_sum(gacc[:, c, i:i + 1], xn[si][:, c, :], AX.X))
                nc.sync.wait_ge(p.dve.h, p.dve.n)
                sp_done[si] = p.dou.inc(nc.sync.dma_start(
                    dt["xn_sp"][:, i * TC:(i + 1) * TC].rearrange("(c p) t -> p c t", p=128),
                    xn[si][:]), 16)
            nc.sync.wait_ge(p.dou.h, p.dou.n)
        p.bar()

    if "gf" in phases:
        with ExitStack() as es:
            t = lambda n, s, d=F32: es.enter_context(nc.sbuf_tensor(n, list(s), d))
            gsum = t("gsum", (128, DC))
            h1g = t("h1g", (128, 4), BF16)
            h2g = t("h2g", (128, 2), BF16)
            w1 = t("gw1", (128, DC, 512), BF16)
            w2 = t("gw2", (128, 4, 256), BF16)
            w3 = t("gw3", (128, 2, 1), BF16)
            fl = t("gfl", (1, 4))
            ps_g = es.enter_context(nc.psum_tensor("psG", [128, 512], F32))
            ps_g2 = es.enter_context(nc.psum_tensor("psG2", [1, 1], F32))
            p.dgp.inc(nc.gpsimd.dma_start(w1[:], dt["gcw1"].rearrange("(c p) m -> p c m", p=128)), 16)
            p.dgp.inc(nc.gpsimd.dma_start(w2[:], dt["gcw2"].rearrange("(c p) m -> p c m", p=128)), 16)
            with nc.allow_non_contiguous_dma(reason="tiny [256,1] weight"):
                p.dgp.inc(nc.gpsimd.dma_start(w3[:], dt["gcw3"].rearrange("(c p) m -> p c m", p=128)), 16)
            p.dve.inc(nc.vector.tensor_reduce(gsum[:], gacc[:], AX.X, ALU.add))
            p.dve.wait(nc.vector)
            p.dve.inc(nc.vector.tensor_scalar_mul(gctx_bf[:], gsum[:], 1.0 / S))
            p.dve.wait(nc.tensor)
            p.dgp.wait(nc.tensor)
            for mc in range(4):
                for kc in range(DC):
                    inst = nc.tensor.matmul(ps_g[:, mc:mc + 1],
                                            w1[:, kc, mc * 128:(mc + 1) * 128],
                                            gctx_bf[:, kc:kc + 1],
                                            start=(mc == 0 and kc == 0),
                                            stop=(mc == 3 and kc == DC - 1))
                p.pe.inc(inst)
            p.pe.wait(nc.scalar)
            for mc in range(4):
                p.act.inc(nc.scalar.activation(h1g[:, mc:mc + 1], ps_g[:, mc:mc + 1], AF.Gelu,
                                               bias=cols["gcb1_c"][:, mc:mc + 1], scale=1.0))
            p.act.wait(nc.tensor)
            for mc in range(2):
                for kc in range(4):
                    inst = nc.tensor.matmul(ps_g[:, 8 + mc:9 + mc],
                                            w2[:, kc, mc * 128:(mc + 1) * 128],
                                            h1g[:, kc:kc + 1],
                                            start=(mc == 0 and kc == 0),
                                            stop=(mc == 1 and kc == 3))
                p.pe.inc(inst)
            p.pe.wait(nc.scalar)
            for mc in range(2):
                p.act.inc(nc.scalar.activation(h2g[:, mc:mc + 1], ps_g[:, 8 + mc:9 + mc], AF.Gelu,
                                               bias=cols["gcb2_c"][:, mc:mc + 1], scale=1.0))
            p.act.wait(nc.tensor)
            for kc in range(2):
                inst = nc.tensor.matmul(ps_g2[:], w3[:, kc, :], h2g[:, kc:kc + 1],
                                        start=(kc == 0), stop=(kc == 1))
            p.pe.inc(inst)
            p.pe.wait(nc.vector)
            fgf, ft, fc1, fc2 = [fl[:, i:i + 1] for i in range(4)]
            p.dve.inc(nc.vector.tensor_tensor(ft, ps_g2[:], cols["gcb3"][:], ALU.add))
            p.dve.wait(nc.scalar)
            p.act.inc(nc.scalar.activation(ft, ft, AF.Tanh, bias=0.0, scale=0.5))
            p.act.wait(nc.vector)
            p.dve.inc(nc.vector.tensor_scalar(fgf, ft, 0.5, 0.5, ALU.mult, ALU.add))
            p.dve.wait(nc.vector)
            p.dve.inc(nc.vector.tensor_scalar(fc1, fgf, 1.4, -1.0, ALU.mult, ALU.add))
            p.dve.wait(nc.vector)
            p.dve.inc(nc.vector.tensor_tensor(fc1, fc1, cols["ar"][:], ALU.mult))
            p.dve.inc(nc.vector.tensor_scalar_mul(fc2, cols["ar"][:], 0.3))
            p.dve.wait(nc.vector)
            p.dve.inc(nc.vector.tensor_tensor(c2p[:], fc2, fc2, ALU.bypass))
            p.dve.inc(nc.vector.tensor_scalar(ft, fc2, 1.0, 0.1, ALU.mult, ALU.add))
            p.dve.wait(nc.vector)
            p.dve.inc(nc.vector.tensor_tensor(c1p[:], fc1, ft, ALU.add))
        p.bar()

    build_rest(p, dt, sb, cols, phases)

    return nc, dt


def build_rest(p, dt, sb, cols, phases):
    nc = p.nc
    ones_col_bf = sb["ones_col_bf"]
    ones_row_f = sb["ones_row_f"]
    ones_row_bf = sb["ones_row_bf"]
    sel2r = sb["sel2r_sb"]
    sel33 = sb["sel33_sb"]
    gacc = sb["gacc"]
    y16 = sb["y16"]
    c1p, c2p = sb["c1p"], sb["c2p"]
    bq8 = sb["bq8"]

    if "b" in phases:
        with ExitStack() as es:
            t = lambda n, s, d=F32: es.enter_context(nc.sbuf_tensor(n, list(s), d))
            xch = [t("xchB0", (128, DC, TC)), t("xchB1", (128, DC, TC))]
            xnch = [t("xnchB0", (128, DC, TC)), t("xnchB1", (128, DC, TC))]
            outp = [t("outB0", (128, DC, TC)), t("outB1", (128, DC, TC))]
            tw = [t("twB0", (128, DC, TC)), t("twB1", (128, DC, TC))]
            trow = [t("trowB0", (1, TC)), t("trowB1", (1, TC))]
            cplrow = [t("cplB0", (1, TC)), t("cplB1", (1, TC))]
            ldb = Sem(nc, "ldB")
            outb = [t("outbB0", (128, DC, TC), BF16), t("outbB1", (128, DC, TC), BF16)]
            h1 = t("h1B", (128, DC, TC), BF16)
            h2 = t("h2B", (128, 4, TC), BF16)
            h3 = t("h3B", (128, 2, TC), BF16)
            w1 = t("cw1", (128, DC, D), BF16)
            w2 = t("cw2", (128, DC, 512), BF16)
            w3 = t("cw3", (128, 4, 256), BF16)
            w4 = t("cw4", (128, 2, 1), BF16)
            ps_h = [es.enter_context(nc.psum_tensor("psB0", [128, TC], F32)),
                    es.enter_context(nc.psum_tensor("psB1", [128, TC], F32))]
            ps_t4 = [es.enter_context(nc.psum_tensor("psBt4a", [1, TC], F32)),
                     es.enter_context(nc.psum_tensor("psBt4b", [1, TC], F32))]
            ps_cb = [es.enter_context(nc.psum_tensor("psBcba", [128, TC], F32)),
                     es.enter_context(nc.psum_tensor("psBcbb", [128, TC], F32))]
            p.dgp.inc(nc.gpsimd.dma_start(w1[:], dt["cnw1"].rearrange("(c p) m -> p c m", p=128)), 16)
            p.dgp.inc(nc.gpsimd.dma_start(w2[:], dt["cnw2"].rearrange("(c p) m -> p c m", p=128)), 16)
            p.dgp.inc(nc.gpsimd.dma_start(w3[:], dt["cnw3"].rearrange("(c p) m -> p c m", p=128)), 16)
            with nc.allow_non_contiguous_dma(reason="tiny [256,1] weight"):
                p.dgp.inc(nc.gpsimd.dma_start(w4[:], dt["cnw4"].rearrange("(c p) m -> p c m", p=128)), 16)
            p.dgp.wait(nc.tensor)
            ld_mark = [0, 0]
            copy_mark = [0, 0]
            x1_mark = [0, 0]
            st_mark = [0, 0]
            cbfree = [0, 0]
            t4free = [0, 0]
            hfree = [p.act.n, p.act.n]
            w1_done = [0, 0]
            t4mm = [0, 0]
            cpl_mark = [0, 0]
            cb_mm = [0, 0]

            def emit_loads(i, s):
                nc.sync.wait_ge(p.dve.h, x1_mark[s])
                ldb.inc(nc.sync.dma_start(
                    xch[s][:], dt["x_fm"][:, i * TC:(i + 1) * TC]
                    .rearrange("(c p) t -> p c t", p=128)), 16)
                nc.sync.wait_ge(p.dve.h, copy_mark[s])
                ldb.inc(nc.sync.dma_start(
                    xnch[s][:], dt["xn_sp"][:, i * TC:(i + 1) * TC]
                    .rearrange("(c p) t -> p c t", p=128)), 16)
                ld_mark[s] = ldb.n

            def emit_init(s):
                nc.vector.wait_ge(ldb.h, ld_mark[s])
                for c in range(DC):
                    p.dve.inc(nc.vector.tensor_copy(outp[s][:, c, :], xnch[s][:, c, :]))
                copy_mark[s] = p.dve.n

            ob_mark = [0, 0]

            def emit_copy(s):
                nc.vector.wait_ge(p.pe.h, w1_done[s])
                p.dve.wait(nc.vector)
                p.dve.inc(nc.vector.tensor_copy(outb[s][:], outp[s][:]))
                ob_mark[s] = p.dve.n

            def emit_mms(s):
                for mc in range(DC):
                    bk = mc % 2
                    nc.tensor.wait_ge(p.act.h, hfree[bk])
                    if mc == 0:
                        nc.tensor.wait_ge(p.dve.h, ob_mark[s])
                    for kc in range(DC):
                        inst = nc.tensor.matmul(ps_h[bk][:], w1[:, kc, mc * 128:(mc + 1) * 128],
                                                outb[s][:, kc, :], start=(kc == 0), stop=(kc == DC - 1))
                    p.pe.inc(inst)
                    p.pe.wait(nc.scalar)
                    hfree[bk] = p.act.inc(nc.scalar.activation(h1[:, mc, :], ps_h[bk][:], AF.Gelu,
                                                               bias=cols["cnb1_c"][:, mc:mc + 1], scale=1.0))
                w1_done[s] = p.pe.n
                p.act.wait(nc.tensor)
                for mc in range(4):
                    bk = mc % 2
                    nc.tensor.wait_ge(p.act.h, hfree[bk])
                    for kc in range(DC):
                        inst = nc.tensor.matmul(ps_h[bk][:], w2[:, kc, mc * 128:(mc + 1) * 128],
                                                h1[:, kc, :], start=(kc == 0), stop=(kc == DC - 1))
                    p.pe.inc(inst)
                    p.pe.wait(nc.scalar)
                    hfree[bk] = p.act.inc(nc.scalar.activation(h2[:, mc, :], ps_h[bk][:], AF.Gelu,
                                                               bias=cols["cnb2_c"][:, mc:mc + 1], scale=1.0))
                p.act.wait(nc.tensor)
                for mc in range(2):
                    bk = mc % 2
                    nc.tensor.wait_ge(p.act.h, hfree[bk])
                    for kc in range(4):
                        inst = nc.tensor.matmul(ps_h[bk][:], w3[:, kc, mc * 128:(mc + 1) * 128],
                                                h2[:, kc, :], start=(kc == 0), stop=(kc == 3))
                    p.pe.inc(inst)
                    p.pe.wait(nc.scalar)
                    hfree[bk] = p.act.inc(nc.scalar.activation(h3[:, mc, :], ps_h[bk][:], AF.Gelu,
                                                               bias=cols["cnb3_c"][:, mc:mc + 1], scale=1.0))
                p.act.wait(nc.tensor)
                nc.tensor.wait_ge(p.dve.h, t4free[s])
                for kc in range(2):
                    inst = nc.tensor.matmul(ps_t4[s][:], w4[:, kc, :], h3[:, kc, :],
                                            start=(kc == 0), stop=(kc == 1))
                t4mm[s] = p.pe.inc(inst)

            def emit_tail1(s):
                nc.vector.wait_ge(p.pe.h, t4mm[s])
                p.dve.inc(nc.vector.tensor_scalar_add(trow[s][:], ps_t4[s][:], cols["cnb4"][:]))
                t4free[s] = p.dve.n
                nc.scalar.wait_ge(p.dve.h, t4free[s])
                tanh_m = p.act.inc(nc.scalar.activation(trow[s][:], trow[s][:], AF.Tanh,
                                                        bias=0.0, scale=0.5))
                nc.vector.wait_ge(p.act.h, tanh_m)
                p.dve.inc(nc.vector.tensor_scalar(cplrow[s][:], trow[s][:], c2p[:], None, ALU.mult))
                p.dve.wait(nc.vector)
                cpl_mark[s] = p.dve.inc(nc.vector.tensor_scalar(cplrow[s][:], cplrow[s][:], c1p[:],
                                                                None, ALU.add))
                nc.tensor.wait_ge(p.dve.h, max(cpl_mark[s], cbfree[s]))
                cb_mm[s] = p.pe.inc(nc.tensor.matmul(ps_cb[s][:], ones_row_f[:], cplrow[s][:],
                                                     start=True, stop=True))

            def emit_tail2(s, cyc):
                nc.vector.wait_ge(p.pe.h, cb_mm[s])
                if cyc == 0:
                    nc.vector.wait_ge(p.dou.h, st_mark[s])
                for j in range(DC):
                    src = (j + 4) % DC
                    p.dve.inc(nc.vector.tensor_mul(tw[s][:, j, :], outp[s][:, src, :], ps_cb[s][:]))
                cbfree[s] = p.dve.n
                p.dve.wait(nc.vector)
                for j in range(DC):
                    op = ALU.add if j in (0, 1, 6, 7) else ALU.subtract
                    p.dve.inc(nc.vector.tensor_tensor(outp[s][:, j, :], outp[s][:, j, :],
                                                      tw[s][:, j, :], op))

            def emit_x1(i, s):
                nc.vector.wait_ge(ldb.h, ld_mark[s])
                p.dve.wait(nc.vector)
                for c in range(DC):
                    p.dve.inc(nc.vector.tensor_add(tw[s][:, c, :], xch[s][:, c, :],
                                                   outp[s][:, c, :]))
                x1_mark[s] = p.dve.n
                p.dve.wait(nc.vector)
                for c in range(DC):
                    p.dve.inc(nc.vector.tensor_scalar_add(tw[s][:, c, :], tw[s][:, c, :],
                                                          cols["ob_c"][:, c:c + 1]))
                if i == 0 and "t16" in phases:
                    p.dve.wait(nc.vector)
                    for c in range(DC):
                        p.dve.inc(nc.vector.tensor_add(tw[s][:, c, 0:16],
                                                       tw[s][:, c, 0:16], y16[:, c, :]))
                nc.sync.wait_ge(p.dve.h, p.dve.n)
                st_mark[s] = p.dou.inc(nc.sync.dma_start(
                    dt["x1_sp"][:, i * TC:(i + 1) * TC].rearrange("(c p) t -> p c t", p=128),
                    tw[s][:]), 16)

            emit_loads(0, 0)
            emit_loads(1, 1)
            for pr in range(NCH // 2):
                a, b = 2 * pr, 2 * pr + 1
                emit_init(0)
                emit_init(1)
                emit_copy(0)
                for cyc in range(3):
                    emit_mms(0)
                    emit_tail1(0)
                    emit_copy(1)
                    emit_tail2(0, cyc)
                    emit_mms(1)
                    emit_tail1(1)
                    if cyc < 2:
                        emit_copy(0)
                    emit_tail2(1, cyc)
                emit_x1(a, 0)
                if a + 2 < NCH:
                    emit_loads(a + 2, 0)
                emit_x1(b, 1)
                if b + 2 < NCH:
                    emit_loads(b + 2, 1)
            nc.sync.wait_ge(p.dou.h, p.dou.n)
        p.bar()

    k_res = v_res = None
    wq_res = wpj_res = None
    wq_mark = 0
    if "c" in phases or "attn" in phases:
        es_kv = ExitStack()
        k_res = es_kv.enter_context(nc.sbuf_tensor("k_res", [128, DC, S], BF16))
        v_res = es_kv.enter_context(nc.sbuf_tensor("v_res", [128, NKT, NH, DH + 1], BF16))

    def load_wq_wpj():
        p.dgp.inc(nc.gpsimd.dma_start(
            wq_res[:], dt["wqkv"][:, 0:D].rearrange("(c p) m -> p c m", p=128)), 16)
        p.dgp.inc(nc.gpsimd.dma_start(
            wpj_res[:], dt["wproj"].rearrange("(c p) m -> p c m", p=128)), 16)
        return p.dgp.n
    if "c" in phases:
        with ExitStack() as es:
            t = lambda n, s, d=F32: es.enter_context(nc.sbuf_tensor(n, list(s), d))
            xch = t("xchC", (128, DC, TC))
            ldc = Sem(nc, "ldC")
            xb = t("xbC", (128, DC, TC), BF16)
            xsq = t("xsqC", (128, DC, TC), BF16)
            xn1 = t("xn1C", (128, DC, TC), BF16)
            rows2 = t("rows2c", (1, 5, TC))
            wk = t("wkC", (128, DC, D), BF16)
            ps_st = es.enter_context(nc.psum_tensor("psCst", [34, TC], F32))
            ps_r = es.enter_context(nc.psum_tensor("psCr", [128, TC], F32))
            ps_m = es.enter_context(nc.psum_tensor("psCm", [128, TC], F32))
            ps_k = [es.enter_context(nc.psum_tensor("psCk0", [128, TC], F32)),
                    es.enter_context(nc.psum_tensor("psCk1", [128, TC], F32))]
            p.dgp.inc(nc.gpsimd.dma_start(
                wk[:], dt["wqkv"][:, D:2 * D].rearrange("(c p) m -> p c m", p=128)), 16)
            p.dgp.wait(nc.tensor)
            stat_mark = 0
            bc_mark = 0
            kmm_mark = 0
            sp_done = 0
            st_free = 0
            addm = 0
            ldc.inc(nc.sync.dma_start(
                xch[:], dt["x1_sp"][:, 0:TC].rearrange("(c p) t -> p c t", p=128)), 16)
            for i in range(NCH):
                ldc.wait(nc.vector)
                ldc.wait(nc.scalar)
                nc.vector.wait_ge(p.pe.h, stat_mark)
                nc.scalar.wait_ge(p.pe.h, stat_mark)
                p.dve.inc(nc.vector.tensor_copy(xb[:], xch[:]))
                p.act.inc(nc.scalar.activation(xsq[:], xch[:], AF.Square))
                p.dve.wait(nc.tensor)
                p.act.wait(nc.tensor)
                nc.tensor.wait_ge(p.dve.h, st_free)
                stat_sums(p, ps_st, xb, xsq, DC, TC, ones_col_bf[:])
                stat_mark = p.pe.n
                nc.vector.wait_ge(p.pe.h, bc_mark)
                nc.scalar.wait_ge(p.pe.h, bc_mark)
                p.pe.wait(nc.vector)
                p.pe.wait(nc.scalar)
                st_free = ln_rows(p, ps_st, 1, TC, 1.0 / D, rows2)
                st_free = p.dve.n
                p.dve.wait(nc.tensor)
                p.act.wait(nc.tensor)
                nc.tensor.wait_ge(p.dve.h, addm)
                nc.tensor.matmul(ps_r[:], ones_row_f[:], rows2[:, 0, :], start=True, stop=True)
                p.pe.inc(nc.tensor.matmul(ps_m[:], ones_row_f[:], rows2[:, 1, :],
                                          start=True, stop=True))
                bc_mark = p.pe.n
                p.pe.wait(nc.vector)
                for c in range(DC):
                    p.dve.inc(nc.vector.tensor_mul(xch[:, c, :], xch[:, c, :], ps_r[:]))
                p.dve.wait(nc.vector)
                nc.vector.wait_ge(p.pe.h, kmm_mark)
                nc.vector.wait_ge(p.dou.h, sp_done)
                for c in range(DC):
                    p.dve.inc(nc.vector.tensor_add(xn1[:, c, :], xch[:, c, :], ps_m[:]))
                addm = p.dve.n
                if i + 1 < NCH:
                    nc.sync.wait_ge(p.dve.h, p.dve.n)
                    ldc.inc(nc.sync.dma_start(
                        xch[:], dt["x1_sp"][:, (i + 1) * TC:(i + 2) * TC]
                        .rearrange("(c p) t -> p c t", p=128)), 16)
                nc.sync.wait_ge(p.dve.h, p.dve.n)
                sp_done = p.dou.inc(nc.sync.dma_start(
                    dt["xn1_sp"][:, i * TC:(i + 1) * TC].rearrange("(c p) t -> p c t", p=128),
                    xn1[:]), 16)
                p.dve.wait(nc.tensor)
                kfree = [p.dve.n, p.dve.n]
                for mc in range(DC):
                    bk = mc % 2
                    nc.tensor.wait_ge(p.dve.h, kfree[bk])
                    for kc in range(DC):
                        inst = nc.tensor.matmul(ps_k[bk][:], wk[:, kc, mc * 128:(mc + 1) * 128],
                                                xn1[:, kc, :], start=(kc == 0), stop=(kc == DC - 1))
                    p.pe.inc(inst)
                    p.pe.wait(nc.vector)
                    kfree[bk] = p.dve.inc(nc.vector.tensor_scalar_add(
                        k_res[:, mc, i * TC:(i + 1) * TC], ps_k[bk][:], cols["bk_c"][:, mc:mc + 1]))
                kmm_mark = p.pe.n
            nc.sync.wait_ge(p.dou.h, p.dou.n)
        p.bar()

        if "attn" in phases:
            wq_res = es_kv.enter_context(nc.sbuf_tensor("wq_res", [128, DC, D], BF16))
            wpj_res = es_kv.enter_context(nc.sbuf_tensor("wpj_res", [128, DC, D], BF16))
            wq_mark = load_wq_wpj()

        with ExitStack() as es:
            t = lambda n, s, d=F32: es.enter_context(nc.sbuf_tensor(n, list(s), d))
            xn1b = [t("xn1C2a", (128, DC, TC), BF16), t("xn1C2b", (128, DC, TC), BF16)]
            ldc2 = Sem(nc, "ldC2")
            ldbv = Sem(nc, "ldBV")
            wv = t("wvC2", (128, DC, D), BF16)
            bvrow_f = t("bvrowC2", (1, D), F32)
            bv_bf = t("bvbfC2", (1, D), BF16)
            ps_v = [es.enter_context(nc.psum_tensor("psCv0", [128, 512], F32)),
                    es.enter_context(nc.psum_tensor("psCv1", [128, 512], F32))]
            p.dgp.inc(nc.gpsimd.dma_start(
                wv[:], dt["wqkv"][:, 2 * D:3 * D].rearrange("(c p) m -> p c m", p=128)), 16)
            ldbv.inc(nc.sync.dma_start(bvrow_f[:], dt["bv_row"][:]), 16)
            ldbv.wait(nc.vector)
            p.dve.inc(nc.vector.tensor_copy(bv_bf[:], bvrow_f[:]))
            p.dve.inc(nc.vector.memset(v_res[:, :, :, DH:DH + 1], 1.0))
            p.dgp.wait(nc.tensor)
            p.dve.wait(nc.tensor)
            ld2_mark = [0, 0]
            vmm_mark = [0, 0]
            for s in range(2):
                ldc2.inc(nc.sync.dma_start(
                    xn1b[s][:], dt["xn1_sp"][:, s * TC:(s + 1) * TC]
                    .rearrange("(c p) t -> p c t", p=128)), 16)
                ld2_mark[s] = ldc2.n
            for i in range(NCH):
                s = i % 2
                xn1 = xn1b[s]
                nc.tensor.wait_ge(ldc2.h, ld2_mark[s])
                kfree = [p.dve.n, p.dve.n]
                for ts4 in range(TC // 128):
                    kt = i * (TC // 128) + ts4
                    for half in range(2):
                        bk = half
                        nc.tensor.wait_ge(p.dve.h, kfree[bk])
                        for kc in range(DC):
                            nc.tensor.matmul(ps_v[bk][:], xn1[:, kc, ts4 * 128:(ts4 + 1) * 128],
                                             wv[:, kc, half * 512:(half + 1) * 512],
                                             start=(kc == 0), stop=False)
                        inst = nc.tensor.matmul(ps_v[bk][:], ones_row_bf[:],
                                                bv_bf[:, half * 512:(half + 1) * 512],
                                                start=False, stop=True)
                        p.pe.inc(inst)
                        p.pe.wait(nc.vector)
                        kfree[bk] = p.dve.inc(nc.vector.tensor_copy(
                            v_res[:, kt, half * 8:(half + 1) * 8, 0:DH],
                            ps_v[bk][:].rearrange("p (h d) -> p h d", d=DH)))
                vmm_mark[s] = p.pe.n
                if i + 2 < NCH:
                    nc.sync.wait_ge(p.pe.h, vmm_mark[s])
                    ldc2.inc(nc.sync.dma_start(
                        xn1b[s][:], dt["xn1_sp"][:, (i + 2) * TC:(i + 3) * TC]
                        .rearrange("(c p) t -> p c t", p=128)), 16)
                    ld2_mark[s] = ldc2.n
        p.bar()

    if "attn" in phases:
        with ExitStack() as es:
            t = lambda n, s, d=F32: es.enter_context(nc.sbuf_tensor(n, list(s), d))
            xn1c = t("xn1At", (128, DC, TC), BF16)
            lda = Sem(nc, "ldAt")
            qfm = t("qAt", (128, DC, TC), BF16)
            pbuf = [t("pA0", (128, TC), BF16), t("pA1", (128, TC), BF16),
                    t("pA2", (128, TC), BF16), t("pA3", (128, TC), BF16)]
            o2s = t("o2sAt", (128, TC))
            r2ab = t("r2abAt", (33, TC))
            nb_sb = t("nbsbAt", (128, TC))
            aout = t("aoutAt", (128, DC, TC), BF16)
            lneg = sb["lneg_sb"]
            rbig = sb["rbig_sb"]
            ps_se = es.enter_context(nc.psum_tensor("psSe", [128, TC], F32))
            ps_so = es.enter_context(nc.psum_tensor("psSo", [128, TC], F32))
            ps_se2 = es.enter_context(nc.psum_tensor("psSe2", [128, TC], F32))
            ps_so2 = es.enter_context(nc.psum_tensor("psSo2", [128, TC], F32))
            ps_o2e = es.enter_context(nc.psum_tensor("psO2e", [128, TC], F32))
            ps_o2o = es.enter_context(nc.psum_tensor("psO2o", [128, TC], F32))
            ps_o2e2 = es.enter_context(nc.psum_tensor("psO2e2", [128, TC], F32))
            ps_o2o2 = es.enter_context(nc.psum_tensor("psO2o2", [128, TC], F32))
            SE = [ps_se, ps_se2]
            SO = [ps_so, ps_so2]
            O2E = [ps_o2e, ps_o2e2]
            O2O = [ps_o2o, ps_o2o2]
            if wq_res is None:
                wq_res = es_kv.enter_context(nc.sbuf_tensor("wq_res", [128, DC, D], BF16))
                wpj_res = es_kv.enter_context(nc.sbuf_tensor("wpj_res", [128, DC, D], BF16))
            if wq_mark == 0:
                wq_mark = load_wq_wpj()
            p.dve.inc(nc.vector.memset(r2ab[:], 0.0))
            qp_mark = 0
            ev_mark = 0
            stq = [0] * DC
            for j in range(NCH):
                q0 = j * TC
                nkt = (j + 1) * (TC // 128)
                if j == 0:
                    lda.inc(nc.sync.dma_start(
                        xn1c[:], dt["xn1_sp"][:, q0:q0 + TC]
                        .rearrange("(c p) t -> p c t", p=128)), 16)
                    nc.tensor.wait_ge(p.dgp.h, wq_mark)
                lda.wait(nc.tensor)
                qfree = [p.act.n, p.act.n]
                for mc in range(DC):
                    bk = mc % 2
                    nc.tensor.wait_ge(p.act.h, qfree[bk])
                    if mc < 2:
                        nc.tensor.wait_ge(p.dve.h, ev_mark)
                    for kc in range(DC):
                        inst = nc.tensor.matmul(SE[bk][:], wq_res[:, kc, mc * 128:(mc + 1) * 128],
                                                xn1c[:, kc, :], start=(kc == 0), stop=(kc == DC - 1))
                    p.pe.inc(inst)
                    p.pe.wait(nc.scalar)
                    nc.scalar.wait_ge(p.dou.h, stq[mc])
                    qfree[bk] = p.act.inc(nc.scalar.activation(
                        qfm[:, mc, :], SE[bk][:], AF.Identity,
                        bias=bq8[:, mc:mc + 1], scale=QSCALE))
                qp_mark = p.pe.n
                if j + 1 < NCH:
                    nc.sync.wait_ge(p.pe.h, qp_mark)
                    lda.inc(nc.sync.dma_start(
                        xn1c[:], dt["xn1_sp"][:, q0 + TC:q0 + 2 * TC]
                        .rearrange("(c p) t -> p c t", p=128)), 16)
                p.act.wait(nc.tensor)
                sfree = [p.act.n, p.act.n]
                o2free = [p.dve.n, p.dve.n]
                state = {"nbdone": p.dve.n, "nbm": p.pe.n}
                prev_norm = None

                def emit_norm(pn):
                    phd, pob, r2m = pn
                    nc.tensor.wait_ge(p.dve.h, r2m)
                    NB = O2E[pob]
                    state["nbm"] = p.pe.inc(nc.tensor.matmul(
                        NB[:], sel33[:], r2ab[:], start=True, stop=True))
                    nc.vector.wait_ge(p.pe.h, state["nbm"])
                    nc.vector.reciprocal(nb_sb[:], NB[:])
                    state["nbdone"] = p.dve.inc(
                        nc.vector.tensor_mul(aout[:, phd, :], o2s[:], nb_sb[:]))
                    o2free[pob] = state["nbdone"]

                def emit_scores(hp, kt, sbk):
                    diag = kt * 128 + 128 > q0
                    base = q0 - kt * 128
                    nc.tensor.wait_ge(p.act.h, sfree[sbk])
                    nc.tensor.matmul(SE[sbk][:], k_res[0:64, hp, kt * 128:(kt + 1) * 128],
                                     qfm[0:64, hp, :], start=True, stop=not diag,
                                     tile_position=(0, 0))
                    inst = nc.tensor.matmul(SO[sbk][:], k_res[64:128, hp, kt * 128:(kt + 1) * 128],
                                            qfm[64:128, hp, :], start=True, stop=not diag,
                                            tile_position=(64, 0))
                    if diag:
                        w = min(TC, 128 - base)
                        rb = rbig[:, 384 + base:384 + base + w]
                        nc.tensor.matmul(SE[sbk][:, 0:w], lneg[:], rb, start=False, stop=True)
                        inst = nc.tensor.matmul(SO[sbk][:, 0:w], lneg[:], rb, start=False, stop=True)
                    return p.pe.inc(inst)

                gl = [(hp, kt) for hp in range(DC) for kt in range(nkt)]
                mm_mark = {}
                pv_mark = {}
                mm_mark[0] = emit_scores(0, 0, 0)
                for g, (hp, kt) in enumerate(gl):
                    he, ho = 2 * hp, 2 * hp + 1
                    ob = hp % 2
                    sbk = g % 2
                    if kt == 1 and prev_norm is not None:
                        emit_norm(prev_norm)
                        prev_norm = None
                    nc.scalar.wait_ge(p.pe.h, mm_mark[g])
                    if g >= 2:
                        nc.scalar.wait_ge(p.pe.h, pv_mark[g - 2])
                    nc.scalar.activation(pbuf[2 * sbk][:], SE[sbk][:], AF.Exp,
                                         bias=0.0, scale=1.0)
                    m = p.act.inc(nc.scalar.activation(pbuf[2 * sbk + 1][:], SO[sbk][:],
                                                       AF.Exp, bias=0.0, scale=1.0))
                    sfree[sbk] = m
                    if g + 1 < len(gl):
                        nhp, nkt_ = gl[g + 1]
                        mm_mark[g + 1] = emit_scores(nhp, nkt_, (g + 1) % 2)
                    if kt == 0:
                        nc.tensor.wait_ge(p.dve.h, o2free[ob])
                    nc.tensor.wait_ge(p.act.h, m)
                    nc.tensor.matmul(O2E[ob][0:65, :], v_res[:, kt, he, :], pbuf[2 * sbk][:],
                                     start=(kt == 0), stop=(kt == nkt - 1),
                                     tile_position=(0, 0))
                    pv_mark[g] = p.pe.inc(nc.tensor.matmul(
                        O2O[ob][0:65, :], v_res[:, kt, ho, :], pbuf[2 * sbk + 1][:],
                        start=(kt == 0), stop=(kt == nkt - 1),
                        tile_position=(0, 0)))
                    if kt == nkt - 1:
                        nc.vector.wait_ge(p.pe.h, max(pv_mark[g], state["nbm"]))
                        nc.vector.tensor_copy(r2ab[0:1, :], O2E[ob][64:65, :])
                        nc.vector.tensor_copy(r2ab[32:33, :], O2O[ob][64:65, :])
                        nc.vector.tensor_copy(o2s[0:64, :], O2E[ob][0:64, :])
                        r2m = p.dve.inc(nc.vector.tensor_copy(o2s[64:128, :], O2O[ob][0:64, :]))
                        prev_norm = (hp, ob, r2m)
                emit_norm(prev_norm)
                prev_norm = None
                nc.tensor.wait_ge(p.dve.h, state["nbdone"])
                nc.tensor.wait_ge(p.act.h, max(sfree))
                ofree = [p.dve.n, p.dve.n]
                for mc in range(DC):
                    bk = mc % 2
                    nc.tensor.wait_ge(p.dve.h, ofree[bk])
                    for kc in range(DC):
                        inst = nc.tensor.matmul(SE[bk][:], wpj_res[:, kc, mc * 128:(mc + 1) * 128],
                                                aout[:, kc, :], start=(kc == 0), stop=(kc == DC - 1))
                    p.pe.inc(inst)
                    p.pe.wait(nc.vector)
                    ofree[bk] = p.dve.inc(nc.vector.tensor_scalar_add(
                        qfm[:, mc, :], SE[bk][:], cols["bp_c"][:, mc:mc + 1]))
                    nc.sync.wait_ge(p.dve.h, p.dve.n)
                    stq[mc] = p.dou.inc(nc.sync.dma_start(
                        dt["xattn_sp"][mc * 128:(mc + 1) * 128, q0:q0 + TC],
                        qfm[:, mc, :]), 16)
                ev_mark = p.dve.n
            nc.sync.wait_ge(p.dou.h, p.dou.n)
        p.bar()

    if k_res is not None:
        es_kv.close()

    if "ffn" in phases:
        with ExitStack() as es:
            t = lambda n, s, d=F32: es.enter_context(nc.sbuf_tensor(n, list(s), d))
            wg = t("wgF", (128, DC, HID), BF16)
            wu = t("wuF", (128, DC, HID), BF16)
            wd = t("wdF", (128, HCN, D), BF16)
            xn1c = t("xn1F", (128, DC, TC), BF16)
            ldf = Sem(nc, "ldF")
            hbuf = t("hF", (128, HCN, TC), BF16)
            xat = t("xatF", (128, DC, TC), BF16)
            xac = t("xaF", (128, DC, TC))
            ps_g = [es.enter_context(nc.psum_tensor("psF0", [128, TC], F32)),
                    es.enter_context(nc.psum_tensor("psF1", [128, TC], F32))]
            ps_u = [es.enter_context(nc.psum_tensor("psFu0", [128, TC], F32)),
                    es.enter_context(nc.psum_tensor("psFu1", [128, TC], F32))]
            ps_d = [es.enter_context(nc.psum_tensor("psFd0", [128, TC], F32)),
                    es.enter_context(nc.psum_tensor("psFd1", [128, TC], F32))]
            p.dgp.inc(nc.gpsimd.dma_start(
                wg[:], dt["wgT"].rearrange("(c p) m -> p c m", p=128)), 16)
            p.dgp.inc(nc.gpsimd.dma_start(
                wu[:], dt["wuT"].rearrange("(c p) m -> p c m", p=128)), 16)
            p.dgp.inc(nc.gpsimd.dma_start(
                wd[:], dt["wdT"].rearrange("(c p) m -> p c m", p=128)), 16)
            p.dgp.wait(nc.tensor)
            ldx = Sem(nc, "ldxF")
            yst_mark = 0
            for i in range(NCH):
                if i == 0:
                    ldf.inc(nc.sync.dma_start(
                        xn1c[:], dt["xn1_sp"][:, 0:TC]
                        .rearrange("(c p) t -> p c t", p=128)), 16)
                    ldx.inc(nc.sync.dma_start(
                        xat[:], dt["xattn_sp"][:, 0:TC]
                        .rearrange("(c p) t -> p c t", p=128)), 16)
                    ldx.inc(nc.sync.dma_start(
                        xac[:], dt["x1_sp"][:, 0:TC]
                        .rearrange("(c p) t -> p c t", p=128)), 16)
                ldf.wait(nc.tensor)
                gfree = [p.act.n] * 2
                ufree = [p.dve.n] * 2
                for hc in range(HCN):
                    bk = hc % 2
                    nc.tensor.wait_ge(p.act.h, gfree[bk])
                    for kc in range(DC):
                        inst = nc.tensor.matmul(ps_g[bk][:], wg[:, kc, hc * 128:(hc + 1) * 128],
                                                xn1c[:, kc, :], start=(kc == 0), stop=(kc == DC - 1))
                    p.pe.inc(inst)
                    nc.tensor.wait_ge(p.dve.h, ufree[bk])
                    for kc in range(DC):
                        inst = nc.tensor.matmul(ps_u[bk][:], wu[:, kc, hc * 128:(hc + 1) * 128],
                                                xn1c[:, kc, :], start=(kc == 0), stop=(kc == DC - 1))
                    p.pe.inc(inst)
                    p.pe.wait(nc.scalar, p.pe.n - 1)
                    gfree[bk] = p.act.inc(nc.scalar.activation(hbuf[:, hc, :], ps_g[bk][:],
                                                               AF.Silu, bias=0.0, scale=QDEQ))
                    p.pe.wait(nc.vector)
                    p.act.wait(nc.vector)
                    ufree[bk] = p.dve.inc(nc.vector.tensor_mul(hbuf[:, hc, :], hbuf[:, hc, :],
                                                               ps_u[bk][:]))
                if i + 1 < NCH:
                    nc.sync.wait_ge(p.pe.h, p.pe.n)
                    ldf.inc(nc.sync.dma_start(
                        xn1c[:], dt["xn1_sp"][:, (i + 1) * TC:(i + 2) * TC]
                        .rearrange("(c p) t -> p c t", p=128)), 16)
                ldx.wait(nc.vector)
                for c in range(DC):
                    p.dve.inc(nc.vector.tensor_tensor(xac[:, c, :], xac[:, c, :],
                                                      xat[:, c, :], ALU.add))
                p.dve.wait(nc.tensor)
                dfree = [p.dve.n] * 2
                for mc in range(DC):
                    bk = mc % 2
                    nc.tensor.wait_ge(p.dve.h, dfree[bk])
                    for hc in range(HCN):
                        inst = nc.tensor.matmul(ps_d[bk][:], wd[:, hc, mc * 128:(mc + 1) * 128],
                                                hbuf[:, hc, :], start=(hc == 0), stop=(hc == HCN - 1))
                    p.pe.inc(inst)
                    p.pe.wait(nc.vector)
                    dfree[bk] = p.dve.inc(nc.vector.scalar_tensor_tensor(
                        xac[:, mc, :], ps_d[bk][:], QDEQ * QDEQ, xac[:, mc, :],
                        ALU.mult, ALU.add))
                nc.sync.wait_ge(p.dve.h, p.dve.n)
                yst_mark = p.dou.inc(nc.sync.dma_start(
                    dt["y"][:, i * TC:(i + 1) * TC].rearrange("(c p) t -> p c t", p=128),
                    xac[:]), 16)
                if i + 1 < NCH:
                    nc.sync.wait_ge(p.dou.h, yst_mark)
                    ldx.inc(nc.sync.dma_start(
                        xat[:], dt["xattn_sp"][:, (i + 1) * TC:(i + 2) * TC]
                        .rearrange("(c p) t -> p c t", p=128)), 16)
                    ldx.inc(nc.sync.dma_start(
                        xac[:], dt["x1_sp"][:, (i + 1) * TC:(i + 2) * TC]
                        .rearrange("(c p) t -> p c t", p=128)), 16)
            nc.sync.wait_ge(p.dou.h, p.dou.n)
        p.bar()
    nc.sync.wait_ge(p.dou.h, p.dou.n)



def _cols(v, nch):
    v = np.ascontiguousarray(np.asarray(v, dtype=np.float32))
    return np.ascontiguousarray(v.reshape(nch, 128).T)


def _t(a):
    return np.ascontiguousarray(np.asarray(a, dtype=np.float32).T)


def _sel33():
    s = np.zeros((33, 128), np.float32)
    s[0, 0:64] = 1.0
    s[32, 64:128] = 1.0
    return s


def prepare_inputs(inputs):
    f = lambda k: np.asarray(inputs[k])
    shared = {
        "qwT": _t(f("L0_attn_qw")), "kwT": _t(f("L0_attn_kw")),
        "vwT": _t(f("L0_attn_vw")), "owT": _t(f("L0_attn_ow")),
        "lqwT": _t(f("L0_attn_lqw")), "lkwT": _t(f("L0_attn_lkw")),
        "lvwT": _t(f("L0_attn_lvw")),
        "qb_c": _cols(f("L0_attn_qb"), 8), "vb_c": _cols(f("L0_attn_vb"), 8),
        "ob_c": _cols(f("L0_attn_ob"), 8),
        "lna_c": _cols(f("L0_attn_lna_w"), 8), "lnb_c": _cols(f("L0_attn_lnb_w"), 8),
        "lnc2": np.ascontiguousarray(np.tile(np.asarray(f("L0_attn_lnc_w"), np.float32), 2)[:, None]),
        "lnd2": np.ascontiguousarray(np.tile(np.asarray(f("L0_attn_lnd_w"), np.float32), 2)[:, None]),
        "lqb2": np.ascontiguousarray(np.tile(np.asarray(f("L0_attn_lqb"), np.float32), 2)[:, None]),
        "lkb2": np.ascontiguousarray(np.tile(np.asarray(f("L0_attn_lkb"), np.float32), 2)[:, None]),
        "lvb2": np.ascontiguousarray(np.tile(np.asarray(f("L0_attn_lvb"), np.float32), 2)[:, None]),
        "cnw1": np.ascontiguousarray(np.asarray(f("L0_ffn_cn_w1"), np.float32)),
        "cnw2": np.ascontiguousarray(np.asarray(f("L0_ffn_cn_w2"), np.float32)),
        "cnw3": np.ascontiguousarray(np.asarray(f("L0_ffn_cn_w3"), np.float32)),
        "cnw4": np.ascontiguousarray(np.asarray(f("L0_ffn_cn_w4"), np.float32)),
        "gcw1": np.ascontiguousarray(np.asarray(f("L0_ffn_gc_w1"), np.float32)),
        "gcw2": np.ascontiguousarray(np.asarray(f("L0_ffn_gc_w2"), np.float32)),
        "gcw3": np.ascontiguousarray(np.asarray(f("L0_ffn_gc_w3"), np.float32)),
        "cnb1_c": _cols(f("L0_ffn_cn_b1"), 8), "cnb2_c": _cols(f("L0_ffn_cn_b2"), 4),
        "cnb3_c": _cols(f("L0_ffn_cn_b3"), 2),
        "cnb4": np.asarray(f("L0_ffn_cn_b4"), np.float32).reshape(1, 1),
        "gcb1_c": _cols(f("L0_ffn_gc_b1"), 4), "gcb2_c": _cols(f("L0_ffn_gc_b2"), 2),
        "gcb3": np.asarray(f("L0_ffn_gc_b3"), np.float32).reshape(1, 1),
        "ar": np.asarray(f("L0_ffn_adaptive_range"), np.float32).reshape(1, 1),
        "wqkv": np.ascontiguousarray(np.asarray(f("L1_attn_W_qkv"), np.float32)),
        "bq_c": _cols(np.asarray(f("L1_attn_b_qkv"), np.float32)[0:D], 8),
        "bk_c": _cols(np.asarray(f("L1_attn_b_qkv"), np.float32)[D:2 * D], 8),
        "bv_row": np.ascontiguousarray(np.asarray(f("L1_attn_b_qkv"), np.float32)[2 * D:3 * D].reshape(1, D)),
        "wproj": np.ascontiguousarray(np.asarray(f("L1_attn_W_proj"), np.float32)),
        "bp_c": _cols(f("L1_attn_b_proj"), 8),
        "wgT": np.ascontiguousarray(np.asarray(f("L1_ffn_w_gate"), np.int32).T),
        "wuT": np.ascontiguousarray(np.asarray(f("L1_ffn_w_up"), np.int32).T),
        "wdT": np.ascontiguousarray(np.asarray(f("L1_ffn_w_down"), np.int32).T),
        "ident": np.eye(128, dtype=BF16NP),
        "lneg": (-60.0 * np.triu(np.ones((128, 128), np.float32))).astype(BF16NP),
        "rbig": (np.arange(128)[:, None] >= np.arange(896)[None, :] - 383)
                .astype(np.float32).astype(BF16NP),
        "mask16": np.kron(np.eye(8, dtype=np.float32),
                          np.ones((16, 16), np.float32)).astype(BF16NP),
        "sel2r": np.ascontiguousarray(np.concatenate(
            [np.concatenate([np.ones(64), np.zeros(64)]).reshape(1, 128),
             np.concatenate([np.zeros(64), np.ones(64)]).reshape(1, 128)]).astype(np.float32)),
        "sel33": _sel33(),
        "sel2c": np.ascontiguousarray(np.concatenate(
            [np.concatenate([np.ones(64), np.zeros(64)]).reshape(1, 128),
             np.concatenate([np.zeros(64), np.ones(64)]).reshape(1, 128)]).astype(np.float32).T
        ).astype(BF16NP),
    }
    x = np.asarray(inputs["L0_attn_x"], dtype=np.float32)
    x16 = x[:, 0:16, :]
    in_maps = []
    for c in range(B):
        m = dict(shared)
        m["x_fm"] = np.ascontiguousarray(x[c].T)
        x16r = np.roll(x16, -c, axis=0).reshape(128, D)
        m["x16_fm"] = np.ascontiguousarray(x16r.T)
        in_maps.append(m)
    return in_maps


def kernel(**inputs):
    in_maps = prepare_inputs(inputs)
    nc, _ = build_program()
    res = run_bass_kernel_spmd(nc, in_maps, core_ids=list(range(B)))
    out = np.stack([np.asarray(r["y"], dtype=np.float32).T for r in res.results])
    return out


# version 16
# speedup vs baseline: 1.2760x; 1.0543x over previous
import numpy as np
import ml_dtypes
import concourse.bass as bass
import concourse.mybir as mybir
from concourse.bass_utils import run_bass_kernel_spmd
from contextlib import ExitStack

F32 = mybir.dt.float32
BF16 = mybir.dt.bfloat16
I32 = mybir.dt.int32
AF = mybir.ActivationFunctionType
ALU = mybir.AluOpType
AX = mybir.AxisListType

B = 8
S = 4096
D = 1024
NH = 16
DH = 64
DC = D // 128
TC = 512
NCH = S // TC
NKT = S // 128
HID = 3072
HCN = HID // 128
EPS = 1e-5
QSCALE = 0.125
QDEQ = 1.0 / 127.0
DIFF_SCALE = 1.0 / (8 * 16 * 4096 * 64)
THRESH = 0.01
FACTOR = 0.1

BF16NP = ml_dtypes.bfloat16


def set_shape(s=4096, tc=512):
    global S, TC, NCH, NKT, DIFF_SCALE
    S, TC = s, tc
    NCH = S // TC
    NKT = S // 128
    DIFF_SCALE = 1.0 / (8 * 16 * S * 64)


class Sem:
    def __init__(self, nc, name):
        self.h = nc.alloc_semaphore(name)
        self.n = 0

    def inc(self, inst, k=1):
        inst.then_inc(self.h, k)
        self.n += k
        return self.n

    def wait(self, eng, val=None):
        eng.wait_ge(self.h, self.n if val is None else val)


class P:

    def __init__(self, nc):
        self.nc = nc
        self.pe = Sem(nc, "s_pe")
        self.act = Sem(nc, "s_act")
        self.dve = Sem(nc, "s_dve")
        self.pool = Sem(nc, "s_pool")
        self.dsy = Sem(nc, "d_sync")
        self.dsc = Sem(nc, "d_scal")
        self.dgp = Sem(nc, "d_gp")
        self.dou = Sem(nc, "d_out")

    def bar(self):
        self.nc.all_engine_barrier()



def ln_rows(p, stats_ps, n, ncols, inv_n, rows):
    nc = p.nc
    rstd = rows[:, 0, :]
    brow = rows[:, 1, :]
    mean = rows[:, 2, :]
    m2 = rows[:, 3, :]
    lnt = rows[:, 4, :]
    p.dve.inc(nc.vector.tensor_scalar_mul(mean, stats_ps[0:n, :], inv_n))
    p.dve.inc(nc.vector.tensor_scalar_mul(rstd, stats_ps[32:32 + n, :], inv_n))
    psum_free = p.dve.n
    p.dve.wait(nc.vector)
    p.dve.inc(nc.vector.tensor_mul(m2, mean, mean))
    p.dve.wait(nc.vector)
    p.dve.inc(nc.vector.scalar_tensor_tensor(rstd, rstd, EPS, m2, ALU.add, ALU.subtract))
    p.dve.wait(nc.scalar)
    p.act.inc(nc.scalar.activation(lnt, rstd, AF.Ln, bias=0.0, scale=1.0))
    p.act.wait(nc.scalar)
    p.act.inc(nc.scalar.activation(rstd, lnt, AF.Exp, bias=0.0, scale=-0.5))
    p.act.wait(nc.vector)
    p.dve.inc(nc.vector.scalar_tensor_tensor(brow, mean, -1.0, rstd, ALU.mult, ALU.mult))
    return psum_free


def stat_sums(p, stats_ps, xb, xsqb, chunks, ncols, ones_col):
    nc = p.nc
    for c in range(chunks):
        nc.tensor.matmul(stats_ps[0:1, :], ones_col, xb[:, c, :],
                         start=(c == 0), stop=(c == chunks - 1),
                         tile_position=(0, 0))
    for c in range(chunks):
        inst = nc.tensor.matmul(stats_ps[32:33, :], ones_col, xsqb[:, c, :],
                                start=(c == 0), stop=(c == chunks - 1),
                                tile_position=(0, 32))
    p.pe.inc(inst)



def build_program(phases=("pre", "t16", "a", "gf", "b", "c", "attn", "ffn"),
                  debug=False):
    nc = bass.Bass("TRN2", target_bir_lowering=False)
    p = P(nc)
    dt = {}

    def din(name, shape, dtyp=F32):
        dt[name] = nc.dram_tensor(name, list(shape), dtyp, kind="ExternalInput").ap()
        return dt[name]

    def dint(name, shape, dtyp=F32):
        dt[name] = nc.dram_tensor(name, list(shape), dtyp, kind="Internal").ap()
        return dt[name]

    def dout(name, shape, dtyp=F32):
        dt[name] = nc.dram_tensor(name, list(shape), dtyp, kind="ExternalOutput").ap()
        return dt[name]

    din("x_fm", (D, S))
    din("x16_fm", (D, 128))
    din("qwT", (D, D)); din("kwT", (D, D)); din("vwT", (D, D)); din("owT", (D, D))
    din("lqwT", (DH, DH)); din("lkwT", (DH, DH)); din("lvwT", (DH, DH))
    din("qb_c", (128, DC)); din("vb_c", (128, DC)); din("ob_c", (128, DC))
    din("lna_c", (128, DC)); din("lnb_c", (128, DC))
    din("lnc2", (128, 1)); din("lnd2", (128, 1))
    din("lqb2", (128, 1)); din("lkb2", (128, 1)); din("lvb2", (128, 1))
    din("cnw1", (D, D)); din("cnw2", (D, 512)); din("cnw3", (512, 256)); din("cnw4", (256, 1))
    din("gcw1", (D, 512)); din("gcw2", (512, 256)); din("gcw3", (256, 1))
    din("cnb1_c", (128, 8)); din("cnb2_c", (128, 4)); din("cnb3_c", (128, 2)); din("cnb4", (1, 1))
    din("gcb1_c", (128, 4)); din("gcb2_c", (128, 2)); din("gcb3", (1, 1))
    din("ar", (1, 1))
    din("wqkv", (D, 3 * D)); din("bq_c", (128, DC)); din("bk_c", (128, DC)); din("bv_row", (1, D))
    din("wproj", (D, D)); din("bp_c", (128, DC))
    din("wgT", (D, HID), I32); din("wuT", (D, HID), I32); din("wdT", (HID, D), I32)
    din("ident", (128, 128), BF16)
    din("mask16", (128, 128), BF16)
    din("lneg", (128, 128), BF16)
    din("rbig", (128, 896), BF16)
    din("sel2r", (2, 128))
    din("sel33", (33, 128))
    din("sel2c", (128, 2), BF16)

    dsc_kind = dout if debug else dint
    dsc_kind("xn_sp", (D, S))
    dsc_kind("x1_sp", (D, S))
    dsc_kind("xattn_sp", (D, S), BF16)
    dsc_kind("xn1_sp", (D, S), BF16)
    dout("y", (D, S))
    if debug:
        for nm, shp, dty in (("dbg_q16", (128, DC, 128), F32), ("dbg_k16", (128, DC, 128), BF16),
                             ("dbg_v16", (128, DC, 128), BF16), ("dbg_kn16", (128, DC, 128), BF16),
                             ("dbg_vi", (128, NH * DH), BF16), ("dbg_attn", (128, DC, 128), F32),
                             ("dbg_y16", (128, DC, 16), F32), ("dbg_xn16", (128, DC, 128), F32),
                             ("dbg_x1pre", (128, DC, 16), F32)):
            dout(nm, shp, dty)

    es_glob = ExitStack()

    sb = {}

    def asb(name, shape, dtyp):
        h = es_glob.enter_context(nc.sbuf_tensor(name, list(shape), dtyp))
        sb[name] = h
        return h

    ones_col_bf = asb("ones_col_bf", (128, 1), BF16)
    ones_row_f = asb("ones_row_f", (1, 128), F32)
    ones_row_bf = asb("ones_row_bf", (1, 128), BF16)
    ident = asb("ident_sb", (128, 128), BF16)
    mask16 = asb("mask16_sb", (128, 128), BF16)
    lneg = asb("lneg_sb", (128, 128), BF16)
    rbig = asb("rbig_sb", (128, 896), BF16)
    sel2r = asb("sel2r_sb", (2, 128), F32)
    sel33 = asb("sel33_sb", (33, 128), F32)
    sel2c = asb("sel2c_sb", (128, 2), BF16)
    cols = {}
    for nm, w in (("qb_c", DC), ("vb_c", DC), ("ob_c", DC), ("lna_c", DC),
                  ("lnb_c", DC), ("bq_c", DC), ("bk_c", DC), ("bp_c", DC),
                  ("cnb1_c", 8), ("cnb2_c", 4), ("cnb3_c", 2),
                  ("gcb1_c", 4), ("gcb2_c", 2)):
        cols[nm] = asb(nm + "_sb", (128, w), F32)
    for nm in ("lnc2", "lnd2", "lqb2", "lkb2", "lvb2"):
        cols[nm] = asb(nm + "_sb", (128, 1), F32)
    for nm in ("cnb4", "gcb3", "ar"):
        cols[nm] = asb(nm + "_sb", (1, 1), F32)
    bq8 = asb("bq8", (128, DC), F32)
    gacc = asb("gacc", (128, DC, NCH), F32)
    gctx_bf = asb("gctx_bf", (128, DC), BF16)
    c1p = asb("c1p", (1, 1), F32)
    c2p = asb("c2p", (1, 1), F32)
    y16 = asb("y16", (128, DC, 16), F32)

    nc.vector.memset(ones_col_bf[:], 1.0)
    nc.vector.memset(ones_row_f[:], 1.0)
    p.dve.inc(nc.vector.memset(ones_row_bf[:], 1.0))
    for nm in ("ident", "mask16", "lneg", "rbig"):
        p.dsy.inc(nc.sync.dma_start(sb[nm + "_sb"][:], dt[nm][:]), 16)
    p.dsy.inc(nc.sync.dma_start(sel2r[:], dt["sel2r"][:]), 16)
    p.dsy.inc(nc.sync.dma_start(sel33[:], dt["sel33"][:]), 16)
    p.dsy.inc(nc.sync.dma_start(sel2c[:], dt["sel2c"][:]), 16)
    for nm in cols:
        p.dsy.inc(nc.sync.dma_start(cols[nm][:], dt[nm][:]), 16)
    p.dsy.wait(nc.vector)
    p.dve.inc(nc.vector.tensor_scalar_mul(bq8[:], cols["bq_c"][:], QSCALE))
    p.bar()

    if "t16" in phases:
        with ExitStack() as es:
            t = lambda n, s, d=F32: es.enter_context(nc.sbuf_tensor(n, list(s), d))
            x16 = t("x16", (128, DC, 128))
            x16b = t("x16b", (128, DC, 128), BF16)
            x16sq = t("x16sq", (128, DC, 128), BF16)
            xn16 = t("xn16", (128, DC, 128))
            a16 = t("a16", (128, DC, 128), BF16)
            b16 = t("b16", (128, DC, 128), BF16)
            wq = t("wq16", (128, DC, D), BF16)
            wk = t("wk16", (128, DC, D), BF16)
            wv = t("wv16", (128, DC, D), BF16)
            wo = t("wo16", (128, DC, D), BF16)
            lbd = {}
            for nm in ("lqwT", "lkwT", "lvwT"):
                lbd[nm] = t(nm + "_bd", (128, 128), BF16)
            rows2 = t("rows2", (1, 5, 128))
            rows4 = t("rows4", (2, 5, 512))
            q16 = t("q16", (128, DC, 128))
            k16 = t("k16", (128, DC, 128), BF16)
            v16 = t("v16", (128, DC, 128), BF16)
            kn16 = t("kn16", (128, DC, 128), BF16)
            vi_tok = t("vi_tok", (128, NH * DH), BF16)
            kk32 = t("kk32", (128, 512))
            kkb = t("kkb", (128, 512), BF16)
            kksq = t("kksq", (128, 512), BF16)
            qcur = [t("qcurA", (128, DC, 128)), t("qcurB", (128, DC, 128))]
            qcb = t("qcb", (128, DC, 128), BF16)
            attn_o = [t("attnA", (128, DC, 128)), t("attnB", (128, DC, 128))]
            prev = [t("prevA", (128, DC, 128)), t("prevB", (128, DC, 128))]
            iout = t("iout", (128, DC, 128))
            pn = t("pn", (128, 128), BF16)
            pt = t("pt", (128, 128), BF16)
            rowred = t("rowred", (128, 1))
            rrecip = t("rrecip", (128, 1))
            dtmp = t("dtmp", (128, 512))
            dcol = t("dcol", (128, DC))
            dcol1 = t("dcol1", (128, 1))
            dcolb = t("dcolb", (128, 1), BF16)
            dscal = t("dscal", (1, 1))
            flt = t("flt", (1, 6))
            donec = t("donec", (128, 1))
            omc = t("omc", (128, 1))
            omnc = t("omnc", (128, 1))

            ps_a = es.enter_context(nc.psum_tensor("ps16a", [128, 512], F32))
            ps_b = es.enter_context(nc.psum_tensor("ps16b", [128, 512], F32))
            ps_st = es.enter_context(nc.psum_tensor("ps16st", [34, 512], F32))
            ps_bc = es.enter_context(nc.psum_tensor("ps16bc", [128, 512], F32))
            ps_bc2 = es.enter_context(nc.psum_tensor("ps16bc2", [128, 512], F32))
            ps_t = es.enter_context(nc.psum_tensor("ps16t", [128, 128], BF16))
            ps_s = es.enter_context(nc.psum_tensor("ps16s", [1, 1], F32))
            ps_c = es.enter_context(nc.psum_tensor("ps16c", [128, 1], F32))

            p.dsy.inc(nc.sync.dma_start(
                x16[:], dt["x16_fm"].rearrange("(c p) t -> p c t", p=128)), 16)
            for wsb, wnm in ((wq, "qwT"), (wk, "kwT"), (wv, "vwT"), (wo, "owT")):
                p.dgp.inc(nc.gpsimd.dma_start(
                    wsb[:], dt[wnm].rearrange("(c p) m -> p c m", p=128)), 16)
            for nm in ("lqwT", "lkwT", "lvwT"):
                p.dve.inc(nc.vector.memset(lbd[nm][0:64, 64:128], 0.0))
                p.dve.inc(nc.vector.memset(lbd[nm][64:128, 0:64], 0.0))
                p.dgp.inc(nc.gpsimd.dma_start(lbd[nm][0:64, 0:64], dt[nm][:]), 16)
                p.dgp.inc(nc.gpsimd.dma_start(lbd[nm][64:128, 64:128], dt[nm][:]), 16)

            p.dsy.wait(nc.vector)
            p.dsy.wait(nc.scalar)
            p.dve.inc(nc.vector.tensor_copy(x16b[:], x16[:]))
            p.act.inc(nc.scalar.activation(x16sq[:], x16[:], AF.Square))
            p.dve.wait(nc.tensor)
            p.act.wait(nc.tensor)
            stat_sums(p, ps_st[:, 0:128], x16b, x16sq, DC, 128, ones_col_bf[:])
            p.pe.wait(nc.vector)
            p.pe.wait(nc.scalar)
            ln_rows(p, ps_st[:, 0:128], 1, 128, 1.0 / D, rows2)
            p.dve.wait(nc.tensor)
            p.act.wait(nc.tensor)
            nc.tensor.matmul(ps_bc[:, 0:128], ones_row_f[:], rows2[:, 0, :], start=True, stop=True)
            p.pe.inc(nc.tensor.matmul(ps_bc2[:, 0:128], ones_row_f[:], rows2[:, 1, :],
                                      start=True, stop=True))
            p.pe.wait(nc.vector)
            for c in range(DC):
                p.dve.inc(nc.vector.tensor_mul(iout[:, c, :], x16[:, c, :], ps_bc[:, 0:128]))
            p.dve.wait(nc.vector)
            for c in range(DC):
                p.dve.inc(nc.vector.tensor_add(xn16[:, c, :], iout[:, c, :], ps_bc2[:, 0:128]))
            p.dve.wait(nc.scalar)
            for c in range(DC):
                p.act.inc(nc.scalar.activation(a16[:, c, :], xn16[:, c, :], AF.Identity,
                                               bias=0.0, scale=cols["lna_c"][:, c:c + 1]))
                p.act.inc(nc.scalar.activation(b16[:, c, :], xn16[:, c, :], AF.Identity,
                                               bias=0.0, scale=cols["lnb_c"][:, c:c + 1]))

            p.act.wait(nc.tensor)
            p.dgp.wait(nc.tensor)
            def proj16(wsb, src, dst, bias_cols, dst_bf=None):
                for mc in range(DC):
                    ps = (ps_a, ps_b)[mc % 2][:, 0:128]
                    if dve_free[mc % 2]:
                        p.dve.wait(nc.tensor, dve_free[mc % 2])
                    for kc in range(DC):
                        inst = nc.tensor.matmul(ps, wsb[:, kc, mc * 128:(mc + 1) * 128],
                                                src[:, kc, :], start=(kc == 0), stop=(kc == DC - 1))
                    p.pe.inc(inst)
                    p.pe.wait(nc.vector)
                    if bias_cols is None:
                        i2 = nc.vector.tensor_copy(dst[:, mc, :], ps)
                    else:
                        i2 = nc.vector.tensor_scalar_add(dst[:, mc, :], ps, bias_cols[:, mc:mc + 1])
                    m = p.dve.inc(i2)
                    if dst_bf is not None:
                        p.dve.wait(nc.vector)
                        m = p.dve.inc(nc.vector.tensor_copy(dst_bf[:, mc, :], dst[:, mc, :]))
                    dve_free[mc % 2] = m
            dve_free = {0: 0, 1: 0}
            proj16(wq, a16, q16, cols["qb_c"])
            proj16(wk, b16, k16, None)
            proj16(wv, b16, v16, cols["vb_c"])

            p.dve.wait(nc.tensor)
            for g in range(DC // 4):
                h0 = 4 * g
                k4 = k16[:, h0:h0 + 4, :].rearrange("p a b -> p (a b)")
                v4 = v16[:, h0:h0 + 4, :].rearrange("p a b -> p (a b)")
                kn4 = kn16[:, h0:h0 + 4, :].rearrange("p a b -> p (a b)")
                p.pe.inc(nc.tensor.matmul(ps_a[:], lbd["lkwT"][:], k4,
                                          start=True, stop=True))
                p.pe.wait(nc.vector)
                p.pe.wait(nc.scalar)
                p.dve.inc(nc.vector.tensor_scalar_add(kk32[:], ps_a[:], cols["lkb2"][:]))
                p.dve.wait(nc.vector)
                p.dve.wait(nc.scalar)
                p.dve.inc(nc.vector.tensor_copy(kkb[:], kk32[:]))
                p.act.inc(nc.scalar.activation(kksq[:], kk32[:], AF.Square))
                p.dve.wait(nc.tensor)
                p.act.wait(nc.tensor)
                nc.tensor.matmul(ps_st[0:2, :], sel2c[:, 0:2], kkb[:],
                                 start=True, stop=True, tile_position=(0, 0))
                inst = nc.tensor.matmul(ps_st[32:34, :], sel2c[:, 0:2], kksq[:],
                                        start=True, stop=True, tile_position=(0, 32))
                p.pe.inc(inst)
                p.pe.wait(nc.vector)
                p.pe.wait(nc.scalar)
                ln_rows(p, ps_st, 2, 512, 1.0 / DH, rows4)
                p.dve.wait(nc.tensor)
                p.act.wait(nc.tensor)
                nc.tensor.matmul(ps_bc[:], sel2r[:], rows4[:, 0, :], start=True, stop=True)
                p.pe.inc(nc.tensor.matmul(ps_bc2[:], sel2r[:], rows4[:, 1, :], start=True, stop=True))
                p.pe.wait(nc.vector)
                p.dve.inc(nc.vector.tensor_mul(dtmp[:], kk32[:], ps_bc[:]))
                p.dve.wait(nc.vector)
                p.dve.inc(nc.vector.tensor_add(dtmp[:], dtmp[:], ps_bc2[:]))
                p.dve.wait(nc.scalar)
                p.act.inc(nc.scalar.activation(kn4, dtmp[:], AF.Identity,
                                               bias=0.0, scale=cols["lnd2"][:]))
                p.act.wait(nc.tensor)
                p.pe.inc(nc.tensor.matmul(ps_b[:], lbd["lvwT"][:], v4,
                                          start=True, stop=True))
                p.pe.wait(nc.vector)
                p.dve.inc(nc.vector.tensor_scalar_add(kkb[:], ps_b[:], cols["lvb2"][:]))
                p.dve.wait(nc.tensor)
                for q4 in range(4):
                    hp = h0 + q4
                    p.pe.inc(nc.tensor.transpose(ps_t[:], kkb[:, q4 * 128:(q4 + 1) * 128],
                                                 ident[:]))
                    p.pe.wait(nc.vector)
                    p.dve.inc(nc.vector.tensor_copy(
                        vi_tok[:, hp * 128:(hp + 1) * 128], ps_t[:]))
                    p.dve.wait(nc.tensor)

            for c in range(DC):
                p.dve.inc(nc.vector.tensor_copy(qcur[0][:, c, :], q16[:, c, :]))
            p.dve.wait(nc.vector)

            cur, nxt = 0, 1
            for it in range(3):
                p.dve.wait(nc.vector)
                for c in range(DC):
                    p.dve.inc(nc.vector.tensor_copy(qcb[:, c, :], qcur[cur][:, c, :]))
                p.dve.wait(nc.tensor)
                for g in range(DC // 4):
                    h0 = 4 * g
                    q4 = qcb[:, h0:h0 + 4, :].rearrange("p a b -> p (a b)")
                    p.pe.inc(nc.tensor.matmul(ps_a[:], lbd["lqwT"][:], q4,
                                              start=True, stop=True))
                    p.pe.wait(nc.vector)
                    p.pe.wait(nc.scalar)
                    p.dve.inc(nc.vector.tensor_scalar_add(kk32[:], ps_a[:], cols["lqb2"][:]))
                    p.dve.wait(nc.vector)
                    p.dve.wait(nc.scalar)
                    p.dve.inc(nc.vector.tensor_copy(kkb[:], kk32[:]))
                    p.act.inc(nc.scalar.activation(kksq[:], kk32[:], AF.Square))
                    p.dve.wait(nc.tensor)
                    p.act.wait(nc.tensor)
                    nc.tensor.matmul(ps_st[0:2, :], sel2c[:, 0:2], kkb[:],
                                     start=True, stop=True, tile_position=(0, 0))
                    inst = nc.tensor.matmul(ps_st[32:34, :], sel2c[:, 0:2], kksq[:],
                                            start=True, stop=True, tile_position=(0, 32))
                    p.pe.inc(inst)
                    p.pe.wait(nc.vector)
                    p.pe.wait(nc.scalar)
                    ln_rows(p, ps_st, 2, 512, 1.0 / DH, rows4)
                    p.dve.wait(nc.tensor)
                    p.act.wait(nc.tensor)
                    nc.tensor.matmul(ps_bc[:], sel2r[:], rows4[:, 0, :], start=True, stop=True)
                    p.pe.inc(nc.tensor.matmul(ps_bc2[:], sel2r[:], rows4[:, 1, :],
                                              start=True, stop=True))
                    p.pe.wait(nc.vector)
                    p.dve.inc(nc.vector.tensor_mul(dtmp[:], kk32[:], ps_bc[:]))
                    p.dve.wait(nc.vector)
                    p.dve.inc(nc.vector.tensor_add(dtmp[:], dtmp[:], ps_bc2[:]))
                    p.dve.wait(nc.scalar)
                    p.pe.wait(nc.scalar)
                    p.act.inc(nc.scalar.activation(q4, dtmp[:], AF.Identity,
                                                   bias=0.0, scale=cols["lnc2"][:]))
                p.act.wait(nc.tensor)
                for h in range(NH):
                    hp, base = h // 2, (h % 2) * 64
                    qn_h = qcb[base:base + 64, hp, :]
                    kn_h = kn16[base:base + 64, hp, :]
                    p.pe.inc(nc.tensor.matmul(ps_a[:, 0:128], qn_h, kn_h, start=True, stop=True))
                    p.pe.wait(nc.scalar)
                    p.act.inc(nc.scalar.activation(pn[:], ps_a[:, 0:128], AF.Exp, bias=0.0,
                                                   scale=QSCALE))
                    p.act.wait(nc.vector)
                    p.dve.inc(nc.vector.tensor_mul(pn[:], pn[:], mask16[:]))
                    p.dve.wait(nc.vector)
                    p.dve.inc(nc.vector.reduce_sum(rowred[:], pn[:], AX.X))
                    p.dve.wait(nc.vector)
                    p.dve.inc(nc.vector.reciprocal(rrecip[:], rowred[:]))
                    p.dve.wait(nc.vector)
                    p.dve.inc(nc.vector.tensor_scalar_mul(pn[:], pn[:], rrecip[:]))
                    p.dve.wait(nc.tensor)
                    p.pe.inc(nc.tensor.transpose(ps_t[:], pn[:], ident[:]))
                    p.pe.wait(nc.vector)
                    p.dve.inc(nc.vector.tensor_copy(pt[:], ps_t[:]))
                    p.dve.wait(nc.tensor)
                    p.pe.inc(nc.tensor.matmul(ps_b[0:64, 0:128], vi_tok[:, h * 64:(h + 1) * 64],
                                              pt[:], start=True, stop=True))
                    p.pe.wait(nc.vector)
                    p.dve.inc(nc.vector.tensor_copy(iout[base:base + 64, hp, :],
                                                    ps_b[0:64, 0:128]))
                p.dve.wait(nc.vector)
                if it == 0:
                    for c in range(DC):
                        p.dve.inc(nc.vector.tensor_copy(attn_o[nxt][:, c, :], iout[:, c, :]))
                        p.dve.inc(nc.vector.tensor_copy(prev[nxt][:, c, :], iout[:, c, :]))
                        p.dve.inc(nc.vector.tensor_add(qcur[nxt][:, c, :], qcur[cur][:, c, :],
                                                       iout[:, c, :]))
                else:
                    for c in range(DC):
                        p.dve.inc(nc.vector.tensor_sub(dtmp[:, 0:128], iout[:, c, :], prev[cur][:, c, :]))
                        p.dve.wait(nc.vector)
                        p.dve.inc(nc.vector.tensor_reduce(dcol[:, c:c + 1], dtmp[:, 0:128], AX.X,
                                                          ALU.add, apply_absolute_value=True))
                        p.dve.wait(nc.vector)
                    p.dve.inc(nc.vector.reduce_sum(dcol1[:], dcol[:], AX.X))
                    p.dve.wait(nc.vector)
                    p.dve.inc(nc.vector.tensor_copy(dcolb[:], dcol1[:]))
                    p.dve.wait(nc.tensor)
                    p.pe.inc(nc.tensor.matmul(ps_s[:], dcolb[:], ones_col_bf[:],
                                              start=True, stop=True))
                    p.pe.wait(nc.vector)
                    fl_diff, fl_t1, fl_lt, fl_brk, fl_done, fl_om = [flt[:, i:i + 1] for i in range(6)]
                    p.dve.inc(nc.vector.tensor_scalar_mul(fl_diff, ps_s[:], DIFF_SCALE))
                    p.dve.wait(nc.vector)
                    p.dve.inc(nc.vector.tensor_scalar(fl_t1, fl_diff, FACTOR, THRESH,
                                                      ALU.mult, ALU.add))
                    p.dve.wait(nc.vector)
                    p.dve.inc(nc.vector.tensor_tensor(fl_lt, fl_diff, fl_t1, ALU.is_lt))
                    p.dve.wait(nc.vector)
                    if it == 1:
                        p.dve.inc(nc.vector.tensor_copy(fl_brk, fl_lt))
                        p.dve.wait(nc.vector)
                        p.dve.inc(nc.vector.tensor_copy(fl_done, flt[:, 2:3]))
                        p.dve.wait(nc.vector)
                        for c in range(DC):
                            p.dve.inc(nc.vector.tensor_copy(attn_o[nxt][:, c, :], iout[:, c, :]))
                            p.dve.inc(nc.vector.tensor_copy(prev[nxt][:, c, :], iout[:, c, :]))
                        p.dve.wait(nc.vector)
                        p.dve.inc(nc.vector.tensor_scalar(fl_om, fl_brk, -1.0, 1.0,
                                                          ALU.mult, ALU.add))
                        p.dve.wait(nc.tensor)
                        p.pe.inc(nc.tensor.matmul(ps_c[:], ones_row_f[:], fl_om,
                                                  start=True, stop=True))
                        p.pe.wait(nc.vector)
                        p.dve.inc(nc.vector.tensor_copy(omnc[:], ps_c[:]))
                        p.dve.wait(nc.vector)
                        for c in range(DC):
                            p.dve.wait(nc.vector)
                            p.dve.inc(nc.vector.tensor_scalar_mul(dtmp[:, 0:128], iout[:, c, :], omnc[:]))
                            p.dve.wait(nc.vector)
                            p.dve.inc(nc.vector.tensor_add(qcur[nxt][:, c, :],
                                                           qcur[cur][:, c, :], dtmp[:, 0:128]))
                        p.dve.inc(nc.vector.tensor_copy(fl_done, fl_brk))
                        p.dve.wait(nc.vector)
                    else:
                        p.dve.inc(nc.vector.tensor_scalar(fl_om, fl_done, -1.0, 1.0,
                                                          ALU.mult, ALU.add))
                        p.dve.wait(nc.tensor)
                        nc.tensor.matmul(ps_c[:], ones_row_f[:], fl_done, start=True, stop=True)
                        p.pe.inc(nc.tensor.matmul(ps_bc[0:128, 0:1], ones_row_f[:], fl_om,
                                                  start=True, stop=True))
                        p.pe.wait(nc.vector)
                        p.dve.inc(nc.vector.tensor_copy(donec[:], ps_c[:]))
                        p.dve.inc(nc.vector.tensor_copy(omc[:], ps_bc[0:128, 0:1]))
                        p.dve.wait(nc.vector)
                        for c in range(DC):
                            p.dve.wait(nc.vector)
                            p.dve.inc(nc.vector.tensor_scalar_mul(dtmp[:, 0:128], iout[:, c, :], omc[:]))
                            p.dve.wait(nc.vector)
                            p.dve.inc(nc.vector.tensor_scalar_mul(iout[:, c, :],
                                                                  attn_o[cur][:, c, :], donec[:]))
                            p.dve.wait(nc.vector)
                            p.dve.inc(nc.vector.tensor_add(attn_o[nxt][:, c, :], dtmp[:, 0:128],
                                                           iout[:, c, :]))
                            p.dve.wait(nc.vector)
                cur, nxt = nxt, cur
                p.dve.wait(nc.tensor)
                p.dve.wait(nc.scalar)

            if "dbg_q16" in dt:
                nc.sync.wait_ge(p.dve.h, p.dve.n)
                nc.sync.wait_ge(p.act.h, p.act.n)
                for nm, sbh in (("dbg_q16", q16), ("dbg_k16", k16), ("dbg_v16", v16),
                                ("dbg_kn16", kn16), ("dbg_vi", vi_tok),
                                ("dbg_attn", attn_o[cur]), ("dbg_xn16", xn16)):
                    p.dou.inc(nc.sync.dma_start(dt[nm][:], sbh[:]), 16)
                nc.sync.wait_ge(p.dou.h, p.dou.n)
            for c in range(DC):
                p.dve.inc(nc.vector.tensor_copy(qcb[:, c, :], attn_o[cur][:, c, :]))
            p.dve.wait(nc.tensor)
            for mc in range(DC):
                for kc in range(DC):
                    inst = nc.tensor.matmul(ps_a[:, 0:128], wo[:, kc, mc * 128:(mc + 1) * 128],
                                            qcb[:, kc, :], start=(kc == 0), stop=(kc == DC - 1))
                p.pe.inc(inst)
                p.pe.wait(nc.vector)
                p.dve.inc(nc.vector.tensor_copy(y16[:, mc, :], ps_a[0:128, 0:16]))
                p.dve.wait(nc.tensor)
            if "dbg_y16" in dt:
                nc.sync.wait_ge(p.dve.h, p.dve.n)
                p.dou.inc(nc.sync.dma_start(dt["dbg_y16"][:], y16[:]), 16)
                nc.sync.wait_ge(p.dou.h, p.dou.n)
        p.bar()

    if "a" in phases:
        with ExitStack() as es:
            t = lambda n, s, d=F32: es.enter_context(nc.sbuf_tensor(n, list(s), d))
            xch = [t("xchA", (128, DC, TC)), t("xchB", (128, DC, TC))]
            dsx = [Sem(nc, "dsxA0"), Sem(nc, "dsxA1")]
            xb = t("xb", (128, DC, TC), BF16)
            xsq = t("xsq", (128, DC, TC), BF16)
            xn = [t("xnA0", (128, DC, TC)), t("xnA1", (128, DC, TC))]
            tmul = t("tmulA", (128, DC, TC))
            rows2 = t("rows2a", (1, 5, TC))
            ps_st = es.enter_context(nc.psum_tensor("psAst", [34, TC], F32))
            ps_r = es.enter_context(nc.psum_tensor("psAr", [128, TC], F32))
            ps_m = es.enter_context(nc.psum_tensor("psAm", [128, TC], F32))
            buf_free = [0, 0]
            buf_freea = [0, 0]
            st_free = 0
            stat_mark = 0
            bc_mark = 0
            addm = 0
            sp_done = [0, 0]
            for ch in range(NCH + 1):
                if ch < NCH:
                    bi = ch % 2
                    nc.sync.wait_ge(p.dve.h, buf_free[bi])
                    nc.sync.wait_ge(p.act.h, buf_freea[bi])
                    dsx[bi].inc(nc.sync.dma_start(
                        xch[bi][:], dt["x_fm"][:, ch * TC:(ch + 1) * TC]
                        .rearrange("(c p) t -> p c t", p=128)), 16)
                if ch == 0:
                    continue
                i = ch - 1
                bi = i % 2
                si = i % 2
                dsx[bi].wait(nc.vector)
                dsx[bi].wait(nc.scalar)
                nc.vector.wait_ge(p.pe.h, stat_mark)
                nc.scalar.wait_ge(p.pe.h, stat_mark)
                p.dve.inc(nc.vector.tensor_copy(xb[:], xch[bi][:]))
                ba = p.act.inc(nc.scalar.activation(xsq[:], xch[bi][:], AF.Square))
                buf_freea[bi] = ba
                p.dve.wait(nc.tensor)
                p.act.wait(nc.tensor)
                nc.tensor.wait_ge(p.dve.h, st_free)
                stat_sums(p, ps_st, xb, xsq, DC, TC, ones_col_bf[:])
                stat_mark = p.pe.n
                nc.vector.wait_ge(p.pe.h, bc_mark)
                nc.scalar.wait_ge(p.pe.h, bc_mark)
                p.pe.wait(nc.vector)
                p.pe.wait(nc.scalar)
                st_free = ln_rows(p, ps_st, 1, TC, 1.0 / D, rows2)
                st_free = p.dve.n
                p.dve.wait(nc.tensor)
                p.act.wait(nc.tensor)
                nc.tensor.wait_ge(p.dve.h, addm)
                nc.tensor.matmul(ps_r[:], ones_row_f[:], rows2[:, 0, :], start=True, stop=True)
                p.pe.inc(nc.tensor.matmul(ps_m[:], ones_row_f[:], rows2[:, 1, :],
                                          start=True, stop=True))
                bc_mark = p.pe.n
                p.pe.wait(nc.vector)
                for c in range(DC):
                    p.dve.inc(nc.vector.tensor_mul(tmul[:, c, :], xch[bi][:, c, :], ps_r[:]))
                buf_free[bi] = p.dve.n
                p.dve.wait(nc.vector)
                nc.vector.wait_ge(p.dou.h, sp_done[si])
                for c in range(DC):
                    p.dve.inc(nc.vector.tensor_add(xn[si][:, c, :], tmul[:, c, :], ps_m[:]))
                addm = p.dve.n
                p.dve.wait(nc.vector)
                for c in range(DC):
                    p.dve.inc(nc.vector.reduce_sum(gacc[:, c, i:i + 1], xn[si][:, c, :], AX.X))
                nc.sync.wait_ge(p.dve.h, p.dve.n)
                sp_done[si] = p.dou.inc(nc.sync.dma_start(
                    dt["xn_sp"][:, i * TC:(i + 1) * TC].rearrange("(c p) t -> p c t", p=128),
                    xn[si][:]), 16)
            nc.sync.wait_ge(p.dou.h, p.dou.n)
        p.bar()

    if "gf" in phases:
        with ExitStack() as es:
            t = lambda n, s, d=F32: es.enter_context(nc.sbuf_tensor(n, list(s), d))
            gsum = t("gsum", (128, DC))
            h1g = t("h1g", (128, 4), BF16)
            h2g = t("h2g", (128, 2), BF16)
            w1 = t("gw1", (128, DC, 512), BF16)
            w2 = t("gw2", (128, 4, 256), BF16)
            w3 = t("gw3", (128, 2, 1), BF16)
            fl = t("gfl", (1, 4))
            ps_g = es.enter_context(nc.psum_tensor("psG", [128, 512], F32))
            ps_g2 = es.enter_context(nc.psum_tensor("psG2", [1, 1], F32))
            p.dgp.inc(nc.gpsimd.dma_start(w1[:], dt["gcw1"].rearrange("(c p) m -> p c m", p=128)), 16)
            p.dgp.inc(nc.gpsimd.dma_start(w2[:], dt["gcw2"].rearrange("(c p) m -> p c m", p=128)), 16)
            with nc.allow_non_contiguous_dma(reason="tiny [256,1] weight"):
                p.dgp.inc(nc.gpsimd.dma_start(w3[:], dt["gcw3"].rearrange("(c p) m -> p c m", p=128)), 16)
            p.dve.inc(nc.vector.tensor_reduce(gsum[:], gacc[:], AX.X, ALU.add))
            p.dve.wait(nc.vector)
            p.dve.inc(nc.vector.tensor_scalar_mul(gctx_bf[:], gsum[:], 1.0 / S))
            p.dve.wait(nc.tensor)
            p.dgp.wait(nc.tensor)
            for mc in range(4):
                for kc in range(DC):
                    inst = nc.tensor.matmul(ps_g[:, mc:mc + 1],
                                            w1[:, kc, mc * 128:(mc + 1) * 128],
                                            gctx_bf[:, kc:kc + 1],
                                            start=(mc == 0 and kc == 0),
                                            stop=(mc == 3 and kc == DC - 1))
                p.pe.inc(inst)
            p.pe.wait(nc.scalar)
            for mc in range(4):
                p.act.inc(nc.scalar.activation(h1g[:, mc:mc + 1], ps_g[:, mc:mc + 1], AF.Gelu,
                                               bias=cols["gcb1_c"][:, mc:mc + 1], scale=1.0))
            p.act.wait(nc.tensor)
            for mc in range(2):
                for kc in range(4):
                    inst = nc.tensor.matmul(ps_g[:, 8 + mc:9 + mc],
                                            w2[:, kc, mc * 128:(mc + 1) * 128],
                                            h1g[:, kc:kc + 1],
                                            start=(mc == 0 and kc == 0),
                                            stop=(mc == 1 and kc == 3))
                p.pe.inc(inst)
            p.pe.wait(nc.scalar)
            for mc in range(2):
                p.act.inc(nc.scalar.activation(h2g[:, mc:mc + 1], ps_g[:, 8 + mc:9 + mc], AF.Gelu,
                                               bias=cols["gcb2_c"][:, mc:mc + 1], scale=1.0))
            p.act.wait(nc.tensor)
            for kc in range(2):
                inst = nc.tensor.matmul(ps_g2[:], w3[:, kc, :], h2g[:, kc:kc + 1],
                                        start=(kc == 0), stop=(kc == 1))
            p.pe.inc(inst)
            p.pe.wait(nc.vector)
            fgf, ft, fc1, fc2 = [fl[:, i:i + 1] for i in range(4)]
            p.dve.inc(nc.vector.tensor_tensor(ft, ps_g2[:], cols["gcb3"][:], ALU.add))
            p.dve.wait(nc.scalar)
            p.act.inc(nc.scalar.activation(ft, ft, AF.Tanh, bias=0.0, scale=0.5))
            p.act.wait(nc.vector)
            p.dve.inc(nc.vector.tensor_scalar(fgf, ft, 0.5, 0.5, ALU.mult, ALU.add))
            p.dve.wait(nc.vector)
            p.dve.inc(nc.vector.tensor_scalar(fc1, fgf, 1.4, -1.0, ALU.mult, ALU.add))
            p.dve.wait(nc.vector)
            p.dve.inc(nc.vector.tensor_tensor(fc1, fc1, cols["ar"][:], ALU.mult))
            p.dve.inc(nc.vector.tensor_scalar_mul(fc2, cols["ar"][:], 0.3))
            p.dve.wait(nc.vector)
            p.dve.inc(nc.vector.tensor_tensor(c2p[:], fc2, fc2, ALU.bypass))
            p.dve.inc(nc.vector.tensor_scalar(ft, fc2, 1.0, 0.1, ALU.mult, ALU.add))
            p.dve.wait(nc.vector)
            p.dve.inc(nc.vector.tensor_tensor(c1p[:], fc1, ft, ALU.add))
        p.bar()

    build_rest(p, dt, sb, cols, phases)

    return nc, dt


def build_rest(p, dt, sb, cols, phases):
    nc = p.nc
    ones_col_bf = sb["ones_col_bf"]
    ones_row_f = sb["ones_row_f"]
    ones_row_bf = sb["ones_row_bf"]
    sel2r = sb["sel2r_sb"]
    sel33 = sb["sel33_sb"]
    gacc = sb["gacc"]
    y16 = sb["y16"]
    c1p, c2p = sb["c1p"], sb["c2p"]
    bq8 = sb["bq8"]

    if "b" in phases:
        with ExitStack() as es:
            t = lambda n, s, d=F32: es.enter_context(nc.sbuf_tensor(n, list(s), d))
            xch = [t("xchB0", (128, DC, TC)), t("xchB1", (128, DC, TC))]
            xnch = [t("xnchB0", (128, DC, TC)), t("xnchB1", (128, DC, TC))]
            outp = [t("outB0", (128, DC, TC)), t("outB1", (128, DC, TC))]
            tw = [t("twB0", (128, DC, TC)), t("twB1", (128, DC, TC))]
            trow = [t("trowB0", (1, TC)), t("trowB1", (1, TC))]
            cplrow = [t("cplB0", (1, TC)), t("cplB1", (1, TC))]
            ldb = Sem(nc, "ldB")
            outb = [t("outbB0", (128, DC, TC), BF16), t("outbB1", (128, DC, TC), BF16)]
            h1 = t("h1B", (128, DC, TC), BF16)
            h2 = t("h2B", (128, 4, TC), BF16)
            h3 = t("h3B", (128, 2, TC), BF16)
            w1 = t("cw1", (128, DC, D), BF16)
            w2 = t("cw2", (128, DC, 512), BF16)
            w3 = t("cw3", (128, 4, 256), BF16)
            w4 = t("cw4", (128, 2, 1), BF16)
            ps_h = [es.enter_context(nc.psum_tensor("psB0", [128, TC], F32)),
                    es.enter_context(nc.psum_tensor("psB1", [128, TC], F32))]
            ps_t4 = [es.enter_context(nc.psum_tensor("psBt4a", [1, TC], F32)),
                     es.enter_context(nc.psum_tensor("psBt4b", [1, TC], F32))]
            ps_cb = [es.enter_context(nc.psum_tensor("psBcba", [128, TC], F32)),
                     es.enter_context(nc.psum_tensor("psBcbb", [128, TC], F32))]
            p.dgp.inc(nc.gpsimd.dma_start(w1[:], dt["cnw1"].rearrange("(c p) m -> p c m", p=128)), 16)
            p.dgp.inc(nc.gpsimd.dma_start(w2[:], dt["cnw2"].rearrange("(c p) m -> p c m", p=128)), 16)
            p.dgp.inc(nc.gpsimd.dma_start(w3[:], dt["cnw3"].rearrange("(c p) m -> p c m", p=128)), 16)
            with nc.allow_non_contiguous_dma(reason="tiny [256,1] weight"):
                p.dgp.inc(nc.gpsimd.dma_start(w4[:], dt["cnw4"].rearrange("(c p) m -> p c m", p=128)), 16)
            p.dgp.wait(nc.tensor)
            ld_mark = [0, 0]
            copy_mark = [0, 0]
            x1_mark = [0, 0]
            st_mark = [0, 0]
            cbfree = [0, 0]
            t4free = [0, 0]
            hfree = [p.act.n, p.act.n]
            w1_done = [0, 0]
            t4mm = [0, 0]
            cpl_mark = [0, 0]
            cb_mm = [0, 0]

            ldxn_mark = [0, 0]
            ldxc_mark = [0, 0]

            def emit_load_xch(i, s):
                nc.sync.wait_ge(p.dve.h, x1_mark[s])
                ldb.inc(nc.sync.dma_start(
                    xch[s][:], dt["x_fm"][:, i * TC:(i + 1) * TC]
                    .rearrange("(c p) t -> p c t", p=128)), 16)
                ldxc_mark[s] = ldb.n

            def emit_load_xnch(i, s):
                nc.sync.wait_ge(p.dve.h, copy_mark[s])
                ldb.inc(nc.sync.dma_start(
                    xnch[s][:], dt["xn_sp"][:, i * TC:(i + 1) * TC]
                    .rearrange("(c p) t -> p c t", p=128)), 16)
                ldxn_mark[s] = ldb.n

            def emit_init(s):
                nc.vector.wait_ge(ldb.h, ldxn_mark[s])
                for c in range(DC):
                    p.dve.inc(nc.vector.tensor_copy(outp[s][:, c, :], xnch[s][:, c, :]))
                copy_mark[s] = p.dve.n

            ob_mark = [0, 0]

            def emit_copy(s):
                nc.vector.wait_ge(p.pe.h, w1_done[s])
                p.dve.wait(nc.vector)
                p.dve.inc(nc.vector.tensor_copy(outb[s][:], outp[s][:]))
                ob_mark[s] = p.dve.n

            def emit_mms(s):
                for mc in range(DC):
                    bk = mc % 2
                    nc.tensor.wait_ge(p.act.h, hfree[bk])
                    if mc == 0:
                        nc.tensor.wait_ge(p.dve.h, ob_mark[s])
                    for kc in range(DC):
                        inst = nc.tensor.matmul(ps_h[bk][:], w1[:, kc, mc * 128:(mc + 1) * 128],
                                                outb[s][:, kc, :], start=(kc == 0), stop=(kc == DC - 1))
                    p.pe.inc(inst)
                    p.pe.wait(nc.scalar)
                    hfree[bk] = p.act.inc(nc.scalar.activation(h1[:, mc, :], ps_h[bk][:], AF.Gelu,
                                                               bias=cols["cnb1_c"][:, mc:mc + 1], scale=1.0))
                w1_done[s] = p.pe.n
                p.act.wait(nc.tensor)
                for mc in range(4):
                    bk = mc % 2
                    nc.tensor.wait_ge(p.act.h, hfree[bk])
                    for kc in range(DC):
                        inst = nc.tensor.matmul(ps_h[bk][:], w2[:, kc, mc * 128:(mc + 1) * 128],
                                                h1[:, kc, :], start=(kc == 0), stop=(kc == DC - 1))
                    p.pe.inc(inst)
                    p.pe.wait(nc.scalar)
                    hfree[bk] = p.act.inc(nc.scalar.activation(h2[:, mc, :], ps_h[bk][:], AF.Gelu,
                                                               bias=cols["cnb2_c"][:, mc:mc + 1], scale=1.0))
                p.act.wait(nc.tensor)
                for mc in range(2):
                    bk = mc % 2
                    nc.tensor.wait_ge(p.act.h, hfree[bk])
                    for kc in range(4):
                        inst = nc.tensor.matmul(ps_h[bk][:], w3[:, kc, mc * 128:(mc + 1) * 128],
                                                h2[:, kc, :], start=(kc == 0), stop=(kc == 3))
                    p.pe.inc(inst)
                    p.pe.wait(nc.scalar)
                    hfree[bk] = p.act.inc(nc.scalar.activation(h3[:, mc, :], ps_h[bk][:], AF.Gelu,
                                                               bias=cols["cnb3_c"][:, mc:mc + 1], scale=1.0))
                p.act.wait(nc.tensor)
                nc.tensor.wait_ge(p.dve.h, t4free[s])
                for kc in range(2):
                    inst = nc.tensor.matmul(ps_t4[s][:], w4[:, kc, :], h3[:, kc, :],
                                            start=(kc == 0), stop=(kc == 1))
                t4mm[s] = p.pe.inc(inst)

            def emit_tail1(s):
                nc.vector.wait_ge(p.pe.h, t4mm[s])
                p.dve.inc(nc.vector.tensor_scalar_add(trow[s][:], ps_t4[s][:], cols["cnb4"][:]))
                t4free[s] = p.dve.n
                nc.scalar.wait_ge(p.dve.h, t4free[s])
                tanh_m = p.act.inc(nc.scalar.activation(trow[s][:], trow[s][:], AF.Tanh,
                                                        bias=0.0, scale=0.5))
                nc.vector.wait_ge(p.act.h, tanh_m)
                p.dve.inc(nc.vector.tensor_scalar(cplrow[s][:], trow[s][:], c2p[:], None, ALU.mult))
                p.dve.wait(nc.vector)
                cpl_mark[s] = p.dve.inc(nc.vector.tensor_scalar(cplrow[s][:], cplrow[s][:], c1p[:],
                                                                None, ALU.add))
                nc.tensor.wait_ge(p.dve.h, max(cpl_mark[s], cbfree[s]))
                cb_mm[s] = p.pe.inc(nc.tensor.matmul(ps_cb[s][:], ones_row_f[:], cplrow[s][:],
                                                     start=True, stop=True))

            def emit_tail2(s, cyc):
                nc.vector.wait_ge(p.pe.h, cb_mm[s])
                if cyc == 0:
                    nc.vector.wait_ge(p.dou.h, st_mark[s])
                for j in range(DC):
                    src = (j + 4) % DC
                    p.dve.inc(nc.vector.tensor_mul(tw[s][:, j, :], outp[s][:, src, :], ps_cb[s][:]))
                cbfree[s] = p.dve.n
                p.dve.wait(nc.vector)
                for j in range(DC):
                    op = ALU.add if j in (0, 1, 6, 7) else ALU.subtract
                    p.dve.inc(nc.vector.tensor_tensor(outp[s][:, j, :], outp[s][:, j, :],
                                                      tw[s][:, j, :], op))

            def emit_x1(i, s):
                nc.vector.wait_ge(ldb.h, ldxc_mark[s])
                p.dve.wait(nc.vector)
                for c in range(DC):
                    p.dve.inc(nc.vector.tensor_add(tw[s][:, c, :], xch[s][:, c, :],
                                                   outp[s][:, c, :]))
                x1_mark[s] = p.dve.n
                p.dve.wait(nc.vector)
                for c in range(DC):
                    p.dve.inc(nc.vector.tensor_scalar_add(tw[s][:, c, :], tw[s][:, c, :],
                                                          cols["ob_c"][:, c:c + 1]))
                if i == 0 and "t16" in phases:
                    p.dve.wait(nc.vector)
                    for c in range(DC):
                        p.dve.inc(nc.vector.tensor_add(tw[s][:, c, 0:16],
                                                       tw[s][:, c, 0:16], y16[:, c, :]))
                nc.sync.wait_ge(p.dve.h, p.dve.n)
                st_mark[s] = p.dou.inc(nc.sync.dma_start(
                    dt["x1_sp"][:, i * TC:(i + 1) * TC].rearrange("(c p) t -> p c t", p=128),
                    tw[s][:]), 16)

            emit_load_xnch(0, 0)
            emit_load_xch(0, 0)
            emit_load_xnch(1, 1)
            emit_load_xch(1, 1)
            for pr in range(NCH // 2):
                a, b = 2 * pr, 2 * pr + 1
                if pr == 0:
                    emit_init(0)
                    emit_init(1)
                    emit_copy(0)
                if a + 2 < NCH:
                    emit_load_xnch(a + 2, 0)
                    emit_load_xnch(b + 2, 1)
                for cyc in range(3):
                    emit_mms(0)
                    emit_tail1(0)
                    emit_copy(1)
                    emit_tail2(0, cyc)
                    if cyc == 2:
                        emit_x1(a, 0)
                        if a + 2 < NCH:
                            emit_load_xch(a + 2, 0)
                            emit_init(0)
                            emit_copy(0)
                    emit_mms(1)
                    emit_tail1(1)
                    if cyc < 2:
                        emit_copy(0)
                    emit_tail2(1, cyc)
                    if cyc == 2:
                        emit_x1(b, 1)
                        if b + 2 < NCH:
                            emit_load_xch(b + 2, 1)
                            emit_init(1)
            nc.sync.wait_ge(p.dou.h, p.dou.n)
        p.bar()

    k_res = v_res = None
    wq_res = wpj_res = None
    wq_mark = 0
    if "c" in phases or "attn" in phases:
        es_kv = ExitStack()
        k_res = es_kv.enter_context(nc.sbuf_tensor("k_res", [128, DC, S], BF16))
        v_res = es_kv.enter_context(nc.sbuf_tensor("v_res", [128, NKT, NH, DH + 1], BF16))

    def load_wq_wpj():
        p.dgp.inc(nc.gpsimd.dma_start(
            wq_res[:], dt["wqkv"][:, 0:D].rearrange("(c p) m -> p c m", p=128)), 16)
        p.dgp.inc(nc.gpsimd.dma_start(
            wpj_res[:], dt["wproj"].rearrange("(c p) m -> p c m", p=128)), 16)
        return p.dgp.n
    if "c" in phases:
        with ExitStack() as es:
            t = lambda n, s, d=F32: es.enter_context(nc.sbuf_tensor(n, list(s), d))
            xch = [t("xchC0", (128, DC, TC), BF16), t("xchC1", (128, DC, TC), BF16)]
            ldc = Sem(nc, "ldC")
            xsq = t("xsqC", (128, DC, TC), BF16)
            xn1 = t("xn1C", (128, DC, TC), BF16)
            rows2 = t("rows2c", (1, 5, TC))
            wk = t("wkC", (128, DC, D), BF16)
            ps_st = es.enter_context(nc.psum_tensor("psCst", [34, TC], F32))
            ps_r = es.enter_context(nc.psum_tensor("psCr", [128, TC], F32))
            ps_m = es.enter_context(nc.psum_tensor("psCm", [128, TC], F32))
            ps_k = [es.enter_context(nc.psum_tensor("psCk0", [128, TC], F32)),
                    es.enter_context(nc.psum_tensor("psCk1", [128, TC], F32))]
            p.dgp.inc(nc.gpsimd.dma_start(
                wk[:], dt["wqkv"][:, D:2 * D].rearrange("(c p) m -> p c m", p=128)), 16)
            p.dgp.wait(nc.tensor)
            stat_mark = 0
            bc_mark = 0
            kmm_mark = 0
            sp_done = 0
            st_free = 0
            addm = 0
            xfree = [0, 0]
            ld_mark = [0, 0]
            kfree = [p.act.n, p.act.n]

            def emit_load(i):
                s = i % 2
                nc.gpsimd.wait_ge(p.dve.h, xfree[s])
                ldc.inc(nc.gpsimd.dma_start(
                    xch[s][:], dt["x1_sp"][:, i * TC:(i + 1) * TC]
                    .rearrange("(c p) t -> p c t", p=128)), 16)
                ld_mark[s] = ldc.n

            def emit_khalf(i, half):
                for mc in range(4 * half, 4 * half + 4):
                    bk = mc % 2
                    nc.tensor.wait_ge(p.act.h, kfree[bk])
                    for kc in range(DC):
                        inst = nc.tensor.matmul(ps_k[bk][:], wk[:, kc, mc * 128:(mc + 1) * 128],
                                                xn1[:, kc, :], start=(kc == 0), stop=(kc == DC - 1))
                    p.pe.inc(inst)
                    p.pe.wait(nc.scalar)
                    kfree[bk] = p.act.inc(nc.scalar.activation(
                        k_res[:, mc, i * TC:(i + 1) * TC], ps_k[bk][:], AF.Identity,
                        bias=cols["bk_c"][:, mc:mc + 1], scale=1.0))

            emit_load(0)
            for i in range(NCH + 1):
                si = i % 2
                if i < NCH:
                    nc.scalar.wait_ge(ldc.h, ld_mark[si])
                    nc.scalar.wait_ge(p.pe.h, stat_mark)
                    p.act.inc(nc.scalar.activation(xsq[:], xch[si][:], AF.Square))
                    p.act.wait(nc.tensor)
                    nc.tensor.wait_ge(ldc.h, ld_mark[si])
                    nc.tensor.wait_ge(p.dve.h, st_free)
                    stat_sums(p, ps_st, xch[si], xsq, DC, TC, ones_col_bf[:])
                    stat_mark = p.pe.n
                if i >= 1:
                    emit_khalf(i - 1, 0)
                if i < NCH:
                    nc.vector.wait_ge(p.pe.h, bc_mark)
                    nc.scalar.wait_ge(p.pe.h, bc_mark)
                    nc.vector.wait_ge(p.pe.h, stat_mark)
                    nc.scalar.wait_ge(p.pe.h, stat_mark)
                    st_free = ln_rows(p, ps_st, 1, TC, 1.0 / D, rows2)
                    st_free = p.dve.n
                    p.dve.wait(nc.tensor)
                    p.act.wait(nc.tensor)
                    nc.tensor.wait_ge(p.dve.h, addm)
                    nc.tensor.matmul(ps_r[:], ones_row_f[:], rows2[:, 0, :],
                                     start=True, stop=True)
                    p.pe.inc(nc.tensor.matmul(ps_m[:], ones_row_f[:], rows2[:, 1, :],
                                              start=True, stop=True))
                    bc_mark = p.pe.n
                    nc.vector.wait_ge(p.pe.h, bc_mark)
                    for c in range(DC):
                        p.dve.inc(nc.vector.tensor_mul(xch[si][:, c, :], xch[si][:, c, :],
                                                       ps_r[:]))
                if i >= 1:
                    emit_khalf(i - 1, 1)
                    kmm_mark = p.pe.n
                if i < NCH:
                    p.dve.wait(nc.vector)
                    nc.vector.wait_ge(p.pe.h, kmm_mark)
                    nc.vector.wait_ge(p.dou.h, sp_done)
                    for c in range(DC):
                        p.dve.inc(nc.vector.tensor_add(xn1[:, c, :], xch[si][:, c, :], ps_m[:]))
                    addm = p.dve.n
                    xfree[si] = p.dve.n
                    if i + 1 < NCH:
                        emit_load(i + 1)
                    nc.sync.wait_ge(p.dve.h, p.dve.n)
                    sp_done = p.dou.inc(nc.sync.dma_start(
                        dt["xn1_sp"][:, i * TC:(i + 1) * TC]
                        .rearrange("(c p) t -> p c t", p=128),
                        xn1[:]), 16)
            nc.sync.wait_ge(p.dou.h, p.dou.n)
        p.bar()

        if "attn" in phases:
            wq_res = es_kv.enter_context(nc.sbuf_tensor("wq_res", [128, DC, D], BF16))
            wpj_res = es_kv.enter_context(nc.sbuf_tensor("wpj_res", [128, DC, D], BF16))
            wq_mark = load_wq_wpj()

        with ExitStack() as es:
            t = lambda n, s, d=F32: es.enter_context(nc.sbuf_tensor(n, list(s), d))
            xn1b = [t("xn1C2a", (128, DC, TC), BF16), t("xn1C2b", (128, DC, TC), BF16)]
            ldc2 = Sem(nc, "ldC2")
            ldbv = Sem(nc, "ldBV")
            wv = t("wvC2", (128, DC, D), BF16)
            bvrow_f = t("bvrowC2", (1, D), F32)
            bv_bf = t("bvbfC2", (1, D), BF16)
            ps_v = [es.enter_context(nc.psum_tensor("psCv0", [128, 512], F32)),
                    es.enter_context(nc.psum_tensor("psCv1", [128, 512], F32))]
            p.dgp.inc(nc.gpsimd.dma_start(
                wv[:], dt["wqkv"][:, 2 * D:3 * D].rearrange("(c p) m -> p c m", p=128)), 16)
            ldbv.inc(nc.sync.dma_start(bvrow_f[:], dt["bv_row"][:]), 16)
            ldbv.wait(nc.vector)
            p.dve.inc(nc.vector.tensor_copy(bv_bf[:], bvrow_f[:]))
            p.dve.inc(nc.vector.memset(v_res[:, :, :, DH:DH + 1], 1.0))
            p.dgp.wait(nc.tensor)
            p.dve.wait(nc.tensor)
            ld2_mark = [0, 0]
            vmm_mark = [0, 0]
            for s in range(2):
                ldc2.inc(nc.sync.dma_start(
                    xn1b[s][:], dt["xn1_sp"][:, s * TC:(s + 1) * TC]
                    .rearrange("(c p) t -> p c t", p=128)), 16)
                ld2_mark[s] = ldc2.n
            for i in range(NCH):
                s = i % 2
                xn1 = xn1b[s]
                nc.tensor.wait_ge(ldc2.h, ld2_mark[s])
                kfree = [p.dve.n, p.dve.n]
                for ts4 in range(TC // 128):
                    kt = i * (TC // 128) + ts4
                    for half in range(2):
                        bk = half
                        nc.tensor.wait_ge(p.dve.h, kfree[bk])
                        for kc in range(DC):
                            nc.tensor.matmul(ps_v[bk][:], xn1[:, kc, ts4 * 128:(ts4 + 1) * 128],
                                             wv[:, kc, half * 512:(half + 1) * 512],
                                             start=(kc == 0), stop=False)
                        inst = nc.tensor.matmul(ps_v[bk][:], ones_row_bf[:],
                                                bv_bf[:, half * 512:(half + 1) * 512],
                                                start=False, stop=True)
                        p.pe.inc(inst)
                        p.pe.wait(nc.vector)
                        kfree[bk] = p.dve.inc(nc.vector.tensor_copy(
                            v_res[:, kt, half * 8:(half + 1) * 8, 0:DH],
                            ps_v[bk][:].rearrange("p (h d) -> p h d", d=DH)))
                vmm_mark[s] = p.pe.n
                if i + 2 < NCH:
                    nc.sync.wait_ge(p.pe.h, vmm_mark[s])
                    ldc2.inc(nc.sync.dma_start(
                        xn1b[s][:], dt["xn1_sp"][:, (i + 2) * TC:(i + 3) * TC]
                        .rearrange("(c p) t -> p c t", p=128)), 16)
                    ld2_mark[s] = ldc2.n
        p.bar()

    if "attn" in phases:
        with ExitStack() as es:
            t = lambda n, s, d=F32: es.enter_context(nc.sbuf_tensor(n, list(s), d))
            xn1c = t("xn1At", (128, DC, TC), BF16)
            lda = Sem(nc, "ldAt")
            qfm = t("qAt", (128, DC, TC), BF16)
            pbuf = [t("pA0", (128, TC), BF16), t("pA1", (128, TC), BF16),
                    t("pA2", (128, TC), BF16), t("pA3", (128, TC), BF16)]
            o2s = t("o2sAt", (128, TC))
            r2ab = t("r2abAt", (33, TC))
            nb_sb = t("nbsbAt", (128, TC))
            aout = t("aoutAt", (128, DC, TC), BF16)
            lneg = sb["lneg_sb"]
            rbig = sb["rbig_sb"]
            ps_se = es.enter_context(nc.psum_tensor("psSe", [128, TC], F32))
            ps_so = es.enter_context(nc.psum_tensor("psSo", [128, TC], F32))
            ps_se2 = es.enter_context(nc.psum_tensor("psSe2", [128, TC], F32))
            ps_so2 = es.enter_context(nc.psum_tensor("psSo2", [128, TC], F32))
            ps_o2e = es.enter_context(nc.psum_tensor("psO2e", [128, TC], F32))
            ps_o2o = es.enter_context(nc.psum_tensor("psO2o", [128, TC], F32))
            ps_o2e2 = es.enter_context(nc.psum_tensor("psO2e2", [128, TC], F32))
            ps_o2o2 = es.enter_context(nc.psum_tensor("psO2o2", [128, TC], F32))
            SE = [ps_se, ps_se2]
            SO = [ps_so, ps_so2]
            O2E = [ps_o2e, ps_o2e2]
            O2O = [ps_o2o, ps_o2o2]
            if wq_res is None:
                wq_res = es_kv.enter_context(nc.sbuf_tensor("wq_res", [128, DC, D], BF16))
                wpj_res = es_kv.enter_context(nc.sbuf_tensor("wpj_res", [128, DC, D], BF16))
            if wq_mark == 0:
                wq_mark = load_wq_wpj()
            p.dve.inc(nc.vector.memset(r2ab[:], 0.0))
            qp_mark = 0
            ev_mark = 0
            stq = [0] * DC
            for j in range(NCH):
                q0 = j * TC
                nkt = (j + 1) * (TC // 128)
                if j == 0:
                    lda.inc(nc.sync.dma_start(
                        xn1c[:], dt["xn1_sp"][:, q0:q0 + TC]
                        .rearrange("(c p) t -> p c t", p=128)), 16)
                    nc.tensor.wait_ge(p.dgp.h, wq_mark)
                lda.wait(nc.tensor)
                qfree = [p.act.n, p.act.n]
                for mc in range(DC):
                    bk = mc % 2
                    nc.tensor.wait_ge(p.act.h, qfree[bk])
                    if mc < 2:
                        nc.tensor.wait_ge(p.dve.h, ev_mark)
                    for kc in range(DC):
                        inst = nc.tensor.matmul(SE[bk][:], wq_res[:, kc, mc * 128:(mc + 1) * 128],
                                                xn1c[:, kc, :], start=(kc == 0), stop=(kc == DC - 1))
                    p.pe.inc(inst)
                    p.pe.wait(nc.scalar)
                    nc.scalar.wait_ge(p.dou.h, stq[mc])
                    qfree[bk] = p.act.inc(nc.scalar.activation(
                        qfm[:, mc, :], SE[bk][:], AF.Identity,
                        bias=bq8[:, mc:mc + 1], scale=QSCALE))
                qp_mark = p.pe.n
                if j + 1 < NCH:
                    nc.sync.wait_ge(p.pe.h, qp_mark)
                    lda.inc(nc.sync.dma_start(
                        xn1c[:], dt["xn1_sp"][:, q0 + TC:q0 + 2 * TC]
                        .rearrange("(c p) t -> p c t", p=128)), 16)
                p.act.wait(nc.tensor)
                sfree = [p.act.n, p.act.n]
                o2free = [p.dve.n, p.dve.n]
                state = {"nbdone": p.dve.n, "nbm": p.pe.n}
                prev_norm = None

                def emit_norm(pn):
                    phd, pob, r2m = pn
                    nc.tensor.wait_ge(p.dve.h, r2m)
                    NB = O2E[pob]
                    state["nbm"] = p.pe.inc(nc.tensor.matmul(
                        NB[:], sel33[:], r2ab[:], start=True, stop=True))
                    nc.vector.wait_ge(p.pe.h, state["nbm"])
                    nc.vector.reciprocal(nb_sb[:], NB[:])
                    state["nbdone"] = p.dve.inc(
                        nc.vector.tensor_mul(aout[:, phd, :], o2s[:], nb_sb[:]))
                    o2free[pob] = state["nbdone"]

                def emit_scores(hp, kt, sbk):
                    diag = kt * 128 + 128 > q0
                    base = q0 - kt * 128
                    nc.tensor.wait_ge(p.act.h, sfree[sbk])
                    nc.tensor.matmul(SE[sbk][:], k_res[0:64, hp, kt * 128:(kt + 1) * 128],
                                     qfm[0:64, hp, :], start=True, stop=not diag,
                                     tile_position=(0, 0))
                    inst = nc.tensor.matmul(SO[sbk][:], k_res[64:128, hp, kt * 128:(kt + 1) * 128],
                                            qfm[64:128, hp, :], start=True, stop=not diag,
                                            tile_position=(64, 0))
                    if diag:
                        w = min(TC, 128 - base)
                        rb = rbig[:, 384 + base:384 + base + w]
                        nc.tensor.matmul(SE[sbk][:, 0:w], lneg[:], rb, start=False, stop=True)
                        inst = nc.tensor.matmul(SO[sbk][:, 0:w], lneg[:], rb, start=False, stop=True)
                    return p.pe.inc(inst)

                gl = [(hp, kt) for hp in range(DC) for kt in range(nkt)]
                mm_mark = {}
                pv_mark = {}
                mm_mark[0] = emit_scores(0, 0, 0)
                for g, (hp, kt) in enumerate(gl):
                    he, ho = 2 * hp, 2 * hp + 1
                    ob = hp % 2
                    sbk = g % 2
                    if kt == 1 and prev_norm is not None:
                        emit_norm(prev_norm)
                        prev_norm = None
                    nc.scalar.wait_ge(p.pe.h, mm_mark[g])
                    if g >= 2:
                        nc.scalar.wait_ge(p.pe.h, pv_mark[g - 2])
                    nc.scalar.activation(pbuf[2 * sbk][:], SE[sbk][:], AF.Exp,
                                         bias=0.0, scale=1.0)
                    m = p.act.inc(nc.scalar.activation(pbuf[2 * sbk + 1][:], SO[sbk][:],
                                                       AF.Exp, bias=0.0, scale=1.0))
                    sfree[sbk] = m
                    if g + 1 < len(gl):
                        nhp, nkt_ = gl[g + 1]
                        mm_mark[g + 1] = emit_scores(nhp, nkt_, (g + 1) % 2)
                    if kt == 0:
                        nc.tensor.wait_ge(p.dve.h, o2free[ob])
                    nc.tensor.wait_ge(p.act.h, m)
                    nc.tensor.matmul(O2E[ob][0:65, :], v_res[:, kt, he, :], pbuf[2 * sbk][:],
                                     start=(kt == 0), stop=(kt == nkt - 1),
                                     tile_position=(0, 0))
                    pv_mark[g] = p.pe.inc(nc.tensor.matmul(
                        O2O[ob][0:65, :], v_res[:, kt, ho, :], pbuf[2 * sbk + 1][:],
                        start=(kt == 0), stop=(kt == nkt - 1),
                        tile_position=(0, 0)))
                    if kt == nkt - 1:
                        nc.vector.wait_ge(p.pe.h, max(pv_mark[g], state["nbm"]))
                        nc.vector.tensor_copy(r2ab[0:1, :], O2E[ob][64:65, :])
                        nc.vector.tensor_copy(r2ab[32:33, :], O2O[ob][64:65, :])
                        nc.vector.tensor_copy(o2s[0:64, :], O2E[ob][0:64, :])
                        r2m = p.dve.inc(nc.vector.tensor_copy(o2s[64:128, :], O2O[ob][0:64, :]))
                        prev_norm = (hp, ob, r2m)
                emit_norm(prev_norm)
                prev_norm = None
                nc.tensor.wait_ge(p.dve.h, state["nbdone"])
                nc.tensor.wait_ge(p.act.h, max(sfree))
                ofree = [p.dve.n, p.dve.n]
                for mc in range(DC):
                    bk = mc % 2
                    nc.tensor.wait_ge(p.dve.h, ofree[bk])
                    for kc in range(DC):
                        inst = nc.tensor.matmul(SE[bk][:], wpj_res[:, kc, mc * 128:(mc + 1) * 128],
                                                aout[:, kc, :], start=(kc == 0), stop=(kc == DC - 1))
                    p.pe.inc(inst)
                    p.pe.wait(nc.vector)
                    ofree[bk] = p.dve.inc(nc.vector.tensor_scalar_add(
                        qfm[:, mc, :], SE[bk][:], cols["bp_c"][:, mc:mc + 1]))
                    nc.sync.wait_ge(p.dve.h, p.dve.n)
                    stq[mc] = p.dou.inc(nc.sync.dma_start(
                        dt["xattn_sp"][mc * 128:(mc + 1) * 128, q0:q0 + TC],
                        qfm[:, mc, :]), 16)
                ev_mark = p.dve.n
            nc.sync.wait_ge(p.dou.h, p.dou.n)
        p.bar()

    if k_res is not None:
        es_kv.close()

    if "ffn" in phases:
        with ExitStack() as es:
            t = lambda n, s, d=F32: es.enter_context(nc.sbuf_tensor(n, list(s), d))
            wg = t("wgF", (128, DC, HID), BF16)
            wu = t("wuF", (128, DC, HID), BF16)
            wd = t("wdF", (128, HCN, D), BF16)
            xn1c = t("xn1F", (128, DC, TC), BF16)
            ldf = Sem(nc, "ldF")
            hbuf = t("hF", (128, HCN, TC), BF16)
            xat = t("xatF", (128, DC, TC), BF16)
            xac = t("xaF", (128, DC, TC))
            ps_g = [es.enter_context(nc.psum_tensor("psF0", [128, TC], F32)),
                    es.enter_context(nc.psum_tensor("psF1", [128, TC], F32))]
            ps_u = [es.enter_context(nc.psum_tensor("psFu0", [128, TC], F32)),
                    es.enter_context(nc.psum_tensor("psFu1", [128, TC], F32))]
            ps_d = [es.enter_context(nc.psum_tensor("psFd0", [128, TC], F32)),
                    es.enter_context(nc.psum_tensor("psFd1", [128, TC], F32))]
            p.dgp.inc(nc.gpsimd.dma_start(
                wg[:], dt["wgT"].rearrange("(c p) m -> p c m", p=128)), 16)
            p.dgp.inc(nc.gpsimd.dma_start(
                wu[:], dt["wuT"].rearrange("(c p) m -> p c m", p=128)), 16)
            p.dgp.inc(nc.gpsimd.dma_start(
                wd[:], dt["wdT"].rearrange("(c p) m -> p c m", p=128)), 16)
            p.dgp.wait(nc.tensor)
            ldx = Sem(nc, "ldxF")
            yst_mark = 0
            for i in range(NCH):
                if i == 0:
                    ldf.inc(nc.sync.dma_start(
                        xn1c[:], dt["xn1_sp"][:, 0:TC]
                        .rearrange("(c p) t -> p c t", p=128)), 16)
                    ldx.inc(nc.sync.dma_start(
                        xat[:], dt["xattn_sp"][:, 0:TC]
                        .rearrange("(c p) t -> p c t", p=128)), 16)
                    ldx.inc(nc.sync.dma_start(
                        xac[:], dt["x1_sp"][:, 0:TC]
                        .rearrange("(c p) t -> p c t", p=128)), 16)
                ldf.wait(nc.tensor)
                gfree = [p.act.n] * 2
                ufree = [p.dve.n] * 2
                for hc in range(HCN):
                    bk = hc % 2
                    nc.tensor.wait_ge(p.act.h, gfree[bk])
                    for kc in range(DC):
                        inst = nc.tensor.matmul(ps_g[bk][:], wg[:, kc, hc * 128:(hc + 1) * 128],
                                                xn1c[:, kc, :], start=(kc == 0), stop=(kc == DC - 1))
                    p.pe.inc(inst)
                    nc.tensor.wait_ge(p.dve.h, ufree[bk])
                    for kc in range(DC):
                        inst = nc.tensor.matmul(ps_u[bk][:], wu[:, kc, hc * 128:(hc + 1) * 128],
                                                xn1c[:, kc, :], start=(kc == 0), stop=(kc == DC - 1))
                    p.pe.inc(inst)
                    p.pe.wait(nc.scalar, p.pe.n - 1)
                    gfree[bk] = p.act.inc(nc.scalar.activation(hbuf[:, hc, :], ps_g[bk][:],
                                                               AF.Silu, bias=0.0, scale=QDEQ))
                    p.pe.wait(nc.vector)
                    p.act.wait(nc.vector)
                    ufree[bk] = p.dve.inc(nc.vector.tensor_mul(hbuf[:, hc, :], hbuf[:, hc, :],
                                                               ps_u[bk][:]))
                if i + 1 < NCH:
                    nc.sync.wait_ge(p.pe.h, p.pe.n)
                    ldf.inc(nc.sync.dma_start(
                        xn1c[:], dt["xn1_sp"][:, (i + 1) * TC:(i + 2) * TC]
                        .rearrange("(c p) t -> p c t", p=128)), 16)
                ldx.wait(nc.vector)
                for c in range(DC):
                    p.dve.inc(nc.vector.tensor_tensor(xac[:, c, :], xac[:, c, :],
                                                      xat[:, c, :], ALU.add))
                p.dve.wait(nc.tensor)
                dfree = [p.dve.n] * 2
                for mc in range(DC):
                    bk = mc % 2
                    nc.tensor.wait_ge(p.dve.h, dfree[bk])
                    for hc in range(HCN):
                        inst = nc.tensor.matmul(ps_d[bk][:], wd[:, hc, mc * 128:(mc + 1) * 128],
                                                hbuf[:, hc, :], start=(hc == 0), stop=(hc == HCN - 1))
                    p.pe.inc(inst)
                    p.pe.wait(nc.vector)
                    dfree[bk] = p.dve.inc(nc.vector.scalar_tensor_tensor(
                        xac[:, mc, :], ps_d[bk][:], QDEQ * QDEQ, xac[:, mc, :],
                        ALU.mult, ALU.add))
                nc.sync.wait_ge(p.dve.h, p.dve.n)
                yst_mark = p.dou.inc(nc.sync.dma_start(
                    dt["y"][:, i * TC:(i + 1) * TC].rearrange("(c p) t -> p c t", p=128),
                    xac[:]), 16)
                if i + 1 < NCH:
                    nc.sync.wait_ge(p.dou.h, yst_mark)
                    ldx.inc(nc.sync.dma_start(
                        xat[:], dt["xattn_sp"][:, (i + 1) * TC:(i + 2) * TC]
                        .rearrange("(c p) t -> p c t", p=128)), 16)
                    ldx.inc(nc.sync.dma_start(
                        xac[:], dt["x1_sp"][:, (i + 1) * TC:(i + 2) * TC]
                        .rearrange("(c p) t -> p c t", p=128)), 16)
            nc.sync.wait_ge(p.dou.h, p.dou.n)
        p.bar()
    nc.sync.wait_ge(p.dou.h, p.dou.n)



def _cols(v, nch):
    v = np.ascontiguousarray(np.asarray(v, dtype=np.float32))
    return np.ascontiguousarray(v.reshape(nch, 128).T)


def _t(a):
    return np.ascontiguousarray(np.asarray(a, dtype=np.float32).T)


def _sel33():
    s = np.zeros((33, 128), np.float32)
    s[0, 0:64] = 1.0
    s[32, 64:128] = 1.0
    return s


def prepare_inputs(inputs):
    f = lambda k: np.asarray(inputs[k])
    shared = {
        "qwT": _t(f("L0_attn_qw")), "kwT": _t(f("L0_attn_kw")),
        "vwT": _t(f("L0_attn_vw")), "owT": _t(f("L0_attn_ow")),
        "lqwT": _t(f("L0_attn_lqw")), "lkwT": _t(f("L0_attn_lkw")),
        "lvwT": _t(f("L0_attn_lvw")),
        "qb_c": _cols(f("L0_attn_qb"), 8), "vb_c": _cols(f("L0_attn_vb"), 8),
        "ob_c": _cols(f("L0_attn_ob"), 8),
        "lna_c": _cols(f("L0_attn_lna_w"), 8), "lnb_c": _cols(f("L0_attn_lnb_w"), 8),
        "lnc2": np.ascontiguousarray(np.tile(np.asarray(f("L0_attn_lnc_w"), np.float32), 2)[:, None]),
        "lnd2": np.ascontiguousarray(np.tile(np.asarray(f("L0_attn_lnd_w"), np.float32), 2)[:, None]),
        "lqb2": np.ascontiguousarray(np.tile(np.asarray(f("L0_attn_lqb"), np.float32), 2)[:, None]),
        "lkb2": np.ascontiguousarray(np.tile(np.asarray(f("L0_attn_lkb"), np.float32), 2)[:, None]),
        "lvb2": np.ascontiguousarray(np.tile(np.asarray(f("L0_attn_lvb"), np.float32), 2)[:, None]),
        "cnw1": np.ascontiguousarray(np.asarray(f("L0_ffn_cn_w1"), np.float32)),
        "cnw2": np.ascontiguousarray(np.asarray(f("L0_ffn_cn_w2"), np.float32)),
        "cnw3": np.ascontiguousarray(np.asarray(f("L0_ffn_cn_w3"), np.float32)),
        "cnw4": np.ascontiguousarray(np.asarray(f("L0_ffn_cn_w4"), np.float32)),
        "gcw1": np.ascontiguousarray(np.asarray(f("L0_ffn_gc_w1"), np.float32)),
        "gcw2": np.ascontiguousarray(np.asarray(f("L0_ffn_gc_w2"), np.float32)),
        "gcw3": np.ascontiguousarray(np.asarray(f("L0_ffn_gc_w3"), np.float32)),
        "cnb1_c": _cols(f("L0_ffn_cn_b1"), 8), "cnb2_c": _cols(f("L0_ffn_cn_b2"), 4),
        "cnb3_c": _cols(f("L0_ffn_cn_b3"), 2),
        "cnb4": np.asarray(f("L0_ffn_cn_b4"), np.float32).reshape(1, 1),
        "gcb1_c": _cols(f("L0_ffn_gc_b1"), 4), "gcb2_c": _cols(f("L0_ffn_gc_b2"), 2),
        "gcb3": np.asarray(f("L0_ffn_gc_b3"), np.float32).reshape(1, 1),
        "ar": np.asarray(f("L0_ffn_adaptive_range"), np.float32).reshape(1, 1),
        "wqkv": np.ascontiguousarray(np.asarray(f("L1_attn_W_qkv"), np.float32)),
        "bq_c": _cols(np.asarray(f("L1_attn_b_qkv"), np.float32)[0:D], 8),
        "bk_c": _cols(np.asarray(f("L1_attn_b_qkv"), np.float32)[D:2 * D], 8),
        "bv_row": np.ascontiguousarray(np.asarray(f("L1_attn_b_qkv"), np.float32)[2 * D:3 * D].reshape(1, D)),
        "wproj": np.ascontiguousarray(np.asarray(f("L1_attn_W_proj"), np.float32)),
        "bp_c": _cols(f("L1_attn_b_proj"), 8),
        "wgT": np.ascontiguousarray(np.asarray(f("L1_ffn_w_gate"), np.int32).T),
        "wuT": np.ascontiguousarray(np.asarray(f("L1_ffn_w_up"), np.int32).T),
        "wdT": np.ascontiguousarray(np.asarray(f("L1_ffn_w_down"), np.int32).T),
        "ident": np.eye(128, dtype=BF16NP),
        "lneg": (-60.0 * np.triu(np.ones((128, 128), np.float32))).astype(BF16NP),
        "rbig": (np.arange(128)[:, None] >= np.arange(896)[None, :] - 383)
                .astype(np.float32).astype(BF16NP),
        "mask16": np.kron(np.eye(8, dtype=np.float32),
                          np.ones((16, 16), np.float32)).astype(BF16NP),
        "sel2r": np.ascontiguousarray(np.concatenate(
            [np.concatenate([np.ones(64), np.zeros(64)]).reshape(1, 128),
             np.concatenate([np.zeros(64), np.ones(64)]).reshape(1, 128)]).astype(np.float32)),
        "sel33": _sel33(),
        "sel2c": np.ascontiguousarray(np.concatenate(
            [np.concatenate([np.ones(64), np.zeros(64)]).reshape(1, 128),
             np.concatenate([np.zeros(64), np.ones(64)]).reshape(1, 128)]).astype(np.float32).T
        ).astype(BF16NP),
    }
    x = np.asarray(inputs["L0_attn_x"], dtype=np.float32)
    x16 = x[:, 0:16, :]
    in_maps = []
    for c in range(B):
        m = dict(shared)
        m["x_fm"] = np.ascontiguousarray(x[c].T)
        x16r = np.roll(x16, -c, axis=0).reshape(128, D)
        m["x16_fm"] = np.ascontiguousarray(x16r.T)
        in_maps.append(m)
    return in_maps


def kernel(**inputs):
    in_maps = prepare_inputs(inputs)
    nc, _ = build_program()
    res = run_bass_kernel_spmd(nc, in_maps, core_ids=list(range(B)))
    out = np.stack([np.asarray(r["y"], dtype=np.float32).T for r in res.results])
    return out


# version 17
# speedup vs baseline: 1.3046x; 1.0224x over previous
import numpy as np
import ml_dtypes
import concourse.bass as bass
import concourse.mybir as mybir
from concourse.bass_utils import run_bass_kernel_spmd
from contextlib import ExitStack

F32 = mybir.dt.float32
BF16 = mybir.dt.bfloat16
I32 = mybir.dt.int32
AF = mybir.ActivationFunctionType
ALU = mybir.AluOpType
AX = mybir.AxisListType

B = 8
S = 4096
D = 1024
NH = 16
DH = 64
DC = D // 128
TC = 512
NCH = S // TC
NKT = S // 128
HID = 3072
HCN = HID // 128
EPS = 1e-5
QSCALE = 0.125
QDEQ = 1.0 / 127.0
DIFF_SCALE = 1.0 / (8 * 16 * 4096 * 64)
THRESH = 0.01
FACTOR = 0.1

BF16NP = ml_dtypes.bfloat16


def set_shape(s=4096, tc=512):
    global S, TC, NCH, NKT, DIFF_SCALE
    S, TC = s, tc
    NCH = S // TC
    NKT = S // 128
    DIFF_SCALE = 1.0 / (8 * 16 * S * 64)


class Sem:
    def __init__(self, nc, name):
        self.h = nc.alloc_semaphore(name)
        self.n = 0

    def inc(self, inst, k=1):
        inst.then_inc(self.h, k)
        self.n += k
        return self.n

    def wait(self, eng, val=None):
        eng.wait_ge(self.h, self.n if val is None else val)


class P:

    def __init__(self, nc):
        self.nc = nc
        self.pe = Sem(nc, "s_pe")
        self.act = Sem(nc, "s_act")
        self.dve = Sem(nc, "s_dve")
        self.pool = Sem(nc, "s_pool")
        self.dsy = Sem(nc, "d_sync")
        self.dsc = Sem(nc, "d_scal")
        self.dgp = Sem(nc, "d_gp")
        self.dou = Sem(nc, "d_out")

    def bar(self):
        self.nc.all_engine_barrier()



def ln_rows(p, stats_ps, n, ncols, inv_n, rows):
    nc = p.nc
    rstd = rows[:, 0, :]
    brow = rows[:, 1, :]
    mean = rows[:, 2, :]
    m2 = rows[:, 3, :]
    lnt = rows[:, 4, :]
    p.dve.inc(nc.vector.tensor_scalar_mul(mean, stats_ps[0:n, :], inv_n))
    p.dve.inc(nc.vector.tensor_scalar_mul(rstd, stats_ps[32:32 + n, :], inv_n))
    psum_free = p.dve.n
    p.dve.wait(nc.vector)
    p.dve.inc(nc.vector.tensor_mul(m2, mean, mean))
    p.dve.wait(nc.vector)
    p.dve.inc(nc.vector.scalar_tensor_tensor(rstd, rstd, EPS, m2, ALU.add, ALU.subtract))
    p.dve.wait(nc.scalar)
    p.act.inc(nc.scalar.activation(lnt, rstd, AF.Ln, bias=0.0, scale=1.0))
    p.act.wait(nc.scalar)
    p.act.inc(nc.scalar.activation(rstd, lnt, AF.Exp, bias=0.0, scale=-0.5))
    p.act.wait(nc.vector)
    p.dve.inc(nc.vector.scalar_tensor_tensor(brow, mean, -1.0, rstd, ALU.mult, ALU.mult))
    return psum_free


def stat_sums(p, stats_ps, xb, xsqb, chunks, ncols, ones_col):
    nc = p.nc
    for c in range(chunks):
        nc.tensor.matmul(stats_ps[0:1, :], ones_col, xb[:, c, :],
                         start=(c == 0), stop=(c == chunks - 1),
                         tile_position=(0, 0))
    for c in range(chunks):
        inst = nc.tensor.matmul(stats_ps[32:33, :], ones_col, xsqb[:, c, :],
                                start=(c == 0), stop=(c == chunks - 1),
                                tile_position=(0, 32))
    p.pe.inc(inst)



def build_program(phases=("pre", "t16", "a", "gf", "b", "c", "attn", "ffn"),
                  debug=False):
    nc = bass.Bass("TRN2", target_bir_lowering=False)
    p = P(nc)
    dt = {}

    def din(name, shape, dtyp=F32):
        dt[name] = nc.dram_tensor(name, list(shape), dtyp, kind="ExternalInput").ap()
        return dt[name]

    def dint(name, shape, dtyp=F32):
        dt[name] = nc.dram_tensor(name, list(shape), dtyp, kind="Internal").ap()
        return dt[name]

    def dout(name, shape, dtyp=F32):
        dt[name] = nc.dram_tensor(name, list(shape), dtyp, kind="ExternalOutput").ap()
        return dt[name]

    din("x_fm", (D, S))
    din("x16_fm", (D, 128))
    din("qwT", (D, D)); din("kwT", (D, D)); din("vwT", (D, D)); din("owT", (D, D))
    din("lqwT", (DH, DH)); din("lkwT", (DH, DH)); din("lvwT", (DH, DH))
    din("qb_c", (128, DC)); din("vb_c", (128, DC)); din("ob_c", (128, DC))
    din("lna_c", (128, DC)); din("lnb_c", (128, DC))
    din("lnc2", (128, 1)); din("lnd2", (128, 1))
    din("lqb2", (128, 1)); din("lkb2", (128, 1)); din("lvb2", (128, 1))
    din("cnw1", (D, D)); din("cnw2", (D, 512)); din("cnw3", (512, 256)); din("cnw4", (256, 1))
    din("gcw1", (D, 512)); din("gcw2", (512, 256)); din("gcw3", (256, 1))
    din("cnb1_c", (128, 8)); din("cnb2_c", (128, 4)); din("cnb3_c", (128, 2)); din("cnb4", (1, 1))
    din("gcb1_c", (128, 4)); din("gcb2_c", (128, 2)); din("gcb3", (1, 1))
    din("ar", (1, 1))
    din("wqkv", (D, 3 * D)); din("bq_c", (128, DC)); din("bk_c", (128, DC)); din("bv_row", (1, D))
    din("wproj", (D, D)); din("bp_c", (128, DC))
    din("wgT", (D, HID), I32); din("wuT", (D, HID), I32); din("wdT", (HID, D), I32)
    din("ident", (128, 128), BF16)
    din("mask16", (128, 128), BF16)
    din("lneg", (128, 128), BF16)
    din("rbig", (128, 896), BF16)
    din("sel2r", (2, 128))
    din("sel33", (33, 128))
    din("sel2c", (128, 2), BF16)

    dsc_kind = dout if debug else dint
    dsc_kind("xn_sp", (D, S))
    dsc_kind("x1_sp", (D, S))
    dsc_kind("xattn_sp", (D, S), BF16)
    dsc_kind("xn1_sp", (D, S), BF16)
    dout("y", (D, S))
    if debug:
        for nm, shp, dty in (("dbg_q16", (128, DC, 128), F32), ("dbg_k16", (128, DC, 128), BF16),
                             ("dbg_v16", (128, DC, 128), BF16), ("dbg_kn16", (128, DC, 128), BF16),
                             ("dbg_vi", (128, NH * DH), BF16), ("dbg_attn", (128, DC, 128), F32),
                             ("dbg_y16", (128, DC, 16), F32), ("dbg_xn16", (128, DC, 128), F32),
                             ("dbg_x1pre", (128, DC, 16), F32)):
            dout(nm, shp, dty)

    es_glob = ExitStack()

    sb = {}

    def asb(name, shape, dtyp):
        h = es_glob.enter_context(nc.sbuf_tensor(name, list(shape), dtyp))
        sb[name] = h
        return h

    ones_col_bf = asb("ones_col_bf", (128, 1), BF16)
    ones_row_f = asb("ones_row_f", (1, 128), F32)
    ones_row_bf = asb("ones_row_bf", (1, 128), BF16)
    ident = asb("ident_sb", (128, 128), BF16)
    mask16 = asb("mask16_sb", (128, 128), BF16)
    lneg = asb("lneg_sb", (128, 128), BF16)
    rbig = asb("rbig_sb", (128, 896), BF16)
    sel2r = asb("sel2r_sb", (2, 128), F32)
    sel33 = asb("sel33_sb", (33, 128), F32)
    sel2c = asb("sel2c_sb", (128, 2), BF16)
    cols = {}
    for nm, w in (("qb_c", DC), ("vb_c", DC), ("ob_c", DC), ("lna_c", DC),
                  ("lnb_c", DC), ("bq_c", DC), ("bk_c", DC), ("bp_c", DC),
                  ("cnb1_c", 8), ("cnb2_c", 4), ("cnb3_c", 2),
                  ("gcb1_c", 4), ("gcb2_c", 2)):
        cols[nm] = asb(nm + "_sb", (128, w), F32)
    for nm in ("lnc2", "lnd2", "lqb2", "lkb2", "lvb2"):
        cols[nm] = asb(nm + "_sb", (128, 1), F32)
    for nm in ("cnb4", "gcb3", "ar"):
        cols[nm] = asb(nm + "_sb", (1, 1), F32)
    bq8 = asb("bq8", (128, DC), F32)
    gacc = asb("gacc", (128, DC, NCH), F32)
    gctx_bf = asb("gctx_bf", (128, DC), BF16)
    c1p = asb("c1p", (1, 1), F32)
    c2p = asb("c2p", (1, 1), F32)
    y16 = asb("y16", (128, DC, 16), F32)

    nc.vector.memset(ones_col_bf[:], 1.0)
    nc.vector.memset(ones_row_f[:], 1.0)
    p.dve.inc(nc.vector.memset(ones_row_bf[:], 1.0))
    for nm in ("ident", "mask16", "lneg", "rbig"):
        p.dsy.inc(nc.sync.dma_start(sb[nm + "_sb"][:], dt[nm][:]), 16)
    p.dsy.inc(nc.sync.dma_start(sel2r[:], dt["sel2r"][:]), 16)
    p.dsy.inc(nc.sync.dma_start(sel33[:], dt["sel33"][:]), 16)
    p.dsy.inc(nc.sync.dma_start(sel2c[:], dt["sel2c"][:]), 16)
    for nm in cols:
        p.dsy.inc(nc.sync.dma_start(cols[nm][:], dt[nm][:]), 16)
    p.dsy.wait(nc.vector)
    p.dve.inc(nc.vector.tensor_scalar_mul(bq8[:], cols["bq_c"][:], QSCALE))
    p.bar()

    if "t16" in phases:
        with ExitStack() as es:
            t = lambda n, s, d=F32: es.enter_context(nc.sbuf_tensor(n, list(s), d))
            x16 = t("x16", (128, DC, 128))
            x16b = t("x16b", (128, DC, 128), BF16)
            x16sq = t("x16sq", (128, DC, 128), BF16)
            xn16 = t("xn16", (128, DC, 128))
            a16 = t("a16", (128, DC, 128), BF16)
            b16 = t("b16", (128, DC, 128), BF16)
            wq = t("wq16", (128, DC, D), BF16)
            wk = t("wk16", (128, DC, D), BF16)
            wv = t("wv16", (128, DC, D), BF16)
            wo = t("wo16", (128, DC, D), BF16)
            lbd = {}
            for nm in ("lqwT", "lkwT", "lvwT"):
                lbd[nm] = t(nm + "_bd", (128, 128), BF16)
            rows2 = t("rows2", (1, 5, 128))
            rows4 = t("rows4", (2, 5, 512))
            q16 = t("q16", (128, DC, 128))
            k16 = t("k16", (128, DC, 128), BF16)
            v16 = t("v16", (128, DC, 128), BF16)
            kn16 = t("kn16", (128, DC, 128), BF16)
            vi_tok = t("vi_tok", (128, NH * DH), BF16)
            kk32 = t("kk32", (128, 512))
            kkb = t("kkb", (128, 512), BF16)
            kksq = t("kksq", (128, 512), BF16)
            qcur = [t("qcurA", (128, DC, 128)), t("qcurB", (128, DC, 128))]
            qcb = t("qcb", (128, DC, 128), BF16)
            attn_o = [t("attnA", (128, DC, 128)), t("attnB", (128, DC, 128))]
            prev = [t("prevA", (128, DC, 128)), t("prevB", (128, DC, 128))]
            iout = t("iout", (128, DC, 128))
            pn = t("pn", (128, 128), BF16)
            pt = t("pt", (128, 128), BF16)
            rowred = t("rowred", (128, 1))
            rrecip = t("rrecip", (128, 1))
            dtmp = t("dtmp", (128, 512))
            dcol = t("dcol", (128, DC))
            dcol1 = t("dcol1", (128, 1))
            dcolb = t("dcolb", (128, 1), BF16)
            dscal = t("dscal", (1, 1))
            flt = t("flt", (1, 6))
            donec = t("donec", (128, 1))
            omc = t("omc", (128, 1))
            omnc = t("omnc", (128, 1))

            ps_a = es.enter_context(nc.psum_tensor("ps16a", [128, 512], F32))
            ps_b = es.enter_context(nc.psum_tensor("ps16b", [128, 512], F32))
            ps_st = es.enter_context(nc.psum_tensor("ps16st", [34, 512], F32))
            ps_bc = es.enter_context(nc.psum_tensor("ps16bc", [128, 512], F32))
            ps_bc2 = es.enter_context(nc.psum_tensor("ps16bc2", [128, 512], F32))
            ps_t = es.enter_context(nc.psum_tensor("ps16t", [128, 128], BF16))
            ps_s = es.enter_context(nc.psum_tensor("ps16s", [1, 1], F32))
            ps_c = es.enter_context(nc.psum_tensor("ps16c", [128, 1], F32))

            p.dsy.inc(nc.sync.dma_start(
                x16[:], dt["x16_fm"].rearrange("(c p) t -> p c t", p=128)), 16)
            for wsb, wnm in ((wq, "qwT"), (wk, "kwT"), (wv, "vwT"), (wo, "owT")):
                p.dgp.inc(nc.gpsimd.dma_start(
                    wsb[:], dt[wnm].rearrange("(c p) m -> p c m", p=128)), 16)
            for nm in ("lqwT", "lkwT", "lvwT"):
                p.dve.inc(nc.vector.memset(lbd[nm][0:64, 64:128], 0.0))
                p.dve.inc(nc.vector.memset(lbd[nm][64:128, 0:64], 0.0))
                p.dgp.inc(nc.gpsimd.dma_start(lbd[nm][0:64, 0:64], dt[nm][:]), 16)
                p.dgp.inc(nc.gpsimd.dma_start(lbd[nm][64:128, 64:128], dt[nm][:]), 16)

            p.dsy.wait(nc.vector)
            p.dsy.wait(nc.scalar)
            p.dve.inc(nc.vector.tensor_copy(x16b[:], x16[:]))
            p.act.inc(nc.scalar.activation(x16sq[:], x16[:], AF.Square))
            p.dve.wait(nc.tensor)
            p.act.wait(nc.tensor)
            stat_sums(p, ps_st[:, 0:128], x16b, x16sq, DC, 128, ones_col_bf[:])
            p.pe.wait(nc.vector)
            p.pe.wait(nc.scalar)
            ln_rows(p, ps_st[:, 0:128], 1, 128, 1.0 / D, rows2)
            p.dve.wait(nc.tensor)
            p.act.wait(nc.tensor)
            nc.tensor.matmul(ps_bc[:, 0:128], ones_row_f[:], rows2[:, 0, :], start=True, stop=True)
            p.pe.inc(nc.tensor.matmul(ps_bc2[:, 0:128], ones_row_f[:], rows2[:, 1, :],
                                      start=True, stop=True))
            p.pe.wait(nc.vector)
            for c in range(DC):
                p.dve.inc(nc.vector.tensor_mul(iout[:, c, :], x16[:, c, :], ps_bc[:, 0:128]))
            p.dve.wait(nc.vector)
            for c in range(DC):
                p.dve.inc(nc.vector.tensor_add(xn16[:, c, :], iout[:, c, :], ps_bc2[:, 0:128]))
            p.dve.wait(nc.scalar)
            for c in range(DC):
                p.act.inc(nc.scalar.activation(a16[:, c, :], xn16[:, c, :], AF.Identity,
                                               bias=0.0, scale=cols["lna_c"][:, c:c + 1]))
                p.act.inc(nc.scalar.activation(b16[:, c, :], xn16[:, c, :], AF.Identity,
                                               bias=0.0, scale=cols["lnb_c"][:, c:c + 1]))

            p.act.wait(nc.tensor)
            p.dgp.wait(nc.tensor)
            def proj16(wsb, src, dst, bias_cols, dst_bf=None):
                for mc in range(DC):
                    ps = (ps_a, ps_b)[mc % 2][:, 0:128]
                    if dve_free[mc % 2]:
                        p.dve.wait(nc.tensor, dve_free[mc % 2])
                    for kc in range(DC):
                        inst = nc.tensor.matmul(ps, wsb[:, kc, mc * 128:(mc + 1) * 128],
                                                src[:, kc, :], start=(kc == 0), stop=(kc == DC - 1))
                    p.pe.inc(inst)
                    p.pe.wait(nc.vector)
                    if bias_cols is None:
                        i2 = nc.vector.tensor_copy(dst[:, mc, :], ps)
                    else:
                        i2 = nc.vector.tensor_scalar_add(dst[:, mc, :], ps, bias_cols[:, mc:mc + 1])
                    m = p.dve.inc(i2)
                    if dst_bf is not None:
                        p.dve.wait(nc.vector)
                        m = p.dve.inc(nc.vector.tensor_copy(dst_bf[:, mc, :], dst[:, mc, :]))
                    dve_free[mc % 2] = m
            dve_free = {0: 0, 1: 0}
            proj16(wq, a16, q16, cols["qb_c"])
            proj16(wk, b16, k16, None)
            proj16(wv, b16, v16, cols["vb_c"])

            p.dve.wait(nc.tensor)
            for g in range(DC // 4):
                h0 = 4 * g
                k4 = k16[:, h0:h0 + 4, :].rearrange("p a b -> p (a b)")
                v4 = v16[:, h0:h0 + 4, :].rearrange("p a b -> p (a b)")
                kn4 = kn16[:, h0:h0 + 4, :].rearrange("p a b -> p (a b)")
                p.pe.inc(nc.tensor.matmul(ps_a[:], lbd["lkwT"][:], k4,
                                          start=True, stop=True))
                p.pe.wait(nc.vector)
                p.pe.wait(nc.scalar)
                p.dve.inc(nc.vector.tensor_scalar_add(kk32[:], ps_a[:], cols["lkb2"][:]))
                p.dve.wait(nc.vector)
                p.dve.wait(nc.scalar)
                p.dve.inc(nc.vector.tensor_copy(kkb[:], kk32[:]))
                p.act.inc(nc.scalar.activation(kksq[:], kk32[:], AF.Square))
                p.dve.wait(nc.tensor)
                p.act.wait(nc.tensor)
                nc.tensor.matmul(ps_st[0:2, :], sel2c[:, 0:2], kkb[:],
                                 start=True, stop=True, tile_position=(0, 0))
                inst = nc.tensor.matmul(ps_st[32:34, :], sel2c[:, 0:2], kksq[:],
                                        start=True, stop=True, tile_position=(0, 32))
                p.pe.inc(inst)
                p.pe.wait(nc.vector)
                p.pe.wait(nc.scalar)
                ln_rows(p, ps_st, 2, 512, 1.0 / DH, rows4)
                p.dve.wait(nc.tensor)
                p.act.wait(nc.tensor)
                nc.tensor.matmul(ps_bc[:], sel2r[:], rows4[:, 0, :], start=True, stop=True)
                p.pe.inc(nc.tensor.matmul(ps_bc2[:], sel2r[:], rows4[:, 1, :], start=True, stop=True))
                p.pe.wait(nc.vector)
                p.dve.inc(nc.vector.tensor_mul(dtmp[:], kk32[:], ps_bc[:]))
                p.dve.wait(nc.vector)
                p.dve.inc(nc.vector.tensor_add(dtmp[:], dtmp[:], ps_bc2[:]))
                p.dve.wait(nc.scalar)
                p.act.inc(nc.scalar.activation(kn4, dtmp[:], AF.Identity,
                                               bias=0.0, scale=cols["lnd2"][:]))
                p.act.wait(nc.tensor)
                p.pe.inc(nc.tensor.matmul(ps_b[:], lbd["lvwT"][:], v4,
                                          start=True, stop=True))
                p.pe.wait(nc.vector)
                p.dve.inc(nc.vector.tensor_scalar_add(kkb[:], ps_b[:], cols["lvb2"][:]))
                p.dve.wait(nc.tensor)
                for q4 in range(4):
                    hp = h0 + q4
                    p.pe.inc(nc.tensor.transpose(ps_t[:], kkb[:, q4 * 128:(q4 + 1) * 128],
                                                 ident[:]))
                    p.pe.wait(nc.vector)
                    p.dve.inc(nc.vector.tensor_copy(
                        vi_tok[:, hp * 128:(hp + 1) * 128], ps_t[:]))
                    p.dve.wait(nc.tensor)

            for c in range(DC):
                p.dve.inc(nc.vector.tensor_copy(qcur[0][:, c, :], q16[:, c, :]))
            p.dve.wait(nc.vector)

            cur, nxt = 0, 1
            for it in range(3):
                p.dve.wait(nc.vector)
                for c in range(DC):
                    p.dve.inc(nc.vector.tensor_copy(qcb[:, c, :], qcur[cur][:, c, :]))
                p.dve.wait(nc.tensor)
                for g in range(DC // 4):
                    h0 = 4 * g
                    q4 = qcb[:, h0:h0 + 4, :].rearrange("p a b -> p (a b)")
                    p.pe.inc(nc.tensor.matmul(ps_a[:], lbd["lqwT"][:], q4,
                                              start=True, stop=True))
                    p.pe.wait(nc.vector)
                    p.pe.wait(nc.scalar)
                    p.dve.inc(nc.vector.tensor_scalar_add(kk32[:], ps_a[:], cols["lqb2"][:]))
                    p.dve.wait(nc.vector)
                    p.dve.wait(nc.scalar)
                    p.dve.inc(nc.vector.tensor_copy(kkb[:], kk32[:]))
                    p.act.inc(nc.scalar.activation(kksq[:], kk32[:], AF.Square))
                    p.dve.wait(nc.tensor)
                    p.act.wait(nc.tensor)
                    nc.tensor.matmul(ps_st[0:2, :], sel2c[:, 0:2], kkb[:],
                                     start=True, stop=True, tile_position=(0, 0))
                    inst = nc.tensor.matmul(ps_st[32:34, :], sel2c[:, 0:2], kksq[:],
                                            start=True, stop=True, tile_position=(0, 32))
                    p.pe.inc(inst)
                    p.pe.wait(nc.vector)
                    p.pe.wait(nc.scalar)
                    ln_rows(p, ps_st, 2, 512, 1.0 / DH, rows4)
                    p.dve.wait(nc.tensor)
                    p.act.wait(nc.tensor)
                    nc.tensor.matmul(ps_bc[:], sel2r[:], rows4[:, 0, :], start=True, stop=True)
                    p.pe.inc(nc.tensor.matmul(ps_bc2[:], sel2r[:], rows4[:, 1, :],
                                              start=True, stop=True))
                    p.pe.wait(nc.vector)
                    p.dve.inc(nc.vector.tensor_mul(dtmp[:], kk32[:], ps_bc[:]))
                    p.dve.wait(nc.vector)
                    p.dve.inc(nc.vector.tensor_add(dtmp[:], dtmp[:], ps_bc2[:]))
                    p.dve.wait(nc.scalar)
                    p.pe.wait(nc.scalar)
                    p.act.inc(nc.scalar.activation(q4, dtmp[:], AF.Identity,
                                                   bias=0.0, scale=cols["lnc2"][:]))
                p.act.wait(nc.tensor)
                for h in range(NH):
                    hp, base = h // 2, (h % 2) * 64
                    qn_h = qcb[base:base + 64, hp, :]
                    kn_h = kn16[base:base + 64, hp, :]
                    p.pe.inc(nc.tensor.matmul(ps_a[:, 0:128], qn_h, kn_h, start=True, stop=True))
                    p.pe.wait(nc.scalar)
                    p.act.inc(nc.scalar.activation(pn[:], ps_a[:, 0:128], AF.Exp, bias=0.0,
                                                   scale=QSCALE))
                    p.act.wait(nc.vector)
                    p.dve.inc(nc.vector.tensor_mul(pn[:], pn[:], mask16[:]))
                    p.dve.wait(nc.vector)
                    p.dve.inc(nc.vector.reduce_sum(rowred[:], pn[:], AX.X))
                    p.dve.wait(nc.vector)
                    p.dve.inc(nc.vector.reciprocal(rrecip[:], rowred[:]))
                    p.dve.wait(nc.vector)
                    p.dve.inc(nc.vector.tensor_scalar_mul(pn[:], pn[:], rrecip[:]))
                    p.dve.wait(nc.tensor)
                    p.pe.inc(nc.tensor.transpose(ps_t[:], pn[:], ident[:]))
                    p.pe.wait(nc.vector)
                    p.dve.inc(nc.vector.tensor_copy(pt[:], ps_t[:]))
                    p.dve.wait(nc.tensor)
                    p.pe.inc(nc.tensor.matmul(ps_b[0:64, 0:128], vi_tok[:, h * 64:(h + 1) * 64],
                                              pt[:], start=True, stop=True))
                    p.pe.wait(nc.vector)
                    p.dve.inc(nc.vector.tensor_copy(iout[base:base + 64, hp, :],
                                                    ps_b[0:64, 0:128]))
                p.dve.wait(nc.vector)
                if it == 0:
                    for c in range(DC):
                        p.dve.inc(nc.vector.tensor_copy(attn_o[nxt][:, c, :], iout[:, c, :]))
                        p.dve.inc(nc.vector.tensor_copy(prev[nxt][:, c, :], iout[:, c, :]))
                        p.dve.inc(nc.vector.tensor_add(qcur[nxt][:, c, :], qcur[cur][:, c, :],
                                                       iout[:, c, :]))
                else:
                    for c in range(DC):
                        p.dve.inc(nc.vector.tensor_sub(dtmp[:, 0:128], iout[:, c, :], prev[cur][:, c, :]))
                        p.dve.wait(nc.vector)
                        p.dve.inc(nc.vector.tensor_reduce(dcol[:, c:c + 1], dtmp[:, 0:128], AX.X,
                                                          ALU.add, apply_absolute_value=True))
                        p.dve.wait(nc.vector)
                    p.dve.inc(nc.vector.reduce_sum(dcol1[:], dcol[:], AX.X))
                    p.dve.wait(nc.vector)
                    p.dve.inc(nc.vector.tensor_copy(dcolb[:], dcol1[:]))
                    p.dve.wait(nc.tensor)
                    p.pe.inc(nc.tensor.matmul(ps_s[:], dcolb[:], ones_col_bf[:],
                                              start=True, stop=True))
                    p.pe.wait(nc.vector)
                    fl_diff, fl_t1, fl_lt, fl_brk, fl_done, fl_om = [flt[:, i:i + 1] for i in range(6)]
                    p.dve.inc(nc.vector.tensor_scalar_mul(fl_diff, ps_s[:], DIFF_SCALE))
                    p.dve.wait(nc.vector)
                    p.dve.inc(nc.vector.tensor_scalar(fl_t1, fl_diff, FACTOR, THRESH,
                                                      ALU.mult, ALU.add))
                    p.dve.wait(nc.vector)
                    p.dve.inc(nc.vector.tensor_tensor(fl_lt, fl_diff, fl_t1, ALU.is_lt))
                    p.dve.wait(nc.vector)
                    if it == 1:
                        p.dve.inc(nc.vector.tensor_copy(fl_brk, fl_lt))
                        p.dve.wait(nc.vector)
                        p.dve.inc(nc.vector.tensor_copy(fl_done, flt[:, 2:3]))
                        p.dve.wait(nc.vector)
                        for c in range(DC):
                            p.dve.inc(nc.vector.tensor_copy(attn_o[nxt][:, c, :], iout[:, c, :]))
                            p.dve.inc(nc.vector.tensor_copy(prev[nxt][:, c, :], iout[:, c, :]))
                        p.dve.wait(nc.vector)
                        p.dve.inc(nc.vector.tensor_scalar(fl_om, fl_brk, -1.0, 1.0,
                                                          ALU.mult, ALU.add))
                        p.dve.wait(nc.tensor)
                        p.pe.inc(nc.tensor.matmul(ps_c[:], ones_row_f[:], fl_om,
                                                  start=True, stop=True))
                        p.pe.wait(nc.vector)
                        p.dve.inc(nc.vector.tensor_copy(omnc[:], ps_c[:]))
                        p.dve.wait(nc.vector)
                        for c in range(DC):
                            p.dve.wait(nc.vector)
                            p.dve.inc(nc.vector.tensor_scalar_mul(dtmp[:, 0:128], iout[:, c, :], omnc[:]))
                            p.dve.wait(nc.vector)
                            p.dve.inc(nc.vector.tensor_add(qcur[nxt][:, c, :],
                                                           qcur[cur][:, c, :], dtmp[:, 0:128]))
                        p.dve.inc(nc.vector.tensor_copy(fl_done, fl_brk))
                        p.dve.wait(nc.vector)
                    else:
                        p.dve.inc(nc.vector.tensor_scalar(fl_om, fl_done, -1.0, 1.0,
                                                          ALU.mult, ALU.add))
                        p.dve.wait(nc.tensor)
                        nc.tensor.matmul(ps_c[:], ones_row_f[:], fl_done, start=True, stop=True)
                        p.pe.inc(nc.tensor.matmul(ps_bc[0:128, 0:1], ones_row_f[:], fl_om,
                                                  start=True, stop=True))
                        p.pe.wait(nc.vector)
                        p.dve.inc(nc.vector.tensor_copy(donec[:], ps_c[:]))
                        p.dve.inc(nc.vector.tensor_copy(omc[:], ps_bc[0:128, 0:1]))
                        p.dve.wait(nc.vector)
                        for c in range(DC):
                            p.dve.wait(nc.vector)
                            p.dve.inc(nc.vector.tensor_scalar_mul(dtmp[:, 0:128], iout[:, c, :], omc[:]))
                            p.dve.wait(nc.vector)
                            p.dve.inc(nc.vector.tensor_scalar_mul(iout[:, c, :],
                                                                  attn_o[cur][:, c, :], donec[:]))
                            p.dve.wait(nc.vector)
                            p.dve.inc(nc.vector.tensor_add(attn_o[nxt][:, c, :], dtmp[:, 0:128],
                                                           iout[:, c, :]))
                            p.dve.wait(nc.vector)
                cur, nxt = nxt, cur
                p.dve.wait(nc.tensor)
                p.dve.wait(nc.scalar)

            if "dbg_q16" in dt:
                nc.sync.wait_ge(p.dve.h, p.dve.n)
                nc.sync.wait_ge(p.act.h, p.act.n)
                for nm, sbh in (("dbg_q16", q16), ("dbg_k16", k16), ("dbg_v16", v16),
                                ("dbg_kn16", kn16), ("dbg_vi", vi_tok),
                                ("dbg_attn", attn_o[cur]), ("dbg_xn16", xn16)):
                    p.dou.inc(nc.sync.dma_start(dt[nm][:], sbh[:]), 16)
                nc.sync.wait_ge(p.dou.h, p.dou.n)
            for c in range(DC):
                p.dve.inc(nc.vector.tensor_copy(qcb[:, c, :], attn_o[cur][:, c, :]))
            p.dve.wait(nc.tensor)
            for mc in range(DC):
                for kc in range(DC):
                    inst = nc.tensor.matmul(ps_a[:, 0:128], wo[:, kc, mc * 128:(mc + 1) * 128],
                                            qcb[:, kc, :], start=(kc == 0), stop=(kc == DC - 1))
                p.pe.inc(inst)
                p.pe.wait(nc.vector)
                p.dve.inc(nc.vector.tensor_copy(y16[:, mc, :], ps_a[0:128, 0:16]))
                p.dve.wait(nc.tensor)
            if "dbg_y16" in dt:
                nc.sync.wait_ge(p.dve.h, p.dve.n)
                p.dou.inc(nc.sync.dma_start(dt["dbg_y16"][:], y16[:]), 16)
                nc.sync.wait_ge(p.dou.h, p.dou.n)
        p.bar()

    if "a" in phases:
        with ExitStack() as es:
            t = lambda n, s, d=F32: es.enter_context(nc.sbuf_tensor(n, list(s), d))
            xch = [t("xchA", (128, DC, TC)), t("xchB", (128, DC, TC))]
            dsx = [Sem(nc, "dsxA0"), Sem(nc, "dsxA1")]
            xb = t("xb", (128, DC, TC), BF16)
            xsq = t("xsq", (128, DC, TC), BF16)
            xn = [t("xnA0", (128, DC, TC)), t("xnA1", (128, DC, TC))]
            tmul = t("tmulA", (128, DC, TC))
            rows2 = t("rows2a", (1, 5, TC))
            ps_st = es.enter_context(nc.psum_tensor("psAst", [34, TC], F32))
            ps_r = es.enter_context(nc.psum_tensor("psAr", [128, TC], F32))
            ps_m = es.enter_context(nc.psum_tensor("psAm", [128, TC], F32))
            buf_free = [0, 0]
            buf_freea = [0, 0]
            st_free = 0
            stat_mark = 0
            bc_mark = 0
            addm = 0
            sp_done = [0, 0]
            stat_slot = [0, 0]

            def emit_load_A(ch):
                bi = ch % 2
                nc.sync.wait_ge(p.dve.h, buf_free[bi])
                nc.sync.wait_ge(p.act.h, buf_freea[bi])
                dsx[bi].inc(nc.sync.dma_start(
                    xch[bi][:], dt["x_fm"][:, ch * TC:(ch + 1) * TC]
                    .rearrange("(c p) t -> p c t", p=128)), 16)

            def emit_stats_A(i):
                nonlocal stat_mark, st_free
                bi = i % 2
                dsx[bi].wait(nc.vector)
                dsx[bi].wait(nc.scalar)
                nc.vector.wait_ge(p.pe.h, stat_mark)
                nc.scalar.wait_ge(p.pe.h, stat_mark)
                p.dve.inc(nc.vector.tensor_copy(xb[:], xch[bi][:]))
                buf_freea[bi] = p.act.inc(nc.scalar.activation(xsq[:], xch[bi][:], AF.Square))
                p.dve.wait(nc.tensor)
                p.act.wait(nc.tensor)
                nc.tensor.wait_ge(p.dve.h, st_free)
                stat_sums(p, ps_st, xb, xsq, DC, TC, ones_col_bf[:])
                stat_mark = p.pe.n

            emit_load_A(0)
            emit_load_A(1)
            emit_stats_A(0)
            for i in range(NCH):
                bi = i % 2
                si = i % 2
                nc.vector.wait_ge(p.pe.h, bc_mark)
                nc.scalar.wait_ge(p.pe.h, bc_mark)
                nc.vector.wait_ge(p.pe.h, stat_mark)
                nc.scalar.wait_ge(p.pe.h, stat_mark)
                st_free = ln_rows(p, ps_st, 1, TC, 1.0 / D, rows2)
                st_free = p.dve.n
                p.dve.wait(nc.tensor)
                p.act.wait(nc.tensor)
                nc.tensor.wait_ge(p.dve.h, addm)
                nc.tensor.matmul(ps_r[:], ones_row_f[:], rows2[:, 0, :], start=True, stop=True)
                p.pe.inc(nc.tensor.matmul(ps_m[:], ones_row_f[:], rows2[:, 1, :],
                                          start=True, stop=True))
                bc_mark = p.pe.n
                if i + 1 < NCH:
                    emit_stats_A(i + 1)
                nc.vector.wait_ge(p.pe.h, bc_mark)
                for c in range(DC):
                    p.dve.inc(nc.vector.tensor_mul(tmul[:, c, :], xch[bi][:, c, :], ps_r[:]))
                buf_free[bi] = p.dve.n
                if i + 2 < NCH:
                    emit_load_A(i + 2)
                p.dve.wait(nc.vector)
                nc.vector.wait_ge(p.dou.h, sp_done[si])
                for c in range(DC):
                    p.dve.inc(nc.vector.tensor_add(xn[si][:, c, :], tmul[:, c, :], ps_m[:]))
                addm = p.dve.n
                p.dve.wait(nc.vector)
                for c in range(DC):
                    p.dve.inc(nc.vector.reduce_sum(gacc[:, c, i:i + 1], xn[si][:, c, :], AX.X))
                nc.sync.wait_ge(p.dve.h, p.dve.n)
                sp_done[si] = p.dou.inc(nc.sync.dma_start(
                    dt["xn_sp"][:, i * TC:(i + 1) * TC].rearrange("(c p) t -> p c t", p=128),
                    xn[si][:]), 16)
            nc.sync.wait_ge(p.dou.h, p.dou.n)
        p.bar()

    if "gf" in phases:
        with ExitStack() as es:
            t = lambda n, s, d=F32: es.enter_context(nc.sbuf_tensor(n, list(s), d))
            gsum = t("gsum", (128, DC))
            h1g = t("h1g", (128, 4), BF16)
            h2g = t("h2g", (128, 2), BF16)
            w1 = t("gw1", (128, DC, 512), BF16)
            w2 = t("gw2", (128, 4, 256), BF16)
            w3 = t("gw3", (128, 2, 1), BF16)
            fl = t("gfl", (1, 4))
            ps_g = es.enter_context(nc.psum_tensor("psG", [128, 512], F32))
            ps_g2 = es.enter_context(nc.psum_tensor("psG2", [1, 1], F32))
            p.dgp.inc(nc.gpsimd.dma_start(w1[:], dt["gcw1"].rearrange("(c p) m -> p c m", p=128)), 16)
            p.dgp.inc(nc.gpsimd.dma_start(w2[:], dt["gcw2"].rearrange("(c p) m -> p c m", p=128)), 16)
            with nc.allow_non_contiguous_dma(reason="tiny [256,1] weight"):
                p.dgp.inc(nc.gpsimd.dma_start(w3[:], dt["gcw3"].rearrange("(c p) m -> p c m", p=128)), 16)
            p.dve.inc(nc.vector.tensor_reduce(gsum[:], gacc[:], AX.X, ALU.add))
            p.dve.wait(nc.vector)
            p.dve.inc(nc.vector.tensor_scalar_mul(gctx_bf[:], gsum[:], 1.0 / S))
            p.dve.wait(nc.tensor)
            p.dgp.wait(nc.tensor)
            for mc in range(4):
                for kc in range(DC):
                    inst = nc.tensor.matmul(ps_g[:, mc:mc + 1],
                                            w1[:, kc, mc * 128:(mc + 1) * 128],
                                            gctx_bf[:, kc:kc + 1],
                                            start=(mc == 0 and kc == 0),
                                            stop=(mc == 3 and kc == DC - 1))
                p.pe.inc(inst)
            p.pe.wait(nc.scalar)
            for mc in range(4):
                p.act.inc(nc.scalar.activation(h1g[:, mc:mc + 1], ps_g[:, mc:mc + 1], AF.Gelu,
                                               bias=cols["gcb1_c"][:, mc:mc + 1], scale=1.0))
            p.act.wait(nc.tensor)
            for mc in range(2):
                for kc in range(4):
                    inst = nc.tensor.matmul(ps_g[:, 8 + mc:9 + mc],
                                            w2[:, kc, mc * 128:(mc + 1) * 128],
                                            h1g[:, kc:kc + 1],
                                            start=(mc == 0 and kc == 0),
                                            stop=(mc == 1 and kc == 3))
                p.pe.inc(inst)
            p.pe.wait(nc.scalar)
            for mc in range(2):
                p.act.inc(nc.scalar.activation(h2g[:, mc:mc + 1], ps_g[:, 8 + mc:9 + mc], AF.Gelu,
                                               bias=cols["gcb2_c"][:, mc:mc + 1], scale=1.0))
            p.act.wait(nc.tensor)
            for kc in range(2):
                inst = nc.tensor.matmul(ps_g2[:], w3[:, kc, :], h2g[:, kc:kc + 1],
                                        start=(kc == 0), stop=(kc == 1))
            p.pe.inc(inst)
            p.pe.wait(nc.vector)
            fgf, ft, fc1, fc2 = [fl[:, i:i + 1] for i in range(4)]
            p.dve.inc(nc.vector.tensor_tensor(ft, ps_g2[:], cols["gcb3"][:], ALU.add))
            p.dve.wait(nc.scalar)
            p.act.inc(nc.scalar.activation(ft, ft, AF.Tanh, bias=0.0, scale=0.5))
            p.act.wait(nc.vector)
            p.dve.inc(nc.vector.tensor_scalar(fgf, ft, 0.5, 0.5, ALU.mult, ALU.add))
            p.dve.wait(nc.vector)
            p.dve.inc(nc.vector.tensor_scalar(fc1, fgf, 1.4, -1.0, ALU.mult, ALU.add))
            p.dve.wait(nc.vector)
            p.dve.inc(nc.vector.tensor_tensor(fc1, fc1, cols["ar"][:], ALU.mult))
            p.dve.inc(nc.vector.tensor_scalar_mul(fc2, cols["ar"][:], 0.3))
            p.dve.wait(nc.vector)
            p.dve.inc(nc.vector.tensor_tensor(c2p[:], fc2, fc2, ALU.bypass))
            p.dve.inc(nc.vector.tensor_scalar(ft, fc2, 1.0, 0.1, ALU.mult, ALU.add))
            p.dve.wait(nc.vector)
            p.dve.inc(nc.vector.tensor_tensor(c1p[:], fc1, ft, ALU.add))
        p.bar()

    build_rest(p, dt, sb, cols, phases)

    return nc, dt


def build_rest(p, dt, sb, cols, phases):
    nc = p.nc
    ones_col_bf = sb["ones_col_bf"]
    ones_row_f = sb["ones_row_f"]
    ones_row_bf = sb["ones_row_bf"]
    sel2r = sb["sel2r_sb"]
    sel33 = sb["sel33_sb"]
    gacc = sb["gacc"]
    y16 = sb["y16"]
    c1p, c2p = sb["c1p"], sb["c2p"]
    bq8 = sb["bq8"]

    if "b" in phases:
        with ExitStack() as es:
            t = lambda n, s, d=F32: es.enter_context(nc.sbuf_tensor(n, list(s), d))
            xch = [t("xchB0", (128, DC, TC)), t("xchB1", (128, DC, TC))]
            xnch = [t("xnchB0", (128, DC, TC)), t("xnchB1", (128, DC, TC))]
            outp = [t("outB0", (128, DC, TC)), t("outB1", (128, DC, TC))]
            tw = [t("twB0", (128, DC, TC)), t("twB1", (128, DC, TC))]
            trow = [t("trowB0", (1, TC)), t("trowB1", (1, TC))]
            cplrow = [t("cplB0", (1, TC)), t("cplB1", (1, TC))]
            ldb = Sem(nc, "ldB")
            outb = [t("outbB0", (128, DC, TC), BF16), t("outbB1", (128, DC, TC), BF16)]
            h1 = t("h1B", (128, DC, TC), BF16)
            h2 = t("h2B", (128, 4, TC), BF16)
            h3 = t("h3B", (128, 2, TC), BF16)
            w1 = t("cw1", (128, DC, D), BF16)
            w2 = t("cw2", (128, DC, 512), BF16)
            w3 = t("cw3", (128, 4, 256), BF16)
            w4 = t("cw4", (128, 2, 1), BF16)
            ps_h = [es.enter_context(nc.psum_tensor("psB0", [128, TC], F32)),
                    es.enter_context(nc.psum_tensor("psB1", [128, TC], F32))]
            ps_t4 = [es.enter_context(nc.psum_tensor("psBt4a", [1, TC], F32)),
                     es.enter_context(nc.psum_tensor("psBt4b", [1, TC], F32))]
            ps_cb = [es.enter_context(nc.psum_tensor("psBcba", [128, TC], F32)),
                     es.enter_context(nc.psum_tensor("psBcbb", [128, TC], F32))]
            p.dgp.inc(nc.gpsimd.dma_start(w1[:], dt["cnw1"].rearrange("(c p) m -> p c m", p=128)), 16)
            p.dgp.inc(nc.gpsimd.dma_start(w2[:], dt["cnw2"].rearrange("(c p) m -> p c m", p=128)), 16)
            p.dgp.inc(nc.gpsimd.dma_start(w3[:], dt["cnw3"].rearrange("(c p) m -> p c m", p=128)), 16)
            with nc.allow_non_contiguous_dma(reason="tiny [256,1] weight"):
                p.dgp.inc(nc.gpsimd.dma_start(w4[:], dt["cnw4"].rearrange("(c p) m -> p c m", p=128)), 16)
            p.dgp.wait(nc.tensor)
            ld_mark = [0, 0]
            copy_mark = [0, 0]
            x1_mark = [0, 0]
            st_mark = [0, 0]
            cbfree = [0, 0]
            t4free = [0, 0]
            hfree = [p.act.n, p.act.n]
            w1_done = [0, 0]
            t4mm = [0, 0]
            cpl_mark = [0, 0]
            cb_mm = [0, 0]

            ldxn_mark = [0, 0]
            ldxc_mark = [0, 0]

            def emit_load_xch(i, s):
                nc.sync.wait_ge(p.dve.h, x1_mark[s])
                ldb.inc(nc.sync.dma_start(
                    xch[s][:], dt["x_fm"][:, i * TC:(i + 1) * TC]
                    .rearrange("(c p) t -> p c t", p=128)), 16)
                ldxc_mark[s] = ldb.n

            def emit_load_xnch(i, s):
                nc.sync.wait_ge(p.dve.h, copy_mark[s])
                ldb.inc(nc.sync.dma_start(
                    xnch[s][:], dt["xn_sp"][:, i * TC:(i + 1) * TC]
                    .rearrange("(c p) t -> p c t", p=128)), 16)
                ldxn_mark[s] = ldb.n

            def emit_init(s):
                nc.vector.wait_ge(ldb.h, ldxn_mark[s])
                for c in range(DC):
                    p.dve.inc(nc.vector.tensor_copy(outp[s][:, c, :], xnch[s][:, c, :]))
                copy_mark[s] = p.dve.n

            ob_mark = [0, 0]

            def emit_copy(s):
                nc.vector.wait_ge(p.pe.h, w1_done[s])
                p.dve.wait(nc.vector)
                p.dve.inc(nc.vector.tensor_copy(outb[s][:], outp[s][:]))
                ob_mark[s] = p.dve.n

            def emit_mms(s):
                for mc in range(DC):
                    bk = mc % 2
                    nc.tensor.wait_ge(p.act.h, hfree[bk])
                    if mc == 0:
                        nc.tensor.wait_ge(p.dve.h, ob_mark[s])
                    for kc in range(DC):
                        inst = nc.tensor.matmul(ps_h[bk][:], w1[:, kc, mc * 128:(mc + 1) * 128],
                                                outb[s][:, kc, :], start=(kc == 0), stop=(kc == DC - 1))
                    p.pe.inc(inst)
                    p.pe.wait(nc.scalar)
                    hfree[bk] = p.act.inc(nc.scalar.activation(h1[:, mc, :], ps_h[bk][:], AF.Gelu,
                                                               bias=cols["cnb1_c"][:, mc:mc + 1], scale=1.0))
                w1_done[s] = p.pe.n
                p.act.wait(nc.tensor)
                for mc in range(4):
                    bk = mc % 2
                    nc.tensor.wait_ge(p.act.h, hfree[bk])
                    for kc in range(DC):
                        inst = nc.tensor.matmul(ps_h[bk][:], w2[:, kc, mc * 128:(mc + 1) * 128],
                                                h1[:, kc, :], start=(kc == 0), stop=(kc == DC - 1))
                    p.pe.inc(inst)
                    p.pe.wait(nc.scalar)
                    hfree[bk] = p.act.inc(nc.scalar.activation(h2[:, mc, :], ps_h[bk][:], AF.Gelu,
                                                               bias=cols["cnb2_c"][:, mc:mc + 1], scale=1.0))
                p.act.wait(nc.tensor)
                for mc in range(2):
                    bk = mc % 2
                    nc.tensor.wait_ge(p.act.h, hfree[bk])
                    for kc in range(4):
                        inst = nc.tensor.matmul(ps_h[bk][:], w3[:, kc, mc * 128:(mc + 1) * 128],
                                                h2[:, kc, :], start=(kc == 0), stop=(kc == 3))
                    p.pe.inc(inst)
                    p.pe.wait(nc.scalar)
                    hfree[bk] = p.act.inc(nc.scalar.activation(h3[:, mc, :], ps_h[bk][:], AF.Gelu,
                                                               bias=cols["cnb3_c"][:, mc:mc + 1], scale=1.0))
                p.act.wait(nc.tensor)
                nc.tensor.wait_ge(p.dve.h, t4free[s])
                for kc in range(2):
                    inst = nc.tensor.matmul(ps_t4[s][:], w4[:, kc, :], h3[:, kc, :],
                                            start=(kc == 0), stop=(kc == 1))
                t4mm[s] = p.pe.inc(inst)

            def emit_tail1(s):
                nc.vector.wait_ge(p.pe.h, t4mm[s])
                p.dve.inc(nc.vector.tensor_scalar_add(trow[s][:], ps_t4[s][:], cols["cnb4"][:]))
                t4free[s] = p.dve.n
                nc.scalar.wait_ge(p.dve.h, t4free[s])
                tanh_m = p.act.inc(nc.scalar.activation(trow[s][:], trow[s][:], AF.Tanh,
                                                        bias=0.0, scale=0.5))
                nc.vector.wait_ge(p.act.h, tanh_m)
                p.dve.inc(nc.vector.tensor_scalar(cplrow[s][:], trow[s][:], c2p[:], None, ALU.mult))
                p.dve.wait(nc.vector)
                cpl_mark[s] = p.dve.inc(nc.vector.tensor_scalar(cplrow[s][:], cplrow[s][:], c1p[:],
                                                                None, ALU.add))
                nc.tensor.wait_ge(p.dve.h, max(cpl_mark[s], cbfree[s]))
                cb_mm[s] = p.pe.inc(nc.tensor.matmul(ps_cb[s][:], ones_row_f[:], cplrow[s][:],
                                                     start=True, stop=True))

            def emit_tail2(s, cyc):
                nc.vector.wait_ge(p.pe.h, cb_mm[s])
                if cyc == 0:
                    nc.vector.wait_ge(p.dou.h, st_mark[s])
                for j in range(DC):
                    src = (j + 4) % DC
                    p.dve.inc(nc.vector.tensor_mul(tw[s][:, j, :], outp[s][:, src, :], ps_cb[s][:]))
                cbfree[s] = p.dve.n
                p.dve.wait(nc.vector)
                for j in range(DC):
                    op = ALU.add if j in (0, 1, 6, 7) else ALU.subtract
                    p.dve.inc(nc.vector.tensor_tensor(outp[s][:, j, :], outp[s][:, j, :],
                                                      tw[s][:, j, :], op))

            def emit_x1(i, s):
                nc.vector.wait_ge(ldb.h, ldxc_mark[s])
                p.dve.wait(nc.vector)
                for c in range(DC):
                    p.dve.inc(nc.vector.tensor_add(tw[s][:, c, :], xch[s][:, c, :],
                                                   outp[s][:, c, :]))
                x1_mark[s] = p.dve.n
                p.dve.wait(nc.vector)
                for c in range(DC):
                    p.dve.inc(nc.vector.tensor_scalar_add(tw[s][:, c, :], tw[s][:, c, :],
                                                          cols["ob_c"][:, c:c + 1]))
                if i == 0 and "t16" in phases:
                    p.dve.wait(nc.vector)
                    for c in range(DC):
                        p.dve.inc(nc.vector.tensor_add(tw[s][:, c, 0:16],
                                                       tw[s][:, c, 0:16], y16[:, c, :]))
                nc.sync.wait_ge(p.dve.h, p.dve.n)
                st_mark[s] = p.dou.inc(nc.sync.dma_start(
                    dt["x1_sp"][:, i * TC:(i + 1) * TC].rearrange("(c p) t -> p c t", p=128),
                    tw[s][:]), 16)

            emit_load_xnch(0, 0)
            emit_load_xch(0, 0)
            emit_load_xnch(1, 1)
            emit_load_xch(1, 1)
            for pr in range(NCH // 2):
                a, b = 2 * pr, 2 * pr + 1
                if pr == 0:
                    emit_init(0)
                    emit_init(1)
                    emit_copy(0)
                if a + 2 < NCH:
                    emit_load_xnch(a + 2, 0)
                    emit_load_xnch(b + 2, 1)
                for cyc in range(3):
                    emit_mms(0)
                    emit_tail1(0)
                    emit_copy(1)
                    emit_tail2(0, cyc)
                    if cyc == 2:
                        emit_x1(a, 0)
                        if a + 2 < NCH:
                            emit_load_xch(a + 2, 0)
                            emit_init(0)
                            emit_copy(0)
                    emit_mms(1)
                    emit_tail1(1)
                    if cyc < 2:
                        emit_copy(0)
                    emit_tail2(1, cyc)
                    if cyc == 2:
                        emit_x1(b, 1)
                        if b + 2 < NCH:
                            emit_load_xch(b + 2, 1)
                            emit_init(1)
            nc.sync.wait_ge(p.dou.h, p.dou.n)
        p.bar()

    k_res = v_res = None
    wq_res = wpj_res = None
    wq_mark = 0
    if "c" in phases or "attn" in phases:
        es_kv = ExitStack()
        k_res = es_kv.enter_context(nc.sbuf_tensor("k_res", [128, DC, S], BF16))
        v_res = es_kv.enter_context(nc.sbuf_tensor("v_res", [128, NKT, NH, DH + 1], BF16))

    def load_wq_wpj():
        p.dgp.inc(nc.gpsimd.dma_start(
            wq_res[:], dt["wqkv"][:, 0:D].rearrange("(c p) m -> p c m", p=128)), 16)
        p.dgp.inc(nc.gpsimd.dma_start(
            wpj_res[:], dt["wproj"].rearrange("(c p) m -> p c m", p=128)), 16)
        return p.dgp.n
    if "c" in phases:
        with ExitStack() as es:
            t = lambda n, s, d=F32: es.enter_context(nc.sbuf_tensor(n, list(s), d))
            xch = [t("xchC0", (128, DC, TC), BF16), t("xchC1", (128, DC, TC), BF16)]
            ldc = Sem(nc, "ldC")
            xsq = t("xsqC", (128, DC, TC), BF16)
            xn1 = t("xn1C", (128, DC, TC), BF16)
            rows2 = t("rows2c", (1, 5, TC))
            wk = t("wkC", (128, DC, D), BF16)
            ps_st = es.enter_context(nc.psum_tensor("psCst", [34, TC], F32))
            ps_r = es.enter_context(nc.psum_tensor("psCr", [128, TC], F32))
            ps_m = es.enter_context(nc.psum_tensor("psCm", [128, TC], F32))
            ps_k = [es.enter_context(nc.psum_tensor("psCk0", [128, TC], F32)),
                    es.enter_context(nc.psum_tensor("psCk1", [128, TC], F32))]
            p.dgp.inc(nc.gpsimd.dma_start(
                wk[:], dt["wqkv"][:, D:2 * D].rearrange("(c p) m -> p c m", p=128)), 16)
            p.dgp.wait(nc.tensor)
            stat_mark = 0
            bc_mark = 0
            kmm_mark = 0
            sp_done = 0
            st_free = 0
            addm = 0
            xfree = [0, 0]
            ld_mark = [0, 0]
            kfree = [p.act.n, p.act.n]

            def emit_load(i):
                s = i % 2
                nc.gpsimd.wait_ge(p.dve.h, xfree[s])
                ldc.inc(nc.gpsimd.dma_start(
                    xch[s][:], dt["x1_sp"][:, i * TC:(i + 1) * TC]
                    .rearrange("(c p) t -> p c t", p=128)), 16)
                ld_mark[s] = ldc.n

            def emit_khalf(i, half):
                for mc in range(4 * half, 4 * half + 4):
                    bk = mc % 2
                    nc.tensor.wait_ge(p.act.h, kfree[bk])
                    for kc in range(DC):
                        inst = nc.tensor.matmul(ps_k[bk][:], wk[:, kc, mc * 128:(mc + 1) * 128],
                                                xn1[:, kc, :], start=(kc == 0), stop=(kc == DC - 1))
                    p.pe.inc(inst)
                    p.pe.wait(nc.scalar)
                    kfree[bk] = p.act.inc(nc.scalar.activation(
                        k_res[:, mc, i * TC:(i + 1) * TC], ps_k[bk][:], AF.Identity,
                        bias=cols["bk_c"][:, mc:mc + 1], scale=1.0))

            emit_load(0)
            for i in range(NCH + 1):
                si = i % 2
                if i < NCH:
                    nc.scalar.wait_ge(ldc.h, ld_mark[si])
                    nc.scalar.wait_ge(p.pe.h, stat_mark)
                    p.act.inc(nc.scalar.activation(xsq[:], xch[si][:], AF.Square))
                    p.act.wait(nc.tensor)
                    nc.tensor.wait_ge(ldc.h, ld_mark[si])
                    nc.tensor.wait_ge(p.dve.h, st_free)
                    stat_sums(p, ps_st, xch[si], xsq, DC, TC, ones_col_bf[:])
                    stat_mark = p.pe.n
                if i >= 1:
                    emit_khalf(i - 1, 0)
                if i < NCH:
                    nc.vector.wait_ge(p.pe.h, bc_mark)
                    nc.scalar.wait_ge(p.pe.h, bc_mark)
                    nc.vector.wait_ge(p.pe.h, stat_mark)
                    nc.scalar.wait_ge(p.pe.h, stat_mark)
                    st_free = ln_rows(p, ps_st, 1, TC, 1.0 / D, rows2)
                    st_free = p.dve.n
                    p.dve.wait(nc.tensor)
                    p.act.wait(nc.tensor)
                    nc.tensor.wait_ge(p.dve.h, addm)
                    nc.tensor.matmul(ps_r[:], ones_row_f[:], rows2[:, 0, :],
                                     start=True, stop=True)
                    p.pe.inc(nc.tensor.matmul(ps_m[:], ones_row_f[:], rows2[:, 1, :],
                                              start=True, stop=True))
                    bc_mark = p.pe.n
                    nc.vector.wait_ge(p.pe.h, bc_mark)
                    for c in range(DC):
                        p.dve.inc(nc.vector.tensor_mul(xch[si][:, c, :], xch[si][:, c, :],
                                                       ps_r[:]))
                if i >= 1:
                    emit_khalf(i - 1, 1)
                    kmm_mark = p.pe.n
                if i < NCH:
                    p.dve.wait(nc.vector)
                    nc.vector.wait_ge(p.pe.h, kmm_mark)
                    nc.vector.wait_ge(p.dou.h, sp_done)
                    for c in range(DC):
                        p.dve.inc(nc.vector.tensor_add(xn1[:, c, :], xch[si][:, c, :], ps_m[:]))
                    addm = p.dve.n
                    xfree[si] = p.dve.n
                    if i + 1 < NCH:
                        emit_load(i + 1)
                    nc.sync.wait_ge(p.dve.h, p.dve.n)
                    sp_done = p.dou.inc(nc.sync.dma_start(
                        dt["xn1_sp"][:, i * TC:(i + 1) * TC]
                        .rearrange("(c p) t -> p c t", p=128),
                        xn1[:]), 16)
            nc.sync.wait_ge(p.dou.h, p.dou.n)
        p.bar()

        if "attn" in phases:
            wq_res = es_kv.enter_context(nc.sbuf_tensor("wq_res", [128, DC, D], BF16))
            wpj_res = es_kv.enter_context(nc.sbuf_tensor("wpj_res", [128, DC, D], BF16))
            wq_mark = load_wq_wpj()

        with ExitStack() as es:
            t = lambda n, s, d=F32: es.enter_context(nc.sbuf_tensor(n, list(s), d))
            xn1b = [t("xn1C2a", (128, DC, TC), BF16), t("xn1C2b", (128, DC, TC), BF16)]
            ldc2 = Sem(nc, "ldC2")
            ldbv = Sem(nc, "ldBV")
            wv = t("wvC2", (128, DC, D), BF16)
            bvrow_f = t("bvrowC2", (1, D), F32)
            bv_bf = t("bvbfC2", (1, D), BF16)
            ps_v = [es.enter_context(nc.psum_tensor("psCv0", [128, 512], F32)),
                    es.enter_context(nc.psum_tensor("psCv1", [128, 512], F32))]
            p.dgp.inc(nc.gpsimd.dma_start(
                wv[:], dt["wqkv"][:, 2 * D:3 * D].rearrange("(c p) m -> p c m", p=128)), 16)
            ldbv.inc(nc.sync.dma_start(bvrow_f[:], dt["bv_row"][:]), 16)
            ldbv.wait(nc.vector)
            p.dve.inc(nc.vector.tensor_copy(bv_bf[:], bvrow_f[:]))
            p.dve.inc(nc.vector.memset(v_res[:, :, :, DH:DH + 1], 1.0))
            p.dgp.wait(nc.tensor)
            p.dve.wait(nc.tensor)
            ld2_mark = [0, 0]
            vmm_mark = [0, 0]
            for s in range(2):
                ldc2.inc(nc.sync.dma_start(
                    xn1b[s][:], dt["xn1_sp"][:, s * TC:(s + 1) * TC]
                    .rearrange("(c p) t -> p c t", p=128)), 16)
                ld2_mark[s] = ldc2.n
            for i in range(NCH):
                s = i % 2
                xn1 = xn1b[s]
                nc.tensor.wait_ge(ldc2.h, ld2_mark[s])
                kfree = [p.dve.n, p.dve.n]
                for ts4 in range(TC // 128):
                    kt = i * (TC // 128) + ts4
                    for half in range(2):
                        bk = half
                        nc.tensor.wait_ge(p.dve.h, kfree[bk])
                        for kc in range(DC):
                            nc.tensor.matmul(ps_v[bk][:], xn1[:, kc, ts4 * 128:(ts4 + 1) * 128],
                                             wv[:, kc, half * 512:(half + 1) * 512],
                                             start=(kc == 0), stop=False)
                        inst = nc.tensor.matmul(ps_v[bk][:], ones_row_bf[:],
                                                bv_bf[:, half * 512:(half + 1) * 512],
                                                start=False, stop=True)
                        p.pe.inc(inst)
                        p.pe.wait(nc.vector)
                        kfree[bk] = p.dve.inc(nc.vector.tensor_copy(
                            v_res[:, kt, half * 8:(half + 1) * 8, 0:DH],
                            ps_v[bk][:].rearrange("p (h d) -> p h d", d=DH)))
                vmm_mark[s] = p.pe.n
                if i + 2 < NCH:
                    nc.sync.wait_ge(p.pe.h, vmm_mark[s])
                    ldc2.inc(nc.sync.dma_start(
                        xn1b[s][:], dt["xn1_sp"][:, (i + 2) * TC:(i + 3) * TC]
                        .rearrange("(c p) t -> p c t", p=128)), 16)
                    ld2_mark[s] = ldc2.n
        p.bar()

    if "attn" in phases:
        with ExitStack() as es:
            t = lambda n, s, d=F32: es.enter_context(nc.sbuf_tensor(n, list(s), d))
            xn1c = t("xn1At", (128, DC, TC), BF16)
            lda = Sem(nc, "ldAt")
            qfm = t("qAt", (128, DC, TC), BF16)
            pbuf = [t("pA0", (128, TC), BF16), t("pA1", (128, TC), BF16),
                    t("pA2", (128, TC), BF16), t("pA3", (128, TC), BF16)]
            o2s = t("o2sAt", (128, TC))
            r2ab = t("r2abAt", (33, TC))
            nb_sb = t("nbsbAt", (128, TC))
            aout = t("aoutAt", (128, DC, TC), BF16)
            lneg = sb["lneg_sb"]
            rbig = sb["rbig_sb"]
            ps_se = es.enter_context(nc.psum_tensor("psSe", [128, TC], F32))
            ps_so = es.enter_context(nc.psum_tensor("psSo", [128, TC], F32))
            ps_se2 = es.enter_context(nc.psum_tensor("psSe2", [128, TC], F32))
            ps_so2 = es.enter_context(nc.psum_tensor("psSo2", [128, TC], F32))
            ps_o2e = es.enter_context(nc.psum_tensor("psO2e", [128, TC], F32))
            ps_o2o = es.enter_context(nc.psum_tensor("psO2o", [128, TC], F32))
            ps_o2e2 = es.enter_context(nc.psum_tensor("psO2e2", [128, TC], F32))
            ps_o2o2 = es.enter_context(nc.psum_tensor("psO2o2", [128, TC], F32))
            SE = [ps_se, ps_se2]
            SO = [ps_so, ps_so2]
            O2E = [ps_o2e, ps_o2e2]
            O2O = [ps_o2o, ps_o2o2]
            if wq_res is None:
                wq_res = es_kv.enter_context(nc.sbuf_tensor("wq_res", [128, DC, D], BF16))
                wpj_res = es_kv.enter_context(nc.sbuf_tensor("wpj_res", [128, DC, D], BF16))
            if wq_mark == 0:
                wq_mark = load_wq_wpj()
            p.dve.inc(nc.vector.memset(r2ab[:], 0.0))
            qp_mark = 0
            ev_mark = 0
            stq = [0] * DC
            for j in range(NCH):
                q0 = j * TC
                nkt = (j + 1) * (TC // 128)
                if j == 0:
                    lda.inc(nc.sync.dma_start(
                        xn1c[:], dt["xn1_sp"][:, q0:q0 + TC]
                        .rearrange("(c p) t -> p c t", p=128)), 16)
                    nc.tensor.wait_ge(p.dgp.h, wq_mark)
                lda.wait(nc.tensor)
                qfree = [p.act.n, p.act.n]
                for mc in range(DC):
                    bk = mc % 2
                    nc.tensor.wait_ge(p.act.h, qfree[bk])
                    if mc < 2:
                        nc.tensor.wait_ge(p.dve.h, ev_mark)
                    for kc in range(DC):
                        inst = nc.tensor.matmul(SE[bk][:], wq_res[:, kc, mc * 128:(mc + 1) * 128],
                                                xn1c[:, kc, :], start=(kc == 0), stop=(kc == DC - 1))
                    p.pe.inc(inst)
                    p.pe.wait(nc.scalar)
                    nc.scalar.wait_ge(p.dou.h, stq[mc])
                    qfree[bk] = p.act.inc(nc.scalar.activation(
                        qfm[:, mc, :], SE[bk][:], AF.Identity,
                        bias=bq8[:, mc:mc + 1], scale=QSCALE))
                qp_mark = p.pe.n
                if j + 1 < NCH:
                    nc.sync.wait_ge(p.pe.h, qp_mark)
                    lda.inc(nc.sync.dma_start(
                        xn1c[:], dt["xn1_sp"][:, q0 + TC:q0 + 2 * TC]
                        .rearrange("(c p) t -> p c t", p=128)), 16)
                p.act.wait(nc.tensor)
                sfree = [p.act.n, p.act.n]
                o2free = [p.dve.n, p.dve.n]
                state = {"nbdone": p.dve.n, "nbm": p.pe.n}
                prev_norm = None

                def emit_norm(pn):
                    phd, pob, r2m = pn
                    nc.tensor.wait_ge(p.dve.h, r2m)
                    NB = O2E[pob]
                    state["nbm"] = p.pe.inc(nc.tensor.matmul(
                        NB[:], sel33[:], r2ab[:], start=True, stop=True))
                    nc.vector.wait_ge(p.pe.h, state["nbm"])
                    nc.vector.reciprocal(nb_sb[:], NB[:])
                    state["nbdone"] = p.dve.inc(
                        nc.vector.tensor_mul(aout[:, phd, :], o2s[:], nb_sb[:]))
                    o2free[pob] = state["nbdone"]

                def emit_scores(hp, kt, sbk):
                    diag = kt * 128 + 128 > q0
                    base = q0 - kt * 128
                    nc.tensor.wait_ge(p.act.h, sfree[sbk])
                    nc.tensor.matmul(SE[sbk][:], k_res[0:64, hp, kt * 128:(kt + 1) * 128],
                                     qfm[0:64, hp, :], start=True, stop=not diag,
                                     tile_position=(0, 0))
                    inst = nc.tensor.matmul(SO[sbk][:], k_res[64:128, hp, kt * 128:(kt + 1) * 128],
                                            qfm[64:128, hp, :], start=True, stop=not diag,
                                            tile_position=(64, 0))
                    if diag:
                        w = min(TC, 128 - base)
                        rb = rbig[:, 384 + base:384 + base + w]
                        nc.tensor.matmul(SE[sbk][:, 0:w], lneg[:], rb, start=False, stop=True)
                        inst = nc.tensor.matmul(SO[sbk][:, 0:w], lneg[:], rb, start=False, stop=True)
                    return p.pe.inc(inst)

                gl = [(hp, kt) for hp in range(DC) for kt in range(nkt)]
                mm_mark = {}
                pv_mark = {}
                mm_mark[0] = emit_scores(0, 0, 0)
                for g, (hp, kt) in enumerate(gl):
                    he, ho = 2 * hp, 2 * hp + 1
                    ob = hp % 2
                    sbk = g % 2
                    if kt == 1 and prev_norm is not None:
                        emit_norm(prev_norm)
                        prev_norm = None
                    nc.scalar.wait_ge(p.pe.h, mm_mark[g])
                    if g >= 2:
                        nc.scalar.wait_ge(p.pe.h, pv_mark[g - 2])
                    nc.scalar.activation(pbuf[2 * sbk][:], SE[sbk][:], AF.Exp,
                                         bias=0.0, scale=1.0)
                    m = p.act.inc(nc.scalar.activation(pbuf[2 * sbk + 1][:], SO[sbk][:],
                                                       AF.Exp, bias=0.0, scale=1.0))
                    sfree[sbk] = m
                    if g + 1 < len(gl):
                        nhp, nkt_ = gl[g + 1]
                        mm_mark[g + 1] = emit_scores(nhp, nkt_, (g + 1) % 2)
                    if kt == 0:
                        nc.tensor.wait_ge(p.dve.h, o2free[ob])
                    nc.tensor.wait_ge(p.act.h, m)
                    nc.tensor.matmul(O2E[ob][0:65, :], v_res[:, kt, he, :], pbuf[2 * sbk][:],
                                     start=(kt == 0), stop=(kt == nkt - 1),
                                     tile_position=(0, 0))
                    pv_mark[g] = p.pe.inc(nc.tensor.matmul(
                        O2O[ob][0:65, :], v_res[:, kt, ho, :], pbuf[2 * sbk + 1][:],
                        start=(kt == 0), stop=(kt == nkt - 1),
                        tile_position=(0, 0)))
                    if kt == nkt - 1:
                        nc.vector.wait_ge(p.pe.h, max(pv_mark[g], state["nbm"]))
                        nc.vector.tensor_copy(r2ab[0:1, :], O2E[ob][64:65, :])
                        nc.vector.tensor_copy(r2ab[32:33, :], O2O[ob][64:65, :])
                        nc.vector.tensor_copy(o2s[0:64, :], O2E[ob][0:64, :])
                        r2m = p.dve.inc(nc.vector.tensor_copy(o2s[64:128, :], O2O[ob][0:64, :]))
                        prev_norm = (hp, ob, r2m)
                emit_norm(prev_norm)
                prev_norm = None
                nc.tensor.wait_ge(p.dve.h, state["nbdone"])
                nc.tensor.wait_ge(p.act.h, max(sfree))
                ofree = [p.dve.n, p.dve.n]
                for mc in range(DC):
                    bk = mc % 2
                    nc.tensor.wait_ge(p.dve.h, ofree[bk])
                    for kc in range(DC):
                        inst = nc.tensor.matmul(SE[bk][:], wpj_res[:, kc, mc * 128:(mc + 1) * 128],
                                                aout[:, kc, :], start=(kc == 0), stop=(kc == DC - 1))
                    p.pe.inc(inst)
                    p.pe.wait(nc.vector)
                    ofree[bk] = p.dve.inc(nc.vector.tensor_scalar_add(
                        qfm[:, mc, :], SE[bk][:], cols["bp_c"][:, mc:mc + 1]))
                    nc.sync.wait_ge(p.dve.h, p.dve.n)
                    stq[mc] = p.dou.inc(nc.sync.dma_start(
                        dt["xattn_sp"][mc * 128:(mc + 1) * 128, q0:q0 + TC],
                        qfm[:, mc, :]), 16)
                ev_mark = p.dve.n
            nc.sync.wait_ge(p.dou.h, p.dou.n)
        p.bar()

    if k_res is not None:
        es_kv.close()

    if "ffn" in phases:
        with ExitStack() as es:
            t = lambda n, s, d=F32: es.enter_context(nc.sbuf_tensor(n, list(s), d))
            wg = t("wgF", (128, DC, HID), BF16)
            wu = t("wuF", (128, DC, HID), BF16)
            wd = t("wdF", (128, HCN, D), BF16)
            xn1c = t("xn1F", (128, DC, TC), BF16)
            ldf = Sem(nc, "ldF")
            hbuf = t("hF", (128, HCN, TC), BF16)
            xat = t("xatF", (128, DC, TC), BF16)
            xac = t("xaF", (128, DC, TC))
            ps_g = [es.enter_context(nc.psum_tensor("psF0", [128, TC], F32)),
                    es.enter_context(nc.psum_tensor("psF1", [128, TC], F32))]
            ps_u = [es.enter_context(nc.psum_tensor("psFu0", [128, TC], F32)),
                    es.enter_context(nc.psum_tensor("psFu1", [128, TC], F32))]
            ps_d = [es.enter_context(nc.psum_tensor("psFd0", [128, TC], F32)),
                    es.enter_context(nc.psum_tensor("psFd1", [128, TC], F32))]
            p.dgp.inc(nc.gpsimd.dma_start(
                wg[:], dt["wgT"].rearrange("(c p) m -> p c m", p=128)), 16)
            p.dgp.inc(nc.gpsimd.dma_start(
                wu[:], dt["wuT"].rearrange("(c p) m -> p c m", p=128)), 16)
            p.dgp.inc(nc.gpsimd.dma_start(
                wd[:], dt["wdT"].rearrange("(c p) m -> p c m", p=128)), 16)
            p.dgp.wait(nc.tensor)
            ldx = Sem(nc, "ldxF")
            yst_mark = 0
            for i in range(NCH):
                if i == 0:
                    ldf.inc(nc.sync.dma_start(
                        xn1c[:], dt["xn1_sp"][:, 0:TC]
                        .rearrange("(c p) t -> p c t", p=128)), 16)
                    ldx.inc(nc.sync.dma_start(
                        xat[:], dt["xattn_sp"][:, 0:TC]
                        .rearrange("(c p) t -> p c t", p=128)), 16)
                    ldx.inc(nc.sync.dma_start(
                        xac[:], dt["x1_sp"][:, 0:TC]
                        .rearrange("(c p) t -> p c t", p=128)), 16)
                ldf.wait(nc.tensor)
                gfree = [p.act.n] * 2
                ufree = [p.dve.n] * 2
                for hc in range(HCN):
                    bk = hc % 2
                    nc.tensor.wait_ge(p.act.h, gfree[bk])
                    for kc in range(DC):
                        inst = nc.tensor.matmul(ps_g[bk][:], wg[:, kc, hc * 128:(hc + 1) * 128],
                                                xn1c[:, kc, :], start=(kc == 0), stop=(kc == DC - 1))
                    p.pe.inc(inst)
                    nc.tensor.wait_ge(p.dve.h, ufree[bk])
                    for kc in range(DC):
                        inst = nc.tensor.matmul(ps_u[bk][:], wu[:, kc, hc * 128:(hc + 1) * 128],
                                                xn1c[:, kc, :], start=(kc == 0), stop=(kc == DC - 1))
                    p.pe.inc(inst)
                    p.pe.wait(nc.scalar, p.pe.n - 1)
                    gfree[bk] = p.act.inc(nc.scalar.activation(hbuf[:, hc, :], ps_g[bk][:],
                                                               AF.Silu, bias=0.0, scale=QDEQ))
                    p.pe.wait(nc.vector)
                    p.act.wait(nc.vector)
                    ufree[bk] = p.dve.inc(nc.vector.tensor_mul(hbuf[:, hc, :], hbuf[:, hc, :],
                                                               ps_u[bk][:]))
                if i + 1 < NCH:
                    nc.sync.wait_ge(p.pe.h, p.pe.n)
                    ldf.inc(nc.sync.dma_start(
                        xn1c[:], dt["xn1_sp"][:, (i + 1) * TC:(i + 2) * TC]
                        .rearrange("(c p) t -> p c t", p=128)), 16)
                ldx.wait(nc.vector)
                for c in range(DC):
                    p.dve.inc(nc.vector.tensor_tensor(xac[:, c, :], xac[:, c, :],
                                                      xat[:, c, :], ALU.add))
                p.dve.wait(nc.tensor)
                dfree = [p.dve.n] * 2
                for mc in range(DC):
                    bk = mc % 2
                    nc.tensor.wait_ge(p.dve.h, dfree[bk])
                    for hc in range(HCN):
                        inst = nc.tensor.matmul(ps_d[bk][:], wd[:, hc, mc * 128:(mc + 1) * 128],
                                                hbuf[:, hc, :], start=(hc == 0), stop=(hc == HCN - 1))
                    p.pe.inc(inst)
                    p.pe.wait(nc.vector)
                    dfree[bk] = p.dve.inc(nc.vector.scalar_tensor_tensor(
                        xac[:, mc, :], ps_d[bk][:], QDEQ * QDEQ, xac[:, mc, :],
                        ALU.mult, ALU.add))
                nc.sync.wait_ge(p.dve.h, p.dve.n)
                yst_mark = p.dou.inc(nc.sync.dma_start(
                    dt["y"][:, i * TC:(i + 1) * TC].rearrange("(c p) t -> p c t", p=128),
                    xac[:]), 16)
                if i + 1 < NCH:
                    nc.sync.wait_ge(p.dou.h, yst_mark)
                    ldx.inc(nc.sync.dma_start(
                        xat[:], dt["xattn_sp"][:, (i + 1) * TC:(i + 2) * TC]
                        .rearrange("(c p) t -> p c t", p=128)), 16)
                    ldx.inc(nc.sync.dma_start(
                        xac[:], dt["x1_sp"][:, (i + 1) * TC:(i + 2) * TC]
                        .rearrange("(c p) t -> p c t", p=128)), 16)
            nc.sync.wait_ge(p.dou.h, p.dou.n)
        p.bar()
    nc.sync.wait_ge(p.dou.h, p.dou.n)



def _cols(v, nch):
    v = np.ascontiguousarray(np.asarray(v, dtype=np.float32))
    return np.ascontiguousarray(v.reshape(nch, 128).T)


def _t(a):
    return np.ascontiguousarray(np.asarray(a, dtype=np.float32).T)


def _sel33():
    s = np.zeros((33, 128), np.float32)
    s[0, 0:64] = 1.0
    s[32, 64:128] = 1.0
    return s


def prepare_inputs(inputs):
    f = lambda k: np.asarray(inputs[k])
    shared = {
        "qwT": _t(f("L0_attn_qw")), "kwT": _t(f("L0_attn_kw")),
        "vwT": _t(f("L0_attn_vw")), "owT": _t(f("L0_attn_ow")),
        "lqwT": _t(f("L0_attn_lqw")), "lkwT": _t(f("L0_attn_lkw")),
        "lvwT": _t(f("L0_attn_lvw")),
        "qb_c": _cols(f("L0_attn_qb"), 8), "vb_c": _cols(f("L0_attn_vb"), 8),
        "ob_c": _cols(f("L0_attn_ob"), 8),
        "lna_c": _cols(f("L0_attn_lna_w"), 8), "lnb_c": _cols(f("L0_attn_lnb_w"), 8),
        "lnc2": np.ascontiguousarray(np.tile(np.asarray(f("L0_attn_lnc_w"), np.float32), 2)[:, None]),
        "lnd2": np.ascontiguousarray(np.tile(np.asarray(f("L0_attn_lnd_w"), np.float32), 2)[:, None]),
        "lqb2": np.ascontiguousarray(np.tile(np.asarray(f("L0_attn_lqb"), np.float32), 2)[:, None]),
        "lkb2": np.ascontiguousarray(np.tile(np.asarray(f("L0_attn_lkb"), np.float32), 2)[:, None]),
        "lvb2": np.ascontiguousarray(np.tile(np.asarray(f("L0_attn_lvb"), np.float32), 2)[:, None]),
        "cnw1": np.ascontiguousarray(np.asarray(f("L0_ffn_cn_w1"), np.float32)),
        "cnw2": np.ascontiguousarray(np.asarray(f("L0_ffn_cn_w2"), np.float32)),
        "cnw3": np.ascontiguousarray(np.asarray(f("L0_ffn_cn_w3"), np.float32)),
        "cnw4": np.ascontiguousarray(np.asarray(f("L0_ffn_cn_w4"), np.float32)),
        "gcw1": np.ascontiguousarray(np.asarray(f("L0_ffn_gc_w1"), np.float32)),
        "gcw2": np.ascontiguousarray(np.asarray(f("L0_ffn_gc_w2"), np.float32)),
        "gcw3": np.ascontiguousarray(np.asarray(f("L0_ffn_gc_w3"), np.float32)),
        "cnb1_c": _cols(f("L0_ffn_cn_b1"), 8), "cnb2_c": _cols(f("L0_ffn_cn_b2"), 4),
        "cnb3_c": _cols(f("L0_ffn_cn_b3"), 2),
        "cnb4": np.asarray(f("L0_ffn_cn_b4"), np.float32).reshape(1, 1),
        "gcb1_c": _cols(f("L0_ffn_gc_b1"), 4), "gcb2_c": _cols(f("L0_ffn_gc_b2"), 2),
        "gcb3": np.asarray(f("L0_ffn_gc_b3"), np.float32).reshape(1, 1),
        "ar": np.asarray(f("L0_ffn_adaptive_range"), np.float32).reshape(1, 1),
        "wqkv": np.ascontiguousarray(np.asarray(f("L1_attn_W_qkv"), np.float32)),
        "bq_c": _cols(np.asarray(f("L1_attn_b_qkv"), np.float32)[0:D], 8),
        "bk_c": _cols(np.asarray(f("L1_attn_b_qkv"), np.float32)[D:2 * D], 8),
        "bv_row": np.ascontiguousarray(np.asarray(f("L1_attn_b_qkv"), np.float32)[2 * D:3 * D].reshape(1, D)),
        "wproj": np.ascontiguousarray(np.asarray(f("L1_attn_W_proj"), np.float32)),
        "bp_c": _cols(f("L1_attn_b_proj"), 8),
        "wgT": np.ascontiguousarray(np.asarray(f("L1_ffn_w_gate"), np.int32).T),
        "wuT": np.ascontiguousarray(np.asarray(f("L1_ffn_w_up"), np.int32).T),
        "wdT": np.ascontiguousarray(np.asarray(f("L1_ffn_w_down"), np.int32).T),
        "ident": np.eye(128, dtype=BF16NP),
        "lneg": (-60.0 * np.triu(np.ones((128, 128), np.float32))).astype(BF16NP),
        "rbig": (np.arange(128)[:, None] >= np.arange(896)[None, :] - 383)
                .astype(np.float32).astype(BF16NP),
        "mask16": np.kron(np.eye(8, dtype=np.float32),
                          np.ones((16, 16), np.float32)).astype(BF16NP),
        "sel2r": np.ascontiguousarray(np.concatenate(
            [np.concatenate([np.ones(64), np.zeros(64)]).reshape(1, 128),
             np.concatenate([np.zeros(64), np.ones(64)]).reshape(1, 128)]).astype(np.float32)),
        "sel33": _sel33(),
        "sel2c": np.ascontiguousarray(np.concatenate(
            [np.concatenate([np.ones(64), np.zeros(64)]).reshape(1, 128),
             np.concatenate([np.zeros(64), np.ones(64)]).reshape(1, 128)]).astype(np.float32).T
        ).astype(BF16NP),
    }
    x = np.asarray(inputs["L0_attn_x"], dtype=np.float32)
    x16 = x[:, 0:16, :]
    in_maps = []
    for c in range(B):
        m = dict(shared)
        m["x_fm"] = np.ascontiguousarray(x[c].T)
        x16r = np.roll(x16, -c, axis=0).reshape(128, D)
        m["x16_fm"] = np.ascontiguousarray(x16r.T)
        in_maps.append(m)
    return in_maps


def kernel(**inputs):
    in_maps = prepare_inputs(inputs)
    nc, _ = build_program()
    res = run_bass_kernel_spmd(nc, in_maps, core_ids=list(range(B)))
    out = np.stack([np.asarray(r["y"], dtype=np.float32).T for r in res.results])
    return out
